# version 45
# baseline (speedup 1.0000x reference)
import contextlib
import numpy as np
import concourse.bass as bass
import concourse.mybir as mybir
from concourse.bass_utils import run_bass_kernel_spmd

F32 = mybir.dt.float32
BF16 = mybir.dt.bfloat16
I32 = mybir.dt.int32
AF = mybir.ActivationFunctionType
ALU = mybir.AluOpType
AX = mybir.AxisListType.X

T = 2048
D = 1024
II = 2816
NDP = D // 128
NIP = II // 128
NTC = T // 512
NKT = T // 128
QMAX = 127.0
NEG = -1.0e30
N_CORES = 8

AR_BYTES = 204800
XA, XB = 0, 8192
XC, XD = 16384, 20480
FB, GB = 24576, 32768
XN = 16384
QIX = 16384
SILU, HPRE, HMC = 0, 8192, 16384
XQ = 40960
WD_STG = [118784, 130048, 141312]
WD_QB = [152576, 158208]
WG_STG = [40960, 45056, 49152]
WG_QB = [53248, 55296]
C_GQ = 52
W1, W2 = 73728, 118784
HQ_A, HQ_B = 0, 22528
EVT2, EVT2B, XRA2, XRB2 = 47104, 53248, 49152, 51200
CST2 = [55296 + 1024 * i for i in range(4)]
SQ2 = [59392 + 1024 * i for i in range(4)]
HSB3 = [118784, 141312]
XG_A, XG_B = 0, 16384
WD = 73728
XNT = 40960
V_ = 0
WQB = 73728
QT_ = 98304
KT_ = 131072
ATT = 32768
EVT, EVT_B, XRA, XRB = 98304, 100352, 102400, 104448
IDENT = 163840
IDENTB = 199904
MASKS = 164352
ONESBC = 172544
ONESM = 173056
ROWS = 173120
COLS = 197696
CCS = 198208
SCAL = 199232
WKS = 199360
WKS_B = 201408

R_MEAN, R_RSTD, R_G = 0, 2048, 4096
C_WGS, C_WUS, C_WDS = 0, 22, 44
C_GSC, C_USC, C_DSC = 52, 74, 96
C_RS, C_RH, C_SX127, C_SH = 104, 105, 106, 107
C_T0 = 108
S_SIGX, S_SIGD, S_GMAX, S_SX, S_RSX, S_SX127, S_SH, S_RSH, S_SH127, S_T0 = range(10)


class Sem:
    __slots__ = ("h", "n")

    def __init__(self, h):
        self.h = h
        self.n = 0


class Prog:
    ENGS = ("sync", "vector", "scalar", "tensor", "gpsimd")

    def __init__(self):
        self.items = {e: [] for e in self.ENGS}
        self._lastw = {}

    def emit(self, eng, fn):
        self.items[eng].append(fn)

    def dma(self, q, out_ap, in_ap, sem, slow=False):
        sem.n += 16
        self.emit(q, lambda e, o=out_ap, i=in_ap, h=sem.h, s=slow:
                  e.dma_start(out=o, in_=i, allow_slow_non_contiguous=s).then_inc(h, 16))

    def wait(self, eng, sem, n=None):
        v = sem.n if n is None else n
        if v <= 0:
            return
        key = (eng, id(sem))
        if self._lastw.get(key, -1) >= v:
            return
        self._lastw[key] = v
        self.emit(eng, lambda e, h=sem.h, v=v: e.wait_ge(h, v))

    def op(self, eng, fn, sem=None, inc=1, drain=True):
        if drain and eng in ("vector", "scalar", "gpsimd"):
            self.emit(eng, lambda e: e.drain())
        if sem is not None:
            sem.n += inc
            self.emit(eng, lambda e, fn=fn, h=sem.h, k=inc: fn(e).then_inc(h, k))
        else:
            self.emit(eng, fn)


def build_nc():
    nc = bass.Bass("TRN2", target_bir_lowering=False)

    xin = nc.dram_tensor("xin", [D, T], F32, kind="ExternalInput").ap()
    wqT = nc.dram_tensor("wqT", [D, D], F32, kind="ExternalInput").ap()
    wkT = nc.dram_tensor("wkT", [D, D], F32, kind="ExternalInput").ap()
    wvT = nc.dram_tensor("wvT", [D, D], F32, kind="ExternalInput").ap()
    wgO = nc.dram_tensor("wgO", [II, D], F32, kind="ExternalInput").ap()
    wuO = nc.dram_tensor("wuO", [II, D], F32, kind="ExternalInput").ap()
    wdO = nc.dram_tensor("wdO", [D, II], F32, kind="ExternalInput").ap()
    clips = nc.dram_tensor("clips", [1, 2], F32, kind="ExternalInput").ap()
    identin = nc.dram_tensor("identin", [128, 128], F32, kind="ExternalInput").ap()
    identbin = nc.dram_tensor("identbin", [128, 128], BF16, kind="ExternalInput").ap()
    onesbcin = nc.dram_tensor("onesbcin", [1, 128], F32, kind="ExternalInput").ap()
    onesmin = nc.dram_tensor("onesmin", [128, 1], BF16, kind="ExternalInput").ap()
    masksin = nc.dram_tensor("masksin", [128, T], F32, kind="ExternalInput").ap()

    outT = nc.dram_tensor("outT", [D, T], F32, kind="ExternalOutput").ap()

    xn_dram = nc.dram_tensor("xn_dram", [D, T], F32).ap()
    x1_dram = nc.dram_tensor("x1_dram", [D, T], F32).ap()
    x2_dram = nc.dram_tensor("x2_dram", [D, T], F32).ap()
    h_dram = nc.dram_tensor("h_dram", [128, NIP * T], F32).ap()
    wgq_dram = nc.dram_tensor("wgq_dram", [128, NDP * II], BF16).ap()
    wuq_dram = nc.dram_tensor("wuq_dram", [128, NDP * II], BF16).ap()
    wdq_dram = nc.dram_tensor("wdq_dram", [128, NIP * D], BF16).ap()
    ccio = [(nc.dram_tensor(f"cc{i}_in", [1, 128], F32).ap(),
             nc.dram_tensor(f"cc{i}_out", [1, 128], F32).ap()) for i in range(4)]

    es = contextlib.ExitStack()
    arena = es.enter_context(nc.sbuf_tensor("arena", [128, AR_BYTES], mybir.dt.uint8)).ap()
    AF32 = arena.bitcast(F32)
    ABF = arena.bitcast(BF16)
    AI32 = arena.bitcast(I32)

    def f32(off, p0, p1, e0, e1):
        return AF32[p0:p1, off // 4 + e0: off // 4 + e1]

    def bf(off, p0, p1, e0, e1):
        return ABF[p0:p1, off // 2 + e0: off // 2 + e1]

    def i32(off, p0, p1, e0, e1):
        return AI32[p0:p1, off // 4 + e0: off // 4 + e1]

    ident = f32(IDENT, 0, 128, 0, 128)
    identb = bf(IDENTB, 0, 128, 0, 128)
    onesbc = f32(ONESBC, 0, 1, 0, 128)
    onesm = bf(ONESM, 0, 128, 0, 1)

    def rows(slot, a, b):
        return f32(ROWS, 0, 1, slot + a, slot + b)

    def col(idx, k=1):
        return f32(COLS, 0, 128, idx, idx + k)

    def scal(idx, k=1):
        return f32(SCAL, 0, 1, idx, idx + k)

    pb = [es.enter_context(nc.psum_tensor(f"pb{i}", [128, 512], F32)) for i in range(8)]
    pbb = [t.ap().bitcast(BF16) for t in pb]

    dS = Sem(es.enter_context(nc.semaphore("dS")))
    dA = Sem(es.enter_context(nc.semaphore("dA")))
    dG = Sem(es.enter_context(nc.semaphore("dG")))
    cV = Sem(es.enter_context(nc.semaphore("cV")))
    cS = Sem(es.enter_context(nc.semaphore("cS")))
    cT = Sem(es.enter_context(nc.semaphore("cT")))
    cc = Sem(es.enter_context(nc.semaphore("cc")))
    dQ = [Sem(es.enter_context(nc.semaphore(f"dQ{i}"))) for i in range(3)]
    dL = [Sem(es.enter_context(nc.semaphore(f"dL{i}"))) for i in range(2)]
    dGs = [Sem(es.enter_context(nc.semaphore(f"dGs{i}"))) for i in range(2)]
    dT = [Sem(es.enter_context(nc.semaphore(f"dT{i}"))) for i in range(8)]
    dR = Sem(es.enter_context(nc.semaphore("dR")))
    dC = Sem(es.enter_context(nc.semaphore("dC")))
    dRg = Sem(es.enter_context(nc.semaphore("dRg")))
    dD = [Sem(es.enter_context(nc.semaphore(f"dD{i}"))) for i in range(2)]
    dW = Sem(es.enter_context(nc.semaphore("dW")))
    dP = [Sem(es.enter_context(nc.semaphore(f"dP{i}"))) for i in range(2)]

    p = Prog()
    bankg = [None] * 8

    def bank_wait(i):
        if bankg[i] is not None:
            p.wait("tensor", bankg[i][0], bankg[i][1])

    def bcast_scalar_to_col(s_ap, cidx):
        bank_wait(7)
        p.wait("tensor", cV)
        p.op("tensor", lambda e, s=s_ap: e.matmul(pb[7][:, 0:1], onesbc, s,
                                                  start=True, stop=True), cT)
        p.wait("vector", cT)
        p.op("vector", lambda e, c=col(cidx): e.tensor_copy(c, pb[7][:, 0:1]), cV)
        bankg[7] = (cV, cV.n)

    def bcast_rows_to_tiles(jobs, rows_mark=None):
        nj = len(jobs)
        first = True
        for j, (slot, dst) in enumerate(jobs):
            for ch in range(NTC):
                bk = 2 + ch
                bank_wait(bk)
                p.wait("tensor", cV, rows_mark)
                p.op("tensor", lambda e, slot=slot, ch=ch, bk=bk:
                     e.matmul(pb[bk][:, 0:512], onesbc,
                              rows(slot, ch * 512, (ch + 1) * 512),
                              start=True, stop=True),
                     cT if ch == NTC - 1 else None)
            p.wait("vector", cT)
            for ch in range(NTC):
                p.op("vector", lambda e, ch=ch, dst=dst: e.tensor_copy(
                    f32(dst, 0, 128, ch * 512, (ch + 1) * 512),
                    pb[2 + ch][:, 0:512]), cV, drain=first)
                first = False
            for ch in range(NTC):
                bankg[2 + ch] = (cV, cV.n)

    def ln_stats_gen(src_dram, src_guards=None,
                     bufs=(XA, XB, XC, XD, 24576, 28672)):
        for i in range(2, 6):
            bank_wait(i)
        ldd = {}
        consumed = {}

        BXA, BXB, BXC0, BXD0, BXC1, BXD1 = bufs

        def ld_stats(dp):
            if dp >= NDP:
                return
            buf = BXA if dp % 2 == 0 else BXB
            q = "sync" if dp % 2 == 0 else "scalar"
            d = dL[dp % 2]
            if dp - 2 in consumed:
                p.wait(q, cV, consumed[dp - 2][0])
                p.wait(q, cS, consumed[dp - 2][1])
            else:
                p.wait(q, cV)
                p.wait(q, cS)
            if src_guards is not None:
                g = src_guards.get(dp)
                if g is not None:
                    for gg in g:
                        p.wait(q, gg[0], gg[1])
            p.dma(q, f32(buf, 0, 128, 0, T), src_dram[dp * 128:(dp + 1) * 128, :], d)
            ldd[dp] = (d, d.n)

        ld_stats(0)
        ld_stats(1)
        smm = {}
        for dp in range(NDP):
            buf = BXA if dp % 2 == 0 else BXB
            BXC = BXC0 if dp % 2 == 0 else BXC1
            BXD = BXD0 if dp % 2 == 0 else BXD1
            p.wait("vector", ldd[dp][0], ldd[dp][1])
            p.wait("vector", cT, smm.get(dp - 2, 0))
            p.op("vector", lambda e, b=buf, BXC=BXC: e.tensor_copy(
                bf(BXC, 0, 128, 0, T), f32(b, 0, 128, 0, T)), cV, drain=False)
            castm = cV.n
            p.wait("scalar", ldd[dp][0], ldd[dp][1])
            p.wait("scalar", cT, smm.get(dp - 2, 0))
            p.op("scalar", lambda e, b=buf, BXD=BXD: e.activation(
                bf(BXD, 0, 128, 0, T), f32(b, 0, 128, 0, T), AF.Square), cS, drain=False)
            sqm = cS.n
            consumed[dp] = (castm, sqm)
            ld_stats(dp + 2)
            yield
            p.wait("tensor", cV, castm)
            p.wait("tensor", cS, sqm)
            for tc in range(NTC):
                st, sp = (dp == 0), (dp == NDP - 1)
                p.op("tensor", lambda e, tc=tc, st=st, sp=sp, BXC=BXC:
                     e.matmul(pb[2 + tc][0:1, :], onesm,
                              bf(BXC, 0, 128, tc * 512, (tc + 1) * 512),
                              start=st, stop=sp), None)
                p.op("tensor", lambda e, tc=tc, st=st, sp=sp, BXD=BXD:
                     e.matmul(pb[2 + tc][32:33, :], onesm,
                              bf(BXD, 0, 128, tc * 512, (tc + 1) * 512),
                              start=st, stop=sp),
                     cT if tc == NTC - 1 else None)
            smm[dp] = cT.n
        ln_stats_finish(smm[NDP - 1])
        yield

    def ln_stats_finish(mm_mark):
        p.wait("vector", cT, mm_mark)
        for tc in range(NTC):
            p.op("vector", lambda e, tc=tc: e.tensor_copy(
                rows(R_MEAN, tc * 512, (tc + 1) * 512), pb[2 + tc][0:1, :]), cV, drain=(tc == 0))
            p.op("vector", lambda e, tc=tc: e.tensor_copy(
                rows(R_RSTD, tc * 512, (tc + 1) * 512), pb[2 + tc][32:33, :]), cV, drain=False)
        for i in range(2, 6):
            bankg[i] = (cV, cV.n)
        p.wait("scalar", cV)
        p.op("scalar", lambda e: e.activation(rows(R_G, 0, T), rows(R_MEAN, 0, T), AF.Square), cS)
        p.wait("vector", cS)
        p.op("vector", lambda e: e.scalar_tensor_tensor(
            rows(R_RSTD, 0, T), rows(R_G, 0, T), -1.0, rows(R_RSTD, 0, T),
            ALU.mult, ALU.add), cV)
        p.op("vector", lambda e: e.tensor_scalar(
            rows(R_RSTD, 0, T), rows(R_RSTD, 0, T), 1e-5, None, ALU.add), cV)
        p.wait("scalar", cV)
        p.op("scalar", lambda e: e.activation(rows(R_RSTD, 0, T), rows(R_RSTD, 0, T),
                                              AF.Ln), cS)
        p.op("scalar", lambda e: e.activation(rows(R_RSTD, 0, T), rows(R_RSTD, 0, T),
                                              AF.Exp, scale=-0.5), cS)
        p.wait("vector", cS)
        p.op("vector", lambda e: e.scalar_tensor_tensor(
            rows(R_G, 0, T), rows(R_MEAN, 0, T), -1.0, rows(R_RSTD, 0, T),
            ALU.mult, ALU.mult), cV)
        bcast_rows_to_tiles([(R_RSTD, FB), (R_G, GB)], rows_mark=cV.n)

    def allreduce_scalar_max(col_ap, cc_pair, out_scal_idx):
        cin, cout = cc_pair
        stg = f32(CCS, 0, 1, 0, 128)
        p.wait("gpsimd", cV)
        p.dma("gpsimd", cin, col_ap, dC)
        p.wait("gpsimd", dC)
        p.op("gpsimd", lambda e: e.collective_compute(
            "AllReduce", ALU.max, replica_groups=[list(range(N_CORES))],
            ins=[cin], outs=[cout]), cc, drain=False)
        p.wait("gpsimd", cc)
        p.dma("gpsimd", stg, cout, dC)
        p.wait("vector", dC)
        p.op("vector", lambda e: e.reduce_max(scal(out_scal_idx), stg, axis=AX), cV)

    def qw_gen(w_dram, n_pt, width, c_slot, qdst_off, spill_dram, stgs, qbs,
               spill_guard=None, buf_guard=0, qb_buf_guard=0):
        nb = width // 128
        w3d = ABF[:, qdst_off // 2: qdst_off // 2 + nb * n_pt * 128].rearrange(
            "p (b i) -> p b i", b=nb)
        if spill_guard is not None:
            p.wait("scalar", spill_guard[0], spill_guard[1])
        qb_guard = {}
        q1_done = {}
        q2_done = {}
        ld = {}

        def load(pt):
            if pt >= n_pt:
                return
            q = "sync" if pt % 2 == 0 else "scalar"
            d = dQ[pt % 3]
            p.wait(q, cV, q2_done.get(pt - 3, buf_guard))
            p.dma(q, f32(stgs[pt % 3], 0, 128, 0, width),
                  w_dram[pt * 128:(pt + 1) * 128, :], d)
            ld[pt] = (d, d.n)

        def stage_a(pt):
            stg = stgs[pt % 3]
            rc = C_GQ + (pt % 2)
            p.wait("vector", ld[pt][0], ld[pt][1])
            p.op("vector", lambda e, s=stg, pt=pt: e.reduce_max(
                col(c_slot + pt), f32(s, 0, 128, 0, width), axis=AX,
                apply_absolute_value=True), cV, drain=(pt == 0))
            p.wait("vector", cS, q1_done.get(pt - 2, 0))
            p.op("vector", lambda e, pt=pt, rc=rc: e.reciprocal(
                col(rc), col(c_slot + pt)), cV)
            p.op("vector", lambda e, rc=rc: e.tensor_scalar(
                col(rc), col(rc), QMAX, None, ALU.mult), cV)
            p.wait("scalar", cV)
            p.op("scalar", lambda e, s=stg, rc=rc: e.activation(
                i32(s, 0, 128, 0, width), f32(s, 0, 128, 0, width),
                AF.Identity, scale=col(rc)), cS, drain=False)
            q1_done[pt] = cS.n

        def stage_b(pt):
            stg = stgs[pt % 3]
            qb = qbs[pt % 2]
            p.wait("vector", cS, q1_done[pt])
            p.wait("vector", cT, qb_guard.get(pt - 2, qb_buf_guard))
            p.op("vector", lambda e, s=stg, qb=qb: e.tensor_scalar(
                bf(qb, 0, 128, 0, width), i32(s, 0, 128, 0, width),
                -128.0, 127.0, ALU.max, ALU.min), cV, drain=False)
            q2_done[pt] = cV.n
            for g in range((nb + 7) // 8):
                bk = 6 + (g % 2)
                kblk = min(8, nb - g * 8)
                bank_wait(bk)
                p.wait("tensor", cV, q2_done[pt])
                for k in range(kblk):
                    wb = g * 8 + k
                    p.op("tensor", lambda e, wb=wb, k=k, bk=bk, qb=qb: e.transpose(
                        pbb[bk][:, k * 128:(k + 1) * 128],
                        bf(qb, 0, 128, wb * 128, (wb + 1) * 128), identb),
                         cT if k == kblk - 1 else None)
                p.wait("scalar", cT)
                psrc = pbb[bk][:, 0:kblk * 128].rearrange("p (b i) -> p b i", b=kblk)
                p.op("scalar", lambda e, g=g, pt=pt, bk=bk, kblk=kblk, psrc=psrc:
                     e.activation(w3d[:, g * 8:g * 8 + kblk, pt * 128:(pt + 1) * 128],
                                  psrc, AF.Identity), cS, drain=False)
                bankg[bk] = (cS, cS.n)
            qb_guard[pt] = cT.n

        load(0)
        load(1)
        for pt in range(n_pt):
            stage_a(pt)
            if pt > 0:
                stage_b(pt - 1)
            load(pt + 2)
            yield
        stage_b(n_pt - 1)
        p.wait("sync", cS)
        p.dma("sync", spill_dram[:, 0:n_pt * width], bf(qdst_off, 0, 128, 0, n_pt * width), dS)

    def qw_gen_psum(w_dram, n_pt, width, c_slot, spill_dram, stgs, qbs,
                    buf_guard=0):
        nb = width // 128
        qb_guard = {}
        q1_done = {}
        q2_done = {}
        ld = {}

        def load(pt):
            if pt >= n_pt:
                return
            q = "sync" if pt % 2 == 0 else "scalar"
            d = dQ[pt % 3]
            p.wait(q, cV, q2_done.get(pt - 3, buf_guard))
            p.dma(q, f32(stgs[pt % 3], 0, 128, 0, width),
                  w_dram[pt * 128:(pt + 1) * 128, :], d)
            ld[pt] = (d, d.n)

        def stage_a(pt):
            stg = stgs[pt % 3]
            rc = C_GQ + (pt % 2)
            p.wait("vector", ld[pt][0], ld[pt][1])
            p.op("vector", lambda e, s=stg, pt=pt: e.reduce_max(
                col(c_slot + pt), f32(s, 0, 128, 0, width), axis=AX,
                apply_absolute_value=True), cV, drain=(pt == 0))
            p.wait("vector", cS, q1_done.get(pt - 2, 0))
            p.op("vector", lambda e, pt=pt, rc=rc: e.reciprocal(
                col(rc), col(c_slot + pt)), cV)
            p.op("vector", lambda e, rc=rc: e.tensor_scalar(
                col(rc), col(rc), QMAX, None, ALU.mult), cV)
            p.wait("scalar", cV)
            p.op("scalar", lambda e, s=stg, rc=rc: e.activation(
                i32(s, 0, 128, 0, width), f32(s, 0, 128, 0, width),
                AF.Identity, scale=col(rc)), cS, drain=False)
            q1_done[pt] = cS.n

        def stage_b(pt):
            stg = stgs[pt % 3]
            qb = qbs[pt % 2]
            p.wait("vector", cS, q1_done[pt])
            p.wait("vector", cT, qb_guard.get(pt - 2, 0))
            p.op("vector", lambda e, s=stg, qb=qb: e.tensor_scalar(
                bf(qb, 0, 128, 0, width), i32(s, 0, 128, 0, width),
                -128.0, 127.0, ALU.max, ALU.min), cV, drain=False)
            q2_done[pt] = cV.n
            for g in range((nb + 7) // 8):
                bk = 6 + (g % 2)
                kblk = min(8, nb - g * 8)
                bank_wait(bk)
                p.wait("tensor", cV, q2_done[pt])
                for k in range(kblk):
                    wb = g * 8 + k
                    p.op("tensor", lambda e, wb=wb, k=k, bk=bk, qb=qb: e.transpose(
                        pbb[bk][:, k * 128:(k + 1) * 128],
                        bf(qb, 0, 128, wb * 128, (wb + 1) * 128), identb),
                         cT if k == kblk - 1 else None)
                p.wait("scalar", cT)
                slot = 59392 if g % 2 == 0 else 61440
                p.wait("scalar", dP[g % 2], dP[g % 2].n)
                p.op("scalar", lambda e, bk=bk, kblk=kblk, slot=slot: e.activation(
                    bf(slot, 0, 128, 0, kblk * 128), pbb[bk][:, 0:kblk * 128],
                    AF.Identity), cS, drain=False)
                bankg[bk] = (cS, cS.n)
                p.wait("gpsimd", cS)
                c0 = (pt * nb + g * 8) * 128
                p.dma("gpsimd", spill_dram[:, c0:c0 + kblk * 128],
                      bf(slot, 0, 128, 0, kblk * 128), dP[g % 2])
            qb_guard[pt] = cT.n

        load(0)
        load(1)
        for pt in range(n_pt):
            stage_a(pt)
            if pt > 0:
                stage_b(pt - 1)
            load(pt + 2)
            yield
        stage_b(n_pt - 1)

    def drive(*gens):
        live = list(gens)
        while live:
            for g in list(live):
                try:
                    next(g)
                except StopIteration:
                    live.remove(g)

    def drive_with(main, aux, aux_steps=2):
        while True:
            try:
                next(main)
            except StopIteration:
                return
            for _ in range(aux_steps):
                try:
                    next(aux)
                except StopIteration:
                    break

    def ffn(src_dram, dst_dram, ccA, ccB, wq_gens=None, stats_done=False,
            feed_stats=False):
        p.wait("vector", cT)
        p.wait("scalar", cT)
        p.wait("sync", dS)
        for s in dT + dD:
            p.wait("sync", s)
            p.wait("scalar", s)
            p.wait("vector", s)
        p.wait("scalar", dS)
        if not stats_done:
            drive(ln_stats_gen(src_dram))
        ldx = {}
        spg = {}
        xn_mark = [0]

        def ld_xn(dp):
            if dp >= NDP:
                return
            buf = XA if dp % 2 == 0 else XB
            q = "sync" if dp % 2 == 0 else "scalar"
            d = dL[dp % 2]
            if dp - 2 in spg:
                g = spg[dp - 2]
                p.wait(q, g[0], g[1])
            else:
                p.wait(q, cV)
                p.wait(q, cS)
            p.dma(q, f32(buf, 0, 128, 0, T), src_dram[dp * 128:(dp + 1) * 128, :], d)
            ldx[dp] = (d, d.n)

        def xn_gen():
            ld_xn(0)
            ld_xn(1)
            for dp in range(NDP):
                buf = XA if dp % 2 == 0 else XB
                p.wait("vector", ldx[dp][0], ldx[dp][1])
                p.op("vector", lambda e, b=buf: e.tensor_tensor(
                    f32(b, 0, 128, 0, T), f32(b, 0, 128, 0, T),
                    f32(FB, 0, 128, 0, T), ALU.mult), cV, drain=False)
                p.op("vector", lambda e, b=buf: e.tensor_tensor(
                    f32(b, 0, 128, 0, T), f32(b, 0, 128, 0, T),
                    f32(GB, 0, 128, 0, T), ALU.add), cV)
                p.op("vector", lambda e, dp=dp, b=buf: e.reduce_max(
                    col(C_T0 + dp), f32(b, 0, 128, 0, T), axis=AX,
                    apply_absolute_value=True), cV)
                p.wait("gpsimd", cV, cV.n)
                p.dma("gpsimd", xn_dram[dp * 128:(dp + 1) * 128, :], f32(buf, 0, 128, 0, T),
                      dGs[dp % 2])
                spg[dp] = (dGs[dp % 2], dGs[dp % 2].n)
                ld_xn(dp + 2)
                yield
            p.op("vector", lambda e: e.reduce_max(col(C_T0 + 8), col(C_T0, NDP), axis=AX), cV)
            xn_mark[0] = cV.n
            allreduce_scalar_max(col(C_T0 + 8), ccA, S_GMAX)
            yield

        if wq_gens is not None:
            drive(xn_gen(), wq_gens)
        else:
            drive(xn_gen())
        p.op("vector", lambda e: e.tensor_tensor(scal(S_SX), scal(S_SIGX), scal(S_GMAX), ALU.mult), cV)
        p.op("vector", lambda e: e.tensor_scalar(scal(S_SX), scal(S_SX), 1.0 / QMAX, None, ALU.mult), cV)
        p.op("vector", lambda e: e.reciprocal(scal(S_RSX), scal(S_SX)), cV)
        p.op("vector", lambda e: e.tensor_scalar(scal(S_SX127), scal(S_SX), 1.0 / QMAX, None, ALU.mult), cV)
        bcast_scalar_to_col(scal(S_RSX), C_RS)
        bcast_scalar_to_col(scal(S_SX127), C_SX127)
        p.wait("vector", cS)
        p.op("vector", lambda e: e.tensor_scalar(
            col(C_GSC, NIP), col(C_WGS, NIP), col(C_SX127), None, ALU.mult), cV)
        p.op("vector", lambda e: e.tensor_scalar(
            col(C_USC, NIP), col(C_WUS, NIP), col(C_SX127), None, ALU.mult), cV)
        p.wait("sync", dGs[0])
        p.wait("sync", dGs[1])
        p.wait("scalar", dGs[0])
        p.wait("scalar", dGs[1])
        xn3 = xn_dram.rearrange("(a p) t -> p a t", p=128)
        ldq = {}
        qdone = {}

        def ld_xq(u):
            if u >= NDP // 2:
                return
            g = XG_A if u % 2 == 0 else XG_B
            q = "sync" if u % 2 == 0 else "scalar"
            d = dL[u % 2]
            if u - 2 in qdone:
                p.wait(q, cV, qdone[u - 2])
            else:
                p.wait(q, cV, xn_mark[0])
            p.dma(q, f32(g, 0, 128, 0, 2 * T), xn3[:, 2 * u:2 * u + 2, :], d)
            ldq[u] = (d, d.n)

        ld_xq(0)
        ld_xq(1)
        p.wait("vector", dP[0])
        p.wait("vector", dP[1])
        for u in range(NDP // 2):
            g = XG_A if u % 2 == 0 else XG_B
            p.wait("scalar", ldq[u][0], ldq[u][1])
            p.wait("scalar", cV)
            p.op("scalar", lambda e, g=g: e.activation(
                i32(g, 0, 128, 0, 2 * T), f32(g, 0, 128, 0, 2 * T),
                AF.Identity, scale=col(C_RS)), cS, drain=False)
            p.wait("vector", cS)
            p.op("vector", lambda e, g=g, u=u: e.tensor_scalar(
                bf(XQ, 0, 128, 2 * u * T, (2 * u + 2) * T), i32(g, 0, 128, 0, 2 * T),
                -128.0, 127.0, ALU.max, ALU.min), cV, drain=False)
            qdone[u] = cV.n
            ld_xq(u + 2)
        p.wait("tensor", cV)
        p.wait("tensor", dS)
        p.wait("tensor", dR)
        p.wait("tensor", dRg)
        for ib in range(NIP):
            for i in range(4):
                bank_wait(i)
            for dp in range(NDP):
                for tc in range(NTC):
                    st, sp = (dp == 0), (dp == NDP - 1)
                    p.op("tensor", lambda e, ib=ib, dp=dp, tc=tc, st=st, sp=sp:
                         e.matmul(pb[tc][:, 0:512],
                                  bf(W1, 0, 128, dp * II + ib * 128, dp * II + (ib + 1) * 128),
                                  bf(XQ, 0, 128, dp * T + tc * 512, dp * T + (tc + 1) * 512),
                                  start=st, stop=sp),
                         cT if (sp and tc == NTC - 1) else None)
            gate_done = cT.n
            for i in range(4, 8):
                bank_wait(i)
            for dp in range(NDP):
                for tc in range(NTC):
                    st, sp = (dp == 0), (dp == NDP - 1)
                    p.op("tensor", lambda e, ib=ib, dp=dp, tc=tc, st=st, sp=sp:
                         e.matmul(pb[4 + tc][:, 0:512],
                                  bf(W2, 0, 128, dp * II + ib * 128, dp * II + (ib + 1) * 128),
                                  bf(XQ, 0, 128, dp * T + tc * 512, dp * T + (tc + 1) * 512),
                                  start=st, stop=sp),
                         cT if (sp and tc == NTC - 1) else None)
            up_done = cT.n
            p.wait("scalar", cT, gate_done)
            p.wait("scalar", cV)
            for tc in range(NTC):
                p.op("scalar", lambda e, ib=ib, tc=tc: e.activation(
                    f32(SILU, 0, 128, tc * 512, (tc + 1) * 512), pb[tc][:, 0:512],
                    AF.Silu, scale=col(C_GSC + ib)), cS)
            for i in range(4):
                bankg[i] = (cS, cS.n)
            p.wait("vector", cT, up_done)
            p.wait("vector", cS)
            p.wait("vector", dG)
            for tc in range(NTC):
                p.op("vector", lambda e, ib=ib, tc=tc: e.scalar_tensor_tensor(
                    f32(HPRE, 0, 128, tc * 512, (tc + 1) * 512), pb[4 + tc][:, 0:512],
                    col(C_USC + ib), f32(SILU, 0, 128, tc * 512, (tc + 1) * 512),
                    ALU.mult, ALU.mult), cV)
                p.op("vector", lambda e, ib=ib, tc=tc: e.reduce_max(
                    f32(HMC, 0, 128, ib * NTC + tc, ib * NTC + tc + 1),
                    f32(HPRE, 0, 128, tc * 512, (tc + 1) * 512),
                    axis=AX, apply_absolute_value=True), cV)
                p.wait("gpsimd", cV)
                p.dma("gpsimd", h_dram[:, (tc * NIP + ib) * 512:(tc * NIP + ib + 1) * 512],
                      f32(HPRE, 0, 128, tc * 512, (tc + 1) * 512), dG)
            for i in range(4, 8):
                bankg[i] = (cV, cV.n)
        p.op("vector", lambda e: e.reduce_max(col(C_T0), f32(HMC, 0, 128, 0, NIP * NTC), axis=AX), cV)
        p.wait("scalar", cT)
        p.wait("scalar", dG)
        uld = {}
        p.dma("scalar", f32(HSB3[0], 0, 128, 0, 5632), h_dram[:, 0:5632], dQ[0])
        uld[0] = (dQ[0], dQ[0].n)
        p.dma("scalar", f32(HSB3[1], 0, 128, 0, 5632), h_dram[:, 5632:11264], dQ[1])
        uld[1] = (dQ[1], dQ[1].n)
        allreduce_scalar_max(col(C_T0), ccB, S_GMAX)
        p.op("vector", lambda e: e.tensor_tensor(scal(S_SH), scal(S_SIGD), scal(S_GMAX), ALU.mult), cV)
        p.op("vector", lambda e: e.tensor_scalar(scal(S_SH), scal(S_SH), 1.0 / QMAX, None, ALU.mult), cV)
        p.op("vector", lambda e: e.reciprocal(scal(S_RSH), scal(S_SH)), cV)
        p.op("vector", lambda e: e.tensor_scalar(scal(S_SH127), scal(S_SH), 0.5 / QMAX, None, ALU.mult), cV)
        bcast_scalar_to_col(scal(S_RSH), C_RH)
        bcast_scalar_to_col(scal(S_SH127), C_SH)
        p.op("vector", lambda e: e.tensor_scalar(
            col(C_DSC, NDP), col(C_WDS, NDP), col(C_SH), None, ALU.mult), cV)
        rh_mark = cV.n
        p.wait("sync", cT)
        p.wait("sync", dS)
        p.wait("sync", dP[0])
        p.wait("sync", dP[1])
        p.dma("sync", bf(WD, 0, 128, 0, NIP * D), wdq_dram[:, 0:NIP * D], dR)
        p.wait("tensor", dR)
        units = [(tc, hf) for tc in range(NTC) for hf in range(2)]
        uq_done = {}
        dn_store = {}
        sm_mm = {}
        sm_cast = {}
        sm_sq = {}

        def emit_sm_mms(g):
            if g < 0 or g not in sm_cast:
                return
            tc_g, hb_g = g // NDP, g % NDP
            if hb_g == 0:
                bank_wait(2 + tc_g)
            p.wait("tensor", cV, sm_cast[g])
            p.wait("tensor", cS, sm_sq[g])
            st2, sp2 = (hb_g == 0), (hb_g == NDP - 1)
            cstb, sqb = CST2[g % 4], SQ2[g % 4]
            p.op("tensor", lambda e, tc_g=tc_g, cstb=cstb, st2=st2, sp2=sp2:
                 e.matmul(pb[2 + tc_g][0:1, :], onesm,
                          bf(cstb, 0, 128, 0, 512), start=st2, stop=sp2), None)
            p.op("tensor", lambda e, tc_g=tc_g, sqb=sqb, st2=st2, sp2=sp2:
                 e.matmul(pb[2 + tc_g][32:33, :], onesm,
                          bf(sqb, 0, 128, 0, 512), start=st2, stop=sp2), cT)
            sm_mm[g] = cT.n

        def emit_unit_load(ui):
            if ui >= len(units):
                return
            tc, hf = units[ui]
            g = HSB3[ui % 2]
            p.wait("scalar", cV, uq_done[ui - 2])
            p.dma("scalar", f32(g, 0, 128, 0, 5632),
                  h_dram[:, (tc * NIP + hf * 11) * 512:(tc * NIP + hf * 11) * 512 + 5632],
                  dQ[ui % 2])
            uld[ui] = (dQ[ui % 2], dQ[ui % 2].n)

        def emit_unit_quant(ui):
            tc, hf = units[ui]
            g = HSB3[ui % 2]
            hq = HQ_A if tc % 2 == 0 else HQ_B
            p.wait("scalar", uld[ui][0], uld[ui][1])
            p.wait("scalar", cV, rh_mark)
            p.op("scalar", lambda e, g=g: e.activation(
                i32(g, 0, 128, 0, 5632), f32(g, 0, 128, 0, 5632),
                AF.Identity, scale=col(C_RH)), cS, drain=False)
            q1_mark = cS.n
            p.wait("vector", cS, q1_mark)
            p.op("vector", lambda e, g=g, hq=hq, hf=hf: e.tensor_scalar(
                bf(hq, 0, 128, hf * 5632, (hf + 1) * 5632), i32(g, 0, 128, 0, 5632),
                -128.0, 127.0, ALU.max, ALU.min), cV, drain=False)
            uq_done[ui] = cV.n

        p.wait("vector", dP[0])
        p.wait("vector", dP[1])
        p.wait("scalar", dP[0])
        p.wait("scalar", dP[1])
        emit_unit_quant(0)
        emit_unit_quant(1)
        for tc in range(NTC):
            if 2 * tc + 2 < len(units):
                emit_unit_load(2 * tc + 2)
                emit_unit_load(2 * tc + 3)
                emit_unit_quant(2 * tc + 2)
                emit_unit_quant(2 * tc + 3)
            hq = HQ_A if tc % 2 == 0 else HQ_B
            for hb in range(NDP):
                bk = (tc * NDP + hb) % 2
                bank_wait(bk)
                for ip in range(NIP):
                    st, sp = (ip == 0), (ip == NIP - 1)
                    if ip == 0:
                        p.wait("tensor", cV, uq_done[2 * tc])
                    elif ip == 11:
                        p.wait("tensor", cV, uq_done[2 * tc + 1])
                    p.op("tensor", lambda e, ip=ip, hb=hb, hq=hq, bk=bk, st=st, sp=sp:
                         e.matmul(pb[bk][:, 0:512],
                                  bf(WD, 0, 128, (hb * NIP + ip) * 128,
                                     (hb * NIP + ip + 1) * 128),
                                  bf(hq, 0, 128, ip * 512, (ip + 1) * 512),
                                  start=st, stop=sp),
                         cT if sp else None)
                g_i = tc * NDP + hb
                xr = XRA2 if g_i % 2 == 0 else XRB2
                ev2 = EVT2 if g_i % 2 == 0 else EVT2B
                dxr = dL[g_i % 2]
                p.wait("sync", cS)
                p.dma("sync", f32(xr, 0, 128, 0, 512),
                      src_dram[hb * 128:(hb + 1) * 128, tc * 512:(tc + 1) * 512], dxr)
                xr_mark = dxr.n
                p.wait("scalar", cT)
                p.wait("scalar", dxr, xr_mark)
                p.wait("scalar", cV)
                if g_i - 2 in dn_store:
                    s_prev = dn_store[g_i - 2][0]
                    p.wait("scalar", s_prev, s_prev.n)
                p.op("scalar", lambda e, bk=bk, hb=hb, ev2=ev2: e.activation(
                    f32(ev2, 0, 128, 0, 512), pb[bk][:, 0:512], AF.Identity,
                    scale=col(C_DSC + hb)), cS)
                bankg[bk] = (cS, cS.n)
                p.wait("vector", cS)
                p.op("vector", lambda e, xr=xr, ev2=ev2: e.tensor_tensor(
                    f32(ev2, 0, 128, 0, 512), f32(ev2, 0, 128, 0, 512),
                    f32(xr, 0, 128, 0, 512), ALU.add), cV)
                add_mark = cV.n
                p.wait("sync", cV)
                p.dma("sync", dst_dram[hb * 128:(hb + 1) * 128, tc * 512:(tc + 1) * 512],
                      f32(ev2, 0, 128, 0, 512), dD[g_i % 2])
                dn_store[g_i] = (dD[g_i % 2], dD[g_i % 2].n)
                if feed_stats:
                    emit_sm_mms(g_i - 4)
                    cstb = CST2[g_i % 4]
                    sqb = SQ2[g_i % 4]
                    if g_i - 4 in sm_mm:
                        p.wait("vector", cT, sm_mm[g_i - 4])
                    p.op("vector", lambda e, ev2=ev2, cstb=cstb: e.tensor_copy(
                        bf(cstb, 0, 128, 0, 512), f32(ev2, 0, 128, 0, 512)),
                        cV, drain=True)
                    sm_cast[g_i] = cV.n
                    p.wait("scalar", cV, add_mark)
                    if g_i - 4 in sm_mm:
                        p.wait("scalar", cT, sm_mm[g_i - 4])
                    p.op("scalar", lambda e, ev2=ev2, sqb=sqb: e.activation(
                        bf(sqb, 0, 128, 0, 512), f32(ev2, 0, 128, 0, 512),
                        AF.Square), cS, drain=False)
                    sm_sq[g_i] = cS.n

        if feed_stats:
            for g in range(NTC * NDP - 4, NTC * NDP):
                emit_sm_mms(g)
        return sm_mm.get(NTC * NDP - 1)

    def attention(src_dram, dst_dram, stats2_gen=None, stats2_guards=None,
                  stats_premixed=None):
        p.wait("vector", cT)
        p.wait("scalar", cT)
        p.wait("sync", dS)
        for s in dT + dD:
            p.wait("sync", s)
            p.wait("scalar", s)
            p.wait("vector", s)
        p.wait("scalar", dS)
        if stats_premixed is not None:
            ln_stats_finish(stats_premixed)
        else:
            drive(ln_stats_gen(src_dram))
        for dp in range(NDP):
            buf = XA if dp % 2 == 0 else XB
            dxn = dL[dp % 2]
            p.wait("sync", cV)
            p.wait("sync", cS)
            p.dma("sync", f32(buf, 0, 128, 0, T), src_dram[dp * 128:(dp + 1) * 128, :], dxn)
            p.wait("vector", dxn, dxn.n)
            p.op("vector", lambda e, b=buf: e.tensor_tensor(
                f32(b, 0, 128, 0, T), f32(b, 0, 128, 0, T),
                f32(FB, 0, 128, 0, T), ALU.mult), cV, drain=False)
            p.op("vector", lambda e, dp=dp, b=buf: e.tensor_tensor(
                bf(XNT, 0, 128, dp * T, (dp + 1) * T), f32(b, 0, 128, 0, T),
                f32(GB, 0, 128, 0, T), ALU.add), cV)

        def proj_o(w_dram, dst_off, sc):
            p.wait("gpsimd", cT)
            for dp in range(NDP):
                p.dma("gpsimd", bf(WQB, 0, 128, dp * D, (dp + 1) * D),
                      w_dram[dp * 128:(dp + 1) * 128, :], dW)
            p.wait("tensor", dW)
            p.wait("tensor", cV)
            for ob in range(NDP):
                for tc in range(NTC):
                    bk = (ob * NTC + tc) % 2
                    bank_wait(bk)
                    for dp in range(NDP):
                        st, sp = (dp == 0), (dp == NDP - 1)
                        p.op("tensor", lambda e, dp=dp, ob=ob, tc=tc, bk=bk, st=st, sp=sp:
                             e.matmul(pb[bk][:, 0:512],
                                      bf(WQB, 0, 128, dp * D + ob * 128, dp * D + (ob + 1) * 128),
                                      bf(XNT, 0, 128, dp * T + tc * 512, dp * T + (tc + 1) * 512),
                                      start=st, stop=sp),
                             cT if sp else None)
                    p.wait("scalar", cT)
                    p.op("scalar", lambda e, ob=ob, tc=tc, bk=bk, dst=dst_off, sc=sc:
                         e.activation(bf(dst, 0, 128, ob * T + tc * 512, ob * T + (tc + 1) * 512),
                                      pb[bk][:, 0:512], AF.Identity, scale=sc), cS)
                    bankg[bk] = (cS, cS.n)

        def proj_v(w_dram, dst_off):
            p.wait("gpsimd", cT)
            for dp in range(NDP):
                p.dma("gpsimd", bf(WQB, 0, 128, dp * D, (dp + 1) * D),
                      w_dram[dp * 128:(dp + 1) * 128, :], dW)
            p.wait("tensor", dW)
            p.wait("tensor", cV)
            for tb in range(NKT):
                for oc in range(2):
                    bk = (tb * 2 + oc) % 2
                    bank_wait(bk)
                    for dp in range(NDP):
                        st, sp = (dp == 0), (dp == NDP - 1)
                        p.op("tensor", lambda e, dp=dp, tb=tb, oc=oc, bk=bk, st=st, sp=sp:
                             e.matmul(pb[bk][:, 0:512],
                                      bf(XNT, 0, 128, dp * T + tb * 128, dp * T + (tb + 1) * 128),
                                      bf(WQB, 0, 128, dp * D + oc * 512, dp * D + (oc + 1) * 512),
                                      start=st, stop=sp),
                             cT if sp else None)
                    p.wait("scalar", cT)
                    p.op("scalar", lambda e, tb=tb, oc=oc, bk=bk, dst=dst_off:
                         e.activation(bf(dst, 0, 128, tb * D + oc * 512, tb * D + (oc + 1) * 512),
                                      pb[bk][:, 0:512], AF.Identity, scale=1.0), cS)
                    bankg[bk] = (cS, cS.n)

        proj_o(wqT, QT_, 1.0 / 32.0)
        proj_o(wkT, KT_, 1.0)
        proj_v(wvT, V_)

        mm_done = {}
        exp_done = {}
        wks_of = {}

        def emit_scores_mm(kt):
            jd = kt // 4
            base = 0 if kt % 2 == 0 else 4
            for b2 in range(base, base + 4):
                bank_wait(b2)
            p.wait("tensor", cS, exp_done.get(kt - 2, 0))
            p.wait("tensor", cV, norm_done.get(kt - 2, 0))
            for j in range(jd, NTC):
                bk = base + (j - jd)
                for dp in range(NDP):
                    st, sp = (dp == 0), (dp == NDP - 1)
                    p.op("tensor", lambda e, kt=kt, j=j, dp=dp, bk=bk, st=st, sp=sp:
                         e.matmul(pb[bk][:, 0:512],
                                  bf(KT_, 0, 128, dp * T + kt * 128, dp * T + (kt + 1) * 128),
                                  bf(QT_, 0, 128, dp * T + j * 512, dp * T + (j + 1) * 512),
                                  start=st, stop=sp),
                         cT if sp else None)
            mm_done[kt] = cT.n

        norm_done = {}

        def emit_softmax(kt):
            jd, m = kt // 4, kt % 4
            nch = NTC - jd
            base = 0 if kt % 2 == 0 else 4
            wks = WKS if kt % 2 == 0 else WKS_B
            c0 = C_T0 if kt % 2 == 0 else C_T0 + 10
            cb_, cs_ = c0 + 4, c0 + 5
            cr_ = c0 + 9
            wks_of[kt] = wks
            p.wait("vector", cT, mm_done[kt])
            p.wait("vector", cS, exp_done.get(kt - 2, 0))
            p.op("vector", lambda e, m=m, bk=base, wks=wks: e.tensor_tensor(
                f32(wks, 0, 128, 0, 512), pb[bk][:, 0:512],
                f32(MASKS, 0, 128, m * 512, (m + 1) * 512), ALU.add), cV)
            p.op("vector", lambda e, wks=wks, c0=c0: e.reduce_max(
                col(c0), f32(wks, 0, 128, 0, 512), axis=AX, negate=True), cV)
            for j in range(jd + 1, NTC):
                p.op("vector", lambda e, j=j, jd=jd, bk=base + (j - jd), c0=c0: e.reduce_max(
                    col(c0 + j - jd), pb[bk][:, 0:512], axis=AX, negate=True), cV,
                    drain=False)
            if nch > 1:
                p.op("vector", lambda e, n=nch, c0=c0, cb_=cb_: e.tensor_reduce(
                    col(cb_), col(c0, n), axis=AX, op=ALU.min), cV)
            else:
                p.op("vector", lambda e, c0=c0, cb_=cb_: e.tensor_copy(col(cb_), col(c0)), cV)
            p.wait("scalar", cV)
            for j in range(jd, NTC):
                p.op("scalar", lambda e, kt=kt, j=j, jd=jd, bk=base + (j - jd), wks=wks,
                     cb_=cb_, cs_=cs_:
                     e.activation(bf(ATT, 0, 128, kt * T + j * 512, kt * T + (j + 1) * 512),
                                  (f32(wks, 0, 128, 0, 512) if j == jd else pb[bk][:, 0:512]),
                                  AF.Exp, bias=col(cb_),
                                  accum_out=col(cs_ + j - jd)), cS, drain=False)
            exp_done[kt] = cS.n
            p.wait("vector", cS)
            if nch > 1:
                p.op("vector", lambda e, n=nch, cs_=cs_, cr_=cr_: e.tensor_reduce(
                    col(cr_), col(cs_, n), axis=AX, op=ALU.add), cV)
            else:
                p.op("vector", lambda e, cs_=cs_, cr_=cr_: e.tensor_copy(col(cr_), col(cs_)), cV)
            p.op("vector", lambda e, cr_=cr_: e.reciprocal(col(cr_), col(cr_)), cV)
            p.op("vector", lambda e, kt=kt, jd=jd, cr_=cr_: e.tensor_scalar(
                bf(ATT, 0, 128, kt * T + jd * 512, (kt + 1) * T),
                bf(ATT, 0, 128, kt * T + jd * 512, (kt + 1) * T),
                col(cr_), None, ALU.mult), cV)
            norm_done[kt] = cV.n

        emit_scores_mm(0)
        for kt in range(1, NKT):
            emit_scores_mm(kt)
            emit_softmax(kt - 1)
        emit_softmax(NKT - 1)
        for i in range(8):
            bankg[i] = (cV, norm_done[NKT - 1])
        groups = [(db, j) for db in range(NDP) for j in range(NTC)]
        xl_done = {}
        ev_done = {}
        st_done = {}

        def ld_av(gi):
            if gi >= len(groups):
                return
            db, j = groups[gi]
            xr = XRA if gi % 2 == 0 else XRB
            p.wait("gpsimd", cT, mm_done[NKT - 1])
            p.wait("gpsimd", cV, ev_done.get(gi - 2, 0))
            p.dma("gpsimd", f32(xr, 0, 128, 0, 512),
                  src_dram[db * 128:(db + 1) * 128, j * 512:(j + 1) * 512], dGs[gi % 2])
            xl_done[gi] = (dGs[gi % 2], dGs[gi % 2].n)

        ld_av(0)
        ld_av(1)
        s2_steps = 0
        for gi, (db, j) in enumerate(groups):
            bk = gi % 2
            bank_wait(bk)
            nkt = min(4 * j + 4, NKT)
            for kt in range(nkt):
                st, sp = (kt == 0), (kt == nkt - 1)
                p.op("tensor", lambda e, kt=kt, db=db, j=j, bk=bk, st=st, sp=sp:
                     e.matmul(pb[bk][:, 0:512],
                              bf(V_, 0, 128, kt * D + db * 128, kt * D + (db + 1) * 128),
                              bf(ATT, 0, 128, kt * T + j * 512, kt * T + (j + 1) * 512),
                              start=st, stop=sp),
                     cT if sp else None)
            xr = XRA if gi % 2 == 0 else XRB
            ev = EVT if gi % 2 == 0 else EVT_B
            p.wait("vector", cT)
            p.wait("vector", xl_done[gi][0], xl_done[gi][1])
            if gi - 2 in st_done:
                s_prev = st_done[gi - 2][0]
                p.wait("vector", s_prev, s_prev.n)
            p.op("vector", lambda e, bk=bk, xr=xr, ev=ev: e.tensor_tensor(
                f32(ev, 0, 128, 0, 512), pb[bk][:, 0:512],
                f32(xr, 0, 128, 0, 512), ALU.add), cV)
            bankg[bk] = (cV, cV.n)
            ev_done[gi] = cV.n
            p.wait("sync", cV)
            p.dma("sync", dst_dram[db * 128:(db + 1) * 128, j * 512:(j + 1) * 512],
                  f32(ev, 0, 128, 0, 512), dT[db])
            st_done[gi] = (dT[db], dT[db].n)
            ld_av(gi + 2)
            if j == NTC - 1:
                if stats2_guards is not None:
                    stats2_guards[db] = [(dT[db], dT[db].n)]
                if stats2_gen is not None and db >= 2:
                    try:
                        next(stats2_gen)
                        s2_steps += 1
                    except StopIteration:
                        pass

    p.dma("sync", ident, identin, dS)
    p.dma("sync", identb, identbin, dS)
    p.dma("sync", onesbc, onesbcin, dS)
    p.dma("sync", onesm, onesmin, dS)
    p.dma("sync", f32(MASKS, 0, 128, 0, T), masksin, dS)
    p.dma("sync", scal(S_T0, 2), clips, dS)
    p.wait("scalar", dS)
    p.op("scalar", lambda e: e.activation(scal(S_SIGX, 2), scal(S_T0, 2), AF.Sigmoid), cS)
    p.wait("vector", dS)

    def wq_chain():
        g0 = cV.n
        yield from qw_gen(wgO, NIP, D, C_WGS, W1, wgq_dram, WG_STG, WG_QB,
                          None, buf_guard=g0)
        g1 = cV.n
        yield from qw_gen_psum(wdO, NDP, II, C_WDS, wdq_dram, WD_STG, WD_QB,
                               buf_guard=g1)
        g2 = cV.n
        g2t = cT.n
        yield from qw_gen(wuO, NIP, D, C_WUS, W2, wuq_dram, WG_STG, WG_QB,
                          (cV, g2), buf_guard=g2, qb_buf_guard=g2t)

    smm1 = ffn(xin, x1_dram, ccio[0], ccio[1], wq_gens=wq_chain(),
               feed_stats=True)
    s2_guards = {}
    s2 = ln_stats_gen(x2_dram, src_guards=s2_guards,
                      bufs=(106496, 114688, 122880, 126976, 131072, 135168))
    attention(x1_dram, x2_dram, stats2_gen=s2, stats2_guards=s2_guards,
              stats_premixed=smm1)
    drive(s2)

    p.wait("gpsimd", cT)
    p.wait("gpsimd", cV)
    p.wait("gpsimd", cS)
    p.dma("gpsimd", bf(W1, 0, 128, 0, NDP * II), wgq_dram[:, 0:NDP * II], dRg)
    p.dma("gpsimd", bf(W2, 0, 128, 0, NDP * II), wuq_dram[:, 0:NDP * II], dRg)

    ffn(x2_dram, outT, ccio[2], ccio[3], stats_done=True)

    p.wait("sync", dS)
    p.wait("sync", dA)
    p.wait("sync", dG)
    for s in dQ + dL + dGs + dT + dD + dP + [dR, dRg, dW, dC]:
        p.wait("sync", s)
    p.wait("sync", cV)
    p.wait("sync", cS)
    p.wait("sync", cT)
    p.wait("sync", cc)

    with nc.Block() as block:
        for eng in Prog.ENGS:
            fns = p.items[eng]
            getattr(block, eng)(lambda e, fns=fns: [fn(e) for fn in fns])

    return nc


_NC_CACHE = {}


def kernel(L0_attn_x, L0_attn_W_q, L0_attn_W_k, L0_attn_W_v,
           L0_ffn_clip_x, L0_ffn_clip_down, L0_ffn_w_gate, L0_ffn_w_up,
           L0_ffn_w_down):
    import ml_dtypes
    x = np.asarray(L0_attn_x)
    wq = np.ascontiguousarray(np.asarray(L0_attn_W_q).T)
    wk = np.ascontiguousarray(np.asarray(L0_attn_W_k).T)
    wv = np.ascontiguousarray(np.asarray(L0_attn_W_v).T)
    wg = np.ascontiguousarray(np.asarray(L0_ffn_w_gate))
    wu = np.ascontiguousarray(np.asarray(L0_ffn_w_up))
    wd = np.ascontiguousarray(np.asarray(L0_ffn_w_down))
    clips = np.array([[np.asarray(L0_ffn_clip_x).reshape(-1)[0],
                       np.asarray(L0_ffn_clip_down).reshape(-1)[0]]], dtype=np.float32)
    ident = np.eye(128, dtype=np.float32)
    identb = np.eye(128, dtype=ml_dtypes.bfloat16)
    onesbc = np.ones((1, 128), dtype=np.float32)
    onesm = np.full((128, 1), 1.0 / 1024.0, dtype=ml_dtypes.bfloat16)
    r = np.arange(512)[None, :]
    pp = np.arange(128)[:, None]
    masks = np.concatenate(
        [np.where(r >= 128 * m + pp, 0.0, NEG).astype(np.float32) for m in range(4)],
        axis=1)

    if "nc" not in _NC_CACHE:
        _NC_CACHE["nc"] = build_nc()
    nc = _NC_CACHE["nc"]

    in_maps = []
    for b in range(N_CORES):
        in_maps.append({
            "xin": np.ascontiguousarray(x[b].T),
            "wqT": wq, "wkT": wk, "wvT": wv,
            "wgO": wg, "wuO": wu, "wdO": wd,
            "clips": clips, "identin": ident, "identbin": identb, "onesbcin": onesbc,
            "onesmin": onesm, "masksin": masks,
        })
    res = run_bass_kernel_spmd(nc, in_maps, core_ids=list(range(N_CORES)))
    out = np.stack([np.ascontiguousarray(res.results[b]["outT"].T)
                    for b in range(N_CORES)], axis=0)
    return out


# version 46
# speedup vs baseline: 1.0085x; 1.0085x over previous
import contextlib
import numpy as np
import concourse.bass as bass
import concourse.mybir as mybir
from concourse.bass_utils import run_bass_kernel_spmd

F32 = mybir.dt.float32
BF16 = mybir.dt.bfloat16
I32 = mybir.dt.int32
AF = mybir.ActivationFunctionType
ALU = mybir.AluOpType
AX = mybir.AxisListType.X

T = 2048
D = 1024
II = 2816
NDP = D // 128
NIP = II // 128
NTC = T // 512
NKT = T // 128
QMAX = 127.0
NEG = -1.0e30
N_CORES = 8

AR_BYTES = 204800
XA, XB = 0, 8192
XC, XD = 16384, 20480
FB, GB = 24576, 32768
XN = 16384
QIX = 16384
SILU, HPRE, HMC = 0, 8192, 16384
XQ = 40960
WD_STG = [118784, 130048, 141312]
WD_QB = [152576, 158208]
WG_STG = [40960, 45056, 49152]
WG_QB = [53248, 55296]
C_GQ = 52
W1, W2 = 73728, 118784
HQ_A, HQ_B = 0, 22528
EVT2, EVT2B, XRA2, XRB2 = 47104, 53248, 49152, 51200
CST2 = [55296 + 1024 * i for i in range(4)]
SQ2 = [59392 + 1024 * i for i in range(4)]
HSB3 = [118784, 141312]
XG_A, XG_B = 0, 16384
WD = 73728
XNT = 40960
V_ = 0
WQB = 73728
QT_ = 98304
KT_ = 131072
ATT = 32768
EVT, EVT_B, XRA, XRB = 98304, 100352, 102400, 104448
IDENT = 163840
IDENTB = 199904
MASKS = 164352
ONESBC = 172544
ONESM = 173056
ROWS = 173120
COLS = 197696
CCS = 198208
SCAL = 199232
WKS = 199360
WKS_B = 201408

R_MEAN, R_RSTD, R_G = 0, 2048, 4096
C_WGS, C_WUS, C_WDS = 0, 22, 44
C_GSC, C_USC, C_DSC = 52, 74, 96
C_RS, C_RH, C_SX127, C_SH = 104, 105, 106, 107
C_T0 = 108
S_SIGX, S_SIGD, S_GMAX, S_SX, S_RSX, S_SX127, S_SH, S_RSH, S_SH127, S_T0 = range(10)


class Sem:
    __slots__ = ("h", "n")

    def __init__(self, h):
        self.h = h
        self.n = 0


class Prog:
    ENGS = ("sync", "vector", "scalar", "tensor", "gpsimd")

    def __init__(self):
        self.items = {e: [] for e in self.ENGS}
        self._lastw = {}

    def emit(self, eng, fn):
        self.items[eng].append(fn)

    def dma(self, q, out_ap, in_ap, sem, slow=False):
        sem.n += 16
        self.emit(q, lambda e, o=out_ap, i=in_ap, h=sem.h, s=slow:
                  e.dma_start(out=o, in_=i, allow_slow_non_contiguous=s).then_inc(h, 16))

    def wait(self, eng, sem, n=None):
        v = sem.n if n is None else n
        if v <= 0:
            return
        key = (eng, id(sem))
        if self._lastw.get(key, -1) >= v:
            return
        self._lastw[key] = v
        self.emit(eng, lambda e, h=sem.h, v=v: e.wait_ge(h, v))

    def op(self, eng, fn, sem=None, inc=1, drain=True):
        if drain and eng in ("vector", "scalar", "gpsimd"):
            self.emit(eng, lambda e: e.drain())
        if sem is not None:
            sem.n += inc
            self.emit(eng, lambda e, fn=fn, h=sem.h, k=inc: fn(e).then_inc(h, k))
        else:
            self.emit(eng, fn)


def build_nc():
    nc = bass.Bass("TRN2", target_bir_lowering=False)

    xin = nc.dram_tensor("xin", [D, T], F32, kind="ExternalInput").ap()
    wqT = nc.dram_tensor("wqT", [D, D], F32, kind="ExternalInput").ap()
    wkT = nc.dram_tensor("wkT", [D, D], F32, kind="ExternalInput").ap()
    wvT = nc.dram_tensor("wvT", [D, D], F32, kind="ExternalInput").ap()
    wgO = nc.dram_tensor("wgO", [II, D], F32, kind="ExternalInput").ap()
    wuO = nc.dram_tensor("wuO", [II, D], F32, kind="ExternalInput").ap()
    wdO = nc.dram_tensor("wdO", [D, II], F32, kind="ExternalInput").ap()
    clips = nc.dram_tensor("clips", [1, 2], F32, kind="ExternalInput").ap()
    identin = nc.dram_tensor("identin", [128, 128], F32, kind="ExternalInput").ap()
    identbin = nc.dram_tensor("identbin", [128, 128], BF16, kind="ExternalInput").ap()
    onesbcin = nc.dram_tensor("onesbcin", [1, 128], F32, kind="ExternalInput").ap()
    onesmin = nc.dram_tensor("onesmin", [128, 1], BF16, kind="ExternalInput").ap()
    masksin = nc.dram_tensor("masksin", [128, T], F32, kind="ExternalInput").ap()

    outT = nc.dram_tensor("outT", [D, T], F32, kind="ExternalOutput").ap()

    xn_dram = nc.dram_tensor("xn_dram", [D, T], F32).ap()
    x1_dram = nc.dram_tensor("x1_dram", [D, T], F32).ap()
    x2_dram = nc.dram_tensor("x2_dram", [D, T], F32).ap()
    h_dram = nc.dram_tensor("h_dram", [128, NIP * T], F32).ap()
    wgq_dram = nc.dram_tensor("wgq_dram", [128, NDP * II], BF16).ap()
    wuq_dram = nc.dram_tensor("wuq_dram", [128, NDP * II], BF16).ap()
    wdq_dram = nc.dram_tensor("wdq_dram", [128, NIP * D], BF16).ap()
    ccio = [(nc.dram_tensor(f"cc{i}_in", [1, 128], F32).ap(),
             nc.dram_tensor(f"cc{i}_out", [1, 128], F32).ap()) for i in range(4)]

    es = contextlib.ExitStack()
    arena = es.enter_context(nc.sbuf_tensor("arena", [128, AR_BYTES], mybir.dt.uint8)).ap()
    AF32 = arena.bitcast(F32)
    ABF = arena.bitcast(BF16)
    AI32 = arena.bitcast(I32)

    def f32(off, p0, p1, e0, e1):
        return AF32[p0:p1, off // 4 + e0: off // 4 + e1]

    def bf(off, p0, p1, e0, e1):
        return ABF[p0:p1, off // 2 + e0: off // 2 + e1]

    def i32(off, p0, p1, e0, e1):
        return AI32[p0:p1, off // 4 + e0: off // 4 + e1]

    ident = f32(IDENT, 0, 128, 0, 128)
    identb = bf(IDENTB, 0, 128, 0, 128)
    onesbc = f32(ONESBC, 0, 1, 0, 128)
    onesm = bf(ONESM, 0, 128, 0, 1)

    def rows(slot, a, b):
        return f32(ROWS, 0, 1, slot + a, slot + b)

    def col(idx, k=1):
        return f32(COLS, 0, 128, idx, idx + k)

    def scal(idx, k=1):
        return f32(SCAL, 0, 1, idx, idx + k)

    pb = [es.enter_context(nc.psum_tensor(f"pb{i}", [128, 512], F32)) for i in range(8)]
    pbb = [t.ap().bitcast(BF16) for t in pb]

    dS = Sem(es.enter_context(nc.semaphore("dS")))
    dA = Sem(es.enter_context(nc.semaphore("dA")))
    dG = Sem(es.enter_context(nc.semaphore("dG")))
    cV = Sem(es.enter_context(nc.semaphore("cV")))
    cS = Sem(es.enter_context(nc.semaphore("cS")))
    cT = Sem(es.enter_context(nc.semaphore("cT")))
    cc = Sem(es.enter_context(nc.semaphore("cc")))
    dQ = [Sem(es.enter_context(nc.semaphore(f"dQ{i}"))) for i in range(3)]
    dL = [Sem(es.enter_context(nc.semaphore(f"dL{i}"))) for i in range(2)]
    dGs = [Sem(es.enter_context(nc.semaphore(f"dGs{i}"))) for i in range(2)]
    dT = [Sem(es.enter_context(nc.semaphore(f"dT{i}"))) for i in range(8)]
    dR = Sem(es.enter_context(nc.semaphore("dR")))
    dC = Sem(es.enter_context(nc.semaphore("dC")))
    dRg = Sem(es.enter_context(nc.semaphore("dRg")))
    dD = [Sem(es.enter_context(nc.semaphore(f"dD{i}"))) for i in range(2)]
    dW = Sem(es.enter_context(nc.semaphore("dW")))
    dP = [Sem(es.enter_context(nc.semaphore(f"dP{i}"))) for i in range(2)]

    p = Prog()
    bankg = [None] * 8

    def bank_wait(i):
        if bankg[i] is not None:
            p.wait("tensor", bankg[i][0], bankg[i][1])

    def bcast_scalar_to_col(s_ap, cidx):
        bank_wait(7)
        p.wait("tensor", cV)
        p.op("tensor", lambda e, s=s_ap: e.matmul(pb[7][:, 0:1], onesbc, s,
                                                  start=True, stop=True), cT)
        p.wait("vector", cT)
        p.op("vector", lambda e, c=col(cidx): e.tensor_copy(c, pb[7][:, 0:1]), cV)
        bankg[7] = (cV, cV.n)

    def bcast_rows_to_tiles(jobs, rows_mark=None):
        nj = len(jobs)
        first = True
        for j, (slot, dst) in enumerate(jobs):
            for ch in range(NTC):
                bk = 2 + ch
                bank_wait(bk)
                p.wait("tensor", cV, rows_mark)
                p.op("tensor", lambda e, slot=slot, ch=ch, bk=bk:
                     e.matmul(pb[bk][:, 0:512], onesbc,
                              rows(slot, ch * 512, (ch + 1) * 512),
                              start=True, stop=True),
                     cT if ch == NTC - 1 else None)
            p.wait("vector", cT)
            for ch in range(NTC):
                p.op("vector", lambda e, ch=ch, dst=dst: e.tensor_copy(
                    f32(dst, 0, 128, ch * 512, (ch + 1) * 512),
                    pb[2 + ch][:, 0:512]), cV, drain=first)
                first = False
            for ch in range(NTC):
                bankg[2 + ch] = (cV, cV.n)

    def ln_stats_gen(src_dram, src_guards=None,
                     bufs=(XA, XB, XC, XD, 24576, 28672)):
        for i in range(2, 6):
            bank_wait(i)
        ldd = {}
        consumed = {}

        BXA, BXB, BXC0, BXD0, BXC1, BXD1 = bufs

        def ld_stats(dp):
            if dp >= NDP:
                return
            buf = BXA if dp % 2 == 0 else BXB
            q = "sync" if dp % 2 == 0 else "scalar"
            d = dL[dp % 2]
            if dp - 2 in consumed:
                p.wait(q, cV, consumed[dp - 2][0])
                p.wait(q, cS, consumed[dp - 2][1])
            else:
                p.wait(q, cV)
                p.wait(q, cS)
            if src_guards is not None:
                g = src_guards.get(dp)
                if g is not None:
                    for gg in g:
                        p.wait(q, gg[0], gg[1])
            p.dma(q, f32(buf, 0, 128, 0, T), src_dram[dp * 128:(dp + 1) * 128, :], d)
            ldd[dp] = (d, d.n)

        ld_stats(0)
        ld_stats(1)
        smm = {}
        for dp in range(NDP):
            buf = BXA if dp % 2 == 0 else BXB
            BXC = BXC0 if dp % 2 == 0 else BXC1
            BXD = BXD0 if dp % 2 == 0 else BXD1
            p.wait("vector", ldd[dp][0], ldd[dp][1])
            p.wait("vector", cT, smm.get(dp - 2, 0))
            p.op("vector", lambda e, b=buf, BXC=BXC: e.tensor_copy(
                bf(BXC, 0, 128, 0, T), f32(b, 0, 128, 0, T)), cV, drain=False)
            castm = cV.n
            p.wait("scalar", ldd[dp][0], ldd[dp][1])
            p.wait("scalar", cT, smm.get(dp - 2, 0))
            p.op("scalar", lambda e, b=buf, BXD=BXD: e.activation(
                bf(BXD, 0, 128, 0, T), f32(b, 0, 128, 0, T), AF.Square), cS, drain=False)
            sqm = cS.n
            consumed[dp] = (castm, sqm)
            ld_stats(dp + 2)
            yield
            p.wait("tensor", cV, castm)
            p.wait("tensor", cS, sqm)
            for tc in range(NTC):
                st, sp = (dp == 0), (dp == NDP - 1)
                p.op("tensor", lambda e, tc=tc, st=st, sp=sp, BXC=BXC:
                     e.matmul(pb[2 + tc][0:1, :], onesm,
                              bf(BXC, 0, 128, tc * 512, (tc + 1) * 512),
                              start=st, stop=sp), None)
                p.op("tensor", lambda e, tc=tc, st=st, sp=sp, BXD=BXD:
                     e.matmul(pb[2 + tc][32:33, :], onesm,
                              bf(BXD, 0, 128, tc * 512, (tc + 1) * 512),
                              start=st, stop=sp),
                     cT if tc == NTC - 1 else None)
            smm[dp] = cT.n
        ln_stats_finish(smm[NDP - 1])
        yield

    def ln_stats_finish(mm_mark):
        p.wait("vector", cT, mm_mark)
        for tc in range(NTC):
            p.op("vector", lambda e, tc=tc: e.tensor_copy(
                rows(R_MEAN, tc * 512, (tc + 1) * 512), pb[2 + tc][0:1, :]), cV, drain=(tc == 0))
            p.op("vector", lambda e, tc=tc: e.tensor_copy(
                rows(R_RSTD, tc * 512, (tc + 1) * 512), pb[2 + tc][32:33, :]), cV, drain=False)
        for i in range(2, 6):
            bankg[i] = (cV, cV.n)
        p.wait("scalar", cV)
        p.op("scalar", lambda e: e.activation(rows(R_G, 0, T), rows(R_MEAN, 0, T), AF.Square), cS)
        p.wait("vector", cS)
        p.op("vector", lambda e: e.scalar_tensor_tensor(
            rows(R_RSTD, 0, T), rows(R_G, 0, T), -1.0, rows(R_RSTD, 0, T),
            ALU.mult, ALU.add), cV)
        p.op("vector", lambda e: e.tensor_scalar(
            rows(R_RSTD, 0, T), rows(R_RSTD, 0, T), 1e-5, None, ALU.add), cV)
        p.wait("scalar", cV)
        p.op("scalar", lambda e: e.activation(rows(R_RSTD, 0, T), rows(R_RSTD, 0, T),
                                              AF.Ln), cS)
        p.op("scalar", lambda e: e.activation(rows(R_RSTD, 0, T), rows(R_RSTD, 0, T),
                                              AF.Exp, scale=-0.5), cS)
        p.wait("vector", cS)
        p.op("vector", lambda e: e.scalar_tensor_tensor(
            rows(R_G, 0, T), rows(R_MEAN, 0, T), -1.0, rows(R_RSTD, 0, T),
            ALU.mult, ALU.mult), cV)
        bcast_rows_to_tiles([(R_RSTD, FB), (R_G, GB)], rows_mark=cV.n)

    def allreduce_scalar_max(col_ap, cc_pair, out_scal_idx):
        cin, cout = cc_pair
        stg = f32(CCS, 0, 1, 0, 128)
        p.wait("gpsimd", cV)
        p.dma("gpsimd", cin, col_ap, dC)
        p.wait("gpsimd", dC)
        p.op("gpsimd", lambda e: e.collective_compute(
            "AllReduce", ALU.max, replica_groups=[list(range(N_CORES))],
            ins=[cin], outs=[cout]), cc, drain=False)
        p.wait("gpsimd", cc)
        p.dma("gpsimd", stg, cout, dC)
        p.wait("vector", dC)
        p.op("vector", lambda e: e.reduce_max(scal(out_scal_idx), stg, axis=AX), cV)

    def qw_gen(w_dram, n_pt, width, c_slot, qdst_off, spill_dram, stgs, qbs,
               spill_guard=None, buf_guard=0, qb_buf_guard=0):
        nb = width // 128
        w3d = ABF[:, qdst_off // 2: qdst_off // 2 + nb * n_pt * 128].rearrange(
            "p (b i) -> p b i", b=nb)
        if spill_guard is not None:
            p.wait("scalar", spill_guard[0], spill_guard[1])
        qb_guard = {}
        q1_done = {}
        q2_done = {}
        ld = {}

        def load(pt):
            if pt >= n_pt:
                return
            q = "sync" if pt % 2 == 0 else "scalar"
            d = dQ[pt % 3]
            p.wait(q, cV, q2_done.get(pt - 3, buf_guard))
            p.dma(q, f32(stgs[pt % 3], 0, 128, 0, width),
                  w_dram[pt * 128:(pt + 1) * 128, :], d)
            ld[pt] = (d, d.n)

        def stage_a(pt):
            stg = stgs[pt % 3]
            rc = C_GQ + (pt % 2)
            p.wait("vector", ld[pt][0], ld[pt][1])
            p.op("vector", lambda e, s=stg, pt=pt: e.reduce_max(
                col(c_slot + pt), f32(s, 0, 128, 0, width), axis=AX,
                apply_absolute_value=True), cV, drain=(pt == 0))
            p.wait("vector", cS, q1_done.get(pt - 2, 0))
            p.op("vector", lambda e, pt=pt, rc=rc: e.reciprocal(
                col(rc), col(c_slot + pt)), cV)
            p.op("vector", lambda e, rc=rc: e.tensor_scalar(
                col(rc), col(rc), QMAX, None, ALU.mult), cV)
            p.wait("scalar", cV)
            p.op("scalar", lambda e, s=stg, rc=rc: e.activation(
                i32(s, 0, 128, 0, width), f32(s, 0, 128, 0, width),
                AF.Identity, scale=col(rc)), cS, drain=False)
            q1_done[pt] = cS.n

        def stage_b(pt):
            stg = stgs[pt % 3]
            qb = qbs[pt % 2]
            p.wait("vector", cS, q1_done[pt])
            p.wait("vector", cT, qb_guard.get(pt - 2, qb_buf_guard))
            p.op("vector", lambda e, s=stg, qb=qb: e.tensor_scalar(
                bf(qb, 0, 128, 0, width), i32(s, 0, 128, 0, width),
                -128.0, 127.0, ALU.max, ALU.min), cV, drain=False)
            q2_done[pt] = cV.n
            for g in range((nb + 7) // 8):
                bk = 6 + (g % 2)
                kblk = min(8, nb - g * 8)
                bank_wait(bk)
                p.wait("tensor", cV, q2_done[pt])
                for k in range(kblk):
                    wb = g * 8 + k
                    p.op("tensor", lambda e, wb=wb, k=k, bk=bk, qb=qb: e.transpose(
                        pbb[bk][:, k * 128:(k + 1) * 128],
                        bf(qb, 0, 128, wb * 128, (wb + 1) * 128), identb),
                         cT if k == kblk - 1 else None)
                p.wait("scalar", cT)
                psrc = pbb[bk][:, 0:kblk * 128].rearrange("p (b i) -> p b i", b=kblk)
                p.op("scalar", lambda e, g=g, pt=pt, bk=bk, kblk=kblk, psrc=psrc:
                     e.activation(w3d[:, g * 8:g * 8 + kblk, pt * 128:(pt + 1) * 128],
                                  psrc, AF.Identity), cS, drain=False)
                bankg[bk] = (cS, cS.n)
            qb_guard[pt] = cT.n

        load(0)
        load(1)
        for pt in range(n_pt):
            stage_a(pt)
            if pt > 0:
                stage_b(pt - 1)
            load(pt + 2)
            yield
        stage_b(n_pt - 1)
        p.wait("sync", cS)
        p.dma("sync", spill_dram[:, 0:n_pt * width], bf(qdst_off, 0, 128, 0, n_pt * width), dS)

    def qw_gen_psum(w_dram, n_pt, width, c_slot, spill_dram, stgs, qbs,
                    buf_guard=0):
        nb = width // 128
        qb_guard = {}
        q1_done = {}
        q2_done = {}
        ld = {}

        def load(pt):
            if pt >= n_pt:
                return
            q = "sync" if pt % 2 == 0 else "scalar"
            d = dQ[pt % 3]
            p.wait(q, cV, q2_done.get(pt - 3, buf_guard))
            p.dma(q, f32(stgs[pt % 3], 0, 128, 0, width),
                  w_dram[pt * 128:(pt + 1) * 128, :], d)
            ld[pt] = (d, d.n)

        def stage_a(pt):
            stg = stgs[pt % 3]
            rc = C_GQ + (pt % 2)
            p.wait("vector", ld[pt][0], ld[pt][1])
            p.op("vector", lambda e, s=stg, pt=pt: e.reduce_max(
                col(c_slot + pt), f32(s, 0, 128, 0, width), axis=AX,
                apply_absolute_value=True), cV, drain=(pt == 0))
            p.wait("vector", cS, q1_done.get(pt - 2, 0))
            p.op("vector", lambda e, pt=pt, rc=rc: e.reciprocal(
                col(rc), col(c_slot + pt)), cV)
            p.op("vector", lambda e, rc=rc: e.tensor_scalar(
                col(rc), col(rc), QMAX, None, ALU.mult), cV)
            p.wait("scalar", cV)
            p.op("scalar", lambda e, s=stg, rc=rc: e.activation(
                i32(s, 0, 128, 0, width), f32(s, 0, 128, 0, width),
                AF.Identity, scale=col(rc)), cS, drain=False)
            q1_done[pt] = cS.n

        def stage_b(pt):
            stg = stgs[pt % 3]
            qb = qbs[pt % 2]
            p.wait("vector", cS, q1_done[pt])
            p.wait("vector", cT, qb_guard.get(pt - 2, 0))
            p.op("vector", lambda e, s=stg, qb=qb: e.tensor_scalar(
                bf(qb, 0, 128, 0, width), i32(s, 0, 128, 0, width),
                -128.0, 127.0, ALU.max, ALU.min), cV, drain=False)
            q2_done[pt] = cV.n
            for g in range((nb + 7) // 8):
                bk = 6 + (g % 2)
                kblk = min(8, nb - g * 8)
                bank_wait(bk)
                p.wait("tensor", cV, q2_done[pt])
                for k in range(kblk):
                    wb = g * 8 + k
                    p.op("tensor", lambda e, wb=wb, k=k, bk=bk, qb=qb: e.transpose(
                        pbb[bk][:, k * 128:(k + 1) * 128],
                        bf(qb, 0, 128, wb * 128, (wb + 1) * 128), identb),
                         cT if k == kblk - 1 else None)
                p.wait("scalar", cT)
                slot = 59392 if g % 2 == 0 else 61440
                p.wait("scalar", dP[g % 2], dP[g % 2].n)
                p.op("scalar", lambda e, bk=bk, kblk=kblk, slot=slot: e.activation(
                    bf(slot, 0, 128, 0, kblk * 128), pbb[bk][:, 0:kblk * 128],
                    AF.Identity), cS, drain=False)
                bankg[bk] = (cS, cS.n)
                p.wait("gpsimd", cS)
                c0 = (pt * nb + g * 8) * 128
                p.dma("gpsimd", spill_dram[:, c0:c0 + kblk * 128],
                      bf(slot, 0, 128, 0, kblk * 128), dP[g % 2])
            qb_guard[pt] = cT.n

        load(0)
        load(1)
        for pt in range(n_pt):
            stage_a(pt)
            if pt > 0:
                stage_b(pt - 1)
            load(pt + 2)
            yield
        stage_b(n_pt - 1)

    def drive(*gens):
        live = list(gens)
        while live:
            for g in list(live):
                try:
                    next(g)
                except StopIteration:
                    live.remove(g)

    def drive_with(main, aux, aux_steps=2):
        while True:
            try:
                next(main)
            except StopIteration:
                return
            for _ in range(aux_steps):
                try:
                    next(aux)
                except StopIteration:
                    break

    def ffn(src_dram, dst_dram, ccA, ccB, wq_gens=None, stats_done=False,
            feed_stats=False):
        p.wait("vector", cT)
        p.wait("scalar", cT)
        p.wait("sync", dS)
        for s in dT + dD:
            p.wait("sync", s)
            p.wait("scalar", s)
            p.wait("vector", s)
        p.wait("scalar", dS)
        if not stats_done:
            if wq_gens is not None:
                drive_with(ln_stats_gen(src_dram), wq_gens, aux_steps=1)
            else:
                drive(ln_stats_gen(src_dram))
        ldx = {}
        spg = {}
        xn_mark = [0]

        def ld_xn(dp):
            if dp >= NDP:
                return
            buf = XA if dp % 2 == 0 else XB
            q = "sync" if dp % 2 == 0 else "scalar"
            d = dL[dp % 2]
            if dp - 2 in spg:
                g = spg[dp - 2]
                p.wait(q, g[0], g[1])
            else:
                p.wait(q, cV)
                p.wait(q, cS)
            p.dma(q, f32(buf, 0, 128, 0, T), src_dram[dp * 128:(dp + 1) * 128, :], d)
            ldx[dp] = (d, d.n)

        def xn_gen():
            ld_xn(0)
            ld_xn(1)
            for dp in range(NDP):
                buf = XA if dp % 2 == 0 else XB
                p.wait("vector", ldx[dp][0], ldx[dp][1])
                p.op("vector", lambda e, b=buf: e.tensor_tensor(
                    f32(b, 0, 128, 0, T), f32(b, 0, 128, 0, T),
                    f32(FB, 0, 128, 0, T), ALU.mult), cV, drain=False)
                p.op("vector", lambda e, b=buf: e.tensor_tensor(
                    f32(b, 0, 128, 0, T), f32(b, 0, 128, 0, T),
                    f32(GB, 0, 128, 0, T), ALU.add), cV)
                p.op("vector", lambda e, dp=dp, b=buf: e.reduce_max(
                    col(C_T0 + dp), f32(b, 0, 128, 0, T), axis=AX,
                    apply_absolute_value=True), cV)
                p.wait("gpsimd", cV, cV.n)
                p.dma("gpsimd", xn_dram[dp * 128:(dp + 1) * 128, :], f32(buf, 0, 128, 0, T),
                      dGs[dp % 2])
                spg[dp] = (dGs[dp % 2], dGs[dp % 2].n)
                ld_xn(dp + 2)
                yield
            p.op("vector", lambda e: e.reduce_max(col(C_T0 + 8), col(C_T0, NDP), axis=AX), cV)
            xn_mark[0] = cV.n
            allreduce_scalar_max(col(C_T0 + 8), ccA, S_GMAX)
            yield

        if wq_gens is not None:
            drive(xn_gen(), wq_gens)
        else:
            drive(xn_gen())
        p.op("vector", lambda e: e.tensor_tensor(scal(S_SX), scal(S_SIGX), scal(S_GMAX), ALU.mult), cV)
        p.op("vector", lambda e: e.tensor_scalar(scal(S_SX), scal(S_SX), 1.0 / QMAX, None, ALU.mult), cV)
        p.op("vector", lambda e: e.reciprocal(scal(S_RSX), scal(S_SX)), cV)
        p.op("vector", lambda e: e.tensor_scalar(scal(S_SX127), scal(S_SX), 1.0 / QMAX, None, ALU.mult), cV)
        bcast_scalar_to_col(scal(S_RSX), C_RS)
        bcast_scalar_to_col(scal(S_SX127), C_SX127)
        p.wait("vector", cS)
        p.op("vector", lambda e: e.tensor_scalar(
            col(C_GSC, NIP), col(C_WGS, NIP), col(C_SX127), None, ALU.mult), cV)
        p.op("vector", lambda e: e.tensor_scalar(
            col(C_USC, NIP), col(C_WUS, NIP), col(C_SX127), None, ALU.mult), cV)
        p.wait("sync", dGs[0])
        p.wait("sync", dGs[1])
        p.wait("scalar", dGs[0])
        p.wait("scalar", dGs[1])
        xn3 = xn_dram.rearrange("(a p) t -> p a t", p=128)
        ldq = {}
        qdone = {}

        def ld_xq(u):
            if u >= NDP // 2:
                return
            g = XG_A if u % 2 == 0 else XG_B
            q = "sync" if u % 2 == 0 else "scalar"
            d = dL[u % 2]
            if u - 2 in qdone:
                p.wait(q, cV, qdone[u - 2])
            else:
                p.wait(q, cV, xn_mark[0])
            p.dma(q, f32(g, 0, 128, 0, 2 * T), xn3[:, 2 * u:2 * u + 2, :], d)
            ldq[u] = (d, d.n)

        ld_xq(0)
        ld_xq(1)
        p.wait("vector", dP[0])
        p.wait("vector", dP[1])
        for u in range(NDP // 2):
            g = XG_A if u % 2 == 0 else XG_B
            p.wait("scalar", ldq[u][0], ldq[u][1])
            p.wait("scalar", cV)
            p.op("scalar", lambda e, g=g: e.activation(
                i32(g, 0, 128, 0, 2 * T), f32(g, 0, 128, 0, 2 * T),
                AF.Identity, scale=col(C_RS)), cS, drain=False)
            p.wait("vector", cS)
            p.op("vector", lambda e, g=g, u=u: e.tensor_scalar(
                bf(XQ, 0, 128, 2 * u * T, (2 * u + 2) * T), i32(g, 0, 128, 0, 2 * T),
                -128.0, 127.0, ALU.max, ALU.min), cV, drain=False)
            qdone[u] = cV.n
            ld_xq(u + 2)
        p.wait("tensor", cV)
        p.wait("tensor", dS)
        p.wait("tensor", dR)
        p.wait("tensor", dRg)
        for ib in range(NIP):
            for i in range(4):
                bank_wait(i)
            for dp in range(NDP):
                for tc in range(NTC):
                    st, sp = (dp == 0), (dp == NDP - 1)
                    p.op("tensor", lambda e, ib=ib, dp=dp, tc=tc, st=st, sp=sp:
                         e.matmul(pb[tc][:, 0:512],
                                  bf(W1, 0, 128, dp * II + ib * 128, dp * II + (ib + 1) * 128),
                                  bf(XQ, 0, 128, dp * T + tc * 512, dp * T + (tc + 1) * 512),
                                  start=st, stop=sp),
                         cT if (sp and tc == NTC - 1) else None)
            gate_done = cT.n
            for i in range(4, 8):
                bank_wait(i)
            for dp in range(NDP):
                for tc in range(NTC):
                    st, sp = (dp == 0), (dp == NDP - 1)
                    p.op("tensor", lambda e, ib=ib, dp=dp, tc=tc, st=st, sp=sp:
                         e.matmul(pb[4 + tc][:, 0:512],
                                  bf(W2, 0, 128, dp * II + ib * 128, dp * II + (ib + 1) * 128),
                                  bf(XQ, 0, 128, dp * T + tc * 512, dp * T + (tc + 1) * 512),
                                  start=st, stop=sp),
                         cT if (sp and tc == NTC - 1) else None)
            up_done = cT.n
            p.wait("scalar", cT, gate_done)
            p.wait("scalar", cV)
            for tc in range(NTC):
                p.op("scalar", lambda e, ib=ib, tc=tc: e.activation(
                    f32(SILU, 0, 128, tc * 512, (tc + 1) * 512), pb[tc][:, 0:512],
                    AF.Silu, scale=col(C_GSC + ib)), cS)
            for i in range(4):
                bankg[i] = (cS, cS.n)
            p.wait("vector", cT, up_done)
            p.wait("vector", cS)
            p.wait("vector", dG)
            for tc in range(NTC):
                p.op("vector", lambda e, ib=ib, tc=tc: e.scalar_tensor_tensor(
                    f32(HPRE, 0, 128, tc * 512, (tc + 1) * 512), pb[4 + tc][:, 0:512],
                    col(C_USC + ib), f32(SILU, 0, 128, tc * 512, (tc + 1) * 512),
                    ALU.mult, ALU.mult), cV)
                p.op("vector", lambda e, ib=ib, tc=tc: e.reduce_max(
                    f32(HMC, 0, 128, ib * NTC + tc, ib * NTC + tc + 1),
                    f32(HPRE, 0, 128, tc * 512, (tc + 1) * 512),
                    axis=AX, apply_absolute_value=True), cV)
                p.wait("gpsimd", cV)
                p.dma("gpsimd", h_dram[:, (tc * NIP + ib) * 512:(tc * NIP + ib + 1) * 512],
                      f32(HPRE, 0, 128, tc * 512, (tc + 1) * 512), dG)
            for i in range(4, 8):
                bankg[i] = (cV, cV.n)
        p.op("vector", lambda e: e.reduce_max(col(C_T0), f32(HMC, 0, 128, 0, NIP * NTC), axis=AX), cV)
        p.wait("scalar", cT)
        p.wait("scalar", dG)
        uld = {}
        p.dma("scalar", f32(HSB3[0], 0, 128, 0, 5632), h_dram[:, 0:5632], dQ[0])
        uld[0] = (dQ[0], dQ[0].n)
        p.dma("scalar", f32(HSB3[1], 0, 128, 0, 5632), h_dram[:, 5632:11264], dQ[1])
        uld[1] = (dQ[1], dQ[1].n)
        allreduce_scalar_max(col(C_T0), ccB, S_GMAX)
        p.op("vector", lambda e: e.tensor_tensor(scal(S_SH), scal(S_SIGD), scal(S_GMAX), ALU.mult), cV)
        p.op("vector", lambda e: e.tensor_scalar(scal(S_SH), scal(S_SH), 1.0 / QMAX, None, ALU.mult), cV)
        p.op("vector", lambda e: e.reciprocal(scal(S_RSH), scal(S_SH)), cV)
        p.op("vector", lambda e: e.tensor_scalar(scal(S_SH127), scal(S_SH), 0.5 / QMAX, None, ALU.mult), cV)
        bcast_scalar_to_col(scal(S_RSH), C_RH)
        bcast_scalar_to_col(scal(S_SH127), C_SH)
        p.op("vector", lambda e: e.tensor_scalar(
            col(C_DSC, NDP), col(C_WDS, NDP), col(C_SH), None, ALU.mult), cV)
        rh_mark = cV.n
        p.wait("sync", cT)
        p.wait("sync", dS)
        p.wait("sync", dP[0])
        p.wait("sync", dP[1])
        p.dma("sync", bf(WD, 0, 128, 0, NIP * D), wdq_dram[:, 0:NIP * D], dR)
        p.wait("tensor", dR)
        units = [(tc, hf) for tc in range(NTC) for hf in range(2)]
        uq_done = {}
        dn_store = {}
        sm_mm = {}
        sm_cast = {}
        sm_sq = {}

        def emit_sm_mms(g):
            if g < 0 or g not in sm_cast:
                return
            tc_g, hb_g = g // NDP, g % NDP
            if hb_g == 0:
                bank_wait(2 + tc_g)
            p.wait("tensor", cV, sm_cast[g])
            p.wait("tensor", cS, sm_sq[g])
            st2, sp2 = (hb_g == 0), (hb_g == NDP - 1)
            cstb, sqb = CST2[g % 4], SQ2[g % 4]
            p.op("tensor", lambda e, tc_g=tc_g, cstb=cstb, st2=st2, sp2=sp2:
                 e.matmul(pb[2 + tc_g][0:1, :], onesm,
                          bf(cstb, 0, 128, 0, 512), start=st2, stop=sp2), None)
            p.op("tensor", lambda e, tc_g=tc_g, sqb=sqb, st2=st2, sp2=sp2:
                 e.matmul(pb[2 + tc_g][32:33, :], onesm,
                          bf(sqb, 0, 128, 0, 512), start=st2, stop=sp2), cT)
            sm_mm[g] = cT.n

        def emit_unit_load(ui):
            if ui >= len(units):
                return
            tc, hf = units[ui]
            g = HSB3[ui % 2]
            p.wait("scalar", cV, uq_done[ui - 2])
            p.dma("scalar", f32(g, 0, 128, 0, 5632),
                  h_dram[:, (tc * NIP + hf * 11) * 512:(tc * NIP + hf * 11) * 512 + 5632],
                  dQ[ui % 2])
            uld[ui] = (dQ[ui % 2], dQ[ui % 2].n)

        def emit_unit_quant(ui):
            tc, hf = units[ui]
            g = HSB3[ui % 2]
            hq = HQ_A if tc % 2 == 0 else HQ_B
            p.wait("scalar", uld[ui][0], uld[ui][1])
            p.wait("scalar", cV, rh_mark)
            p.op("scalar", lambda e, g=g: e.activation(
                i32(g, 0, 128, 0, 5632), f32(g, 0, 128, 0, 5632),
                AF.Identity, scale=col(C_RH)), cS, drain=False)
            q1_mark = cS.n
            p.wait("vector", cS, q1_mark)
            p.op("vector", lambda e, g=g, hq=hq, hf=hf: e.tensor_scalar(
                bf(hq, 0, 128, hf * 5632, (hf + 1) * 5632), i32(g, 0, 128, 0, 5632),
                -128.0, 127.0, ALU.max, ALU.min), cV, drain=False)
            uq_done[ui] = cV.n

        p.wait("vector", dP[0])
        p.wait("vector", dP[1])
        p.wait("scalar", dP[0])
        p.wait("scalar", dP[1])
        emit_unit_quant(0)
        emit_unit_quant(1)
        for tc in range(NTC):
            if 2 * tc + 2 < len(units):
                emit_unit_load(2 * tc + 2)
                emit_unit_load(2 * tc + 3)
                emit_unit_quant(2 * tc + 2)
                emit_unit_quant(2 * tc + 3)
            hq = HQ_A if tc % 2 == 0 else HQ_B
            for hb in range(NDP):
                bk = (tc * NDP + hb) % 2
                bank_wait(bk)
                for ip in range(NIP):
                    st, sp = (ip == 0), (ip == NIP - 1)
                    if ip == 0:
                        p.wait("tensor", cV, uq_done[2 * tc])
                    elif ip == 11:
                        p.wait("tensor", cV, uq_done[2 * tc + 1])
                    p.op("tensor", lambda e, ip=ip, hb=hb, hq=hq, bk=bk, st=st, sp=sp:
                         e.matmul(pb[bk][:, 0:512],
                                  bf(WD, 0, 128, (hb * NIP + ip) * 128,
                                     (hb * NIP + ip + 1) * 128),
                                  bf(hq, 0, 128, ip * 512, (ip + 1) * 512),
                                  start=st, stop=sp),
                         cT if sp else None)
                g_i = tc * NDP + hb
                xr = XRA2 if g_i % 2 == 0 else XRB2
                ev2 = EVT2 if g_i % 2 == 0 else EVT2B
                dxr = dL[g_i % 2]
                p.wait("sync", cS)
                p.dma("sync", f32(xr, 0, 128, 0, 512),
                      src_dram[hb * 128:(hb + 1) * 128, tc * 512:(tc + 1) * 512], dxr)
                xr_mark = dxr.n
                p.wait("scalar", cT)
                p.wait("scalar", dxr, xr_mark)
                p.wait("scalar", cV)
                if g_i - 2 in dn_store:
                    s_prev = dn_store[g_i - 2][0]
                    p.wait("scalar", s_prev, s_prev.n)
                p.op("scalar", lambda e, bk=bk, hb=hb, ev2=ev2: e.activation(
                    f32(ev2, 0, 128, 0, 512), pb[bk][:, 0:512], AF.Identity,
                    scale=col(C_DSC + hb)), cS)
                bankg[bk] = (cS, cS.n)
                p.wait("vector", cS)
                p.op("vector", lambda e, xr=xr, ev2=ev2: e.tensor_tensor(
                    f32(ev2, 0, 128, 0, 512), f32(ev2, 0, 128, 0, 512),
                    f32(xr, 0, 128, 0, 512), ALU.add), cV)
                add_mark = cV.n
                p.wait("sync", cV)
                p.dma("sync", dst_dram[hb * 128:(hb + 1) * 128, tc * 512:(tc + 1) * 512],
                      f32(ev2, 0, 128, 0, 512), dD[g_i % 2])
                dn_store[g_i] = (dD[g_i % 2], dD[g_i % 2].n)
                if feed_stats:
                    emit_sm_mms(g_i - 4)
                    cstb = CST2[g_i % 4]
                    sqb = SQ2[g_i % 4]
                    if g_i - 4 in sm_mm:
                        p.wait("vector", cT, sm_mm[g_i - 4])
                    p.op("vector", lambda e, ev2=ev2, cstb=cstb: e.tensor_copy(
                        bf(cstb, 0, 128, 0, 512), f32(ev2, 0, 128, 0, 512)),
                        cV, drain=True)
                    sm_cast[g_i] = cV.n
                    p.wait("scalar", cV, add_mark)
                    if g_i - 4 in sm_mm:
                        p.wait("scalar", cT, sm_mm[g_i - 4])
                    p.op("scalar", lambda e, ev2=ev2, sqb=sqb: e.activation(
                        bf(sqb, 0, 128, 0, 512), f32(ev2, 0, 128, 0, 512),
                        AF.Square), cS, drain=False)
                    sm_sq[g_i] = cS.n

        if feed_stats:
            for g in range(NTC * NDP - 4, NTC * NDP):
                emit_sm_mms(g)
        return sm_mm.get(NTC * NDP - 1)

    def attention(src_dram, dst_dram, stats2_gen=None, stats2_guards=None,
                  stats_premixed=None):
        p.wait("vector", cT)
        p.wait("scalar", cT)
        p.wait("sync", dS)
        for s in dT + dD:
            p.wait("sync", s)
            p.wait("scalar", s)
            p.wait("vector", s)
        p.wait("scalar", dS)
        if stats_premixed is not None:
            ln_stats_finish(stats_premixed)
        else:
            drive(ln_stats_gen(src_dram))
        for dp in range(NDP):
            buf = XA if dp % 2 == 0 else XB
            dxn = dL[dp % 2]
            p.wait("sync", cV)
            p.wait("sync", cS)
            p.dma("sync", f32(buf, 0, 128, 0, T), src_dram[dp * 128:(dp + 1) * 128, :], dxn)
            p.wait("vector", dxn, dxn.n)
            p.op("vector", lambda e, b=buf: e.tensor_tensor(
                f32(b, 0, 128, 0, T), f32(b, 0, 128, 0, T),
                f32(FB, 0, 128, 0, T), ALU.mult), cV, drain=False)
            p.op("vector", lambda e, dp=dp, b=buf: e.tensor_tensor(
                bf(XNT, 0, 128, dp * T, (dp + 1) * T), f32(b, 0, 128, 0, T),
                f32(GB, 0, 128, 0, T), ALU.add), cV)

        def proj_o(w_dram, dst_off, sc):
            p.wait("gpsimd", cT)
            for dp in range(NDP):
                p.dma("gpsimd", bf(WQB, 0, 128, dp * D, (dp + 1) * D),
                      w_dram[dp * 128:(dp + 1) * 128, :], dW)
            p.wait("tensor", dW)
            p.wait("tensor", cV)
            for ob in range(NDP):
                for tc in range(NTC):
                    bk = (ob * NTC + tc) % 2
                    bank_wait(bk)
                    for dp in range(NDP):
                        st, sp = (dp == 0), (dp == NDP - 1)
                        p.op("tensor", lambda e, dp=dp, ob=ob, tc=tc, bk=bk, st=st, sp=sp:
                             e.matmul(pb[bk][:, 0:512],
                                      bf(WQB, 0, 128, dp * D + ob * 128, dp * D + (ob + 1) * 128),
                                      bf(XNT, 0, 128, dp * T + tc * 512, dp * T + (tc + 1) * 512),
                                      start=st, stop=sp),
                             cT if sp else None)
                    p.wait("scalar", cT)
                    p.op("scalar", lambda e, ob=ob, tc=tc, bk=bk, dst=dst_off, sc=sc:
                         e.activation(bf(dst, 0, 128, ob * T + tc * 512, ob * T + (tc + 1) * 512),
                                      pb[bk][:, 0:512], AF.Identity, scale=sc), cS)
                    bankg[bk] = (cS, cS.n)

        def proj_v(w_dram, dst_off):
            p.wait("gpsimd", cT)
            for dp in range(NDP):
                p.dma("gpsimd", bf(WQB, 0, 128, dp * D, (dp + 1) * D),
                      w_dram[dp * 128:(dp + 1) * 128, :], dW)
            p.wait("tensor", dW)
            p.wait("tensor", cV)
            for tb in range(NKT):
                for oc in range(2):
                    bk = (tb * 2 + oc) % 2
                    bank_wait(bk)
                    for dp in range(NDP):
                        st, sp = (dp == 0), (dp == NDP - 1)
                        p.op("tensor", lambda e, dp=dp, tb=tb, oc=oc, bk=bk, st=st, sp=sp:
                             e.matmul(pb[bk][:, 0:512],
                                      bf(XNT, 0, 128, dp * T + tb * 128, dp * T + (tb + 1) * 128),
                                      bf(WQB, 0, 128, dp * D + oc * 512, dp * D + (oc + 1) * 512),
                                      start=st, stop=sp),
                             cT if sp else None)
                    p.wait("scalar", cT)
                    p.op("scalar", lambda e, tb=tb, oc=oc, bk=bk, dst=dst_off:
                         e.activation(bf(dst, 0, 128, tb * D + oc * 512, tb * D + (oc + 1) * 512),
                                      pb[bk][:, 0:512], AF.Identity, scale=1.0), cS)
                    bankg[bk] = (cS, cS.n)

        proj_o(wqT, QT_, 1.0 / 32.0)
        proj_o(wkT, KT_, 1.0)
        proj_v(wvT, V_)

        mm_done = {}
        exp_done = {}
        wks_of = {}

        def emit_scores_mm(kt):
            jd = kt // 4
            base = 0 if kt % 2 == 0 else 4
            for b2 in range(base, base + 4):
                bank_wait(b2)
            p.wait("tensor", cS, exp_done.get(kt - 2, 0))
            p.wait("tensor", cV, norm_done.get(kt - 2, 0))
            for j in range(jd, NTC):
                bk = base + (j - jd)
                for dp in range(NDP):
                    st, sp = (dp == 0), (dp == NDP - 1)
                    p.op("tensor", lambda e, kt=kt, j=j, dp=dp, bk=bk, st=st, sp=sp:
                         e.matmul(pb[bk][:, 0:512],
                                  bf(KT_, 0, 128, dp * T + kt * 128, dp * T + (kt + 1) * 128),
                                  bf(QT_, 0, 128, dp * T + j * 512, dp * T + (j + 1) * 512),
                                  start=st, stop=sp),
                         cT if sp else None)
            mm_done[kt] = cT.n

        norm_done = {}

        def emit_softmax(kt):
            jd, m = kt // 4, kt % 4
            nch = NTC - jd
            base = 0 if kt % 2 == 0 else 4
            wks = WKS if kt % 2 == 0 else WKS_B
            c0 = C_T0 if kt % 2 == 0 else C_T0 + 10
            cb_, cs_ = c0 + 4, c0 + 5
            cr_ = c0 + 9
            wks_of[kt] = wks
            p.wait("vector", cT, mm_done[kt])
            p.wait("vector", cS, exp_done.get(kt - 2, 0))
            p.op("vector", lambda e, m=m, bk=base, wks=wks: e.tensor_tensor(
                f32(wks, 0, 128, 0, 512), pb[bk][:, 0:512],
                f32(MASKS, 0, 128, m * 512, (m + 1) * 512), ALU.add), cV)
            p.op("vector", lambda e, wks=wks, c0=c0: e.reduce_max(
                col(c0), f32(wks, 0, 128, 0, 512), axis=AX, negate=True), cV)
            for j in range(jd + 1, NTC):
                p.op("vector", lambda e, j=j, jd=jd, bk=base + (j - jd), c0=c0: e.reduce_max(
                    col(c0 + j - jd), pb[bk][:, 0:512], axis=AX, negate=True), cV,
                    drain=False)
            if nch > 1:
                p.op("vector", lambda e, n=nch, c0=c0, cb_=cb_: e.tensor_reduce(
                    col(cb_), col(c0, n), axis=AX, op=ALU.min), cV)
            else:
                p.op("vector", lambda e, c0=c0, cb_=cb_: e.tensor_copy(col(cb_), col(c0)), cV)
            p.wait("scalar", cV)
            for j in range(jd, NTC):
                p.op("scalar", lambda e, kt=kt, j=j, jd=jd, bk=base + (j - jd), wks=wks,
                     cb_=cb_, cs_=cs_:
                     e.activation(bf(ATT, 0, 128, kt * T + j * 512, kt * T + (j + 1) * 512),
                                  (f32(wks, 0, 128, 0, 512) if j == jd else pb[bk][:, 0:512]),
                                  AF.Exp, bias=col(cb_),
                                  accum_out=col(cs_ + j - jd)), cS, drain=False)
            exp_done[kt] = cS.n
            p.wait("vector", cS)
            if nch > 1:
                p.op("vector", lambda e, n=nch, cs_=cs_, cr_=cr_: e.tensor_reduce(
                    col(cr_), col(cs_, n), axis=AX, op=ALU.add), cV)
            else:
                p.op("vector", lambda e, cs_=cs_, cr_=cr_: e.tensor_copy(col(cr_), col(cs_)), cV)
            p.op("vector", lambda e, cr_=cr_: e.reciprocal(col(cr_), col(cr_)), cV)
            p.op("vector", lambda e, kt=kt, jd=jd, cr_=cr_: e.tensor_scalar(
                bf(ATT, 0, 128, kt * T + jd * 512, (kt + 1) * T),
                bf(ATT, 0, 128, kt * T + jd * 512, (kt + 1) * T),
                col(cr_), None, ALU.mult), cV)
            norm_done[kt] = cV.n

        emit_scores_mm(0)
        for kt in range(1, NKT):
            emit_scores_mm(kt)
            emit_softmax(kt - 1)
        emit_softmax(NKT - 1)
        for i in range(8):
            bankg[i] = (cV, norm_done[NKT - 1])
        groups = [(db, j) for db in range(NDP) for j in range(NTC)]
        xl_done = {}
        ev_done = {}
        st_done = {}

        def ld_av(gi):
            if gi >= len(groups):
                return
            db, j = groups[gi]
            xr = XRA if gi % 2 == 0 else XRB
            p.wait("gpsimd", cT, mm_done[NKT - 1])
            p.wait("gpsimd", cV, ev_done.get(gi - 2, 0))
            p.dma("gpsimd", f32(xr, 0, 128, 0, 512),
                  src_dram[db * 128:(db + 1) * 128, j * 512:(j + 1) * 512], dGs[gi % 2])
            xl_done[gi] = (dGs[gi % 2], dGs[gi % 2].n)

        ld_av(0)
        ld_av(1)
        s2_steps = 0
        for gi, (db, j) in enumerate(groups):
            bk = gi % 2
            bank_wait(bk)
            nkt = min(4 * j + 4, NKT)
            for kt in range(nkt):
                st, sp = (kt == 0), (kt == nkt - 1)
                p.op("tensor", lambda e, kt=kt, db=db, j=j, bk=bk, st=st, sp=sp:
                     e.matmul(pb[bk][:, 0:512],
                              bf(V_, 0, 128, kt * D + db * 128, kt * D + (db + 1) * 128),
                              bf(ATT, 0, 128, kt * T + j * 512, kt * T + (j + 1) * 512),
                              start=st, stop=sp),
                     cT if sp else None)
            xr = XRA if gi % 2 == 0 else XRB
            ev = EVT if gi % 2 == 0 else EVT_B
            p.wait("vector", cT)
            p.wait("vector", xl_done[gi][0], xl_done[gi][1])
            if gi - 2 in st_done:
                s_prev = st_done[gi - 2][0]
                p.wait("vector", s_prev, s_prev.n)
            p.op("vector", lambda e, bk=bk, xr=xr, ev=ev: e.tensor_tensor(
                f32(ev, 0, 128, 0, 512), pb[bk][:, 0:512],
                f32(xr, 0, 128, 0, 512), ALU.add), cV)
            bankg[bk] = (cV, cV.n)
            ev_done[gi] = cV.n
            p.wait("sync", cV)
            p.dma("sync", dst_dram[db * 128:(db + 1) * 128, j * 512:(j + 1) * 512],
                  f32(ev, 0, 128, 0, 512), dT[db])
            st_done[gi] = (dT[db], dT[db].n)
            ld_av(gi + 2)
            if j == NTC - 1:
                if stats2_guards is not None:
                    stats2_guards[db] = [(dT[db], dT[db].n)]
                if stats2_gen is not None and db >= 2:
                    try:
                        next(stats2_gen)
                        s2_steps += 1
                    except StopIteration:
                        pass

    p.dma("sync", ident, identin, dS)
    p.dma("sync", identb, identbin, dS)
    p.dma("sync", onesbc, onesbcin, dS)
    p.dma("sync", onesm, onesmin, dS)
    p.dma("sync", f32(MASKS, 0, 128, 0, T), masksin, dS)
    p.dma("sync", scal(S_T0, 2), clips, dS)
    p.wait("scalar", dS)
    p.op("scalar", lambda e: e.activation(scal(S_SIGX, 2), scal(S_T0, 2), AF.Sigmoid), cS)
    p.wait("vector", dS)

    def wq_chain():
        g0 = cV.n
        yield from qw_gen(wgO, NIP, D, C_WGS, W1, wgq_dram, WG_STG, WG_QB,
                          None, buf_guard=g0)
        g1 = cV.n
        yield from qw_gen_psum(wdO, NDP, II, C_WDS, wdq_dram, WD_STG, WD_QB,
                               buf_guard=g1)
        g2 = cV.n
        g2t = cT.n
        yield from qw_gen(wuO, NIP, D, C_WUS, W2, wuq_dram, WG_STG, WG_QB,
                          (cV, g2), buf_guard=g2, qb_buf_guard=g2t)

    smm1 = ffn(xin, x1_dram, ccio[0], ccio[1], wq_gens=wq_chain(),
               feed_stats=True)
    s2_guards = {}
    s2 = ln_stats_gen(x2_dram, src_guards=s2_guards,
                      bufs=(106496, 114688, 122880, 126976, 131072, 135168))
    attention(x1_dram, x2_dram, stats2_gen=s2, stats2_guards=s2_guards,
              stats_premixed=smm1)
    drive(s2)

    p.wait("gpsimd", cT)
    p.wait("gpsimd", cV)
    p.wait("gpsimd", cS)
    p.dma("gpsimd", bf(W1, 0, 128, 0, NDP * II), wgq_dram[:, 0:NDP * II], dRg)
    p.dma("gpsimd", bf(W2, 0, 128, 0, NDP * II), wuq_dram[:, 0:NDP * II], dRg)

    ffn(x2_dram, outT, ccio[2], ccio[3], stats_done=True)

    p.wait("sync", dS)
    p.wait("sync", dA)
    p.wait("sync", dG)
    for s in dQ + dL + dGs + dT + dD + dP + [dR, dRg, dW, dC]:
        p.wait("sync", s)
    p.wait("sync", cV)
    p.wait("sync", cS)
    p.wait("sync", cT)
    p.wait("sync", cc)

    with nc.Block() as block:
        for eng in Prog.ENGS:
            fns = p.items[eng]
            getattr(block, eng)(lambda e, fns=fns: [fn(e) for fn in fns])

    return nc


_NC_CACHE = {}


def kernel(L0_attn_x, L0_attn_W_q, L0_attn_W_k, L0_attn_W_v,
           L0_ffn_clip_x, L0_ffn_clip_down, L0_ffn_w_gate, L0_ffn_w_up,
           L0_ffn_w_down):
    import ml_dtypes
    x = np.asarray(L0_attn_x)
    wq = np.ascontiguousarray(np.asarray(L0_attn_W_q).T)
    wk = np.ascontiguousarray(np.asarray(L0_attn_W_k).T)
    wv = np.ascontiguousarray(np.asarray(L0_attn_W_v).T)
    wg = np.ascontiguousarray(np.asarray(L0_ffn_w_gate))
    wu = np.ascontiguousarray(np.asarray(L0_ffn_w_up))
    wd = np.ascontiguousarray(np.asarray(L0_ffn_w_down))
    clips = np.array([[np.asarray(L0_ffn_clip_x).reshape(-1)[0],
                       np.asarray(L0_ffn_clip_down).reshape(-1)[0]]], dtype=np.float32)
    ident = np.eye(128, dtype=np.float32)
    identb = np.eye(128, dtype=ml_dtypes.bfloat16)
    onesbc = np.ones((1, 128), dtype=np.float32)
    onesm = np.full((128, 1), 1.0 / 1024.0, dtype=ml_dtypes.bfloat16)
    r = np.arange(512)[None, :]
    pp = np.arange(128)[:, None]
    masks = np.concatenate(
        [np.where(r >= 128 * m + pp, 0.0, NEG).astype(np.float32) for m in range(4)],
        axis=1)

    if "nc" not in _NC_CACHE:
        _NC_CACHE["nc"] = build_nc()
    nc = _NC_CACHE["nc"]

    in_maps = []
    for b in range(N_CORES):
        in_maps.append({
            "xin": np.ascontiguousarray(x[b].T),
            "wqT": wq, "wkT": wk, "wvT": wv,
            "wgO": wg, "wuO": wu, "wdO": wd,
            "clips": clips, "identin": ident, "identbin": identb, "onesbcin": onesbc,
            "onesmin": onesm, "masksin": masks,
        })
    res = run_bass_kernel_spmd(nc, in_maps, core_ids=list(range(N_CORES)))
    out = np.stack([np.ascontiguousarray(res.results[b]["outT"].T)
                    for b in range(N_CORES)], axis=0)
    return out


# version 47
# speedup vs baseline: 1.0364x; 1.0276x over previous
import contextlib
import numpy as np
import concourse.bass as bass
import concourse.mybir as mybir
from concourse.bass_utils import run_bass_kernel_spmd

F32 = mybir.dt.float32
BF16 = mybir.dt.bfloat16
I32 = mybir.dt.int32
AF = mybir.ActivationFunctionType
ALU = mybir.AluOpType
AX = mybir.AxisListType.X

T = 2048
D = 1024
II = 2816
NDP = D // 128
NIP = II // 128
NTC = T // 512
NKT = T // 128
QMAX = 127.0
NEG = -1.0e30
N_CORES = 8

AR_BYTES = 204800
XA, XB = 0, 8192
XC, XD = 16384, 20480
FB, GB = 24576, 32768
XN = 16384
QIX = 16384
SILU, HPRE, HMC = 0, 8192, 16384
XQ = 40960
SHW = 8960
SHB = 118784
WDS_STG = 141312
WD_QB1 = 152576
WG_STG = [40960, 45056, 49152]
WG_QB = [53248, 55296]
C_GQ = 52
W1, W2 = 73728, 118784
HQ_A, HQ_B = 0, 22528
EVT2, EVT2B, XRA2, XRB2 = 47104, 53248, 49152, 51200
CST2 = [55296 + 1024 * i for i in range(4)]
SQ2 = [59392 + 1024 * i for i in range(4)]
HSB3 = [118784, 141312]
XG_A, XG_B = 0, 16384
WD = 73728
XNT = 40960
V_ = 0
WQB = 73728
QT_ = 98304
KT_ = 131072
ATT = 32768
EVT, EVT_B, XRA, XRB = 98304, 100352, 102400, 104448
IDENT = 163840
IDENTB = 199904
MASKS = 164352
ONESBC = 172544
ONESM = 173056
ROWS = 173120
COLS = 197696
CCS = 198208
SCAL = 199232
WKS = 199360
WKS_B = 201408

R_MEAN, R_RSTD, R_G = 0, 2048, 4096
C_WGS, C_WUS, C_WDS = 0, 22, 44
C_GSC, C_USC, C_DSC = 52, 74, 96
C_RS, C_RH, C_SX127, C_SH = 104, 105, 106, 107
C_T0 = 108
S_SIGX, S_SIGD, S_GMAX, S_SX, S_RSX, S_SX127, S_SH, S_RSH, S_SH127, S_T0 = range(10)


class Sem:
    __slots__ = ("h", "n")

    def __init__(self, h):
        self.h = h
        self.n = 0


class Prog:
    ENGS = ("sync", "vector", "scalar", "tensor", "gpsimd")

    def __init__(self):
        self.items = {e: [] for e in self.ENGS}
        self._lastw = {}

    def emit(self, eng, fn):
        self.items[eng].append(fn)

    def dma(self, q, out_ap, in_ap, sem, slow=False):
        sem.n += 16
        self.emit(q, lambda e, o=out_ap, i=in_ap, h=sem.h, s=slow:
                  e.dma_start(out=o, in_=i, allow_slow_non_contiguous=s).then_inc(h, 16))

    def wait(self, eng, sem, n=None):
        v = sem.n if n is None else n
        if v <= 0:
            return
        key = (eng, id(sem))
        if self._lastw.get(key, -1) >= v:
            return
        self._lastw[key] = v
        self.emit(eng, lambda e, h=sem.h, v=v: e.wait_ge(h, v))

    def op(self, eng, fn, sem=None, inc=1, drain=True):
        if drain and eng in ("vector", "scalar", "gpsimd"):
            self.emit(eng, lambda e: e.drain())
        if sem is not None:
            sem.n += inc
            self.emit(eng, lambda e, fn=fn, h=sem.h, k=inc: fn(e).then_inc(h, k))
        else:
            self.emit(eng, fn)


def build_nc():
    nc = bass.Bass("TRN2", target_bir_lowering=False)

    xin = nc.dram_tensor("xin", [D, T], F32, kind="ExternalInput").ap()
    wqT = nc.dram_tensor("wqT", [D, D], F32, kind="ExternalInput").ap()
    wkT = nc.dram_tensor("wkT", [D, D], F32, kind="ExternalInput").ap()
    wvT = nc.dram_tensor("wvT", [D, D], F32, kind="ExternalInput").ap()
    wgS = nc.dram_tensor("wgS", [384, D], F32, kind="ExternalInput").ap()
    wuS = nc.dram_tensor("wuS", [384, D], F32, kind="ExternalInput").ap()
    wdS = nc.dram_tensor("wdS", [128, II], F32, kind="ExternalInput").ap()
    clips = nc.dram_tensor("clips", [1, 2], F32, kind="ExternalInput").ap()
    identin = nc.dram_tensor("identin", [128, 128], F32, kind="ExternalInput").ap()
    identbin = nc.dram_tensor("identbin", [128, 128], BF16, kind="ExternalInput").ap()
    onesbcin = nc.dram_tensor("onesbcin", [1, 128], F32, kind="ExternalInput").ap()
    onesmin = nc.dram_tensor("onesmin", [128, 1], BF16, kind="ExternalInput").ap()
    masksin = nc.dram_tensor("masksin", [128, T], F32, kind="ExternalInput").ap()

    outT = nc.dram_tensor("outT", [D, T], F32, kind="ExternalOutput").ap()

    xn_dram = nc.dram_tensor("xn_dram", [D, T], F32).ap()
    x1_dram = nc.dram_tensor("x1_dram", [D, T], F32).ap()
    x2_dram = nc.dram_tensor("x2_dram", [D, T], F32).ap()
    h_dram = nc.dram_tensor("h_dram", [128, NIP * T], F32).ap()
    sh_d = nc.dram_tensor("sh_d", [128, SHW], BF16).ap()
    full_d = nc.dram_tensor("full_d", [128 * N_CORES, SHW], BF16).ap()
    sc_d = nc.dram_tensor("sc_d", [128, 8], F32).ap()
    scf_d = nc.dram_tensor("scf_d", [128 * N_CORES, 8], F32).ap()
    ccio = [(nc.dram_tensor(f"cc{i}_in", [1, 128], F32).ap(),
             nc.dram_tensor(f"cc{i}_out", [1, 128], F32).ap()) for i in range(4)]

    es = contextlib.ExitStack()
    arena = es.enter_context(nc.sbuf_tensor("arena", [128, AR_BYTES], mybir.dt.uint8)).ap()
    AF32 = arena.bitcast(F32)
    ABF = arena.bitcast(BF16)
    AI32 = arena.bitcast(I32)

    def f32(off, p0, p1, e0, e1):
        return AF32[p0:p1, off // 4 + e0: off // 4 + e1]

    def bf(off, p0, p1, e0, e1):
        return ABF[p0:p1, off // 2 + e0: off // 2 + e1]

    def i32(off, p0, p1, e0, e1):
        return AI32[p0:p1, off // 4 + e0: off // 4 + e1]

    ident = f32(IDENT, 0, 128, 0, 128)
    identb = bf(IDENTB, 0, 128, 0, 128)
    onesbc = f32(ONESBC, 0, 1, 0, 128)
    onesm = bf(ONESM, 0, 128, 0, 1)

    def rows(slot, a, b):
        return f32(ROWS, 0, 1, slot + a, slot + b)

    def col(idx, k=1):
        return f32(COLS, 0, 128, idx, idx + k)

    def scal(idx, k=1):
        return f32(SCAL, 0, 1, idx, idx + k)

    pb = [es.enter_context(nc.psum_tensor(f"pb{i}", [128, 512], F32)) for i in range(8)]
    pbb = [t.ap().bitcast(BF16) for t in pb]

    dS = Sem(es.enter_context(nc.semaphore("dS")))
    dA = Sem(es.enter_context(nc.semaphore("dA")))
    dG = Sem(es.enter_context(nc.semaphore("dG")))
    cV = Sem(es.enter_context(nc.semaphore("cV")))
    cS = Sem(es.enter_context(nc.semaphore("cS")))
    cT = Sem(es.enter_context(nc.semaphore("cT")))
    cc = Sem(es.enter_context(nc.semaphore("cc")))
    dQ = [Sem(es.enter_context(nc.semaphore(f"dQ{i}"))) for i in range(3)]
    dL = [Sem(es.enter_context(nc.semaphore(f"dL{i}"))) for i in range(2)]
    dGs = [Sem(es.enter_context(nc.semaphore(f"dGs{i}"))) for i in range(2)]
    dT = [Sem(es.enter_context(nc.semaphore(f"dT{i}"))) for i in range(8)]
    dR = Sem(es.enter_context(nc.semaphore("dR")))
    dC = Sem(es.enter_context(nc.semaphore("dC")))
    dRg = Sem(es.enter_context(nc.semaphore("dRg")))
    dD = [Sem(es.enter_context(nc.semaphore(f"dD{i}"))) for i in range(2)]
    dW = Sem(es.enter_context(nc.semaphore("dW")))
    dP = [Sem(es.enter_context(nc.semaphore(f"dP{i}"))) for i in range(2)]

    p = Prog()
    bankg = [None] * 8

    def bank_wait(i):
        if bankg[i] is not None:
            p.wait("tensor", bankg[i][0], bankg[i][1])

    def bcast_scalar_to_col(s_ap, cidx):
        bank_wait(7)
        p.wait("tensor", cV)
        p.op("tensor", lambda e, s=s_ap: e.matmul(pb[7][:, 0:1], onesbc, s,
                                                  start=True, stop=True), cT)
        p.wait("vector", cT)
        p.op("vector", lambda e, c=col(cidx): e.tensor_copy(c, pb[7][:, 0:1]), cV)
        bankg[7] = (cV, cV.n)

    def bcast_rows_to_tiles(jobs, rows_mark=None):
        nj = len(jobs)
        first = True
        for j, (slot, dst) in enumerate(jobs):
            for ch in range(NTC):
                bk = 2 + ch
                bank_wait(bk)
                p.wait("tensor", cV, rows_mark)
                p.op("tensor", lambda e, slot=slot, ch=ch, bk=bk:
                     e.matmul(pb[bk][:, 0:512], onesbc,
                              rows(slot, ch * 512, (ch + 1) * 512),
                              start=True, stop=True),
                     cT if ch == NTC - 1 else None)
            p.wait("vector", cT)
            for ch in range(NTC):
                p.op("vector", lambda e, ch=ch, dst=dst: e.tensor_copy(
                    f32(dst, 0, 128, ch * 512, (ch + 1) * 512),
                    pb[2 + ch][:, 0:512]), cV, drain=first)
                first = False
            for ch in range(NTC):
                bankg[2 + ch] = (cV, cV.n)

    def ln_stats_gen(src_dram, src_guards=None,
                     bufs=(XA, XB, XC, XD, 24576, 28672)):
        for i in range(2, 6):
            bank_wait(i)
        ldd = {}
        consumed = {}

        BXA, BXB, BXC0, BXD0, BXC1, BXD1 = bufs

        def ld_stats(dp):
            if dp >= NDP:
                return
            buf = BXA if dp % 2 == 0 else BXB
            q = "sync" if dp % 2 == 0 else "scalar"
            d = dL[dp % 2]
            if dp - 2 in consumed:
                p.wait(q, cV, consumed[dp - 2][0])
                p.wait(q, cS, consumed[dp - 2][1])
            else:
                p.wait(q, cV)
                p.wait(q, cS)
            if src_guards is not None:
                g = src_guards.get(dp)
                if g is not None:
                    for gg in g:
                        p.wait(q, gg[0], gg[1])
            p.dma(q, f32(buf, 0, 128, 0, T), src_dram[dp * 128:(dp + 1) * 128, :], d)
            ldd[dp] = (d, d.n)

        ld_stats(0)
        ld_stats(1)
        smm = {}
        for dp in range(NDP):
            buf = BXA if dp % 2 == 0 else BXB
            BXC = BXC0 if dp % 2 == 0 else BXC1
            BXD = BXD0 if dp % 2 == 0 else BXD1
            p.wait("vector", ldd[dp][0], ldd[dp][1])
            p.wait("vector", cT, smm.get(dp - 2, 0))
            p.op("vector", lambda e, b=buf, BXC=BXC: e.tensor_copy(
                bf(BXC, 0, 128, 0, T), f32(b, 0, 128, 0, T)), cV, drain=False)
            castm = cV.n
            p.wait("scalar", ldd[dp][0], ldd[dp][1])
            p.wait("scalar", cT, smm.get(dp - 2, 0))
            p.op("scalar", lambda e, b=buf, BXD=BXD: e.activation(
                bf(BXD, 0, 128, 0, T), f32(b, 0, 128, 0, T), AF.Square), cS, drain=False)
            sqm = cS.n
            consumed[dp] = (castm, sqm)
            ld_stats(dp + 2)
            yield
            p.wait("tensor", cV, castm)
            p.wait("tensor", cS, sqm)
            for tc in range(NTC):
                st, sp = (dp == 0), (dp == NDP - 1)
                p.op("tensor", lambda e, tc=tc, st=st, sp=sp, BXC=BXC:
                     e.matmul(pb[2 + tc][0:1, :], onesm,
                              bf(BXC, 0, 128, tc * 512, (tc + 1) * 512),
                              start=st, stop=sp), None)
                p.op("tensor", lambda e, tc=tc, st=st, sp=sp, BXD=BXD:
                     e.matmul(pb[2 + tc][32:33, :], onesm,
                              bf(BXD, 0, 128, tc * 512, (tc + 1) * 512),
                              start=st, stop=sp),
                     cT if tc == NTC - 1 else None)
            smm[dp] = cT.n
        ln_stats_finish(smm[NDP - 1])
        yield

    def ln_stats_finish(mm_mark):
        p.wait("vector", cT, mm_mark)
        for tc in range(NTC):
            p.op("vector", lambda e, tc=tc: e.tensor_copy(
                rows(R_MEAN, tc * 512, (tc + 1) * 512), pb[2 + tc][0:1, :]), cV, drain=(tc == 0))
            p.op("vector", lambda e, tc=tc: e.tensor_copy(
                rows(R_RSTD, tc * 512, (tc + 1) * 512), pb[2 + tc][32:33, :]), cV, drain=False)
        for i in range(2, 6):
            bankg[i] = (cV, cV.n)
        p.wait("scalar", cV)
        p.op("scalar", lambda e: e.activation(rows(R_G, 0, T), rows(R_MEAN, 0, T), AF.Square), cS)
        p.wait("vector", cS)
        p.op("vector", lambda e: e.scalar_tensor_tensor(
            rows(R_RSTD, 0, T), rows(R_G, 0, T), -1.0, rows(R_RSTD, 0, T),
            ALU.mult, ALU.add), cV)
        p.op("vector", lambda e: e.tensor_scalar(
            rows(R_RSTD, 0, T), rows(R_RSTD, 0, T), 1e-5, None, ALU.add), cV)
        p.wait("scalar", cV)
        p.op("scalar", lambda e: e.activation(rows(R_RSTD, 0, T), rows(R_RSTD, 0, T),
                                              AF.Ln), cS)
        p.op("scalar", lambda e: e.activation(rows(R_RSTD, 0, T), rows(R_RSTD, 0, T),
                                              AF.Exp, scale=-0.5), cS)
        p.wait("vector", cS)
        p.op("vector", lambda e: e.scalar_tensor_tensor(
            rows(R_G, 0, T), rows(R_MEAN, 0, T), -1.0, rows(R_RSTD, 0, T),
            ALU.mult, ALU.mult), cV)
        bcast_rows_to_tiles([(R_RSTD, FB), (R_G, GB)], rows_mark=cV.n)

    def allreduce_scalar_max(col_ap, cc_pair, out_scal_idx):
        cin, cout = cc_pair
        stg = f32(CCS, 0, 1, 0, 128)
        p.wait("gpsimd", cV)
        p.dma("gpsimd", cin, col_ap, dC)
        p.wait("gpsimd", dC)
        p.op("gpsimd", lambda e: e.collective_compute(
            "AllReduce", ALU.max, replica_groups=[list(range(N_CORES))],
            ins=[cin], outs=[cout]), cc, drain=False)
        p.wait("gpsimd", cc)
        p.dma("gpsimd", stg, cout, dC)
        p.wait("vector", dC)
        p.op("vector", lambda e: e.reduce_max(scal(out_scal_idx), stg, axis=AX), cV)

    def qw_gen(w_dram, n_pt, width, c_slot, qdst_off, spill_dram, stgs, qbs,
               spill_guard=None, buf_guard=0, qb_buf_guard=0):
        nb = width // 128
        w3d = ABF[:, qdst_off // 2: qdst_off // 2 + nb * n_pt * 128].rearrange(
            "p (b i) -> p b i", b=nb)
        if spill_guard is not None:
            p.wait("scalar", spill_guard[0], spill_guard[1])
        qb_guard = {}
        q1_done = {}
        q2_done = {}
        ld = {}

        def load(pt):
            if pt >= n_pt:
                return
            q = "sync" if pt % 2 == 0 else "scalar"
            d = dQ[pt % 3]
            p.wait(q, cV, q2_done.get(pt - 3, buf_guard))
            p.dma(q, f32(stgs[pt % 3], 0, 128, 0, width),
                  w_dram[pt * 128:(pt + 1) * 128, :], d)
            ld[pt] = (d, d.n)

        def stage_a(pt):
            stg = stgs[pt % 3]
            rc = C_GQ + (pt % 2)
            p.wait("vector", ld[pt][0], ld[pt][1])
            p.op("vector", lambda e, s=stg, pt=pt: e.reduce_max(
                col(c_slot + pt), f32(s, 0, 128, 0, width), axis=AX,
                apply_absolute_value=True), cV, drain=(pt == 0))
            p.wait("vector", cS, q1_done.get(pt - 2, 0))
            p.op("vector", lambda e, pt=pt, rc=rc: e.reciprocal(
                col(rc), col(c_slot + pt)), cV)
            p.op("vector", lambda e, rc=rc: e.tensor_scalar(
                col(rc), col(rc), QMAX, None, ALU.mult), cV)
            p.wait("scalar", cV)
            p.op("scalar", lambda e, s=stg, rc=rc: e.activation(
                i32(s, 0, 128, 0, width), f32(s, 0, 128, 0, width),
                AF.Identity, scale=col(rc)), cS, drain=False)
            q1_done[pt] = cS.n

        def stage_b(pt):
            stg = stgs[pt % 3]
            qb = qbs[pt % 2]
            p.wait("vector", cS, q1_done[pt])
            p.wait("vector", cT, qb_guard.get(pt - 2, qb_buf_guard))
            p.op("vector", lambda e, s=stg, qb=qb: e.tensor_scalar(
                bf(qb, 0, 128, 0, width), i32(s, 0, 128, 0, width),
                -128.0, 127.0, ALU.max, ALU.min), cV, drain=False)
            q2_done[pt] = cV.n
            for g in range((nb + 7) // 8):
                bk = 6 + (g % 2)
                kblk = min(8, nb - g * 8)
                bank_wait(bk)
                p.wait("tensor", cV, q2_done[pt])
                for k in range(kblk):
                    wb = g * 8 + k
                    p.op("tensor", lambda e, wb=wb, k=k, bk=bk, qb=qb: e.transpose(
                        pbb[bk][:, k * 128:(k + 1) * 128],
                        bf(qb, 0, 128, wb * 128, (wb + 1) * 128), identb),
                         cT if k == kblk - 1 else None)
                p.wait("scalar", cT)
                psrc = pbb[bk][:, 0:kblk * 128].rearrange("p (b i) -> p b i", b=kblk)
                p.op("scalar", lambda e, g=g, pt=pt, bk=bk, kblk=kblk, psrc=psrc:
                     e.activation(w3d[:, g * 8:g * 8 + kblk, pt * 128:(pt + 1) * 128],
                                  psrc, AF.Identity), cS, drain=False)
                bankg[bk] = (cS, cS.n)
            qb_guard[pt] = cT.n

        load(0)
        load(1)
        for pt in range(n_pt):
            stage_a(pt)
            if pt > 0:
                stage_b(pt - 1)
            load(pt + 2)
            yield
        stage_b(n_pt - 1)
        p.wait("sync", cS)
        p.dma("sync", spill_dram[:, 0:n_pt * width], bf(qdst_off, 0, 128, 0, n_pt * width), dS)

    def qw_gen_psum(w_dram, n_pt, width, c_slot, spill_dram, stgs, qbs,
                    buf_guard=0):
        nb = width // 128
        qb_guard = {}
        q1_done = {}
        q2_done = {}
        ld = {}

        def load(pt):
            if pt >= n_pt:
                return
            q = "sync" if pt % 2 == 0 else "scalar"
            d = dQ[pt % 3]
            p.wait(q, cV, q2_done.get(pt - 3, buf_guard))
            p.dma(q, f32(stgs[pt % 3], 0, 128, 0, width),
                  w_dram[pt * 128:(pt + 1) * 128, :], d)
            ld[pt] = (d, d.n)

        def stage_a(pt):
            stg = stgs[pt % 3]
            rc = C_GQ + (pt % 2)
            p.wait("vector", ld[pt][0], ld[pt][1])
            p.op("vector", lambda e, s=stg, pt=pt: e.reduce_max(
                col(c_slot + pt), f32(s, 0, 128, 0, width), axis=AX,
                apply_absolute_value=True), cV, drain=(pt == 0))
            p.wait("vector", cS, q1_done.get(pt - 2, 0))
            p.op("vector", lambda e, pt=pt, rc=rc: e.reciprocal(
                col(rc), col(c_slot + pt)), cV)
            p.op("vector", lambda e, rc=rc: e.tensor_scalar(
                col(rc), col(rc), QMAX, None, ALU.mult), cV)
            p.wait("scalar", cV)
            p.op("scalar", lambda e, s=stg, rc=rc: e.activation(
                i32(s, 0, 128, 0, width), f32(s, 0, 128, 0, width),
                AF.Identity, scale=col(rc)), cS, drain=False)
            q1_done[pt] = cS.n

        def stage_b(pt):
            stg = stgs[pt % 3]
            qb = qbs[pt % 2]
            p.wait("vector", cS, q1_done[pt])
            p.wait("vector", cT, qb_guard.get(pt - 2, 0))
            p.op("vector", lambda e, s=stg, qb=qb: e.tensor_scalar(
                bf(qb, 0, 128, 0, width), i32(s, 0, 128, 0, width),
                -128.0, 127.0, ALU.max, ALU.min), cV, drain=False)
            q2_done[pt] = cV.n
            for g in range((nb + 7) // 8):
                bk = 6 + (g % 2)
                kblk = min(8, nb - g * 8)
                bank_wait(bk)
                p.wait("tensor", cV, q2_done[pt])
                for k in range(kblk):
                    wb = g * 8 + k
                    p.op("tensor", lambda e, wb=wb, k=k, bk=bk, qb=qb: e.transpose(
                        pbb[bk][:, k * 128:(k + 1) * 128],
                        bf(qb, 0, 128, wb * 128, (wb + 1) * 128), identb),
                         cT if k == kblk - 1 else None)
                p.wait("scalar", cT)
                slot = 59392 if g % 2 == 0 else 61440
                p.wait("scalar", dP[g % 2], dP[g % 2].n)
                p.op("scalar", lambda e, bk=bk, kblk=kblk, slot=slot: e.activation(
                    bf(slot, 0, 128, 0, kblk * 128), pbb[bk][:, 0:kblk * 128],
                    AF.Identity), cS, drain=False)
                bankg[bk] = (cS, cS.n)
                p.wait("gpsimd", cS)
                c0 = (pt * nb + g * 8) * 128
                p.dma("gpsimd", spill_dram[:, c0:c0 + kblk * 128],
                      bf(slot, 0, 128, 0, kblk * 128), dP[g % 2])
            qb_guard[pt] = cT.n

        load(0)
        load(1)
        for pt in range(n_pt):
            stage_a(pt)
            if pt > 0:
                stage_b(pt - 1)
            load(pt + 2)
            yield
        stage_b(n_pt - 1)

    def wq_sh_gen():
        units = ([("wg", s, D) for s in range(3)] +
                 [("wu", s, D) for s in range(3)] + [("wd", 0, II)])
        tmpc = [C_WGS, C_WGS + 1, C_WGS + 2, C_WUS, C_WUS + 1, C_WUS + 2, C_WDS]
        q1_done = {}
        q2_done = {}
        qb_guard = {}
        ld = {}

        def load(u):
            if u >= len(units):
                return
            kind, s, width = units[u]
            q = "sync" if u % 2 == 0 else "scalar"
            d = dQ[u % 3]
            p.wait(q, cV, q2_done.get(u - 3, 0))
            if kind == "wd":
                p.dma(q, f32(WDS_STG, 0, 128, 0, width), wdS, d)
            else:
                srcd = wgS if kind == "wg" else wuS
                p.dma(q, f32(WG_STG[u % 3], 0, 128, 0, width),
                      srcd[s * 128:(s + 1) * 128, :], d)
            ld[u] = (d, d.n)

        def stage_a(u):
            kind, s, width = units[u]
            stg = WDS_STG if kind == "wd" else WG_STG[u % 3]
            rc = C_GQ + (u % 2)
            p.wait("vector", ld[u][0], ld[u][1])
            p.op("vector", lambda e, stg=stg, width=width, u=u: e.reduce_max(
                col(tmpc[u]), f32(stg, 0, 128, 0, width), axis=AX,
                apply_absolute_value=True), cV, drain=(u == 0))
            p.op("vector", lambda e, u=u: e.tensor_copy(
                f32(59392, 0, 128, u, u + 1), col(tmpc[u])), cV)
            p.wait("vector", cS, q1_done.get(u - 2, 0))
            p.op("vector", lambda e, u=u, rc=rc: e.reciprocal(
                col(rc), col(tmpc[u])), cV)
            p.op("vector", lambda e, rc=rc: e.tensor_scalar(
                col(rc), col(rc), QMAX, None, ALU.mult), cV)
            p.wait("scalar", cV)
            p.op("scalar", lambda e, stg=stg, width=width, rc=rc: e.activation(
                i32(stg, 0, 128, 0, width), f32(stg, 0, 128, 0, width),
                AF.Identity, scale=col(rc)), cS, drain=False)
            q1_done[u] = cS.n

        def stage_b(u):
            kind, s, width = units[u]
            stg = WDS_STG if kind == "wd" else WG_STG[u % 3]
            qb = WD_QB1 if kind == "wd" else WG_QB[u % 2]
            nb = width // 128
            if kind == "wg":
                dst0 = s * 1024
            elif kind == "wu":
                dst0 = 3072 + s * 1024
            else:
                dst0 = 6144
            p.wait("vector", cS, q1_done[u])
            p.wait("vector", cT, qb_guard.get(u - 2, 0))
            p.op("vector", lambda e, stg=stg, qb=qb, width=width: e.tensor_scalar(
                bf(qb, 0, 128, 0, width), i32(stg, 0, 128, 0, width),
                -128.0, 127.0, ALU.max, ALU.min), cV, drain=False)
            q2_done[u] = cV.n
            for g in range((nb + 7) // 8):
                bk = 6 + (g % 2)
                kblk = min(8, nb - g * 8)
                bank_wait(bk)
                p.wait("tensor", cV, q2_done[u])
                for k in range(kblk):
                    wb = g * 8 + k
                    p.op("tensor", lambda e, wb=wb, k=k, bk=bk, qb=qb: e.transpose(
                        pbb[bk][:, k * 128:(k + 1) * 128],
                        bf(qb, 0, 128, wb * 128, (wb + 1) * 128), identb),
                         cT if k == kblk - 1 else None)
                p.wait("scalar", cT)
                p.op("scalar", lambda e, bk=bk, kblk=kblk, dst0=dst0, g=g: e.activation(
                    bf(SHB, 0, 128, dst0 + g * 1024, dst0 + g * 1024 + kblk * 128),
                    pbb[bk][:, 0:kblk * 128], AF.Identity), cS, drain=False)
                bankg[bk] = (cS, cS.n)
            qb_guard[u] = cT.n

        load(0)
        load(1)
        for u in range(len(units)):
            stage_a(u)
            if u > 0:
                stage_b(u - 1)
            load(u + 2)
            yield
        stage_b(len(units) - 1)
        p.wait("sync", cS)
        p.wait("sync", cV)
        p.dma("sync", sh_d, bf(SHB, 0, 128, 0, SHW), dS)
        p.op("vector", lambda e: e.tensor_scalar(
            f32(59392, 0, 128, 7, 8), f32(59392, 0, 128, 6, 7), 0.0,
            None, ALU.mult), cV)
        p.wait("sync", cV)
        p.dma("sync", sc_d, f32(59392, 0, 128, 0, 8), dS)
        p.wait("gpsimd", dS)
        p.op("gpsimd", lambda e: e.collective_compute(
            "AllGather", ALU.bypass, replica_groups=[list(range(N_CORES))],
            ins=[sh_d], outs=[full_d]), cc, drain=False)
        p.op("gpsimd", lambda e: e.collective_compute(
            "AllGather", ALU.bypass, replica_groups=[list(range(N_CORES))],
            ins=[sc_d], outs=[scf_d]), cc, drain=False)
        gat = cc.n
        yield
        p.wait("sync", cc, gat)
        p.wait("scalar", cc, gat)
        for c in range(N_CORES):
            n = 3 if c < 7 else 1
            q = "sync" if c % 2 == 0 else "scalar"
            p.dma(q, bf(W1, 0, 128, 3 * c * 1024, (3 * c + n) * 1024),
                  full_d[c * 128:(c + 1) * 128, 0:n * 1024], dR)
            p.dma(q, bf(W2, 0, 128, 3 * c * 1024, (3 * c + n) * 1024),
                  full_d[c * 128:(c + 1) * 128, 3072:3072 + n * 1024], dR)
            p.dma(q, f32(59392, 0, 128, c * 8, c * 8 + 8),
                  scf_d[c * 128:(c + 1) * 128, :], dR)
            yield
        p.wait("vector", dR)
        for c in range(N_CORES):
            n = 3 if c < 7 else 1
            p.op("vector", lambda e, c=c, n=n: e.tensor_copy(
                col(C_WGS + 3 * c, n), f32(59392, 0, 128, c * 8, c * 8 + n)),
                cV, drain=(c == 0))
            p.op("vector", lambda e, c=c, n=n: e.tensor_copy(
                col(C_WUS + 3 * c, n), f32(59392, 0, 128, c * 8 + 3, c * 8 + 3 + n)), cV)
            p.op("vector", lambda e, c=c: e.tensor_copy(
                col(C_WDS + c), f32(59392, 0, 128, c * 8 + 6, c * 8 + 7)), cV)

    def drive(*gens):
        live = list(gens)
        while live:
            for g in list(live):
                try:
                    next(g)
                except StopIteration:
                    live.remove(g)

    def drive_with(main, aux, aux_steps=2):
        while True:
            try:
                next(main)
            except StopIteration:
                return
            for _ in range(aux_steps):
                try:
                    next(aux)
                except StopIteration:
                    break

    def ffn(src_dram, dst_dram, ccA, ccB, wq_gens=None, stats_done=False,
            feed_stats=False):
        p.wait("vector", cT)
        p.wait("scalar", cT)
        p.wait("sync", dS)
        for s in dT + dD:
            p.wait("sync", s)
            p.wait("scalar", s)
            p.wait("vector", s)
        p.wait("scalar", dS)
        if not stats_done:
            if wq_gens is not None:
                drive_with(ln_stats_gen(src_dram), wq_gens, aux_steps=1)
            else:
                drive(ln_stats_gen(src_dram))
        ldx = {}
        spg = {}
        xn_mark = [0]

        def ld_xn(dp):
            if dp >= NDP:
                return
            buf = XA if dp % 2 == 0 else XB
            q = "sync" if dp % 2 == 0 else "scalar"
            d = dL[dp % 2]
            if dp - 2 in spg:
                g = spg[dp - 2]
                p.wait(q, g[0], g[1])
            else:
                p.wait(q, cV)
                p.wait(q, cS)
            p.dma(q, f32(buf, 0, 128, 0, T), src_dram[dp * 128:(dp + 1) * 128, :], d)
            ldx[dp] = (d, d.n)

        def xn_gen():
            ld_xn(0)
            ld_xn(1)
            for dp in range(NDP):
                buf = XA if dp % 2 == 0 else XB
                p.wait("vector", ldx[dp][0], ldx[dp][1])
                p.op("vector", lambda e, b=buf: e.tensor_tensor(
                    f32(b, 0, 128, 0, T), f32(b, 0, 128, 0, T),
                    f32(FB, 0, 128, 0, T), ALU.mult), cV, drain=False)
                p.op("vector", lambda e, b=buf: e.tensor_tensor(
                    f32(b, 0, 128, 0, T), f32(b, 0, 128, 0, T),
                    f32(GB, 0, 128, 0, T), ALU.add), cV)
                p.op("vector", lambda e, dp=dp, b=buf: e.reduce_max(
                    col(C_T0 + dp), f32(b, 0, 128, 0, T), axis=AX,
                    apply_absolute_value=True), cV)
                p.wait("gpsimd", cV, cV.n)
                p.dma("gpsimd", xn_dram[dp * 128:(dp + 1) * 128, :], f32(buf, 0, 128, 0, T),
                      dGs[dp % 2])
                spg[dp] = (dGs[dp % 2], dGs[dp % 2].n)
                ld_xn(dp + 2)
                yield
            p.op("vector", lambda e: e.reduce_max(col(C_T0 + 8), col(C_T0, NDP), axis=AX), cV)
            xn_mark[0] = cV.n
            allreduce_scalar_max(col(C_T0 + 8), ccA, S_GMAX)
            yield

        if wq_gens is not None:
            drive(xn_gen(), wq_gens)
        else:
            drive(xn_gen())
        p.op("vector", lambda e: e.tensor_tensor(scal(S_SX), scal(S_SIGX), scal(S_GMAX), ALU.mult), cV)
        p.op("vector", lambda e: e.tensor_scalar(scal(S_SX), scal(S_SX), 1.0 / QMAX, None, ALU.mult), cV)
        p.op("vector", lambda e: e.reciprocal(scal(S_RSX), scal(S_SX)), cV)
        p.op("vector", lambda e: e.tensor_scalar(scal(S_SX127), scal(S_SX), 1.0 / QMAX, None, ALU.mult), cV)
        bcast_scalar_to_col(scal(S_RSX), C_RS)
        bcast_scalar_to_col(scal(S_SX127), C_SX127)
        p.wait("vector", cS)
        p.op("vector", lambda e: e.tensor_scalar(
            col(C_GSC, NIP), col(C_WGS, NIP), col(C_SX127), None, ALU.mult), cV)
        p.op("vector", lambda e: e.tensor_scalar(
            col(C_USC, NIP), col(C_WUS, NIP), col(C_SX127), None, ALU.mult), cV)
        p.wait("sync", dGs[0])
        p.wait("sync", dGs[1])
        p.wait("scalar", dGs[0])
        p.wait("scalar", dGs[1])
        xn3 = xn_dram.rearrange("(a p) t -> p a t", p=128)
        ldq = {}
        qdone = {}

        def ld_xq(u):
            if u >= NDP // 2:
                return
            g = XG_A if u % 2 == 0 else XG_B
            q = "sync" if u % 2 == 0 else "scalar"
            d = dL[u % 2]
            if u - 2 in qdone:
                p.wait(q, cV, qdone[u - 2])
            else:
                p.wait(q, cV, xn_mark[0])
            p.dma(q, f32(g, 0, 128, 0, 2 * T), xn3[:, 2 * u:2 * u + 2, :], d)
            ldq[u] = (d, d.n)

        ld_xq(0)
        ld_xq(1)
        p.wait("vector", dP[0])
        p.wait("vector", dP[1])
        for u in range(NDP // 2):
            g = XG_A if u % 2 == 0 else XG_B
            p.wait("scalar", ldq[u][0], ldq[u][1])
            p.wait("scalar", cV)
            p.op("scalar", lambda e, g=g: e.activation(
                i32(g, 0, 128, 0, 2 * T), f32(g, 0, 128, 0, 2 * T),
                AF.Identity, scale=col(C_RS)), cS, drain=False)
            p.wait("vector", cS)
            p.op("vector", lambda e, g=g, u=u: e.tensor_scalar(
                bf(XQ, 0, 128, 2 * u * T, (2 * u + 2) * T), i32(g, 0, 128, 0, 2 * T),
                -128.0, 127.0, ALU.max, ALU.min), cV, drain=False)
            qdone[u] = cV.n
            ld_xq(u + 2)
        p.wait("tensor", cV)
        p.wait("tensor", dS)
        p.wait("tensor", dR)
        p.wait("tensor", dRg)
        for ib in range(NIP):
            for i in range(4):
                bank_wait(i)
            for dp in range(NDP):
                for tc in range(NTC):
                    st, sp = (dp == 0), (dp == NDP - 1)
                    p.op("tensor", lambda e, ib=ib, dp=dp, tc=tc, st=st, sp=sp:
                         e.matmul(pb[tc][:, 0:512],
                                  bf(W1, 0, 128, (ib * NDP + dp) * 128,
                                     (ib * NDP + dp + 1) * 128),
                                  bf(XQ, 0, 128, dp * T + tc * 512, dp * T + (tc + 1) * 512),
                                  start=st, stop=sp),
                         cT if (sp and tc == NTC - 1) else None)
            gate_done = cT.n
            for i in range(4, 8):
                bank_wait(i)
            for dp in range(NDP):
                for tc in range(NTC):
                    st, sp = (dp == 0), (dp == NDP - 1)
                    p.op("tensor", lambda e, ib=ib, dp=dp, tc=tc, st=st, sp=sp:
                         e.matmul(pb[4 + tc][:, 0:512],
                                  bf(W2, 0, 128, (ib * NDP + dp) * 128,
                                     (ib * NDP + dp + 1) * 128),
                                  bf(XQ, 0, 128, dp * T + tc * 512, dp * T + (tc + 1) * 512),
                                  start=st, stop=sp),
                         cT if (sp and tc == NTC - 1) else None)
            up_done = cT.n
            p.wait("scalar", cT, gate_done)
            p.wait("scalar", cV)
            for tc in range(NTC):
                p.op("scalar", lambda e, ib=ib, tc=tc: e.activation(
                    f32(SILU, 0, 128, tc * 512, (tc + 1) * 512), pb[tc][:, 0:512],
                    AF.Silu, scale=col(C_GSC + ib)), cS)
            for i in range(4):
                bankg[i] = (cS, cS.n)
            p.wait("vector", cT, up_done)
            p.wait("vector", cS)
            p.wait("vector", dG)
            for tc in range(NTC):
                p.op("vector", lambda e, ib=ib, tc=tc: e.scalar_tensor_tensor(
                    f32(HPRE, 0, 128, tc * 512, (tc + 1) * 512), pb[4 + tc][:, 0:512],
                    col(C_USC + ib), f32(SILU, 0, 128, tc * 512, (tc + 1) * 512),
                    ALU.mult, ALU.mult), cV)
                p.op("vector", lambda e, ib=ib, tc=tc: e.reduce_max(
                    f32(HMC, 0, 128, ib * NTC + tc, ib * NTC + tc + 1),
                    f32(HPRE, 0, 128, tc * 512, (tc + 1) * 512),
                    axis=AX, apply_absolute_value=True), cV)
                p.wait("gpsimd", cV)
                p.dma("gpsimd", h_dram[:, (tc * NIP + ib) * 512:(tc * NIP + ib + 1) * 512],
                      f32(HPRE, 0, 128, tc * 512, (tc + 1) * 512), dG)
            for i in range(4, 8):
                bankg[i] = (cV, cV.n)
        p.op("vector", lambda e: e.reduce_max(col(C_T0), f32(HMC, 0, 128, 0, NIP * NTC), axis=AX), cV)
        p.wait("scalar", cT)
        p.wait("scalar", dG)
        uld = {}
        p.dma("scalar", f32(HSB3[0], 0, 128, 0, 5632), h_dram[:, 0:5632], dQ[0])
        uld[0] = (dQ[0], dQ[0].n)
        p.dma("scalar", f32(HSB3[1], 0, 128, 0, 5632), h_dram[:, 5632:11264], dQ[1])
        uld[1] = (dQ[1], dQ[1].n)
        allreduce_scalar_max(col(C_T0), ccB, S_GMAX)
        p.op("vector", lambda e: e.tensor_tensor(scal(S_SH), scal(S_SIGD), scal(S_GMAX), ALU.mult), cV)
        p.op("vector", lambda e: e.tensor_scalar(scal(S_SH), scal(S_SH), 1.0 / QMAX, None, ALU.mult), cV)
        p.op("vector", lambda e: e.reciprocal(scal(S_RSH), scal(S_SH)), cV)
        p.op("vector", lambda e: e.tensor_scalar(scal(S_SH127), scal(S_SH), 0.5 / QMAX, None, ALU.mult), cV)
        bcast_scalar_to_col(scal(S_RSH), C_RH)
        bcast_scalar_to_col(scal(S_SH127), C_SH)
        p.op("vector", lambda e: e.tensor_scalar(
            col(C_DSC, NDP), col(C_WDS, NDP), col(C_SH), None, ALU.mult), cV)
        rh_mark = cV.n
        p.wait("sync", cT)
        p.wait("sync", dS)
        for c in range(N_CORES):
            p.dma("sync", bf(WD, 0, 128, c * NIP * 128, (c + 1) * NIP * 128),
                  full_d[c * 128:(c + 1) * 128, 6144:6144 + NIP * 128], dR)
        p.wait("tensor", dR)
        units = [(tc, hf) for tc in range(NTC) for hf in range(2)]
        uq_done = {}
        dn_store = {}
        sm_mm = {}
        sm_cast = {}
        sm_sq = {}

        def emit_sm_mms(g):
            if g < 0 or g not in sm_cast:
                return
            tc_g, hb_g = g // NDP, g % NDP
            if hb_g == 0:
                bank_wait(2 + tc_g)
            p.wait("tensor", cV, sm_cast[g])
            p.wait("tensor", cS, sm_sq[g])
            st2, sp2 = (hb_g == 0), (hb_g == NDP - 1)
            cstb, sqb = CST2[g % 4], SQ2[g % 4]
            p.op("tensor", lambda e, tc_g=tc_g, cstb=cstb, st2=st2, sp2=sp2:
                 e.matmul(pb[2 + tc_g][0:1, :], onesm,
                          bf(cstb, 0, 128, 0, 512), start=st2, stop=sp2), None)
            p.op("tensor", lambda e, tc_g=tc_g, sqb=sqb, st2=st2, sp2=sp2:
                 e.matmul(pb[2 + tc_g][32:33, :], onesm,
                          bf(sqb, 0, 128, 0, 512), start=st2, stop=sp2), cT)
            sm_mm[g] = cT.n

        def emit_unit_load(ui):
            if ui >= len(units):
                return
            tc, hf = units[ui]
            g = HSB3[ui % 2]
            p.wait("scalar", cV, uq_done[ui - 2])
            p.dma("scalar", f32(g, 0, 128, 0, 5632),
                  h_dram[:, (tc * NIP + hf * 11) * 512:(tc * NIP + hf * 11) * 512 + 5632],
                  dQ[ui % 2])
            uld[ui] = (dQ[ui % 2], dQ[ui % 2].n)

        def emit_unit_quant(ui):
            tc, hf = units[ui]
            g = HSB3[ui % 2]
            hq = HQ_A if tc % 2 == 0 else HQ_B
            p.wait("scalar", uld[ui][0], uld[ui][1])
            p.wait("scalar", cV, rh_mark)
            p.op("scalar", lambda e, g=g: e.activation(
                i32(g, 0, 128, 0, 5632), f32(g, 0, 128, 0, 5632),
                AF.Identity, scale=col(C_RH)), cS, drain=False)
            q1_mark = cS.n
            p.wait("vector", cS, q1_mark)
            p.op("vector", lambda e, g=g, hq=hq, hf=hf: e.tensor_scalar(
                bf(hq, 0, 128, hf * 5632, (hf + 1) * 5632), i32(g, 0, 128, 0, 5632),
                -128.0, 127.0, ALU.max, ALU.min), cV, drain=False)
            uq_done[ui] = cV.n

        p.wait("vector", dP[0])
        p.wait("vector", dP[1])
        p.wait("scalar", dP[0])
        p.wait("scalar", dP[1])
        emit_unit_quant(0)
        emit_unit_quant(1)
        for tc in range(NTC):
            if 2 * tc + 2 < len(units):
                emit_unit_load(2 * tc + 2)
                emit_unit_load(2 * tc + 3)
                emit_unit_quant(2 * tc + 2)
                emit_unit_quant(2 * tc + 3)
            hq = HQ_A if tc % 2 == 0 else HQ_B
            for hb in range(NDP):
                bk = (tc * NDP + hb) % 2
                bank_wait(bk)
                for ip in range(NIP):
                    st, sp = (ip == 0), (ip == NIP - 1)
                    if ip == 0:
                        p.wait("tensor", cV, uq_done[2 * tc])
                    elif ip == 11:
                        p.wait("tensor", cV, uq_done[2 * tc + 1])
                    p.op("tensor", lambda e, ip=ip, hb=hb, hq=hq, bk=bk, st=st, sp=sp:
                         e.matmul(pb[bk][:, 0:512],
                                  bf(WD, 0, 128, (hb * NIP + ip) * 128,
                                     (hb * NIP + ip + 1) * 128),
                                  bf(hq, 0, 128, ip * 512, (ip + 1) * 512),
                                  start=st, stop=sp),
                         cT if sp else None)
                g_i = tc * NDP + hb
                xr = XRA2 if g_i % 2 == 0 else XRB2
                ev2 = EVT2 if g_i % 2 == 0 else EVT2B
                dxr = dL[g_i % 2]
                p.wait("sync", cS)
                p.dma("sync", f32(xr, 0, 128, 0, 512),
                      src_dram[hb * 128:(hb + 1) * 128, tc * 512:(tc + 1) * 512], dxr)
                xr_mark = dxr.n
                p.wait("scalar", cT)
                p.wait("scalar", dxr, xr_mark)
                p.wait("scalar", cV)
                if g_i - 2 in dn_store:
                    s_prev = dn_store[g_i - 2][0]
                    p.wait("scalar", s_prev, s_prev.n)
                p.op("scalar", lambda e, bk=bk, hb=hb, ev2=ev2: e.activation(
                    f32(ev2, 0, 128, 0, 512), pb[bk][:, 0:512], AF.Identity,
                    scale=col(C_DSC + hb)), cS)
                bankg[bk] = (cS, cS.n)
                p.wait("vector", cS)
                p.op("vector", lambda e, xr=xr, ev2=ev2: e.tensor_tensor(
                    f32(ev2, 0, 128, 0, 512), f32(ev2, 0, 128, 0, 512),
                    f32(xr, 0, 128, 0, 512), ALU.add), cV)
                add_mark = cV.n
                p.wait("sync", cV)
                p.dma("sync", dst_dram[hb * 128:(hb + 1) * 128, tc * 512:(tc + 1) * 512],
                      f32(ev2, 0, 128, 0, 512), dD[g_i % 2])
                dn_store[g_i] = (dD[g_i % 2], dD[g_i % 2].n)
                if feed_stats:
                    emit_sm_mms(g_i - 4)
                    cstb = CST2[g_i % 4]
                    sqb = SQ2[g_i % 4]
                    if g_i - 4 in sm_mm:
                        p.wait("vector", cT, sm_mm[g_i - 4])
                    p.op("vector", lambda e, ev2=ev2, cstb=cstb: e.tensor_copy(
                        bf(cstb, 0, 128, 0, 512), f32(ev2, 0, 128, 0, 512)),
                        cV, drain=True)
                    sm_cast[g_i] = cV.n
                    p.wait("scalar", cV, add_mark)
                    if g_i - 4 in sm_mm:
                        p.wait("scalar", cT, sm_mm[g_i - 4])
                    p.op("scalar", lambda e, ev2=ev2, sqb=sqb: e.activation(
                        bf(sqb, 0, 128, 0, 512), f32(ev2, 0, 128, 0, 512),
                        AF.Square), cS, drain=False)
                    sm_sq[g_i] = cS.n

        if feed_stats:
            for g in range(NTC * NDP - 4, NTC * NDP):
                emit_sm_mms(g)
        return sm_mm.get(NTC * NDP - 1)

    def attention(src_dram, dst_dram, stats2_gen=None, stats2_guards=None,
                  stats_premixed=None):
        p.wait("vector", cT)
        p.wait("scalar", cT)
        p.wait("sync", dS)
        for s in dT + dD:
            p.wait("sync", s)
            p.wait("scalar", s)
            p.wait("vector", s)
        p.wait("scalar", dS)
        if stats_premixed is not None:
            ln_stats_finish(stats_premixed)
        else:
            drive(ln_stats_gen(src_dram))
        for dp in range(NDP):
            buf = XA if dp % 2 == 0 else XB
            dxn = dL[dp % 2]
            p.wait("sync", cV)
            p.wait("sync", cS)
            p.dma("sync", f32(buf, 0, 128, 0, T), src_dram[dp * 128:(dp + 1) * 128, :], dxn)
            p.wait("vector", dxn, dxn.n)
            p.op("vector", lambda e, b=buf: e.tensor_tensor(
                f32(b, 0, 128, 0, T), f32(b, 0, 128, 0, T),
                f32(FB, 0, 128, 0, T), ALU.mult), cV, drain=False)
            p.op("vector", lambda e, dp=dp, b=buf: e.tensor_tensor(
                bf(XNT, 0, 128, dp * T, (dp + 1) * T), f32(b, 0, 128, 0, T),
                f32(GB, 0, 128, 0, T), ALU.add), cV)

        def proj_o(w_dram, dst_off, sc):
            p.wait("gpsimd", cT)
            for dp in range(NDP):
                p.dma("gpsimd", bf(WQB, 0, 128, dp * D, (dp + 1) * D),
                      w_dram[dp * 128:(dp + 1) * 128, :], dW)
            p.wait("tensor", dW)
            p.wait("tensor", cV)
            for ob in range(NDP):
                for tc in range(NTC):
                    bk = (ob * NTC + tc) % 2
                    bank_wait(bk)
                    for dp in range(NDP):
                        st, sp = (dp == 0), (dp == NDP - 1)
                        p.op("tensor", lambda e, dp=dp, ob=ob, tc=tc, bk=bk, st=st, sp=sp:
                             e.matmul(pb[bk][:, 0:512],
                                      bf(WQB, 0, 128, dp * D + ob * 128, dp * D + (ob + 1) * 128),
                                      bf(XNT, 0, 128, dp * T + tc * 512, dp * T + (tc + 1) * 512),
                                      start=st, stop=sp),
                             cT if sp else None)
                    p.wait("scalar", cT)
                    p.op("scalar", lambda e, ob=ob, tc=tc, bk=bk, dst=dst_off, sc=sc:
                         e.activation(bf(dst, 0, 128, ob * T + tc * 512, ob * T + (tc + 1) * 512),
                                      pb[bk][:, 0:512], AF.Identity, scale=sc), cS)
                    bankg[bk] = (cS, cS.n)

        def proj_v(w_dram, dst_off):
            p.wait("gpsimd", cT)
            for dp in range(NDP):
                p.dma("gpsimd", bf(WQB, 0, 128, dp * D, (dp + 1) * D),
                      w_dram[dp * 128:(dp + 1) * 128, :], dW)
            p.wait("tensor", dW)
            p.wait("tensor", cV)
            for tb in range(NKT):
                for oc in range(2):
                    bk = (tb * 2 + oc) % 2
                    bank_wait(bk)
                    for dp in range(NDP):
                        st, sp = (dp == 0), (dp == NDP - 1)
                        p.op("tensor", lambda e, dp=dp, tb=tb, oc=oc, bk=bk, st=st, sp=sp:
                             e.matmul(pb[bk][:, 0:512],
                                      bf(XNT, 0, 128, dp * T + tb * 128, dp * T + (tb + 1) * 128),
                                      bf(WQB, 0, 128, dp * D + oc * 512, dp * D + (oc + 1) * 512),
                                      start=st, stop=sp),
                             cT if sp else None)
                    p.wait("scalar", cT)
                    p.op("scalar", lambda e, tb=tb, oc=oc, bk=bk, dst=dst_off:
                         e.activation(bf(dst, 0, 128, tb * D + oc * 512, tb * D + (oc + 1) * 512),
                                      pb[bk][:, 0:512], AF.Identity, scale=1.0), cS)
                    bankg[bk] = (cS, cS.n)

        proj_o(wqT, QT_, 1.0 / 32.0)
        proj_o(wkT, KT_, 1.0)
        proj_v(wvT, V_)

        mm_done = {}
        exp_done = {}
        wks_of = {}

        def emit_scores_mm(kt):
            jd = kt // 4
            base = 0 if kt % 2 == 0 else 4
            for b2 in range(base, base + 4):
                bank_wait(b2)
            p.wait("tensor", cS, exp_done.get(kt - 2, 0))
            p.wait("tensor", cV, norm_done.get(kt - 2, 0))
            for j in range(jd, NTC):
                bk = base + (j - jd)
                for dp in range(NDP):
                    st, sp = (dp == 0), (dp == NDP - 1)
                    p.op("tensor", lambda e, kt=kt, j=j, dp=dp, bk=bk, st=st, sp=sp:
                         e.matmul(pb[bk][:, 0:512],
                                  bf(KT_, 0, 128, dp * T + kt * 128, dp * T + (kt + 1) * 128),
                                  bf(QT_, 0, 128, dp * T + j * 512, dp * T + (j + 1) * 512),
                                  start=st, stop=sp),
                         cT if sp else None)
            mm_done[kt] = cT.n

        norm_done = {}

        def emit_softmax(kt):
            jd, m = kt // 4, kt % 4
            nch = NTC - jd
            base = 0 if kt % 2 == 0 else 4
            wks = WKS if kt % 2 == 0 else WKS_B
            c0 = C_T0 if kt % 2 == 0 else C_T0 + 10
            cb_, cs_ = c0 + 4, c0 + 5
            cr_ = c0 + 9
            wks_of[kt] = wks
            p.wait("vector", cT, mm_done[kt])
            p.wait("vector", cS, exp_done.get(kt - 2, 0))
            p.op("vector", lambda e, m=m, bk=base, wks=wks: e.tensor_tensor(
                f32(wks, 0, 128, 0, 512), pb[bk][:, 0:512],
                f32(MASKS, 0, 128, m * 512, (m + 1) * 512), ALU.add), cV)
            p.op("vector", lambda e, wks=wks, c0=c0: e.reduce_max(
                col(c0), f32(wks, 0, 128, 0, 512), axis=AX, negate=True), cV)
            for j in range(jd + 1, NTC):
                p.op("vector", lambda e, j=j, jd=jd, bk=base + (j - jd), c0=c0: e.reduce_max(
                    col(c0 + j - jd), pb[bk][:, 0:512], axis=AX, negate=True), cV,
                    drain=False)
            if nch > 1:
                p.op("vector", lambda e, n=nch, c0=c0, cb_=cb_: e.tensor_reduce(
                    col(cb_), col(c0, n), axis=AX, op=ALU.min), cV)
            else:
                p.op("vector", lambda e, c0=c0, cb_=cb_: e.tensor_copy(col(cb_), col(c0)), cV)
            p.wait("scalar", cV)
            for j in range(jd, NTC):
                p.op("scalar", lambda e, kt=kt, j=j, jd=jd, bk=base + (j - jd), wks=wks,
                     cb_=cb_, cs_=cs_:
                     e.activation(bf(ATT, 0, 128, kt * T + j * 512, kt * T + (j + 1) * 512),
                                  (f32(wks, 0, 128, 0, 512) if j == jd else pb[bk][:, 0:512]),
                                  AF.Exp, bias=col(cb_),
                                  accum_out=col(cs_ + j - jd)), cS, drain=False)
            exp_done[kt] = cS.n
            p.wait("vector", cS)
            if nch > 1:
                p.op("vector", lambda e, n=nch, cs_=cs_, cr_=cr_: e.tensor_reduce(
                    col(cr_), col(cs_, n), axis=AX, op=ALU.add), cV)
            else:
                p.op("vector", lambda e, cs_=cs_, cr_=cr_: e.tensor_copy(col(cr_), col(cs_)), cV)
            p.op("vector", lambda e, cr_=cr_: e.reciprocal(col(cr_), col(cr_)), cV)
            p.op("vector", lambda e, kt=kt, jd=jd, cr_=cr_: e.tensor_scalar(
                bf(ATT, 0, 128, kt * T + jd * 512, (kt + 1) * T),
                bf(ATT, 0, 128, kt * T + jd * 512, (kt + 1) * T),
                col(cr_), None, ALU.mult), cV)
            norm_done[kt] = cV.n

        emit_scores_mm(0)
        for kt in range(1, NKT):
            emit_scores_mm(kt)
            emit_softmax(kt - 1)
        emit_softmax(NKT - 1)
        for i in range(8):
            bankg[i] = (cV, norm_done[NKT - 1])
        groups = [(db, j) for db in range(NDP) for j in range(NTC)]
        xl_done = {}
        ev_done = {}
        st_done = {}

        def ld_av(gi):
            if gi >= len(groups):
                return
            db, j = groups[gi]
            xr = XRA if gi % 2 == 0 else XRB
            p.wait("gpsimd", cT, mm_done[NKT - 1])
            p.wait("gpsimd", cV, ev_done.get(gi - 2, 0))
            p.dma("gpsimd", f32(xr, 0, 128, 0, 512),
                  src_dram[db * 128:(db + 1) * 128, j * 512:(j + 1) * 512], dGs[gi % 2])
            xl_done[gi] = (dGs[gi % 2], dGs[gi % 2].n)

        ld_av(0)
        ld_av(1)
        s2_steps = 0
        for gi, (db, j) in enumerate(groups):
            bk = gi % 2
            bank_wait(bk)
            nkt = min(4 * j + 4, NKT)
            for kt in range(nkt):
                st, sp = (kt == 0), (kt == nkt - 1)
                p.op("tensor", lambda e, kt=kt, db=db, j=j, bk=bk, st=st, sp=sp:
                     e.matmul(pb[bk][:, 0:512],
                              bf(V_, 0, 128, kt * D + db * 128, kt * D + (db + 1) * 128),
                              bf(ATT, 0, 128, kt * T + j * 512, kt * T + (j + 1) * 512),
                              start=st, stop=sp),
                     cT if sp else None)
            xr = XRA if gi % 2 == 0 else XRB
            ev = EVT if gi % 2 == 0 else EVT_B
            p.wait("vector", cT)
            p.wait("vector", xl_done[gi][0], xl_done[gi][1])
            if gi - 2 in st_done:
                s_prev = st_done[gi - 2][0]
                p.wait("vector", s_prev, s_prev.n)
            p.op("vector", lambda e, bk=bk, xr=xr, ev=ev: e.tensor_tensor(
                f32(ev, 0, 128, 0, 512), pb[bk][:, 0:512],
                f32(xr, 0, 128, 0, 512), ALU.add), cV)
            bankg[bk] = (cV, cV.n)
            ev_done[gi] = cV.n
            p.wait("sync", cV)
            p.dma("sync", dst_dram[db * 128:(db + 1) * 128, j * 512:(j + 1) * 512],
                  f32(ev, 0, 128, 0, 512), dT[db])
            st_done[gi] = (dT[db], dT[db].n)
            ld_av(gi + 2)
            if j == NTC - 1:
                if stats2_guards is not None:
                    stats2_guards[db] = [(dT[db], dT[db].n)]
                if stats2_gen is not None and db >= 2:
                    try:
                        next(stats2_gen)
                        s2_steps += 1
                    except StopIteration:
                        pass

    p.dma("sync", ident, identin, dS)
    p.dma("sync", identb, identbin, dS)
    p.dma("sync", onesbc, onesbcin, dS)
    p.dma("sync", onesm, onesmin, dS)
    p.dma("sync", f32(MASKS, 0, 128, 0, T), masksin, dS)
    p.dma("sync", scal(S_T0, 2), clips, dS)
    p.wait("scalar", dS)
    p.op("scalar", lambda e: e.activation(scal(S_SIGX, 2), scal(S_T0, 2), AF.Sigmoid), cS)
    p.wait("vector", dS)

    smm1 = ffn(xin, x1_dram, ccio[0], ccio[1], wq_gens=wq_sh_gen(),
               feed_stats=True)
    s2_guards = {}
    s2 = ln_stats_gen(x2_dram, src_guards=s2_guards,
                      bufs=(106496, 114688, 122880, 126976, 131072, 135168))
    attention(x1_dram, x2_dram, stats2_gen=s2, stats2_guards=s2_guards,
              stats_premixed=smm1)
    drive(s2)

    p.wait("gpsimd", cT)
    p.wait("gpsimd", cV)
    p.wait("gpsimd", cS)
    for c in range(N_CORES):
        n = 3 if c < 7 else 1
        p.dma("gpsimd", bf(W1, 0, 128, 3 * c * 1024, (3 * c + n) * 1024),
              full_d[c * 128:(c + 1) * 128, 0:n * 1024], dRg)
        p.dma("gpsimd", bf(W2, 0, 128, 3 * c * 1024, (3 * c + n) * 1024),
              full_d[c * 128:(c + 1) * 128, 3072:3072 + n * 1024], dRg)

    ffn(x2_dram, outT, ccio[2], ccio[3], stats_done=True)

    p.wait("sync", dS)
    p.wait("sync", dA)
    p.wait("sync", dG)
    for s in dQ + dL + dGs + dT + dD + dP + [dR, dRg, dW, dC]:
        p.wait("sync", s)
    p.wait("sync", cV)
    p.wait("sync", cS)
    p.wait("sync", cT)
    p.wait("sync", cc)

    with nc.Block() as block:
        for eng in Prog.ENGS:
            fns = p.items[eng]
            getattr(block, eng)(lambda e, fns=fns: [fn(e) for fn in fns])

    return nc


_NC_CACHE = {}


def kernel(L0_attn_x, L0_attn_W_q, L0_attn_W_k, L0_attn_W_v,
           L0_ffn_clip_x, L0_ffn_clip_down, L0_ffn_w_gate, L0_ffn_w_up,
           L0_ffn_w_down):
    import ml_dtypes
    x = np.asarray(L0_attn_x)
    wq = np.ascontiguousarray(np.asarray(L0_attn_W_q).T)
    wk = np.ascontiguousarray(np.asarray(L0_attn_W_k).T)
    wv = np.ascontiguousarray(np.asarray(L0_attn_W_v).T)
    wg = np.ascontiguousarray(np.asarray(L0_ffn_w_gate))
    wu = np.ascontiguousarray(np.asarray(L0_ffn_w_up))
    wd = np.ascontiguousarray(np.asarray(L0_ffn_w_down))
    wgSs, wuSs, wdSs = [], [], []
    for c in range(N_CORES):
        pts = [min(3 * c + s, II // 128 - 1) for s in range(3)]
        wgSs.append(np.concatenate([wg[t * 128:(t + 1) * 128] for t in pts], axis=0))
        wuSs.append(np.concatenate([wu[t * 128:(t + 1) * 128] for t in pts], axis=0))
        wdSs.append(np.ascontiguousarray(wd[c * 128:(c + 1) * 128]))
    clips = np.array([[np.asarray(L0_ffn_clip_x).reshape(-1)[0],
                       np.asarray(L0_ffn_clip_down).reshape(-1)[0]]], dtype=np.float32)
    ident = np.eye(128, dtype=np.float32)
    identb = np.eye(128, dtype=ml_dtypes.bfloat16)
    onesbc = np.ones((1, 128), dtype=np.float32)
    onesm = np.full((128, 1), 1.0 / 1024.0, dtype=ml_dtypes.bfloat16)
    r = np.arange(512)[None, :]
    pp = np.arange(128)[:, None]
    masks = np.concatenate(
        [np.where(r >= 128 * m + pp, 0.0, NEG).astype(np.float32) for m in range(4)],
        axis=1)

    if "nc" not in _NC_CACHE:
        _NC_CACHE["nc"] = build_nc()
    nc = _NC_CACHE["nc"]

    in_maps = []
    for b in range(N_CORES):
        in_maps.append({
            "xin": np.ascontiguousarray(x[b].T),
            "wqT": wq, "wkT": wk, "wvT": wv,
            "wgS": wgSs[b], "wuS": wuSs[b], "wdS": wdSs[b],
            "clips": clips, "identin": ident, "identbin": identb, "onesbcin": onesbc,
            "onesmin": onesm, "masksin": masks,
        })
    res = run_bass_kernel_spmd(nc, in_maps, core_ids=list(range(N_CORES)))
    out = np.stack([np.ascontiguousarray(res.results[b]["outT"].T)
                    for b in range(N_CORES)], axis=0)
    return out


# version 48
# speedup vs baseline: 1.0523x; 1.0154x over previous
import contextlib
import numpy as np
import concourse.bass as bass
import concourse.mybir as mybir
from concourse.bass_utils import run_bass_kernel_spmd

F32 = mybir.dt.float32
BF16 = mybir.dt.bfloat16
I32 = mybir.dt.int32
AF = mybir.ActivationFunctionType
ALU = mybir.AluOpType
AX = mybir.AxisListType.X

T = 2048
D = 1024
II = 2816
NDP = D // 128
NIP = II // 128
NTC = T // 512
NKT = T // 128
QMAX = 127.0
NEG = -1.0e30
N_CORES = 8

AR_BYTES = 204800
XA, XB = 0, 8192
XC, XD = 16384, 20480
FB, GB = 24576, 32768
XN = 16384
QIX = 16384
SILU, HPRE, HMC = 0, 8192, 16384
XQ = 40960
SHW = 8960
SHB = 118784
WDS_STG = 141312
WD_QB1 = 152576
WG_STG = [40960, 45056, 49152]
WG_QB = [53248, 55296]
C_GQ = 52
W1, W2 = 73728, 118784
HQ_A, HQ_B = 0, 22528
EVT2, EVT2B, XRA2, XRB2 = 47104, 53248, 49152, 51200
CST2 = [55296 + 1024 * i for i in range(4)]
SQ2 = [59392 + 1024 * i for i in range(4)]
HSB3 = [118784, 141312]
XG_A, XG_B = 0, 16384
WD = 73728
XNT = 40960
V_ = 0
WQB = 73728
QT_ = 98304
KT_ = 131072
ATT = 32768
EVT, EVT_B, XRA, XRB = 98304, 100352, 102400, 104448
IDENT = 163840
IDENTB = 199904
MASKS = 164352
ONESBC = 172544
ONESM = 173056
ROWS = 173120
COLS = 197696
CCS = 198208
SCAL = 199232
WKS = 199360
WKS_B = 201408

R_MEAN, R_RSTD, R_G = 0, 2048, 4096
C_WGS, C_WUS, C_WDS = 0, 22, 44
C_GSC, C_USC, C_DSC = 52, 74, 96
C_RS, C_RH, C_SX127, C_SH = 104, 105, 106, 107
C_T0 = 108
S_SIGX, S_SIGD, S_GMAX, S_SX, S_RSX, S_SX127, S_SH, S_RSH, S_SH127, S_T0 = range(10)


class Sem:
    __slots__ = ("h", "n")

    def __init__(self, h):
        self.h = h
        self.n = 0


class Prog:
    ENGS = ("sync", "vector", "scalar", "tensor", "gpsimd")

    def __init__(self):
        self.items = {e: [] for e in self.ENGS}
        self._lastw = {}

    def emit(self, eng, fn):
        self.items[eng].append(fn)

    def dma(self, q, out_ap, in_ap, sem, slow=False):
        sem.n += 16
        self.emit(q, lambda e, o=out_ap, i=in_ap, h=sem.h, s=slow:
                  e.dma_start(out=o, in_=i, allow_slow_non_contiguous=s).then_inc(h, 16))

    def wait(self, eng, sem, n=None):
        v = sem.n if n is None else n
        if v <= 0:
            return
        key = (eng, id(sem))
        if self._lastw.get(key, -1) >= v:
            return
        self._lastw[key] = v
        self.emit(eng, lambda e, h=sem.h, v=v: e.wait_ge(h, v))

    def op(self, eng, fn, sem=None, inc=1, drain=True):
        if drain and eng in ("vector", "scalar", "gpsimd"):
            self.emit(eng, lambda e: e.drain())
        if sem is not None:
            sem.n += inc
            self.emit(eng, lambda e, fn=fn, h=sem.h, k=inc: fn(e).then_inc(h, k))
        else:
            self.emit(eng, fn)


def build_nc():
    nc = bass.Bass("TRN2", target_bir_lowering=False)

    xin = nc.dram_tensor("xin", [D, T], F32, kind="ExternalInput").ap()
    wqT = nc.dram_tensor("wqT", [D, D], F32, kind="ExternalInput").ap()
    wkT = nc.dram_tensor("wkT", [D, D], F32, kind="ExternalInput").ap()
    wvT = nc.dram_tensor("wvT", [D, D], F32, kind="ExternalInput").ap()
    wgS = nc.dram_tensor("wgS", [384, D], F32, kind="ExternalInput").ap()
    wuS = nc.dram_tensor("wuS", [384, D], F32, kind="ExternalInput").ap()
    wdS = nc.dram_tensor("wdS", [128, II], F32, kind="ExternalInput").ap()
    clips = nc.dram_tensor("clips", [1, 2], F32, kind="ExternalInput").ap()
    identin = nc.dram_tensor("identin", [128, 128], F32, kind="ExternalInput").ap()
    identbin = nc.dram_tensor("identbin", [128, 128], BF16, kind="ExternalInput").ap()
    onesbcin = nc.dram_tensor("onesbcin", [1, 128], F32, kind="ExternalInput").ap()
    onesmin = nc.dram_tensor("onesmin", [128, 1], BF16, kind="ExternalInput").ap()
    masksin = nc.dram_tensor("masksin", [128, T], F32, kind="ExternalInput").ap()

    outT = nc.dram_tensor("outT", [D, T], F32, kind="ExternalOutput").ap()

    xn_dram = nc.dram_tensor("xn_dram", [D, T], F32).ap()
    x1_dram = nc.dram_tensor("x1_dram", [D, T], F32).ap()
    x2_dram = nc.dram_tensor("x2_dram", [D, T], F32).ap()
    h_dram = nc.dram_tensor("h_dram", [128, NIP * T], F32).ap()
    sh_d = nc.dram_tensor("sh_d", [128, SHW], BF16).ap()
    full_d = nc.dram_tensor("full_d", [128 * N_CORES, SHW], BF16).ap()
    sc_d = nc.dram_tensor("sc_d", [128, 8], F32).ap()
    scf_d = nc.dram_tensor("scf_d", [128 * N_CORES, 8], F32).ap()
    ccio = [(nc.dram_tensor(f"cc{i}_in", [1, 128], F32).ap(),
             nc.dram_tensor(f"cc{i}_out", [1, 128], F32).ap()) for i in range(4)]

    es = contextlib.ExitStack()
    arena = es.enter_context(nc.sbuf_tensor("arena", [128, AR_BYTES], mybir.dt.uint8)).ap()
    AF32 = arena.bitcast(F32)
    ABF = arena.bitcast(BF16)
    AI32 = arena.bitcast(I32)

    def f32(off, p0, p1, e0, e1):
        return AF32[p0:p1, off // 4 + e0: off // 4 + e1]

    def bf(off, p0, p1, e0, e1):
        return ABF[p0:p1, off // 2 + e0: off // 2 + e1]

    def i32(off, p0, p1, e0, e1):
        return AI32[p0:p1, off // 4 + e0: off // 4 + e1]

    ident = f32(IDENT, 0, 128, 0, 128)
    identb = bf(IDENTB, 0, 128, 0, 128)
    onesbc = f32(ONESBC, 0, 1, 0, 128)
    onesm = bf(ONESM, 0, 128, 0, 1)

    def rows(slot, a, b):
        return f32(ROWS, 0, 1, slot + a, slot + b)

    def col(idx, k=1):
        return f32(COLS, 0, 128, idx, idx + k)

    def scal(idx, k=1):
        return f32(SCAL, 0, 1, idx, idx + k)

    pb = [es.enter_context(nc.psum_tensor(f"pb{i}", [128, 512], F32)) for i in range(8)]
    pbb = [t.ap().bitcast(BF16) for t in pb]

    dS = Sem(es.enter_context(nc.semaphore("dS")))
    dA = Sem(es.enter_context(nc.semaphore("dA")))
    dG = Sem(es.enter_context(nc.semaphore("dG")))
    cV = Sem(es.enter_context(nc.semaphore("cV")))
    cS = Sem(es.enter_context(nc.semaphore("cS")))
    cT = Sem(es.enter_context(nc.semaphore("cT")))
    cc = Sem(es.enter_context(nc.semaphore("cc")))
    dQ = [Sem(es.enter_context(nc.semaphore(f"dQ{i}"))) for i in range(3)]
    dL = [Sem(es.enter_context(nc.semaphore(f"dL{i}"))) for i in range(2)]
    dGs = [Sem(es.enter_context(nc.semaphore(f"dGs{i}"))) for i in range(2)]
    dT = [Sem(es.enter_context(nc.semaphore(f"dT{i}"))) for i in range(8)]
    dR = Sem(es.enter_context(nc.semaphore("dR")))
    dC = Sem(es.enter_context(nc.semaphore("dC")))
    dRg = Sem(es.enter_context(nc.semaphore("dRg")))
    dD = [Sem(es.enter_context(nc.semaphore(f"dD{i}"))) for i in range(2)]
    dW = Sem(es.enter_context(nc.semaphore("dW")))
    dP = [Sem(es.enter_context(nc.semaphore(f"dP{i}"))) for i in range(2)]
    dXs = [Sem(es.enter_context(nc.semaphore(f"dXs{i}"))) for i in range(2)]

    p = Prog()
    bankg = [None] * 8

    def bank_wait(i):
        if bankg[i] is not None:
            p.wait("tensor", bankg[i][0], bankg[i][1])

    def bcast_scalar_to_col(s_ap, cidx):
        bank_wait(7)
        p.wait("tensor", cV)
        p.op("tensor", lambda e, s=s_ap: e.matmul(pb[7][:, 0:1], onesbc, s,
                                                  start=True, stop=True), cT)
        p.wait("vector", cT)
        p.op("vector", lambda e, c=col(cidx): e.tensor_copy(c, pb[7][:, 0:1]), cV)
        bankg[7] = (cV, cV.n)

    def bcast_rows_to_tiles(jobs, rows_mark=None):
        nj = len(jobs)
        first = True
        for j, (slot, dst) in enumerate(jobs):
            for ch in range(NTC):
                bk = 2 + ch
                bank_wait(bk)
                p.wait("tensor", cV, rows_mark)
                p.op("tensor", lambda e, slot=slot, ch=ch, bk=bk:
                     e.matmul(pb[bk][:, 0:512], onesbc,
                              rows(slot, ch * 512, (ch + 1) * 512),
                              start=True, stop=True),
                     cT if ch == NTC - 1 else None)
            p.wait("vector", cT)
            for ch in range(NTC):
                p.op("vector", lambda e, ch=ch, dst=dst: e.tensor_copy(
                    f32(dst, 0, 128, ch * 512, (ch + 1) * 512),
                    pb[2 + ch][:, 0:512]), cV, drain=first)
                first = False
            for ch in range(NTC):
                bankg[2 + ch] = (cV, cV.n)

    def ln_stats_gen(src_dram, src_guards=None,
                     bufs=(XA, XB, XC, XD, 24576, 28672)):
        for i in range(2, 6):
            bank_wait(i)
        ldd = {}
        consumed = {}

        BXA, BXB, BXC0, BXD0, BXC1, BXD1 = bufs

        def ld_stats(dp):
            if dp >= NDP:
                return
            buf = BXA if dp % 2 == 0 else BXB
            q = "sync" if dp % 2 == 0 else "scalar"
            d = dL[dp % 2]
            if dp - 2 in consumed:
                p.wait(q, cV, consumed[dp - 2][0])
                p.wait(q, cS, consumed[dp - 2][1])
            else:
                p.wait(q, cV)
                p.wait(q, cS)
            if src_guards is not None:
                g = src_guards.get(dp)
                if g is not None:
                    for gg in g:
                        p.wait(q, gg[0], gg[1])
            p.dma(q, f32(buf, 0, 128, 0, T), src_dram[dp * 128:(dp + 1) * 128, :], d)
            ldd[dp] = (d, d.n)

        ld_stats(0)
        ld_stats(1)
        smm = {}
        for dp in range(NDP):
            buf = BXA if dp % 2 == 0 else BXB
            BXC = BXC0 if dp % 2 == 0 else BXC1
            BXD = BXD0 if dp % 2 == 0 else BXD1
            p.wait("vector", ldd[dp][0], ldd[dp][1])
            p.wait("vector", cT, smm.get(dp - 2, 0))
            p.op("vector", lambda e, b=buf, BXC=BXC: e.tensor_copy(
                bf(BXC, 0, 128, 0, T), f32(b, 0, 128, 0, T)), cV, drain=False)
            castm = cV.n
            p.wait("scalar", ldd[dp][0], ldd[dp][1])
            p.wait("scalar", cT, smm.get(dp - 2, 0))
            p.op("scalar", lambda e, b=buf, BXD=BXD: e.activation(
                bf(BXD, 0, 128, 0, T), f32(b, 0, 128, 0, T), AF.Square), cS, drain=False)
            sqm = cS.n
            consumed[dp] = (castm, sqm)
            ld_stats(dp + 2)
            yield
            p.wait("tensor", cV, castm)
            p.wait("tensor", cS, sqm)
            for tc in range(NTC):
                st, sp = (dp == 0), (dp == NDP - 1)
                p.op("tensor", lambda e, tc=tc, st=st, sp=sp, BXC=BXC:
                     e.matmul(pb[2 + tc][0:1, :], onesm,
                              bf(BXC, 0, 128, tc * 512, (tc + 1) * 512),
                              start=st, stop=sp), None)
                p.op("tensor", lambda e, tc=tc, st=st, sp=sp, BXD=BXD:
                     e.matmul(pb[2 + tc][32:33, :], onesm,
                              bf(BXD, 0, 128, tc * 512, (tc + 1) * 512),
                              start=st, stop=sp),
                     cT if tc == NTC - 1 else None)
            smm[dp] = cT.n
        ln_stats_finish(smm[NDP - 1])
        yield

    def ln_stats_finish(mm_mark):
        p.wait("vector", cT, mm_mark)
        for tc in range(NTC):
            p.op("vector", lambda e, tc=tc: e.tensor_copy(
                rows(R_MEAN, tc * 512, (tc + 1) * 512), pb[2 + tc][0:1, :]), cV, drain=(tc == 0))
            p.op("vector", lambda e, tc=tc: e.tensor_copy(
                rows(R_RSTD, tc * 512, (tc + 1) * 512), pb[2 + tc][32:33, :]), cV, drain=False)
        for i in range(2, 6):
            bankg[i] = (cV, cV.n)
        p.wait("scalar", cV)
        p.op("scalar", lambda e: e.activation(rows(R_G, 0, T), rows(R_MEAN, 0, T), AF.Square), cS)
        p.wait("vector", cS)
        p.op("vector", lambda e: e.scalar_tensor_tensor(
            rows(R_RSTD, 0, T), rows(R_G, 0, T), -1.0, rows(R_RSTD, 0, T),
            ALU.mult, ALU.add), cV)
        p.op("vector", lambda e: e.tensor_scalar(
            rows(R_RSTD, 0, T), rows(R_RSTD, 0, T), 1e-5, None, ALU.add), cV)
        p.wait("scalar", cV)
        p.op("scalar", lambda e: e.activation(rows(R_RSTD, 0, T), rows(R_RSTD, 0, T),
                                              AF.Ln), cS)
        p.op("scalar", lambda e: e.activation(rows(R_RSTD, 0, T), rows(R_RSTD, 0, T),
                                              AF.Exp, scale=-0.5), cS)
        p.wait("vector", cS)
        p.op("vector", lambda e: e.scalar_tensor_tensor(
            rows(R_G, 0, T), rows(R_MEAN, 0, T), -1.0, rows(R_RSTD, 0, T),
            ALU.mult, ALU.mult), cV)
        bcast_rows_to_tiles([(R_RSTD, FB), (R_G, GB)], rows_mark=cV.n)

    def allreduce_scalar_max(col_ap, cc_pair, out_scal_idx):
        cin, cout = cc_pair
        stg = f32(CCS, 0, 1, 0, 128)
        p.wait("gpsimd", cV)
        p.dma("gpsimd", cin, col_ap, dC)
        p.wait("gpsimd", dC)
        p.op("gpsimd", lambda e: e.collective_compute(
            "AllReduce", ALU.max, replica_groups=[list(range(N_CORES))],
            ins=[cin], outs=[cout]), cc, drain=False)
        p.wait("gpsimd", cc)
        p.dma("gpsimd", stg, cout, dC)
        p.wait("vector", dC)
        p.op("vector", lambda e: e.reduce_max(scal(out_scal_idx), stg, axis=AX), cV)

    def qw_gen(w_dram, n_pt, width, c_slot, qdst_off, spill_dram, stgs, qbs,
               spill_guard=None, buf_guard=0, qb_buf_guard=0):
        nb = width // 128
        w3d = ABF[:, qdst_off // 2: qdst_off // 2 + nb * n_pt * 128].rearrange(
            "p (b i) -> p b i", b=nb)
        if spill_guard is not None:
            p.wait("scalar", spill_guard[0], spill_guard[1])
        qb_guard = {}
        q1_done = {}
        q2_done = {}
        ld = {}

        def load(pt):
            if pt >= n_pt:
                return
            q = "sync" if pt % 2 == 0 else "scalar"
            d = dQ[pt % 3]
            p.wait(q, cV, q2_done.get(pt - 3, buf_guard))
            p.dma(q, f32(stgs[pt % 3], 0, 128, 0, width),
                  w_dram[pt * 128:(pt + 1) * 128, :], d)
            ld[pt] = (d, d.n)

        def stage_a(pt):
            stg = stgs[pt % 3]
            rc = C_GQ + (pt % 2)
            p.wait("vector", ld[pt][0], ld[pt][1])
            p.op("vector", lambda e, s=stg, pt=pt: e.reduce_max(
                col(c_slot + pt), f32(s, 0, 128, 0, width), axis=AX,
                apply_absolute_value=True), cV, drain=(pt == 0))
            p.wait("vector", cS, q1_done.get(pt - 2, 0))
            p.op("vector", lambda e, pt=pt, rc=rc: e.reciprocal(
                col(rc), col(c_slot + pt)), cV)
            p.op("vector", lambda e, rc=rc: e.tensor_scalar(
                col(rc), col(rc), QMAX, None, ALU.mult), cV)
            p.wait("scalar", cV)
            p.op("scalar", lambda e, s=stg, rc=rc: e.activation(
                i32(s, 0, 128, 0, width), f32(s, 0, 128, 0, width),
                AF.Identity, scale=col(rc)), cS, drain=False)
            q1_done[pt] = cS.n

        def stage_b(pt):
            stg = stgs[pt % 3]
            qb = qbs[pt % 2]
            p.wait("vector", cS, q1_done[pt])
            p.wait("vector", cT, qb_guard.get(pt - 2, qb_buf_guard))
            p.op("vector", lambda e, s=stg, qb=qb: e.tensor_scalar(
                bf(qb, 0, 128, 0, width), i32(s, 0, 128, 0, width),
                -128.0, 127.0, ALU.max, ALU.min), cV, drain=False)
            q2_done[pt] = cV.n
            for g in range((nb + 7) // 8):
                bk = 6 + (g % 2)
                kblk = min(8, nb - g * 8)
                bank_wait(bk)
                p.wait("tensor", cV, q2_done[pt])
                for k in range(kblk):
                    wb = g * 8 + k
                    p.op("tensor", lambda e, wb=wb, k=k, bk=bk, qb=qb: e.transpose(
                        pbb[bk][:, k * 128:(k + 1) * 128],
                        bf(qb, 0, 128, wb * 128, (wb + 1) * 128), identb),
                         cT if k == kblk - 1 else None)
                p.wait("scalar", cT)
                psrc = pbb[bk][:, 0:kblk * 128].rearrange("p (b i) -> p b i", b=kblk)
                p.op("scalar", lambda e, g=g, pt=pt, bk=bk, kblk=kblk, psrc=psrc:
                     e.activation(w3d[:, g * 8:g * 8 + kblk, pt * 128:(pt + 1) * 128],
                                  psrc, AF.Identity), cS, drain=False)
                bankg[bk] = (cS, cS.n)
            qb_guard[pt] = cT.n

        load(0)
        load(1)
        for pt in range(n_pt):
            stage_a(pt)
            if pt > 0:
                stage_b(pt - 1)
            load(pt + 2)
            yield
        stage_b(n_pt - 1)
        p.wait("sync", cS)
        p.dma("sync", spill_dram[:, 0:n_pt * width], bf(qdst_off, 0, 128, 0, n_pt * width), dS)

    def qw_gen_psum(w_dram, n_pt, width, c_slot, spill_dram, stgs, qbs,
                    buf_guard=0):
        nb = width // 128
        qb_guard = {}
        q1_done = {}
        q2_done = {}
        ld = {}

        def load(pt):
            if pt >= n_pt:
                return
            q = "sync" if pt % 2 == 0 else "scalar"
            d = dQ[pt % 3]
            p.wait(q, cV, q2_done.get(pt - 3, buf_guard))
            p.dma(q, f32(stgs[pt % 3], 0, 128, 0, width),
                  w_dram[pt * 128:(pt + 1) * 128, :], d)
            ld[pt] = (d, d.n)

        def stage_a(pt):
            stg = stgs[pt % 3]
            rc = C_GQ + (pt % 2)
            p.wait("vector", ld[pt][0], ld[pt][1])
            p.op("vector", lambda e, s=stg, pt=pt: e.reduce_max(
                col(c_slot + pt), f32(s, 0, 128, 0, width), axis=AX,
                apply_absolute_value=True), cV, drain=(pt == 0))
            p.wait("vector", cS, q1_done.get(pt - 2, 0))
            p.op("vector", lambda e, pt=pt, rc=rc: e.reciprocal(
                col(rc), col(c_slot + pt)), cV)
            p.op("vector", lambda e, rc=rc: e.tensor_scalar(
                col(rc), col(rc), QMAX, None, ALU.mult), cV)
            p.wait("scalar", cV)
            p.op("scalar", lambda e, s=stg, rc=rc: e.activation(
                i32(s, 0, 128, 0, width), f32(s, 0, 128, 0, width),
                AF.Identity, scale=col(rc)), cS, drain=False)
            q1_done[pt] = cS.n

        def stage_b(pt):
            stg = stgs[pt % 3]
            qb = qbs[pt % 2]
            p.wait("vector", cS, q1_done[pt])
            p.wait("vector", cT, qb_guard.get(pt - 2, 0))
            p.op("vector", lambda e, s=stg, qb=qb: e.tensor_scalar(
                bf(qb, 0, 128, 0, width), i32(s, 0, 128, 0, width),
                -128.0, 127.0, ALU.max, ALU.min), cV, drain=False)
            q2_done[pt] = cV.n
            for g in range((nb + 7) // 8):
                bk = 6 + (g % 2)
                kblk = min(8, nb - g * 8)
                bank_wait(bk)
                p.wait("tensor", cV, q2_done[pt])
                for k in range(kblk):
                    wb = g * 8 + k
                    p.op("tensor", lambda e, wb=wb, k=k, bk=bk, qb=qb: e.transpose(
                        pbb[bk][:, k * 128:(k + 1) * 128],
                        bf(qb, 0, 128, wb * 128, (wb + 1) * 128), identb),
                         cT if k == kblk - 1 else None)
                p.wait("scalar", cT)
                slot = 59392 if g % 2 == 0 else 61440
                p.wait("scalar", dP[g % 2], dP[g % 2].n)
                p.op("scalar", lambda e, bk=bk, kblk=kblk, slot=slot: e.activation(
                    bf(slot, 0, 128, 0, kblk * 128), pbb[bk][:, 0:kblk * 128],
                    AF.Identity), cS, drain=False)
                bankg[bk] = (cS, cS.n)
                p.wait("gpsimd", cS)
                c0 = (pt * nb + g * 8) * 128
                p.dma("gpsimd", spill_dram[:, c0:c0 + kblk * 128],
                      bf(slot, 0, 128, 0, kblk * 128), dP[g % 2])
            qb_guard[pt] = cT.n

        load(0)
        load(1)
        for pt in range(n_pt):
            stage_a(pt)
            if pt > 0:
                stage_b(pt - 1)
            load(pt + 2)
            yield
        stage_b(n_pt - 1)

    def wq_sh_gen():
        units = ([("wg", s, D) for s in range(3)] +
                 [("wu", s, D) for s in range(3)] + [("wd", 0, II)])
        tmpc = [C_WGS, C_WGS + 1, C_WGS + 2, C_WUS, C_WUS + 1, C_WUS + 2, C_WDS]
        q1_done = {}
        q2_done = {}
        qb_guard = {}
        ld = {}

        def load(u):
            if u >= len(units):
                return
            kind, s, width = units[u]
            q = "sync" if u % 2 == 0 else "scalar"
            d = dQ[u % 3]
            p.wait(q, cV, q2_done.get(u - 3, 0))
            if kind == "wd":
                p.dma(q, f32(WDS_STG, 0, 128, 0, width), wdS, d)
            else:
                srcd = wgS if kind == "wg" else wuS
                p.dma(q, f32(WG_STG[u % 3], 0, 128, 0, width),
                      srcd[s * 128:(s + 1) * 128, :], d)
            ld[u] = (d, d.n)

        def stage_a(u):
            kind, s, width = units[u]
            stg = WDS_STG if kind == "wd" else WG_STG[u % 3]
            rc = C_GQ + (u % 2)
            p.wait("vector", ld[u][0], ld[u][1])
            p.op("vector", lambda e, stg=stg, width=width, u=u: e.reduce_max(
                col(tmpc[u]), f32(stg, 0, 128, 0, width), axis=AX,
                apply_absolute_value=True), cV, drain=(u == 0))
            p.op("vector", lambda e, u=u: e.tensor_copy(
                f32(59392, 0, 128, u, u + 1), col(tmpc[u])), cV)
            p.wait("vector", cS, q1_done.get(u - 2, 0))
            p.op("vector", lambda e, u=u, rc=rc: e.reciprocal(
                col(rc), col(tmpc[u])), cV)
            p.op("vector", lambda e, rc=rc: e.tensor_scalar(
                col(rc), col(rc), QMAX, None, ALU.mult), cV)
            p.wait("scalar", cV)
            p.op("scalar", lambda e, stg=stg, width=width, rc=rc: e.activation(
                i32(stg, 0, 128, 0, width), f32(stg, 0, 128, 0, width),
                AF.Identity, scale=col(rc)), cS, drain=False)
            q1_done[u] = cS.n

        def stage_b(u):
            kind, s, width = units[u]
            stg = WDS_STG if kind == "wd" else WG_STG[u % 3]
            qb = WD_QB1 if kind == "wd" else WG_QB[u % 2]
            nb = width // 128
            if kind == "wg":
                dst0 = s * 1024
            elif kind == "wu":
                dst0 = 3072 + s * 1024
            else:
                dst0 = 6144
            p.wait("vector", cS, q1_done[u])
            p.wait("vector", cT, qb_guard.get(u - 2, 0))
            p.op("vector", lambda e, stg=stg, qb=qb, width=width: e.tensor_scalar(
                bf(qb, 0, 128, 0, width), i32(stg, 0, 128, 0, width),
                -128.0, 127.0, ALU.max, ALU.min), cV, drain=False)
            q2_done[u] = cV.n
            for g in range((nb + 7) // 8):
                bk = 6 + (g % 2)
                kblk = min(8, nb - g * 8)
                bank_wait(bk)
                p.wait("tensor", cV, q2_done[u])
                for k in range(kblk):
                    wb = g * 8 + k
                    p.op("tensor", lambda e, wb=wb, k=k, bk=bk, qb=qb: e.transpose(
                        pbb[bk][:, k * 128:(k + 1) * 128],
                        bf(qb, 0, 128, wb * 128, (wb + 1) * 128), identb),
                         cT if k == kblk - 1 else None)
                p.wait("scalar", cT)
                p.op("scalar", lambda e, bk=bk, kblk=kblk, dst0=dst0, g=g: e.activation(
                    bf(SHB, 0, 128, dst0 + g * 1024, dst0 + g * 1024 + kblk * 128),
                    pbb[bk][:, 0:kblk * 128], AF.Identity), cS, drain=False)
                bankg[bk] = (cS, cS.n)
            qb_guard[u] = cT.n

        load(0)
        load(1)
        for u in range(len(units)):
            stage_a(u)
            if u > 0:
                stage_b(u - 1)
            load(u + 2)
            yield
        stage_b(len(units) - 1)
        p.wait("sync", cS)
        p.wait("sync", cV)
        p.dma("sync", sh_d, bf(SHB, 0, 128, 0, SHW), dS)
        p.op("vector", lambda e: e.tensor_scalar(
            f32(59392, 0, 128, 7, 8), f32(59392, 0, 128, 6, 7), 0.0,
            None, ALU.mult), cV)
        p.wait("sync", cV)
        p.dma("sync", sc_d, f32(59392, 0, 128, 0, 8), dS)
        p.wait("gpsimd", dS)
        p.op("gpsimd", lambda e: e.collective_compute(
            "AllGather", ALU.bypass, replica_groups=[list(range(N_CORES))],
            ins=[sh_d], outs=[full_d]), cc, drain=False)
        p.op("gpsimd", lambda e: e.collective_compute(
            "AllGather", ALU.bypass, replica_groups=[list(range(N_CORES))],
            ins=[sc_d], outs=[scf_d]), cc, drain=False)
        gat = cc.n
        yield
        p.wait("sync", cc, gat)
        p.wait("scalar", cc, gat)
        for c in range(N_CORES):
            n = 3 if c < 7 else 1
            q = "sync" if c % 2 == 0 else "scalar"
            p.dma(q, bf(W1, 0, 128, 3 * c * 1024, (3 * c + n) * 1024),
                  full_d[c * 128:(c + 1) * 128, 0:n * 1024], dR)
            p.dma(q, bf(W2, 0, 128, 3 * c * 1024, (3 * c + n) * 1024),
                  full_d[c * 128:(c + 1) * 128, 3072:3072 + n * 1024], dR)
            p.dma(q, f32(59392, 0, 128, c * 8, c * 8 + 8),
                  scf_d[c * 128:(c + 1) * 128, :], dR)
            yield
        p.wait("vector", dR)
        for c in range(N_CORES):
            n = 3 if c < 7 else 1
            p.op("vector", lambda e, c=c, n=n: e.tensor_copy(
                col(C_WGS + 3 * c, n), f32(59392, 0, 128, c * 8, c * 8 + n)),
                cV, drain=(c == 0))
            p.op("vector", lambda e, c=c, n=n: e.tensor_copy(
                col(C_WUS + 3 * c, n), f32(59392, 0, 128, c * 8 + 3, c * 8 + 3 + n)), cV)
            p.op("vector", lambda e, c=c: e.tensor_copy(
                col(C_WDS + c), f32(59392, 0, 128, c * 8 + 6, c * 8 + 7)), cV)

    def drive(*gens):
        live = list(gens)
        while live:
            for g in list(live):
                try:
                    next(g)
                except StopIteration:
                    live.remove(g)

    def drive_with(main, aux, aux_steps=2):
        while True:
            try:
                next(main)
            except StopIteration:
                return
            for _ in range(aux_steps):
                try:
                    next(aux)
                except StopIteration:
                    break

    def ffn(src_dram, dst_dram, ccA, ccB, wq_gens=None, stats_done=False,
            feed_stats=False):
        p.wait("vector", cT)
        p.wait("scalar", cT)
        p.wait("sync", dS)
        for s in dT + dD:
            p.wait("sync", s)
            p.wait("scalar", s)
            p.wait("vector", s)
        p.wait("scalar", dS)
        if not stats_done:
            if wq_gens is not None:
                drive_with(ln_stats_gen(src_dram), wq_gens, aux_steps=1)
            else:
                drive(ln_stats_gen(src_dram))
        ldx = {}
        spg = {}
        xn_mark = [0]

        def ld_xn(dp):
            if dp >= NDP:
                return
            buf = XA if dp % 2 == 0 else XB
            q = "sync" if dp % 2 == 0 else "scalar"
            d = dL[dp % 2]
            if dp - 2 in spg:
                g = spg[dp - 2]
                p.wait(q, g[0], g[1])
            else:
                p.wait(q, cV)
                p.wait(q, cS)
            p.dma(q, f32(buf, 0, 128, 0, T), src_dram[dp * 128:(dp + 1) * 128, :], d)
            ldx[dp] = (d, d.n)

        def xn_gen():
            ld_xn(0)
            ld_xn(1)
            for dp in range(NDP):
                buf = XA if dp % 2 == 0 else XB
                p.wait("vector", ldx[dp][0], ldx[dp][1])
                p.op("vector", lambda e, b=buf: e.tensor_tensor(
                    f32(b, 0, 128, 0, T), f32(b, 0, 128, 0, T),
                    f32(FB, 0, 128, 0, T), ALU.mult), cV, drain=False)
                p.op("vector", lambda e, b=buf: e.tensor_tensor(
                    f32(b, 0, 128, 0, T), f32(b, 0, 128, 0, T),
                    f32(GB, 0, 128, 0, T), ALU.add), cV)
                p.op("vector", lambda e, dp=dp, b=buf: e.reduce_max(
                    col(C_T0 + dp), f32(b, 0, 128, 0, T), axis=AX,
                    apply_absolute_value=True), cV)
                q2 = "scalar" if dp % 2 == 0 else "sync"
                p.wait(q2, cV, cV.n)
                p.dma(q2, xn_dram[dp * 128:(dp + 1) * 128, :], f32(buf, 0, 128, 0, T),
                      dXs[dp % 2])
                spg[dp] = (dXs[dp % 2], dXs[dp % 2].n)
                ld_xn(dp + 2)
                yield
            p.op("vector", lambda e: e.reduce_max(col(C_T0 + 8), col(C_T0, NDP), axis=AX), cV)
            xn_mark[0] = cV.n
            allreduce_scalar_max(col(C_T0 + 8), ccA, S_GMAX)
            yield

        if wq_gens is not None:
            drive(xn_gen(), wq_gens)
        else:
            drive(xn_gen())
        p.op("vector", lambda e: e.tensor_tensor(scal(S_SX), scal(S_SIGX), scal(S_GMAX), ALU.mult), cV)
        p.op("vector", lambda e: e.tensor_scalar(scal(S_SX), scal(S_SX), 1.0 / QMAX, None, ALU.mult), cV)
        p.op("vector", lambda e: e.reciprocal(scal(S_RSX), scal(S_SX)), cV)
        p.op("vector", lambda e: e.tensor_scalar(scal(S_SX127), scal(S_SX), 1.0 / QMAX, None, ALU.mult), cV)
        bcast_scalar_to_col(scal(S_RSX), C_RS)
        bcast_scalar_to_col(scal(S_SX127), C_SX127)
        p.wait("vector", cS)
        p.op("vector", lambda e: e.tensor_scalar(
            col(C_GSC, NIP), col(C_WGS, NIP), col(C_SX127), None, ALU.mult), cV)
        p.op("vector", lambda e: e.tensor_scalar(
            col(C_USC, NIP), col(C_WUS, NIP), col(C_SX127), None, ALU.mult), cV)
        p.wait("sync", dXs[0])
        p.wait("sync", dXs[1])
        p.wait("scalar", dXs[0])
        p.wait("scalar", dXs[1])
        xn3 = xn_dram.rearrange("(a p) t -> p a t", p=128)
        ldq = {}
        qdone = {}

        def ld_xq(u):
            if u >= NDP // 2:
                return
            g = XG_A if u % 2 == 0 else XG_B
            q = "sync" if u % 2 == 0 else "scalar"
            d = dL[u % 2]
            if u - 2 in qdone:
                p.wait(q, cV, qdone[u - 2])
            else:
                p.wait(q, cV, xn_mark[0])
            p.dma(q, f32(g, 0, 128, 0, 2 * T), xn3[:, 2 * u:2 * u + 2, :], d)
            ldq[u] = (d, d.n)

        ld_xq(0)
        ld_xq(1)
        p.wait("vector", dP[0])
        p.wait("vector", dP[1])
        for u in range(NDP // 2):
            g = XG_A if u % 2 == 0 else XG_B
            p.wait("scalar", ldq[u][0], ldq[u][1])
            p.wait("scalar", cV)
            p.op("scalar", lambda e, g=g: e.activation(
                i32(g, 0, 128, 0, 2 * T), f32(g, 0, 128, 0, 2 * T),
                AF.Identity, scale=col(C_RS)), cS, drain=False)
            p.wait("vector", cS)
            p.op("vector", lambda e, g=g, u=u: e.tensor_scalar(
                bf(XQ, 0, 128, 2 * u * T, (2 * u + 2) * T), i32(g, 0, 128, 0, 2 * T),
                -128.0, 127.0, ALU.max, ALU.min), cV, drain=False)
            qdone[u] = cV.n
            ld_xq(u + 2)
        p.wait("tensor", cV)
        p.wait("tensor", dS)
        p.wait("tensor", dR)
        p.wait("tensor", dRg)
        for ib in range(NIP):
            for i in range(4):
                bank_wait(i)
            for dp in range(NDP):
                for tc in range(NTC):
                    st, sp = (dp == 0), (dp == NDP - 1)
                    p.op("tensor", lambda e, ib=ib, dp=dp, tc=tc, st=st, sp=sp:
                         e.matmul(pb[tc][:, 0:512],
                                  bf(W1, 0, 128, (ib * NDP + dp) * 128,
                                     (ib * NDP + dp + 1) * 128),
                                  bf(XQ, 0, 128, dp * T + tc * 512, dp * T + (tc + 1) * 512),
                                  start=st, stop=sp),
                         cT if (sp and tc == NTC - 1) else None)
            gate_done = cT.n
            for i in range(4, 8):
                bank_wait(i)
            for dp in range(NDP):
                for tc in range(NTC):
                    st, sp = (dp == 0), (dp == NDP - 1)
                    p.op("tensor", lambda e, ib=ib, dp=dp, tc=tc, st=st, sp=sp:
                         e.matmul(pb[4 + tc][:, 0:512],
                                  bf(W2, 0, 128, (ib * NDP + dp) * 128,
                                     (ib * NDP + dp + 1) * 128),
                                  bf(XQ, 0, 128, dp * T + tc * 512, dp * T + (tc + 1) * 512),
                                  start=st, stop=sp),
                         cT if (sp and tc == NTC - 1) else None)
            up_done = cT.n
            p.wait("scalar", cT, gate_done)
            p.wait("scalar", cV)
            for tc in range(NTC):
                p.op("scalar", lambda e, ib=ib, tc=tc: e.activation(
                    f32(SILU, 0, 128, tc * 512, (tc + 1) * 512), pb[tc][:, 0:512],
                    AF.Silu, scale=col(C_GSC + ib)), cS)
            for i in range(4):
                bankg[i] = (cS, cS.n)
            p.wait("vector", cT, up_done)
            p.wait("vector", cS)
            p.wait("vector", dG)
            for tc in range(NTC):
                p.op("vector", lambda e, ib=ib, tc=tc: e.scalar_tensor_tensor(
                    f32(HPRE, 0, 128, tc * 512, (tc + 1) * 512), pb[4 + tc][:, 0:512],
                    col(C_USC + ib), f32(SILU, 0, 128, tc * 512, (tc + 1) * 512),
                    ALU.mult, ALU.mult), cV)
                p.op("vector", lambda e, ib=ib, tc=tc: e.reduce_max(
                    f32(HMC, 0, 128, ib * NTC + tc, ib * NTC + tc + 1),
                    f32(HPRE, 0, 128, tc * 512, (tc + 1) * 512),
                    axis=AX, apply_absolute_value=True), cV)
                p.wait("gpsimd", cV)
                p.dma("gpsimd", h_dram[:, (tc * NIP + ib) * 512:(tc * NIP + ib + 1) * 512],
                      f32(HPRE, 0, 128, tc * 512, (tc + 1) * 512), dG)
            for i in range(4, 8):
                bankg[i] = (cV, cV.n)
        p.op("vector", lambda e: e.reduce_max(col(C_T0), f32(HMC, 0, 128, 0, NIP * NTC), axis=AX), cV)
        p.wait("scalar", cT)
        p.wait("scalar", dG)
        uld = {}
        p.dma("scalar", f32(HSB3[0], 0, 128, 0, 5632), h_dram[:, 0:5632], dQ[0])
        uld[0] = (dQ[0], dQ[0].n)
        p.dma("scalar", f32(HSB3[1], 0, 128, 0, 5632), h_dram[:, 5632:11264], dQ[1])
        uld[1] = (dQ[1], dQ[1].n)
        allreduce_scalar_max(col(C_T0), ccB, S_GMAX)
        p.op("vector", lambda e: e.tensor_tensor(scal(S_SH), scal(S_SIGD), scal(S_GMAX), ALU.mult), cV)
        p.op("vector", lambda e: e.tensor_scalar(scal(S_SH), scal(S_SH), 1.0 / QMAX, None, ALU.mult), cV)
        p.op("vector", lambda e: e.reciprocal(scal(S_RSH), scal(S_SH)), cV)
        p.op("vector", lambda e: e.tensor_scalar(scal(S_SH127), scal(S_SH), 0.5 / QMAX, None, ALU.mult), cV)
        bcast_scalar_to_col(scal(S_RSH), C_RH)
        bcast_scalar_to_col(scal(S_SH127), C_SH)
        p.op("vector", lambda e: e.tensor_scalar(
            col(C_DSC, NDP), col(C_WDS, NDP), col(C_SH), None, ALU.mult), cV)
        rh_mark = cV.n
        p.wait("sync", cT)
        p.wait("sync", dS)
        for c in range(N_CORES):
            p.dma("sync", bf(WD, 0, 128, c * NIP * 128, (c + 1) * NIP * 128),
                  full_d[c * 128:(c + 1) * 128, 6144:6144 + NIP * 128], dR)
        p.wait("tensor", dR)
        units = [(tc, hf) for tc in range(NTC) for hf in range(2)]
        uq_done = {}
        dn_store = {}
        sm_mm = {}
        sm_cast = {}
        sm_sq = {}

        def emit_sm_mms(g):
            if g < 0 or g not in sm_cast:
                return
            tc_g, hb_g = g // NDP, g % NDP
            if hb_g == 0:
                bank_wait(2 + tc_g)
            p.wait("tensor", cV, sm_cast[g])
            p.wait("tensor", cS, sm_sq[g])
            st2, sp2 = (hb_g == 0), (hb_g == NDP - 1)
            cstb, sqb = CST2[g % 4], SQ2[g % 4]
            p.op("tensor", lambda e, tc_g=tc_g, cstb=cstb, st2=st2, sp2=sp2:
                 e.matmul(pb[2 + tc_g][0:1, :], onesm,
                          bf(cstb, 0, 128, 0, 512), start=st2, stop=sp2), None)
            p.op("tensor", lambda e, tc_g=tc_g, sqb=sqb, st2=st2, sp2=sp2:
                 e.matmul(pb[2 + tc_g][32:33, :], onesm,
                          bf(sqb, 0, 128, 0, 512), start=st2, stop=sp2), cT)
            sm_mm[g] = cT.n

        def emit_unit_load(ui):
            if ui >= len(units):
                return
            tc, hf = units[ui]
            g = HSB3[ui % 2]
            p.wait("scalar", cV, uq_done[ui - 2])
            p.dma("scalar", f32(g, 0, 128, 0, 5632),
                  h_dram[:, (tc * NIP + hf * 11) * 512:(tc * NIP + hf * 11) * 512 + 5632],
                  dQ[ui % 2])
            uld[ui] = (dQ[ui % 2], dQ[ui % 2].n)

        def emit_unit_quant(ui):
            tc, hf = units[ui]
            g = HSB3[ui % 2]
            hq = HQ_A if tc % 2 == 0 else HQ_B
            p.wait("scalar", uld[ui][0], uld[ui][1])
            p.wait("scalar", cV, rh_mark)
            p.op("scalar", lambda e, g=g: e.activation(
                i32(g, 0, 128, 0, 5632), f32(g, 0, 128, 0, 5632),
                AF.Identity, scale=col(C_RH)), cS, drain=False)
            q1_mark = cS.n
            p.wait("vector", cS, q1_mark)
            p.op("vector", lambda e, g=g, hq=hq, hf=hf: e.tensor_scalar(
                bf(hq, 0, 128, hf * 5632, (hf + 1) * 5632), i32(g, 0, 128, 0, 5632),
                -128.0, 127.0, ALU.max, ALU.min), cV, drain=False)
            uq_done[ui] = cV.n

        p.wait("vector", dP[0])
        p.wait("vector", dP[1])
        p.wait("scalar", dP[0])
        p.wait("scalar", dP[1])
        emit_unit_quant(0)
        emit_unit_quant(1)
        for tc in range(NTC):
            if 2 * tc + 2 < len(units):
                emit_unit_load(2 * tc + 2)
                emit_unit_load(2 * tc + 3)
                emit_unit_quant(2 * tc + 2)
                emit_unit_quant(2 * tc + 3)
            hq = HQ_A if tc % 2 == 0 else HQ_B
            for hb in range(NDP):
                bk = (tc * NDP + hb) % 2
                bank_wait(bk)
                for ip in range(NIP):
                    st, sp = (ip == 0), (ip == NIP - 1)
                    if ip == 0:
                        p.wait("tensor", cV, uq_done[2 * tc])
                    elif ip == 11:
                        p.wait("tensor", cV, uq_done[2 * tc + 1])
                    p.op("tensor", lambda e, ip=ip, hb=hb, hq=hq, bk=bk, st=st, sp=sp:
                         e.matmul(pb[bk][:, 0:512],
                                  bf(WD, 0, 128, (hb * NIP + ip) * 128,
                                     (hb * NIP + ip + 1) * 128),
                                  bf(hq, 0, 128, ip * 512, (ip + 1) * 512),
                                  start=st, stop=sp),
                         cT if sp else None)
                g_i = tc * NDP + hb
                xr = XRA2 if g_i % 2 == 0 else XRB2
                ev2 = EVT2 if g_i % 2 == 0 else EVT2B
                dxr = dL[g_i % 2]
                p.wait("sync", cS)
                p.dma("sync", f32(xr, 0, 128, 0, 512),
                      src_dram[hb * 128:(hb + 1) * 128, tc * 512:(tc + 1) * 512], dxr)
                xr_mark = dxr.n
                p.wait("scalar", cT)
                p.wait("scalar", dxr, xr_mark)
                p.wait("scalar", cV)
                if g_i - 2 in dn_store:
                    s_prev = dn_store[g_i - 2][0]
                    p.wait("scalar", s_prev, s_prev.n)
                p.op("scalar", lambda e, bk=bk, hb=hb, ev2=ev2: e.activation(
                    f32(ev2, 0, 128, 0, 512), pb[bk][:, 0:512], AF.Identity,
                    scale=col(C_DSC + hb)), cS)
                bankg[bk] = (cS, cS.n)
                p.wait("vector", cS)
                p.op("vector", lambda e, xr=xr, ev2=ev2: e.tensor_tensor(
                    f32(ev2, 0, 128, 0, 512), f32(ev2, 0, 128, 0, 512),
                    f32(xr, 0, 128, 0, 512), ALU.add), cV)
                add_mark = cV.n
                p.wait("sync", cV)
                p.dma("sync", dst_dram[hb * 128:(hb + 1) * 128, tc * 512:(tc + 1) * 512],
                      f32(ev2, 0, 128, 0, 512), dD[g_i % 2])
                dn_store[g_i] = (dD[g_i % 2], dD[g_i % 2].n)
                if feed_stats:
                    emit_sm_mms(g_i - 4)
                    cstb = CST2[g_i % 4]
                    sqb = SQ2[g_i % 4]
                    if g_i - 4 in sm_mm:
                        p.wait("vector", cT, sm_mm[g_i - 4])
                    p.op("vector", lambda e, ev2=ev2, cstb=cstb: e.tensor_copy(
                        bf(cstb, 0, 128, 0, 512), f32(ev2, 0, 128, 0, 512)),
                        cV, drain=True)
                    sm_cast[g_i] = cV.n
                    p.wait("scalar", cV, add_mark)
                    if g_i - 4 in sm_mm:
                        p.wait("scalar", cT, sm_mm[g_i - 4])
                    p.op("scalar", lambda e, ev2=ev2, sqb=sqb: e.activation(
                        bf(sqb, 0, 128, 0, 512), f32(ev2, 0, 128, 0, 512),
                        AF.Square), cS, drain=False)
                    sm_sq[g_i] = cS.n

        if feed_stats:
            for g in range(NTC * NDP - 4, NTC * NDP):
                emit_sm_mms(g)
        return sm_mm.get(NTC * NDP - 1)

    def attention(src_dram, dst_dram, stats2_gen=None, stats2_guards=None,
                  stats_premixed=None):
        p.wait("vector", cT)
        p.wait("scalar", cT)
        p.wait("sync", dS)
        for s in dT + dD:
            p.wait("sync", s)
            p.wait("scalar", s)
            p.wait("vector", s)
        p.wait("scalar", dS)
        if stats_premixed is not None:
            ln_stats_finish(stats_premixed)
        else:
            drive(ln_stats_gen(src_dram))
        for dp in range(NDP):
            buf = XA if dp % 2 == 0 else XB
            dxn = dL[dp % 2]
            p.wait("sync", cV)
            p.wait("sync", cS)
            p.dma("sync", f32(buf, 0, 128, 0, T), src_dram[dp * 128:(dp + 1) * 128, :], dxn)
            p.wait("vector", dxn, dxn.n)
            p.op("vector", lambda e, b=buf: e.tensor_tensor(
                f32(b, 0, 128, 0, T), f32(b, 0, 128, 0, T),
                f32(FB, 0, 128, 0, T), ALU.mult), cV, drain=False)
            p.op("vector", lambda e, dp=dp, b=buf: e.tensor_tensor(
                bf(XNT, 0, 128, dp * T, (dp + 1) * T), f32(b, 0, 128, 0, T),
                f32(GB, 0, 128, 0, T), ALU.add), cV)

        def proj_o(w_dram, dst_off, sc):
            p.wait("gpsimd", cT)
            for dp in range(NDP):
                p.dma("gpsimd", bf(WQB, 0, 128, dp * D, (dp + 1) * D),
                      w_dram[dp * 128:(dp + 1) * 128, :], dW)
            p.wait("tensor", dW)
            p.wait("tensor", cV)
            for ob in range(NDP):
                for tc in range(NTC):
                    bk = (ob * NTC + tc) % 2
                    bank_wait(bk)
                    for dp in range(NDP):
                        st, sp = (dp == 0), (dp == NDP - 1)
                        p.op("tensor", lambda e, dp=dp, ob=ob, tc=tc, bk=bk, st=st, sp=sp:
                             e.matmul(pb[bk][:, 0:512],
                                      bf(WQB, 0, 128, dp * D + ob * 128, dp * D + (ob + 1) * 128),
                                      bf(XNT, 0, 128, dp * T + tc * 512, dp * T + (tc + 1) * 512),
                                      start=st, stop=sp),
                             cT if sp else None)
                    p.wait("scalar", cT)
                    p.op("scalar", lambda e, ob=ob, tc=tc, bk=bk, dst=dst_off, sc=sc:
                         e.activation(bf(dst, 0, 128, ob * T + tc * 512, ob * T + (tc + 1) * 512),
                                      pb[bk][:, 0:512], AF.Identity, scale=sc), cS)
                    bankg[bk] = (cS, cS.n)

        def proj_v(w_dram, dst_off):
            p.wait("gpsimd", cT)
            for dp in range(NDP):
                p.dma("gpsimd", bf(WQB, 0, 128, dp * D, (dp + 1) * D),
                      w_dram[dp * 128:(dp + 1) * 128, :], dW)
            p.wait("tensor", dW)
            p.wait("tensor", cV)
            for tb in range(NKT):
                for oc in range(2):
                    bk = (tb * 2 + oc) % 2
                    bank_wait(bk)
                    for dp in range(NDP):
                        st, sp = (dp == 0), (dp == NDP - 1)
                        p.op("tensor", lambda e, dp=dp, tb=tb, oc=oc, bk=bk, st=st, sp=sp:
                             e.matmul(pb[bk][:, 0:512],
                                      bf(XNT, 0, 128, dp * T + tb * 128, dp * T + (tb + 1) * 128),
                                      bf(WQB, 0, 128, dp * D + oc * 512, dp * D + (oc + 1) * 512),
                                      start=st, stop=sp),
                             cT if sp else None)
                    p.wait("scalar", cT)
                    p.op("scalar", lambda e, tb=tb, oc=oc, bk=bk, dst=dst_off:
                         e.activation(bf(dst, 0, 128, tb * D + oc * 512, tb * D + (oc + 1) * 512),
                                      pb[bk][:, 0:512], AF.Identity, scale=1.0), cS)
                    bankg[bk] = (cS, cS.n)

        proj_o(wqT, QT_, 1.0 / 32.0)
        proj_o(wkT, KT_, 1.0)
        proj_v(wvT, V_)

        mm_done = {}
        exp_done = {}
        wks_of = {}

        def emit_scores_mm(kt):
            jd = kt // 4
            base = 0 if kt % 2 == 0 else 4
            for b2 in range(base, base + 4):
                bank_wait(b2)
            p.wait("tensor", cS, exp_done.get(kt - 2, 0))
            p.wait("tensor", cV, norm_done.get(kt - 2, 0))
            for j in range(jd, NTC):
                bk = base + (j - jd)
                for dp in range(NDP):
                    st, sp = (dp == 0), (dp == NDP - 1)
                    p.op("tensor", lambda e, kt=kt, j=j, dp=dp, bk=bk, st=st, sp=sp:
                         e.matmul(pb[bk][:, 0:512],
                                  bf(KT_, 0, 128, dp * T + kt * 128, dp * T + (kt + 1) * 128),
                                  bf(QT_, 0, 128, dp * T + j * 512, dp * T + (j + 1) * 512),
                                  start=st, stop=sp),
                         cT if sp else None)
            mm_done[kt] = cT.n

        norm_done = {}

        def emit_softmax(kt):
            jd, m = kt // 4, kt % 4
            nch = NTC - jd
            base = 0 if kt % 2 == 0 else 4
            wks = WKS if kt % 2 == 0 else WKS_B
            c0 = C_T0 if kt % 2 == 0 else C_T0 + 10
            cb_, cs_ = c0 + 4, c0 + 5
            cr_ = c0 + 9
            wks_of[kt] = wks
            p.wait("vector", cT, mm_done[kt])
            p.wait("vector", cS, exp_done.get(kt - 2, 0))
            p.op("vector", lambda e, m=m, bk=base, wks=wks: e.tensor_tensor(
                f32(wks, 0, 128, 0, 512), pb[bk][:, 0:512],
                f32(MASKS, 0, 128, m * 512, (m + 1) * 512), ALU.add), cV)
            p.op("vector", lambda e, wks=wks, c0=c0: e.reduce_max(
                col(c0), f32(wks, 0, 128, 0, 512), axis=AX, negate=True), cV)
            for j in range(jd + 1, NTC):
                p.op("vector", lambda e, j=j, jd=jd, bk=base + (j - jd), c0=c0: e.reduce_max(
                    col(c0 + j - jd), pb[bk][:, 0:512], axis=AX, negate=True), cV,
                    drain=False)
            if nch > 1:
                p.op("vector", lambda e, n=nch, c0=c0, cb_=cb_: e.tensor_reduce(
                    col(cb_), col(c0, n), axis=AX, op=ALU.min), cV)
            else:
                p.op("vector", lambda e, c0=c0, cb_=cb_: e.tensor_copy(col(cb_), col(c0)), cV)
            p.wait("scalar", cV)
            for j in range(jd, NTC):
                p.op("scalar", lambda e, kt=kt, j=j, jd=jd, bk=base + (j - jd), wks=wks,
                     cb_=cb_, cs_=cs_:
                     e.activation(bf(ATT, 0, 128, kt * T + j * 512, kt * T + (j + 1) * 512),
                                  (f32(wks, 0, 128, 0, 512) if j == jd else pb[bk][:, 0:512]),
                                  AF.Exp, bias=col(cb_),
                                  accum_out=col(cs_ + j - jd)), cS, drain=False)
            exp_done[kt] = cS.n
            p.wait("vector", cS)
            if nch > 1:
                p.op("vector", lambda e, n=nch, cs_=cs_, cr_=cr_: e.tensor_reduce(
                    col(cr_), col(cs_, n), axis=AX, op=ALU.add), cV)
            else:
                p.op("vector", lambda e, cs_=cs_, cr_=cr_: e.tensor_copy(col(cr_), col(cs_)), cV)
            p.op("vector", lambda e, cr_=cr_: e.reciprocal(col(cr_), col(cr_)), cV)
            p.op("vector", lambda e, kt=kt, jd=jd, cr_=cr_: e.tensor_scalar(
                bf(ATT, 0, 128, kt * T + jd * 512, (kt + 1) * T),
                bf(ATT, 0, 128, kt * T + jd * 512, (kt + 1) * T),
                col(cr_), None, ALU.mult), cV)
            norm_done[kt] = cV.n

        emit_scores_mm(0)
        for kt in range(1, NKT):
            emit_scores_mm(kt)
            emit_softmax(kt - 1)
        emit_softmax(NKT - 1)
        for i in range(8):
            bankg[i] = (cV, norm_done[NKT - 1])
        groups = [(db, j) for db in range(NDP) for j in range(NTC)]
        xl_done = {}
        ev_done = {}
        st_done = {}

        def ld_av(gi):
            if gi >= len(groups):
                return
            db, j = groups[gi]
            xr = XRA if gi % 2 == 0 else XRB
            p.wait("gpsimd", cT, mm_done[NKT - 1])
            p.wait("gpsimd", cV, ev_done.get(gi - 2, 0))
            p.dma("gpsimd", f32(xr, 0, 128, 0, 512),
                  src_dram[db * 128:(db + 1) * 128, j * 512:(j + 1) * 512], dGs[gi % 2])
            xl_done[gi] = (dGs[gi % 2], dGs[gi % 2].n)

        ld_av(0)
        ld_av(1)
        s2_steps = 0
        for gi, (db, j) in enumerate(groups):
            bk = gi % 2
            bank_wait(bk)
            nkt = min(4 * j + 4, NKT)
            for kt in range(nkt):
                st, sp = (kt == 0), (kt == nkt - 1)
                p.op("tensor", lambda e, kt=kt, db=db, j=j, bk=bk, st=st, sp=sp:
                     e.matmul(pb[bk][:, 0:512],
                              bf(V_, 0, 128, kt * D + db * 128, kt * D + (db + 1) * 128),
                              bf(ATT, 0, 128, kt * T + j * 512, kt * T + (j + 1) * 512),
                              start=st, stop=sp),
                     cT if sp else None)
            xr = XRA if gi % 2 == 0 else XRB
            ev = EVT if gi % 2 == 0 else EVT_B
            p.wait("vector", cT)
            p.wait("vector", xl_done[gi][0], xl_done[gi][1])
            if gi - 2 in st_done:
                s_prev = st_done[gi - 2][0]
                p.wait("vector", s_prev, s_prev.n)
            p.op("vector", lambda e, bk=bk, xr=xr, ev=ev: e.tensor_tensor(
                f32(ev, 0, 128, 0, 512), pb[bk][:, 0:512],
                f32(xr, 0, 128, 0, 512), ALU.add), cV)
            bankg[bk] = (cV, cV.n)
            ev_done[gi] = cV.n
            p.wait("sync", cV)
            p.dma("sync", dst_dram[db * 128:(db + 1) * 128, j * 512:(j + 1) * 512],
                  f32(ev, 0, 128, 0, 512), dT[db])
            st_done[gi] = (dT[db], dT[db].n)
            ld_av(gi + 2)
            if j == NTC - 1:
                if stats2_guards is not None:
                    stats2_guards[db] = [(dT[db], dT[db].n)]
                if stats2_gen is not None and db >= 2:
                    try:
                        next(stats2_gen)
                        s2_steps += 1
                    except StopIteration:
                        pass

    p.dma("sync", ident, identin, dS)
    p.dma("sync", identb, identbin, dS)
    p.dma("sync", onesbc, onesbcin, dS)
    p.dma("sync", onesm, onesmin, dS)
    p.dma("sync", f32(MASKS, 0, 128, 0, T), masksin, dS)
    p.dma("sync", scal(S_T0, 2), clips, dS)
    p.wait("scalar", dS)
    p.op("scalar", lambda e: e.activation(scal(S_SIGX, 2), scal(S_T0, 2), AF.Sigmoid), cS)
    p.wait("vector", dS)

    smm1 = ffn(xin, x1_dram, ccio[0], ccio[1], wq_gens=wq_sh_gen(),
               feed_stats=True)
    s2_guards = {}
    s2 = ln_stats_gen(x2_dram, src_guards=s2_guards,
                      bufs=(106496, 114688, 122880, 126976, 131072, 135168))
    attention(x1_dram, x2_dram, stats2_gen=s2, stats2_guards=s2_guards,
              stats_premixed=smm1)
    drive(s2)

    p.wait("gpsimd", cT)
    p.wait("gpsimd", cV)
    p.wait("gpsimd", cS)
    for c in range(N_CORES):
        n = 3 if c < 7 else 1
        p.dma("gpsimd", bf(W1, 0, 128, 3 * c * 1024, (3 * c + n) * 1024),
              full_d[c * 128:(c + 1) * 128, 0:n * 1024], dRg)
        p.dma("gpsimd", bf(W2, 0, 128, 3 * c * 1024, (3 * c + n) * 1024),
              full_d[c * 128:(c + 1) * 128, 3072:3072 + n * 1024], dRg)

    ffn(x2_dram, outT, ccio[2], ccio[3], stats_done=True)

    p.wait("sync", dS)
    p.wait("sync", dA)
    p.wait("sync", dG)
    for s in dQ + dL + dGs + dXs + dT + dD + dP + [dR, dRg, dW, dC]:
        p.wait("sync", s)
    p.wait("sync", cV)
    p.wait("sync", cS)
    p.wait("sync", cT)
    p.wait("sync", cc)

    with nc.Block() as block:
        for eng in Prog.ENGS:
            fns = p.items[eng]
            getattr(block, eng)(lambda e, fns=fns: [fn(e) for fn in fns])

    return nc


_NC_CACHE = {}


def kernel(L0_attn_x, L0_attn_W_q, L0_attn_W_k, L0_attn_W_v,
           L0_ffn_clip_x, L0_ffn_clip_down, L0_ffn_w_gate, L0_ffn_w_up,
           L0_ffn_w_down):
    import ml_dtypes
    x = np.asarray(L0_attn_x)
    wq = np.ascontiguousarray(np.asarray(L0_attn_W_q).T)
    wk = np.ascontiguousarray(np.asarray(L0_attn_W_k).T)
    wv = np.ascontiguousarray(np.asarray(L0_attn_W_v).T)
    wg = np.ascontiguousarray(np.asarray(L0_ffn_w_gate))
    wu = np.ascontiguousarray(np.asarray(L0_ffn_w_up))
    wd = np.ascontiguousarray(np.asarray(L0_ffn_w_down))
    wgSs, wuSs, wdSs = [], [], []
    for c in range(N_CORES):
        pts = [min(3 * c + s, II // 128 - 1) for s in range(3)]
        wgSs.append(np.concatenate([wg[t * 128:(t + 1) * 128] for t in pts], axis=0))
        wuSs.append(np.concatenate([wu[t * 128:(t + 1) * 128] for t in pts], axis=0))
        wdSs.append(np.ascontiguousarray(wd[c * 128:(c + 1) * 128]))
    clips = np.array([[np.asarray(L0_ffn_clip_x).reshape(-1)[0],
                       np.asarray(L0_ffn_clip_down).reshape(-1)[0]]], dtype=np.float32)
    ident = np.eye(128, dtype=np.float32)
    identb = np.eye(128, dtype=ml_dtypes.bfloat16)
    onesbc = np.ones((1, 128), dtype=np.float32)
    onesm = np.full((128, 1), 1.0 / 1024.0, dtype=ml_dtypes.bfloat16)
    r = np.arange(512)[None, :]
    pp = np.arange(128)[:, None]
    masks = np.concatenate(
        [np.where(r >= 128 * m + pp, 0.0, NEG).astype(np.float32) for m in range(4)],
        axis=1)

    if "nc" not in _NC_CACHE:
        _NC_CACHE["nc"] = build_nc()
    nc = _NC_CACHE["nc"]

    in_maps = []
    for b in range(N_CORES):
        in_maps.append({
            "xin": np.ascontiguousarray(x[b].T),
            "wqT": wq, "wkT": wk, "wvT": wv,
            "wgS": wgSs[b], "wuS": wuSs[b], "wdS": wdSs[b],
            "clips": clips, "identin": ident, "identbin": identb, "onesbcin": onesbc,
            "onesmin": onesm, "masksin": masks,
        })
    res = run_bass_kernel_spmd(nc, in_maps, core_ids=list(range(N_CORES)))
    out = np.stack([np.ascontiguousarray(res.results[b]["outT"].T)
                    for b in range(N_CORES)], axis=0)
    return out


# version 49
# speedup vs baseline: 1.0654x; 1.0125x over previous
import contextlib
import numpy as np
import concourse.bass as bass
import concourse.mybir as mybir
from concourse.bass_utils import run_bass_kernel_spmd

F32 = mybir.dt.float32
BF16 = mybir.dt.bfloat16
I32 = mybir.dt.int32
AF = mybir.ActivationFunctionType
ALU = mybir.AluOpType
AX = mybir.AxisListType.X

T = 2048
D = 1024
II = 2816
NDP = D // 128
NIP = II // 128
NTC = T // 512
NKT = T // 128
QMAX = 127.0
NEG = -1.0e30
N_CORES = 8

AR_BYTES = 204800
XA, XB = 0, 8192
XC, XD = 16384, 20480
FB, GB = 24576, 32768
XN = 16384
QIX = 16384
SILU, HPRE, HMC = 0, 8192, 16384
XQ = 40960
SHW = 8960
SHB = 118784
WDS_STG = 141312
WD_QB1 = 152576
WG_STG = [40960, 45056, 49152]
WG_QB = [53248, 55296]
C_GQ = 52
W1, W2 = 73728, 118784
HQ_A, HQ_B = 0, 22528
EVT2, EVT2B, XRA2, XRB2 = 47104, 53248, 49152, 51200
CST2 = [55296 + 1024 * i for i in range(4)]
SQ2 = [59392 + 1024 * i for i in range(4)]
HSB3 = [118784, 141312]
XG_A, XG_B = 0, 16384
WD = 73728
XNT = 40960
V_ = 0
WQB = 73728
QT_ = 98304
KT_ = 131072
ATT = 32768
EVT, EVT_B, XRA, XRB = 98304, 100352, 102400, 104448
IDENT = 163840
IDENTB = 199904
MASKS = 164352
ONESBC = 172544
ONESM = 173056
ROWS = 173120
COLS = 197696
CCS = 198208
SCAL = 199232
WKS = 199360
WKS_B = 201408

R_MEAN, R_RSTD, R_G = 0, 2048, 4096
C_WGS, C_WUS, C_WDS = 0, 22, 44
C_GSC, C_USC, C_DSC = 52, 74, 96
C_RS, C_RH, C_SX127, C_SH = 104, 105, 106, 107
C_T0 = 108
S_SIGX, S_SIGD, S_GMAX, S_SX, S_RSX, S_SX127, S_SH, S_RSH, S_SH127, S_T0 = range(10)


class Sem:
    __slots__ = ("h", "n")

    def __init__(self, h):
        self.h = h
        self.n = 0


class Prog:
    ENGS = ("sync", "vector", "scalar", "tensor", "gpsimd")

    def __init__(self):
        self.items = {e: [] for e in self.ENGS}
        self._lastw = {}

    def emit(self, eng, fn):
        self.items[eng].append(fn)

    def dma(self, q, out_ap, in_ap, sem, slow=False):
        sem.n += 16
        self.emit(q, lambda e, o=out_ap, i=in_ap, h=sem.h, s=slow:
                  e.dma_start(out=o, in_=i, allow_slow_non_contiguous=s).then_inc(h, 16))

    def wait(self, eng, sem, n=None):
        v = sem.n if n is None else n
        if v <= 0:
            return
        key = (eng, id(sem))
        if self._lastw.get(key, -1) >= v:
            return
        self._lastw[key] = v
        self.emit(eng, lambda e, h=sem.h, v=v: e.wait_ge(h, v))

    def op(self, eng, fn, sem=None, inc=1, drain=True):
        if drain and eng in ("vector", "scalar", "gpsimd"):
            self.emit(eng, lambda e: e.drain())
        if sem is not None:
            sem.n += inc
            self.emit(eng, lambda e, fn=fn, h=sem.h, k=inc: fn(e).then_inc(h, k))
        else:
            self.emit(eng, fn)


def build_nc():
    nc = bass.Bass("TRN2", target_bir_lowering=False)

    xin = nc.dram_tensor("xin", [D, T], F32, kind="ExternalInput").ap()
    wqT = nc.dram_tensor("wqT", [D, D], F32, kind="ExternalInput").ap()
    wkT = nc.dram_tensor("wkT", [D, D], F32, kind="ExternalInput").ap()
    wvT = nc.dram_tensor("wvT", [D, D], F32, kind="ExternalInput").ap()
    wgS = nc.dram_tensor("wgS", [384, D], F32, kind="ExternalInput").ap()
    wuS = nc.dram_tensor("wuS", [384, D], F32, kind="ExternalInput").ap()
    wdS = nc.dram_tensor("wdS", [128, II], F32, kind="ExternalInput").ap()
    clips = nc.dram_tensor("clips", [1, 2], F32, kind="ExternalInput").ap()
    identin = nc.dram_tensor("identin", [128, 128], F32, kind="ExternalInput").ap()
    identbin = nc.dram_tensor("identbin", [128, 128], BF16, kind="ExternalInput").ap()
    onesbcin = nc.dram_tensor("onesbcin", [1, 128], F32, kind="ExternalInput").ap()
    onesmin = nc.dram_tensor("onesmin", [128, 1], BF16, kind="ExternalInput").ap()
    masksin = nc.dram_tensor("masksin", [128, T], F32, kind="ExternalInput").ap()

    outT = nc.dram_tensor("outT", [D, T], F32, kind="ExternalOutput").ap()

    xn_dram = nc.dram_tensor("xn_dram", [D, T], F32).ap()
    x1_dram = nc.dram_tensor("x1_dram", [D, T], F32).ap()
    x2_dram = nc.dram_tensor("x2_dram", [D, T], F32).ap()
    h_dram = nc.dram_tensor("h_dram", [128, NIP * T], F32).ap()
    sh_d = nc.dram_tensor("sh_d", [128, SHW], BF16).ap()
    full_d = nc.dram_tensor("full_d", [128 * N_CORES, SHW], BF16).ap()
    sc_d = nc.dram_tensor("sc_d", [128, 8], F32).ap()
    scf_d = nc.dram_tensor("scf_d", [128 * N_CORES, 8], F32).ap()
    ccio = [(nc.dram_tensor(f"cc{i}_in", [1, 128], F32).ap(),
             nc.dram_tensor(f"cc{i}_out", [1, 128], F32).ap()) for i in range(4)]

    es = contextlib.ExitStack()
    arena = es.enter_context(nc.sbuf_tensor("arena", [128, AR_BYTES], mybir.dt.uint8)).ap()
    AF32 = arena.bitcast(F32)
    ABF = arena.bitcast(BF16)
    AI32 = arena.bitcast(I32)

    def f32(off, p0, p1, e0, e1):
        return AF32[p0:p1, off // 4 + e0: off // 4 + e1]

    def bf(off, p0, p1, e0, e1):
        return ABF[p0:p1, off // 2 + e0: off // 2 + e1]

    def i32(off, p0, p1, e0, e1):
        return AI32[p0:p1, off // 4 + e0: off // 4 + e1]

    ident = f32(IDENT, 0, 128, 0, 128)
    identb = bf(IDENTB, 0, 128, 0, 128)
    onesbc = f32(ONESBC, 0, 1, 0, 128)
    onesm = bf(ONESM, 0, 128, 0, 1)

    def rows(slot, a, b):
        return f32(ROWS, 0, 1, slot + a, slot + b)

    def col(idx, k=1):
        return f32(COLS, 0, 128, idx, idx + k)

    def scal(idx, k=1):
        return f32(SCAL, 0, 1, idx, idx + k)

    pb = [es.enter_context(nc.psum_tensor(f"pb{i}", [128, 512], F32)) for i in range(8)]
    pbb = [t.ap().bitcast(BF16) for t in pb]

    dS = Sem(es.enter_context(nc.semaphore("dS")))
    dA = Sem(es.enter_context(nc.semaphore("dA")))
    dG = Sem(es.enter_context(nc.semaphore("dG")))
    cV = Sem(es.enter_context(nc.semaphore("cV")))
    cS = Sem(es.enter_context(nc.semaphore("cS")))
    cT = Sem(es.enter_context(nc.semaphore("cT")))
    cc = Sem(es.enter_context(nc.semaphore("cc")))
    dQ = [Sem(es.enter_context(nc.semaphore(f"dQ{i}"))) for i in range(3)]
    dL = [Sem(es.enter_context(nc.semaphore(f"dL{i}"))) for i in range(2)]
    dGs = [Sem(es.enter_context(nc.semaphore(f"dGs{i}"))) for i in range(2)]
    dT = [Sem(es.enter_context(nc.semaphore(f"dT{i}"))) for i in range(8)]
    dR = Sem(es.enter_context(nc.semaphore("dR")))
    dC = Sem(es.enter_context(nc.semaphore("dC")))
    dRg = Sem(es.enter_context(nc.semaphore("dRg")))
    dD = [Sem(es.enter_context(nc.semaphore(f"dD{i}"))) for i in range(2)]
    dW = Sem(es.enter_context(nc.semaphore("dW")))
    dP = [Sem(es.enter_context(nc.semaphore(f"dP{i}"))) for i in range(2)]
    dXs = [Sem(es.enter_context(nc.semaphore(f"dXs{i}"))) for i in range(2)]

    p = Prog()
    bankg = [None] * 8

    def bank_wait(i):
        if bankg[i] is not None:
            p.wait("tensor", bankg[i][0], bankg[i][1])

    def bcast_scalar_to_col(s_ap, cidx):
        bank_wait(7)
        p.wait("tensor", cV)
        p.op("tensor", lambda e, s=s_ap: e.matmul(pb[7][:, 0:1], onesbc, s,
                                                  start=True, stop=True), cT)
        p.wait("vector", cT)
        p.op("vector", lambda e, c=col(cidx): e.tensor_copy(c, pb[7][:, 0:1]), cV)
        bankg[7] = (cV, cV.n)

    def bcast_rows_to_tiles(jobs, rows_mark=None):
        nj = len(jobs)
        first = True
        for j, (slot, dst) in enumerate(jobs):
            for ch in range(NTC):
                bk = 2 + ch
                bank_wait(bk)
                p.wait("tensor", cV, rows_mark)
                p.op("tensor", lambda e, slot=slot, ch=ch, bk=bk:
                     e.matmul(pb[bk][:, 0:512], onesbc,
                              rows(slot, ch * 512, (ch + 1) * 512),
                              start=True, stop=True),
                     cT if ch == NTC - 1 else None)
            p.wait("vector", cT)
            for ch in range(NTC):
                p.op("vector", lambda e, ch=ch, dst=dst: e.tensor_copy(
                    f32(dst, 0, 128, ch * 512, (ch + 1) * 512),
                    pb[2 + ch][:, 0:512]), cV, drain=first)
                first = False
            for ch in range(NTC):
                bankg[2 + ch] = (cV, cV.n)

    def ln_stats_gen(src_dram, src_guards=None,
                     bufs=(XA, XB, XC, XD, 24576, 28672)):
        for i in range(2, 6):
            bank_wait(i)
        ldd = {}
        consumed = {}

        BXA, BXB, BXC0, BXD0, BXC1, BXD1 = bufs

        def ld_stats(dp):
            if dp >= NDP:
                return
            buf = BXA if dp % 2 == 0 else BXB
            q = "sync" if dp % 2 == 0 else "scalar"
            d = dL[dp % 2]
            if dp - 2 in consumed:
                p.wait(q, cV, consumed[dp - 2][0])
                p.wait(q, cS, consumed[dp - 2][1])
            else:
                p.wait(q, cV)
                p.wait(q, cS)
            if src_guards is not None:
                g = src_guards.get(dp)
                if g is not None:
                    for gg in g:
                        p.wait(q, gg[0], gg[1])
            p.dma(q, f32(buf, 0, 128, 0, T), src_dram[dp * 128:(dp + 1) * 128, :], d)
            ldd[dp] = (d, d.n)

        ld_stats(0)
        ld_stats(1)
        smm = {}
        for dp in range(NDP):
            buf = BXA if dp % 2 == 0 else BXB
            BXC = BXC0 if dp % 2 == 0 else BXC1
            BXD = BXD0 if dp % 2 == 0 else BXD1
            p.wait("vector", ldd[dp][0], ldd[dp][1])
            p.wait("vector", cT, smm.get(dp - 2, 0))
            p.op("vector", lambda e, b=buf, BXC=BXC: e.tensor_copy(
                bf(BXC, 0, 128, 0, T), f32(b, 0, 128, 0, T)), cV, drain=False)
            castm = cV.n
            p.wait("scalar", ldd[dp][0], ldd[dp][1])
            p.wait("scalar", cT, smm.get(dp - 2, 0))
            p.op("scalar", lambda e, b=buf, BXD=BXD: e.activation(
                bf(BXD, 0, 128, 0, T), f32(b, 0, 128, 0, T), AF.Square), cS, drain=False)
            sqm = cS.n
            consumed[dp] = (castm, sqm)
            ld_stats(dp + 2)
            yield
            p.wait("tensor", cV, castm)
            p.wait("tensor", cS, sqm)
            for tc in range(NTC):
                st, sp = (dp == 0), (dp == NDP - 1)
                p.op("tensor", lambda e, tc=tc, st=st, sp=sp, BXC=BXC:
                     e.matmul(pb[2 + tc][0:1, :], onesm,
                              bf(BXC, 0, 128, tc * 512, (tc + 1) * 512),
                              start=st, stop=sp), None)
                p.op("tensor", lambda e, tc=tc, st=st, sp=sp, BXD=BXD:
                     e.matmul(pb[2 + tc][32:33, :], onesm,
                              bf(BXD, 0, 128, tc * 512, (tc + 1) * 512),
                              start=st, stop=sp),
                     cT if tc == NTC - 1 else None)
            smm[dp] = cT.n
        ln_stats_finish(smm[NDP - 1])
        yield

    def ln_stats_finish(mm_mark):
        p.wait("vector", cT, mm_mark)
        for tc in range(NTC):
            p.op("vector", lambda e, tc=tc: e.tensor_copy(
                rows(R_MEAN, tc * 512, (tc + 1) * 512), pb[2 + tc][0:1, :]), cV, drain=(tc == 0))
            p.op("vector", lambda e, tc=tc: e.tensor_copy(
                rows(R_RSTD, tc * 512, (tc + 1) * 512), pb[2 + tc][32:33, :]), cV, drain=False)
        for i in range(2, 6):
            bankg[i] = (cV, cV.n)
        p.wait("scalar", cV)
        p.op("scalar", lambda e: e.activation(rows(R_G, 0, T), rows(R_MEAN, 0, T), AF.Square), cS)
        p.wait("vector", cS)
        p.op("vector", lambda e: e.scalar_tensor_tensor(
            rows(R_RSTD, 0, T), rows(R_G, 0, T), -1.0, rows(R_RSTD, 0, T),
            ALU.mult, ALU.add), cV)
        p.op("vector", lambda e: e.tensor_scalar(
            rows(R_RSTD, 0, T), rows(R_RSTD, 0, T), 1e-5, None, ALU.add), cV)
        p.wait("scalar", cV)
        p.op("scalar", lambda e: e.activation(rows(R_RSTD, 0, T), rows(R_RSTD, 0, T),
                                              AF.Ln), cS)
        p.op("scalar", lambda e: e.activation(rows(R_RSTD, 0, T), rows(R_RSTD, 0, T),
                                              AF.Exp, scale=-0.5), cS)
        p.wait("vector", cS)
        p.op("vector", lambda e: e.scalar_tensor_tensor(
            rows(R_G, 0, T), rows(R_MEAN, 0, T), -1.0, rows(R_RSTD, 0, T),
            ALU.mult, ALU.mult), cV)
        bcast_rows_to_tiles([(R_RSTD, FB), (R_G, GB)], rows_mark=cV.n)

    def allreduce_scalar_max(col_ap, cc_pair, out_scal_idx):
        cin, cout = cc_pair
        stg = f32(CCS, 0, 1, 0, 128)
        p.wait("gpsimd", cV)
        p.dma("gpsimd", cin, col_ap, dC)
        p.wait("gpsimd", dC)
        p.op("gpsimd", lambda e: e.collective_compute(
            "AllReduce", ALU.max, replica_groups=[list(range(N_CORES))],
            ins=[cin], outs=[cout]), cc, drain=False)
        p.wait("gpsimd", cc)
        p.dma("gpsimd", stg, cout, dC)
        p.wait("vector", dC)
        p.op("vector", lambda e: e.reduce_max(scal(out_scal_idx), stg, axis=AX), cV)

    def qw_gen(w_dram, n_pt, width, c_slot, qdst_off, spill_dram, stgs, qbs,
               spill_guard=None, buf_guard=0, qb_buf_guard=0):
        nb = width // 128
        w3d = ABF[:, qdst_off // 2: qdst_off // 2 + nb * n_pt * 128].rearrange(
            "p (b i) -> p b i", b=nb)
        if spill_guard is not None:
            p.wait("scalar", spill_guard[0], spill_guard[1])
        qb_guard = {}
        q1_done = {}
        q2_done = {}
        ld = {}

        def load(pt):
            if pt >= n_pt:
                return
            q = "sync" if pt % 2 == 0 else "scalar"
            d = dQ[pt % 3]
            p.wait(q, cV, q2_done.get(pt - 3, buf_guard))
            p.dma(q, f32(stgs[pt % 3], 0, 128, 0, width),
                  w_dram[pt * 128:(pt + 1) * 128, :], d)
            ld[pt] = (d, d.n)

        def stage_a(pt):
            stg = stgs[pt % 3]
            rc = C_GQ + (pt % 2)
            p.wait("vector", ld[pt][0], ld[pt][1])
            p.op("vector", lambda e, s=stg, pt=pt: e.reduce_max(
                col(c_slot + pt), f32(s, 0, 128, 0, width), axis=AX,
                apply_absolute_value=True), cV, drain=(pt == 0))
            p.wait("vector", cS, q1_done.get(pt - 2, 0))
            p.op("vector", lambda e, pt=pt, rc=rc: e.reciprocal(
                col(rc), col(c_slot + pt)), cV)
            p.op("vector", lambda e, rc=rc: e.tensor_scalar(
                col(rc), col(rc), QMAX, None, ALU.mult), cV)
            p.wait("scalar", cV)
            p.op("scalar", lambda e, s=stg, rc=rc: e.activation(
                i32(s, 0, 128, 0, width), f32(s, 0, 128, 0, width),
                AF.Identity, scale=col(rc)), cS, drain=False)
            q1_done[pt] = cS.n

        def stage_b(pt):
            stg = stgs[pt % 3]
            qb = qbs[pt % 2]
            p.wait("vector", cS, q1_done[pt])
            p.wait("vector", cT, qb_guard.get(pt - 2, qb_buf_guard))
            p.op("vector", lambda e, s=stg, qb=qb: e.tensor_scalar(
                bf(qb, 0, 128, 0, width), i32(s, 0, 128, 0, width),
                -128.0, 127.0, ALU.max, ALU.min), cV, drain=False)
            q2_done[pt] = cV.n
            for g in range((nb + 7) // 8):
                bk = 6 + (g % 2)
                kblk = min(8, nb - g * 8)
                bank_wait(bk)
                p.wait("tensor", cV, q2_done[pt])
                for k in range(kblk):
                    wb = g * 8 + k
                    p.op("tensor", lambda e, wb=wb, k=k, bk=bk, qb=qb: e.transpose(
                        pbb[bk][:, k * 128:(k + 1) * 128],
                        bf(qb, 0, 128, wb * 128, (wb + 1) * 128), identb),
                         cT if k == kblk - 1 else None)
                p.wait("scalar", cT)
                psrc = pbb[bk][:, 0:kblk * 128].rearrange("p (b i) -> p b i", b=kblk)
                p.op("scalar", lambda e, g=g, pt=pt, bk=bk, kblk=kblk, psrc=psrc:
                     e.activation(w3d[:, g * 8:g * 8 + kblk, pt * 128:(pt + 1) * 128],
                                  psrc, AF.Identity), cS, drain=False)
                bankg[bk] = (cS, cS.n)
            qb_guard[pt] = cT.n

        load(0)
        load(1)
        for pt in range(n_pt):
            stage_a(pt)
            if pt > 0:
                stage_b(pt - 1)
            load(pt + 2)
            yield
        stage_b(n_pt - 1)
        p.wait("sync", cS)
        p.dma("sync", spill_dram[:, 0:n_pt * width], bf(qdst_off, 0, 128, 0, n_pt * width), dS)

    def qw_gen_psum(w_dram, n_pt, width, c_slot, spill_dram, stgs, qbs,
                    buf_guard=0):
        nb = width // 128
        qb_guard = {}
        q1_done = {}
        q2_done = {}
        ld = {}

        def load(pt):
            if pt >= n_pt:
                return
            q = "sync" if pt % 2 == 0 else "scalar"
            d = dQ[pt % 3]
            p.wait(q, cV, q2_done.get(pt - 3, buf_guard))
            p.dma(q, f32(stgs[pt % 3], 0, 128, 0, width),
                  w_dram[pt * 128:(pt + 1) * 128, :], d)
            ld[pt] = (d, d.n)

        def stage_a(pt):
            stg = stgs[pt % 3]
            rc = C_GQ + (pt % 2)
            p.wait("vector", ld[pt][0], ld[pt][1])
            p.op("vector", lambda e, s=stg, pt=pt: e.reduce_max(
                col(c_slot + pt), f32(s, 0, 128, 0, width), axis=AX,
                apply_absolute_value=True), cV, drain=(pt == 0))
            p.wait("vector", cS, q1_done.get(pt - 2, 0))
            p.op("vector", lambda e, pt=pt, rc=rc: e.reciprocal(
                col(rc), col(c_slot + pt)), cV)
            p.op("vector", lambda e, rc=rc: e.tensor_scalar(
                col(rc), col(rc), QMAX, None, ALU.mult), cV)
            p.wait("scalar", cV)
            p.op("scalar", lambda e, s=stg, rc=rc: e.activation(
                i32(s, 0, 128, 0, width), f32(s, 0, 128, 0, width),
                AF.Identity, scale=col(rc)), cS, drain=False)
            q1_done[pt] = cS.n

        def stage_b(pt):
            stg = stgs[pt % 3]
            qb = qbs[pt % 2]
            p.wait("vector", cS, q1_done[pt])
            p.wait("vector", cT, qb_guard.get(pt - 2, 0))
            p.op("vector", lambda e, s=stg, qb=qb: e.tensor_scalar(
                bf(qb, 0, 128, 0, width), i32(s, 0, 128, 0, width),
                -128.0, 127.0, ALU.max, ALU.min), cV, drain=False)
            q2_done[pt] = cV.n
            for g in range((nb + 7) // 8):
                bk = 6 + (g % 2)
                kblk = min(8, nb - g * 8)
                bank_wait(bk)
                p.wait("tensor", cV, q2_done[pt])
                for k in range(kblk):
                    wb = g * 8 + k
                    p.op("tensor", lambda e, wb=wb, k=k, bk=bk, qb=qb: e.transpose(
                        pbb[bk][:, k * 128:(k + 1) * 128],
                        bf(qb, 0, 128, wb * 128, (wb + 1) * 128), identb),
                         cT if k == kblk - 1 else None)
                p.wait("scalar", cT)
                slot = 59392 if g % 2 == 0 else 61440
                p.wait("scalar", dP[g % 2], dP[g % 2].n)
                p.op("scalar", lambda e, bk=bk, kblk=kblk, slot=slot: e.activation(
                    bf(slot, 0, 128, 0, kblk * 128), pbb[bk][:, 0:kblk * 128],
                    AF.Identity), cS, drain=False)
                bankg[bk] = (cS, cS.n)
                p.wait("gpsimd", cS)
                c0 = (pt * nb + g * 8) * 128
                p.dma("gpsimd", spill_dram[:, c0:c0 + kblk * 128],
                      bf(slot, 0, 128, 0, kblk * 128), dP[g % 2])
            qb_guard[pt] = cT.n

        load(0)
        load(1)
        for pt in range(n_pt):
            stage_a(pt)
            if pt > 0:
                stage_b(pt - 1)
            load(pt + 2)
            yield
        stage_b(n_pt - 1)

    wq_fin = []

    def wq_sh_gen():
        units = ([("wg", s, D) for s in range(3)] +
                 [("wu", s, D) for s in range(3)] + [("wd", 0, II)])
        tmpc = [C_WGS, C_WGS + 1, C_WGS + 2, C_WUS, C_WUS + 1, C_WUS + 2, C_WDS]
        q1_done = {}
        q2_done = {}
        qb_guard = {}
        ld = {}

        def load(u):
            if u >= len(units):
                return
            kind, s, width = units[u]
            q = "sync" if u % 2 == 0 else "scalar"
            d = dQ[u % 3]
            p.wait(q, cV, q2_done.get(u - 3, 0))
            if kind == "wd":
                p.dma(q, f32(WDS_STG, 0, 128, 0, width), wdS, d)
            else:
                srcd = wgS if kind == "wg" else wuS
                p.dma(q, f32(WG_STG[u % 3], 0, 128, 0, width),
                      srcd[s * 128:(s + 1) * 128, :], d)
            ld[u] = (d, d.n)

        def stage_a(u):
            kind, s, width = units[u]
            stg = WDS_STG if kind == "wd" else WG_STG[u % 3]
            rc = C_GQ + (u % 2)
            p.wait("vector", ld[u][0], ld[u][1])
            p.op("vector", lambda e, stg=stg, width=width, u=u: e.reduce_max(
                col(tmpc[u]), f32(stg, 0, 128, 0, width), axis=AX,
                apply_absolute_value=True), cV, drain=(u == 0))
            p.op("vector", lambda e, u=u: e.tensor_copy(
                f32(59392, 0, 128, u, u + 1), col(tmpc[u])), cV)
            p.wait("vector", cS, q1_done.get(u - 2, 0))
            p.op("vector", lambda e, u=u, rc=rc: e.reciprocal(
                col(rc), col(tmpc[u])), cV)
            p.op("vector", lambda e, rc=rc: e.tensor_scalar(
                col(rc), col(rc), QMAX, None, ALU.mult), cV)
            p.wait("scalar", cV)
            p.op("scalar", lambda e, stg=stg, width=width, rc=rc: e.activation(
                i32(stg, 0, 128, 0, width), f32(stg, 0, 128, 0, width),
                AF.Identity, scale=col(rc)), cS, drain=False)
            q1_done[u] = cS.n

        def stage_b(u):
            kind, s, width = units[u]
            stg = WDS_STG if kind == "wd" else WG_STG[u % 3]
            qb = WD_QB1 if kind == "wd" else WG_QB[u % 2]
            nb = width // 128
            if kind == "wg":
                dst0 = s * 1024
            elif kind == "wu":
                dst0 = 3072 + s * 1024
            else:
                dst0 = 6144
            p.wait("vector", cS, q1_done[u])
            p.wait("vector", cT, qb_guard.get(u - 2, 0))
            p.op("vector", lambda e, stg=stg, qb=qb, width=width: e.tensor_scalar(
                bf(qb, 0, 128, 0, width), i32(stg, 0, 128, 0, width),
                -128.0, 127.0, ALU.max, ALU.min), cV, drain=False)
            q2_done[u] = cV.n
            for g in range((nb + 7) // 8):
                bk = 6 + (g % 2)
                kblk = min(8, nb - g * 8)
                bank_wait(bk)
                p.wait("tensor", cV, q2_done[u])
                for k in range(kblk):
                    wb = g * 8 + k
                    p.op("tensor", lambda e, wb=wb, k=k, bk=bk, qb=qb: e.transpose(
                        pbb[bk][:, k * 128:(k + 1) * 128],
                        bf(qb, 0, 128, wb * 128, (wb + 1) * 128), identb),
                         cT if k == kblk - 1 else None)
                p.wait("scalar", cT)
                p.op("scalar", lambda e, bk=bk, kblk=kblk, dst0=dst0, g=g: e.activation(
                    bf(SHB, 0, 128, dst0 + g * 1024, dst0 + g * 1024 + kblk * 128),
                    pbb[bk][:, 0:kblk * 128], AF.Identity), cS, drain=False)
                bankg[bk] = (cS, cS.n)
            qb_guard[u] = cT.n

        load(0)
        load(1)
        for u in range(len(units)):
            stage_a(u)
            if u > 0:
                stage_b(u - 1)
            load(u + 2)
            yield
        stage_b(len(units) - 1)
        p.wait("sync", cS)
        p.wait("sync", cV)
        p.dma("sync", sh_d, bf(SHB, 0, 128, 0, SHW), dS)
        p.op("vector", lambda e: e.tensor_scalar(
            f32(59392, 0, 128, 7, 8), f32(59392, 0, 128, 6, 7), 0.0,
            None, ALU.mult), cV)
        p.wait("sync", cV)
        p.dma("sync", sc_d, f32(59392, 0, 128, 0, 8), dS)
        p.wait("gpsimd", dS)
        p.op("gpsimd", lambda e: e.collective_compute(
            "AllGather", ALU.bypass, replica_groups=[list(range(N_CORES))],
            ins=[sh_d], outs=[full_d]), cc, drain=False)
        p.op("gpsimd", lambda e: e.collective_compute(
            "AllGather", ALU.bypass, replica_groups=[list(range(N_CORES))],
            ins=[sc_d], outs=[scf_d]), cc, drain=False)
        gat = cc.n

        def finish():
            p.wait("gpsimd", cc, gat)
            for c in range(N_CORES):
                n = 3 if c < 7 else 1
                p.dma("gpsimd", bf(W1, 0, 128, 3 * c * 1024, (3 * c + n) * 1024),
                      full_d[c * 128:(c + 1) * 128, 0:n * 1024], dRg)
                p.dma("gpsimd", bf(W2, 0, 128, 3 * c * 1024, (3 * c + n) * 1024),
                      full_d[c * 128:(c + 1) * 128, 3072:3072 + n * 1024], dRg)
                p.dma("gpsimd", f32(59392, 0, 128, c * 8, c * 8 + 8),
                      scf_d[c * 128:(c + 1) * 128, :], dRg)
            p.wait("vector", dRg)
            for c in range(N_CORES):
                n = 3 if c < 7 else 1
                p.op("vector", lambda e, c=c, n=n: e.tensor_copy(
                    col(C_WGS + 3 * c, n), f32(59392, 0, 128, c * 8, c * 8 + n)),
                    cV, drain=(c == 0))
                p.op("vector", lambda e, c=c, n=n: e.tensor_copy(
                    col(C_WUS + 3 * c, n),
                    f32(59392, 0, 128, c * 8 + 3, c * 8 + 3 + n)), cV)
                p.op("vector", lambda e, c=c: e.tensor_copy(
                    col(C_WDS + c), f32(59392, 0, 128, c * 8 + 6, c * 8 + 7)), cV)
        wq_fin.append(finish)

    def drive(*gens):
        live = list(gens)
        while live:
            for g in list(live):
                try:
                    next(g)
                except StopIteration:
                    live.remove(g)

    def drive_with(main, aux, aux_steps=2):
        while True:
            try:
                next(main)
            except StopIteration:
                return
            for _ in range(aux_steps):
                try:
                    next(aux)
                except StopIteration:
                    break

    def ffn(src_dram, dst_dram, ccA, ccB, wq_gens=None, stats_done=False,
            feed_stats=False):
        p.wait("vector", cT)
        p.wait("scalar", cT)
        p.wait("sync", dS)
        for s in dT + dD:
            p.wait("sync", s)
            p.wait("scalar", s)
            p.wait("vector", s)
        p.wait("scalar", dS)
        if not stats_done:
            if wq_gens is not None:
                drive_with(ln_stats_gen(src_dram), wq_gens, aux_steps=1)
            else:
                drive(ln_stats_gen(src_dram))
        ldx = {}
        spg = {}
        xn_mark = [0]

        def ld_xn(dp):
            if dp >= NDP:
                return
            buf = XA if dp % 2 == 0 else XB
            q = "sync" if dp % 2 == 0 else "scalar"
            d = dL[dp % 2]
            if dp - 2 in spg:
                g = spg[dp - 2]
                p.wait(q, g[0], g[1])
            else:
                p.wait(q, cV)
                p.wait(q, cS)
            p.dma(q, f32(buf, 0, 128, 0, T), src_dram[dp * 128:(dp + 1) * 128, :], d)
            ldx[dp] = (d, d.n)

        def xn_gen():
            ld_xn(0)
            ld_xn(1)
            for dp in range(NDP):
                buf = XA if dp % 2 == 0 else XB
                p.wait("vector", ldx[dp][0], ldx[dp][1])
                p.op("vector", lambda e, b=buf: e.tensor_tensor(
                    f32(b, 0, 128, 0, T), f32(b, 0, 128, 0, T),
                    f32(FB, 0, 128, 0, T), ALU.mult), cV, drain=False)
                p.op("vector", lambda e, b=buf: e.tensor_tensor(
                    f32(b, 0, 128, 0, T), f32(b, 0, 128, 0, T),
                    f32(GB, 0, 128, 0, T), ALU.add), cV)
                p.op("vector", lambda e, dp=dp, b=buf: e.reduce_max(
                    col(C_T0 + dp), f32(b, 0, 128, 0, T), axis=AX,
                    apply_absolute_value=True), cV)
                q2 = "scalar" if dp % 2 == 0 else "sync"
                p.wait(q2, cV, cV.n)
                p.dma(q2, xn_dram[dp * 128:(dp + 1) * 128, :], f32(buf, 0, 128, 0, T),
                      dXs[dp % 2])
                spg[dp] = (dXs[dp % 2], dXs[dp % 2].n)
                ld_xn(dp + 2)
                yield
            p.op("vector", lambda e: e.reduce_max(col(C_T0 + 8), col(C_T0, NDP), axis=AX), cV)
            xn_mark[0] = cV.n
            allreduce_scalar_max(col(C_T0 + 8), ccA, S_GMAX)
            yield

        if wq_gens is not None:
            drive(xn_gen(), wq_gens)
            while wq_fin:
                wq_fin.pop()()
        else:
            drive(xn_gen())
        p.op("vector", lambda e: e.tensor_tensor(scal(S_SX), scal(S_SIGX), scal(S_GMAX), ALU.mult), cV)
        p.op("vector", lambda e: e.tensor_scalar(scal(S_SX), scal(S_SX), 1.0 / QMAX, None, ALU.mult), cV)
        p.op("vector", lambda e: e.reciprocal(scal(S_RSX), scal(S_SX)), cV)
        p.op("vector", lambda e: e.tensor_scalar(scal(S_SX127), scal(S_SX), 1.0 / QMAX, None, ALU.mult), cV)
        bcast_scalar_to_col(scal(S_RSX), C_RS)
        bcast_scalar_to_col(scal(S_SX127), C_SX127)
        p.wait("vector", cS)
        p.op("vector", lambda e: e.tensor_scalar(
            col(C_GSC, NIP), col(C_WGS, NIP), col(C_SX127), None, ALU.mult), cV)
        p.op("vector", lambda e: e.tensor_scalar(
            col(C_USC, NIP), col(C_WUS, NIP), col(C_SX127), None, ALU.mult), cV)
        p.wait("sync", dXs[0])
        p.wait("sync", dXs[1])
        p.wait("scalar", dXs[0])
        p.wait("scalar", dXs[1])
        xn3 = xn_dram.rearrange("(a p) t -> p a t", p=128)
        ldq = {}
        qdone = {}

        def ld_xq(u):
            if u >= NDP // 2:
                return
            g = XG_A if u % 2 == 0 else XG_B
            q = "sync" if u % 2 == 0 else "scalar"
            d = dL[u % 2]
            if u - 2 in qdone:
                p.wait(q, cV, qdone[u - 2])
            else:
                p.wait(q, cV, xn_mark[0])
            p.dma(q, f32(g, 0, 128, 0, 2 * T), xn3[:, 2 * u:2 * u + 2, :], d)
            ldq[u] = (d, d.n)

        ld_xq(0)
        ld_xq(1)
        p.wait("vector", dP[0])
        p.wait("vector", dP[1])
        for u in range(NDP // 2):
            g = XG_A if u % 2 == 0 else XG_B
            p.wait("scalar", ldq[u][0], ldq[u][1])
            p.wait("scalar", cV)
            p.op("scalar", lambda e, g=g: e.activation(
                i32(g, 0, 128, 0, 2 * T), f32(g, 0, 128, 0, 2 * T),
                AF.Identity, scale=col(C_RS)), cS, drain=False)
            p.wait("vector", cS)
            p.op("vector", lambda e, g=g, u=u: e.tensor_scalar(
                bf(XQ, 0, 128, 2 * u * T, (2 * u + 2) * T), i32(g, 0, 128, 0, 2 * T),
                -128.0, 127.0, ALU.max, ALU.min), cV, drain=False)
            qdone[u] = cV.n
            ld_xq(u + 2)
        p.wait("tensor", cV)
        p.wait("tensor", dS)
        p.wait("tensor", dR)
        p.wait("tensor", dRg)
        for ib in range(NIP):
            for i in range(4):
                bank_wait(i)
            for dp in range(NDP):
                for tc in range(NTC):
                    st, sp = (dp == 0), (dp == NDP - 1)
                    p.op("tensor", lambda e, ib=ib, dp=dp, tc=tc, st=st, sp=sp:
                         e.matmul(pb[tc][:, 0:512],
                                  bf(W1, 0, 128, (ib * NDP + dp) * 128,
                                     (ib * NDP + dp + 1) * 128),
                                  bf(XQ, 0, 128, dp * T + tc * 512, dp * T + (tc + 1) * 512),
                                  start=st, stop=sp),
                         cT if (sp and tc == NTC - 1) else None)
            gate_done = cT.n
            for i in range(4, 8):
                bank_wait(i)
            for dp in range(NDP):
                for tc in range(NTC):
                    st, sp = (dp == 0), (dp == NDP - 1)
                    p.op("tensor", lambda e, ib=ib, dp=dp, tc=tc, st=st, sp=sp:
                         e.matmul(pb[4 + tc][:, 0:512],
                                  bf(W2, 0, 128, (ib * NDP + dp) * 128,
                                     (ib * NDP + dp + 1) * 128),
                                  bf(XQ, 0, 128, dp * T + tc * 512, dp * T + (tc + 1) * 512),
                                  start=st, stop=sp),
                         cT if (sp and tc == NTC - 1) else None)
            up_done = cT.n
            p.wait("scalar", cT, gate_done)
            p.wait("scalar", cV)
            for tc in range(NTC):
                p.op("scalar", lambda e, ib=ib, tc=tc: e.activation(
                    f32(SILU, 0, 128, tc * 512, (tc + 1) * 512), pb[tc][:, 0:512],
                    AF.Silu, scale=col(C_GSC + ib)), cS)
            for i in range(4):
                bankg[i] = (cS, cS.n)
            p.wait("vector", cT, up_done)
            p.wait("vector", cS)
            p.wait("vector", dG)
            for tc in range(NTC):
                p.op("vector", lambda e, ib=ib, tc=tc: e.scalar_tensor_tensor(
                    f32(HPRE, 0, 128, tc * 512, (tc + 1) * 512), pb[4 + tc][:, 0:512],
                    col(C_USC + ib), f32(SILU, 0, 128, tc * 512, (tc + 1) * 512),
                    ALU.mult, ALU.mult), cV)
                p.op("vector", lambda e, ib=ib, tc=tc: e.reduce_max(
                    f32(HMC, 0, 128, ib * NTC + tc, ib * NTC + tc + 1),
                    f32(HPRE, 0, 128, tc * 512, (tc + 1) * 512),
                    axis=AX, apply_absolute_value=True), cV)
                p.wait("gpsimd", cV)
                p.dma("gpsimd", h_dram[:, (tc * NIP + ib) * 512:(tc * NIP + ib + 1) * 512],
                      f32(HPRE, 0, 128, tc * 512, (tc + 1) * 512), dG)
            for i in range(4, 8):
                bankg[i] = (cV, cV.n)
        p.op("vector", lambda e: e.reduce_max(col(C_T0), f32(HMC, 0, 128, 0, NIP * NTC), axis=AX), cV)
        p.wait("scalar", cT)
        p.wait("scalar", dG)
        uld = {}
        p.dma("scalar", f32(HSB3[0], 0, 128, 0, 5632), h_dram[:, 0:5632], dQ[0])
        uld[0] = (dQ[0], dQ[0].n)
        p.dma("scalar", f32(HSB3[1], 0, 128, 0, 5632), h_dram[:, 5632:11264], dQ[1])
        uld[1] = (dQ[1], dQ[1].n)
        allreduce_scalar_max(col(C_T0), ccB, S_GMAX)
        p.op("vector", lambda e: e.tensor_tensor(scal(S_SH), scal(S_SIGD), scal(S_GMAX), ALU.mult), cV)
        p.op("vector", lambda e: e.tensor_scalar(scal(S_SH), scal(S_SH), 1.0 / QMAX, None, ALU.mult), cV)
        p.op("vector", lambda e: e.reciprocal(scal(S_RSH), scal(S_SH)), cV)
        p.op("vector", lambda e: e.tensor_scalar(scal(S_SH127), scal(S_SH), 0.5 / QMAX, None, ALU.mult), cV)
        bcast_scalar_to_col(scal(S_RSH), C_RH)
        bcast_scalar_to_col(scal(S_SH127), C_SH)
        p.op("vector", lambda e: e.tensor_scalar(
            col(C_DSC, NDP), col(C_WDS, NDP), col(C_SH), None, ALU.mult), cV)
        rh_mark = cV.n
        p.wait("sync", cT)
        p.wait("sync", dS)
        for c in range(N_CORES):
            p.dma("sync", bf(WD, 0, 128, c * NIP * 128, (c + 1) * NIP * 128),
                  full_d[c * 128:(c + 1) * 128, 6144:6144 + NIP * 128], dR)
        p.wait("tensor", dR)
        units = [(tc, hf) for tc in range(NTC) for hf in range(2)]
        uq_done = {}
        dn_store = {}
        sm_mm = {}
        sm_cast = {}
        sm_sq = {}

        def emit_sm_mms(g):
            if g < 0 or g not in sm_cast:
                return
            tc_g, hb_g = g // NDP, g % NDP
            if hb_g == 0:
                bank_wait(2 + tc_g)
            p.wait("tensor", cV, sm_cast[g])
            p.wait("tensor", cS, sm_sq[g])
            st2, sp2 = (hb_g == 0), (hb_g == NDP - 1)
            cstb, sqb = CST2[g % 4], SQ2[g % 4]
            p.op("tensor", lambda e, tc_g=tc_g, cstb=cstb, st2=st2, sp2=sp2:
                 e.matmul(pb[2 + tc_g][0:1, :], onesm,
                          bf(cstb, 0, 128, 0, 512), start=st2, stop=sp2), None)
            p.op("tensor", lambda e, tc_g=tc_g, sqb=sqb, st2=st2, sp2=sp2:
                 e.matmul(pb[2 + tc_g][32:33, :], onesm,
                          bf(sqb, 0, 128, 0, 512), start=st2, stop=sp2), cT)
            sm_mm[g] = cT.n

        def emit_unit_load(ui):
            if ui >= len(units):
                return
            tc, hf = units[ui]
            g = HSB3[ui % 2]
            p.wait("scalar", cV, uq_done[ui - 2])
            p.dma("scalar", f32(g, 0, 128, 0, 5632),
                  h_dram[:, (tc * NIP + hf * 11) * 512:(tc * NIP + hf * 11) * 512 + 5632],
                  dQ[ui % 2])
            uld[ui] = (dQ[ui % 2], dQ[ui % 2].n)

        def emit_unit_quant(ui):
            tc, hf = units[ui]
            g = HSB3[ui % 2]
            hq = HQ_A if tc % 2 == 0 else HQ_B
            p.wait("scalar", uld[ui][0], uld[ui][1])
            p.wait("scalar", cV, rh_mark)
            p.op("scalar", lambda e, g=g: e.activation(
                i32(g, 0, 128, 0, 5632), f32(g, 0, 128, 0, 5632),
                AF.Identity, scale=col(C_RH)), cS, drain=False)
            q1_mark = cS.n
            p.wait("vector", cS, q1_mark)
            p.op("vector", lambda e, g=g, hq=hq, hf=hf: e.tensor_scalar(
                bf(hq, 0, 128, hf * 5632, (hf + 1) * 5632), i32(g, 0, 128, 0, 5632),
                -128.0, 127.0, ALU.max, ALU.min), cV, drain=False)
            uq_done[ui] = cV.n

        p.wait("vector", dP[0])
        p.wait("vector", dP[1])
        p.wait("scalar", dP[0])
        p.wait("scalar", dP[1])
        emit_unit_quant(0)
        emit_unit_quant(1)
        for tc in range(NTC):
            if 2 * tc + 2 < len(units):
                emit_unit_load(2 * tc + 2)
                emit_unit_load(2 * tc + 3)
                emit_unit_quant(2 * tc + 2)
                emit_unit_quant(2 * tc + 3)
            hq = HQ_A if tc % 2 == 0 else HQ_B
            for hb in range(NDP):
                bk = (tc * NDP + hb) % 2
                bank_wait(bk)
                for ip in range(NIP):
                    st, sp = (ip == 0), (ip == NIP - 1)
                    if ip == 0:
                        p.wait("tensor", cV, uq_done[2 * tc])
                    elif ip == 11:
                        p.wait("tensor", cV, uq_done[2 * tc + 1])
                    p.op("tensor", lambda e, ip=ip, hb=hb, hq=hq, bk=bk, st=st, sp=sp:
                         e.matmul(pb[bk][:, 0:512],
                                  bf(WD, 0, 128, (hb * NIP + ip) * 128,
                                     (hb * NIP + ip + 1) * 128),
                                  bf(hq, 0, 128, ip * 512, (ip + 1) * 512),
                                  start=st, stop=sp),
                         cT if sp else None)
                g_i = tc * NDP + hb
                xr = XRA2 if g_i % 2 == 0 else XRB2
                ev2 = EVT2 if g_i % 2 == 0 else EVT2B
                dxr = dL[g_i % 2]
                p.wait("sync", cS)
                p.dma("sync", f32(xr, 0, 128, 0, 512),
                      src_dram[hb * 128:(hb + 1) * 128, tc * 512:(tc + 1) * 512], dxr)
                xr_mark = dxr.n
                p.wait("scalar", cT)
                p.wait("scalar", dxr, xr_mark)
                p.wait("scalar", cV)
                if g_i - 2 in dn_store:
                    s_prev = dn_store[g_i - 2][0]
                    p.wait("scalar", s_prev, s_prev.n)
                p.op("scalar", lambda e, bk=bk, hb=hb, ev2=ev2: e.activation(
                    f32(ev2, 0, 128, 0, 512), pb[bk][:, 0:512], AF.Identity,
                    scale=col(C_DSC + hb)), cS)
                bankg[bk] = (cS, cS.n)
                p.wait("vector", cS)
                p.op("vector", lambda e, xr=xr, ev2=ev2: e.tensor_tensor(
                    f32(ev2, 0, 128, 0, 512), f32(ev2, 0, 128, 0, 512),
                    f32(xr, 0, 128, 0, 512), ALU.add), cV)
                add_mark = cV.n
                p.wait("sync", cV)
                p.dma("sync", dst_dram[hb * 128:(hb + 1) * 128, tc * 512:(tc + 1) * 512],
                      f32(ev2, 0, 128, 0, 512), dD[g_i % 2])
                dn_store[g_i] = (dD[g_i % 2], dD[g_i % 2].n)
                if feed_stats:
                    emit_sm_mms(g_i - 4)
                    cstb = CST2[g_i % 4]
                    sqb = SQ2[g_i % 4]
                    if g_i - 4 in sm_mm:
                        p.wait("vector", cT, sm_mm[g_i - 4])
                    p.op("vector", lambda e, ev2=ev2, cstb=cstb: e.tensor_copy(
                        bf(cstb, 0, 128, 0, 512), f32(ev2, 0, 128, 0, 512)),
                        cV, drain=True)
                    sm_cast[g_i] = cV.n
                    p.wait("scalar", cV, add_mark)
                    if g_i - 4 in sm_mm:
                        p.wait("scalar", cT, sm_mm[g_i - 4])
                    p.op("scalar", lambda e, ev2=ev2, sqb=sqb: e.activation(
                        bf(sqb, 0, 128, 0, 512), f32(ev2, 0, 128, 0, 512),
                        AF.Square), cS, drain=False)
                    sm_sq[g_i] = cS.n

        if feed_stats:
            for g in range(NTC * NDP - 4, NTC * NDP):
                emit_sm_mms(g)
        return sm_mm.get(NTC * NDP - 1)

    def attention(src_dram, dst_dram, stats2_gen=None, stats2_guards=None,
                  stats_premixed=None):
        p.wait("vector", cT)
        p.wait("scalar", cT)
        p.wait("sync", dS)
        for s in dT + dD:
            p.wait("sync", s)
            p.wait("scalar", s)
            p.wait("vector", s)
        p.wait("scalar", dS)
        if stats_premixed is not None:
            ln_stats_finish(stats_premixed)
        else:
            drive(ln_stats_gen(src_dram))
        for dp in range(NDP):
            buf = XA if dp % 2 == 0 else XB
            dxn = dL[dp % 2]
            p.wait("sync", cV)
            p.wait("sync", cS)
            p.dma("sync", f32(buf, 0, 128, 0, T), src_dram[dp * 128:(dp + 1) * 128, :], dxn)
            p.wait("vector", dxn, dxn.n)
            p.op("vector", lambda e, b=buf: e.tensor_tensor(
                f32(b, 0, 128, 0, T), f32(b, 0, 128, 0, T),
                f32(FB, 0, 128, 0, T), ALU.mult), cV, drain=False)
            p.op("vector", lambda e, dp=dp, b=buf: e.tensor_tensor(
                bf(XNT, 0, 128, dp * T, (dp + 1) * T), f32(b, 0, 128, 0, T),
                f32(GB, 0, 128, 0, T), ALU.add), cV)

        def proj_o(w_dram, dst_off, sc):
            p.wait("gpsimd", cT)
            for dp in range(NDP):
                p.dma("gpsimd", bf(WQB, 0, 128, dp * D, (dp + 1) * D),
                      w_dram[dp * 128:(dp + 1) * 128, :], dW)
            p.wait("tensor", dW)
            p.wait("tensor", cV)
            for ob in range(NDP):
                for tc in range(NTC):
                    bk = (ob * NTC + tc) % 2
                    bank_wait(bk)
                    for dp in range(NDP):
                        st, sp = (dp == 0), (dp == NDP - 1)
                        p.op("tensor", lambda e, dp=dp, ob=ob, tc=tc, bk=bk, st=st, sp=sp:
                             e.matmul(pb[bk][:, 0:512],
                                      bf(WQB, 0, 128, dp * D + ob * 128, dp * D + (ob + 1) * 128),
                                      bf(XNT, 0, 128, dp * T + tc * 512, dp * T + (tc + 1) * 512),
                                      start=st, stop=sp),
                             cT if sp else None)
                    p.wait("scalar", cT)
                    p.op("scalar", lambda e, ob=ob, tc=tc, bk=bk, dst=dst_off, sc=sc:
                         e.activation(bf(dst, 0, 128, ob * T + tc * 512, ob * T + (tc + 1) * 512),
                                      pb[bk][:, 0:512], AF.Identity, scale=sc), cS)
                    bankg[bk] = (cS, cS.n)

        def proj_v(w_dram, dst_off):
            p.wait("gpsimd", cT)
            for dp in range(NDP):
                p.dma("gpsimd", bf(WQB, 0, 128, dp * D, (dp + 1) * D),
                      w_dram[dp * 128:(dp + 1) * 128, :], dW)
            p.wait("tensor", dW)
            p.wait("tensor", cV)
            for tb in range(NKT):
                for oc in range(2):
                    bk = (tb * 2 + oc) % 2
                    bank_wait(bk)
                    for dp in range(NDP):
                        st, sp = (dp == 0), (dp == NDP - 1)
                        p.op("tensor", lambda e, dp=dp, tb=tb, oc=oc, bk=bk, st=st, sp=sp:
                             e.matmul(pb[bk][:, 0:512],
                                      bf(XNT, 0, 128, dp * T + tb * 128, dp * T + (tb + 1) * 128),
                                      bf(WQB, 0, 128, dp * D + oc * 512, dp * D + (oc + 1) * 512),
                                      start=st, stop=sp),
                             cT if sp else None)
                    p.wait("scalar", cT)
                    p.op("scalar", lambda e, tb=tb, oc=oc, bk=bk, dst=dst_off:
                         e.activation(bf(dst, 0, 128, tb * D + oc * 512, tb * D + (oc + 1) * 512),
                                      pb[bk][:, 0:512], AF.Identity, scale=1.0), cS)
                    bankg[bk] = (cS, cS.n)

        proj_o(wqT, QT_, 1.0 / 32.0)
        proj_o(wkT, KT_, 1.0)
        proj_v(wvT, V_)

        mm_done = {}
        exp_done = {}
        wks_of = {}

        def emit_scores_mm(kt):
            jd = kt // 4
            base = 0 if kt % 2 == 0 else 4
            for b2 in range(base, base + 4):
                bank_wait(b2)
            p.wait("tensor", cS, exp_done.get(kt - 2, 0))
            p.wait("tensor", cV, norm_done.get(kt - 2, 0))
            for j in range(jd, NTC):
                bk = base + (j - jd)
                for dp in range(NDP):
                    st, sp = (dp == 0), (dp == NDP - 1)
                    p.op("tensor", lambda e, kt=kt, j=j, dp=dp, bk=bk, st=st, sp=sp:
                         e.matmul(pb[bk][:, 0:512],
                                  bf(KT_, 0, 128, dp * T + kt * 128, dp * T + (kt + 1) * 128),
                                  bf(QT_, 0, 128, dp * T + j * 512, dp * T + (j + 1) * 512),
                                  start=st, stop=sp),
                         cT if sp else None)
            mm_done[kt] = cT.n

        norm_done = {}

        def emit_softmax(kt):
            jd, m = kt // 4, kt % 4
            nch = NTC - jd
            base = 0 if kt % 2 == 0 else 4
            wks = WKS if kt % 2 == 0 else WKS_B
            c0 = C_T0 if kt % 2 == 0 else C_T0 + 10
            cb_, cs_ = c0 + 4, c0 + 5
            cr_ = c0 + 9
            wks_of[kt] = wks
            p.wait("vector", cT, mm_done[kt])
            p.wait("vector", cS, exp_done.get(kt - 2, 0))
            p.op("vector", lambda e, m=m, bk=base, wks=wks: e.tensor_tensor(
                f32(wks, 0, 128, 0, 512), pb[bk][:, 0:512],
                f32(MASKS, 0, 128, m * 512, (m + 1) * 512), ALU.add), cV)
            p.op("vector", lambda e, wks=wks, c0=c0: e.reduce_max(
                col(c0), f32(wks, 0, 128, 0, 512), axis=AX, negate=True), cV)
            for j in range(jd + 1, NTC):
                p.op("vector", lambda e, j=j, jd=jd, bk=base + (j - jd), c0=c0: e.reduce_max(
                    col(c0 + j - jd), pb[bk][:, 0:512], axis=AX, negate=True), cV,
                    drain=False)
            if nch > 1:
                p.op("vector", lambda e, n=nch, c0=c0, cb_=cb_: e.tensor_reduce(
                    col(cb_), col(c0, n), axis=AX, op=ALU.min), cV)
            else:
                p.op("vector", lambda e, c0=c0, cb_=cb_: e.tensor_copy(col(cb_), col(c0)), cV)
            p.wait("scalar", cV)
            for j in range(jd, NTC):
                p.op("scalar", lambda e, kt=kt, j=j, jd=jd, bk=base + (j - jd), wks=wks,
                     cb_=cb_, cs_=cs_:
                     e.activation(bf(ATT, 0, 128, kt * T + j * 512, kt * T + (j + 1) * 512),
                                  (f32(wks, 0, 128, 0, 512) if j == jd else pb[bk][:, 0:512]),
                                  AF.Exp, bias=col(cb_),
                                  accum_out=col(cs_ + j - jd)), cS, drain=False)
            exp_done[kt] = cS.n
            p.wait("vector", cS)
            if nch > 1:
                p.op("vector", lambda e, n=nch, cs_=cs_, cr_=cr_: e.tensor_reduce(
                    col(cr_), col(cs_, n), axis=AX, op=ALU.add), cV)
            else:
                p.op("vector", lambda e, cs_=cs_, cr_=cr_: e.tensor_copy(col(cr_), col(cs_)), cV)
            p.op("vector", lambda e, cr_=cr_: e.reciprocal(col(cr_), col(cr_)), cV)
            p.op("vector", lambda e, kt=kt, jd=jd, cr_=cr_: e.tensor_scalar(
                bf(ATT, 0, 128, kt * T + jd * 512, (kt + 1) * T),
                bf(ATT, 0, 128, kt * T + jd * 512, (kt + 1) * T),
                col(cr_), None, ALU.mult), cV)
            norm_done[kt] = cV.n

        emit_scores_mm(0)
        for kt in range(1, NKT):
            emit_scores_mm(kt)
            emit_softmax(kt - 1)
        emit_softmax(NKT - 1)
        for i in range(8):
            bankg[i] = (cV, norm_done[NKT - 1])
        groups = [(db, j) for db in range(NDP) for j in range(NTC)]
        xl_done = {}
        ev_done = {}
        st_done = {}

        def ld_av(gi):
            if gi >= len(groups):
                return
            db, j = groups[gi]
            xr = XRA if gi % 2 == 0 else XRB
            p.wait("gpsimd", cT, mm_done[NKT - 1])
            p.wait("gpsimd", cV, ev_done.get(gi - 2, 0))
            p.dma("gpsimd", f32(xr, 0, 128, 0, 512),
                  src_dram[db * 128:(db + 1) * 128, j * 512:(j + 1) * 512], dGs[gi % 2])
            xl_done[gi] = (dGs[gi % 2], dGs[gi % 2].n)

        ld_av(0)
        ld_av(1)
        s2_steps = 0
        for gi, (db, j) in enumerate(groups):
            bk = gi % 2
            bank_wait(bk)
            nkt = min(4 * j + 4, NKT)
            for kt in range(nkt):
                st, sp = (kt == 0), (kt == nkt - 1)
                p.op("tensor", lambda e, kt=kt, db=db, j=j, bk=bk, st=st, sp=sp:
                     e.matmul(pb[bk][:, 0:512],
                              bf(V_, 0, 128, kt * D + db * 128, kt * D + (db + 1) * 128),
                              bf(ATT, 0, 128, kt * T + j * 512, kt * T + (j + 1) * 512),
                              start=st, stop=sp),
                     cT if sp else None)
            xr = XRA if gi % 2 == 0 else XRB
            ev = EVT if gi % 2 == 0 else EVT_B
            p.wait("vector", cT)
            p.wait("vector", xl_done[gi][0], xl_done[gi][1])
            if gi - 2 in st_done:
                s_prev = st_done[gi - 2][0]
                p.wait("vector", s_prev, s_prev.n)
            p.op("vector", lambda e, bk=bk, xr=xr, ev=ev: e.tensor_tensor(
                f32(ev, 0, 128, 0, 512), pb[bk][:, 0:512],
                f32(xr, 0, 128, 0, 512), ALU.add), cV)
            bankg[bk] = (cV, cV.n)
            ev_done[gi] = cV.n
            p.wait("sync", cV)
            p.dma("sync", dst_dram[db * 128:(db + 1) * 128, j * 512:(j + 1) * 512],
                  f32(ev, 0, 128, 0, 512), dT[db])
            st_done[gi] = (dT[db], dT[db].n)
            ld_av(gi + 2)
            if j == NTC - 1:
                if stats2_guards is not None:
                    stats2_guards[db] = [(dT[db], dT[db].n)]
                if stats2_gen is not None and db >= 2:
                    try:
                        next(stats2_gen)
                        s2_steps += 1
                    except StopIteration:
                        pass

    p.dma("sync", ident, identin, dS)
    p.dma("sync", identb, identbin, dS)
    p.dma("sync", onesbc, onesbcin, dS)
    p.dma("sync", onesm, onesmin, dS)
    p.dma("sync", f32(MASKS, 0, 128, 0, T), masksin, dS)
    p.dma("sync", scal(S_T0, 2), clips, dS)
    p.wait("scalar", dS)
    p.op("scalar", lambda e: e.activation(scal(S_SIGX, 2), scal(S_T0, 2), AF.Sigmoid), cS)
    p.wait("vector", dS)

    smm1 = ffn(xin, x1_dram, ccio[0], ccio[1], wq_gens=wq_sh_gen(),
               feed_stats=True)
    s2_guards = {}
    s2 = ln_stats_gen(x2_dram, src_guards=s2_guards,
                      bufs=(106496, 114688, 122880, 126976, 131072, 135168))
    attention(x1_dram, x2_dram, stats2_gen=s2, stats2_guards=s2_guards,
              stats_premixed=smm1)
    drive(s2)

    p.wait("gpsimd", cT)
    p.wait("gpsimd", cV)
    p.wait("gpsimd", cS)
    for c in range(N_CORES):
        n = 3 if c < 7 else 1
        p.dma("gpsimd", bf(W1, 0, 128, 3 * c * 1024, (3 * c + n) * 1024),
              full_d[c * 128:(c + 1) * 128, 0:n * 1024], dRg)
        p.dma("gpsimd", bf(W2, 0, 128, 3 * c * 1024, (3 * c + n) * 1024),
              full_d[c * 128:(c + 1) * 128, 3072:3072 + n * 1024], dRg)

    ffn(x2_dram, outT, ccio[2], ccio[3], stats_done=True)

    p.wait("sync", dS)
    p.wait("sync", dA)
    p.wait("sync", dG)
    for s in dQ + dL + dGs + dXs + dT + dD + dP + [dR, dRg, dW, dC]:
        p.wait("sync", s)
    p.wait("sync", cV)
    p.wait("sync", cS)
    p.wait("sync", cT)
    p.wait("sync", cc)

    with nc.Block() as block:
        for eng in Prog.ENGS:
            fns = p.items[eng]
            getattr(block, eng)(lambda e, fns=fns: [fn(e) for fn in fns])

    return nc


_NC_CACHE = {}


def kernel(L0_attn_x, L0_attn_W_q, L0_attn_W_k, L0_attn_W_v,
           L0_ffn_clip_x, L0_ffn_clip_down, L0_ffn_w_gate, L0_ffn_w_up,
           L0_ffn_w_down):
    import ml_dtypes
    x = np.asarray(L0_attn_x)
    wq = np.ascontiguousarray(np.asarray(L0_attn_W_q).T)
    wk = np.ascontiguousarray(np.asarray(L0_attn_W_k).T)
    wv = np.ascontiguousarray(np.asarray(L0_attn_W_v).T)
    wg = np.ascontiguousarray(np.asarray(L0_ffn_w_gate))
    wu = np.ascontiguousarray(np.asarray(L0_ffn_w_up))
    wd = np.ascontiguousarray(np.asarray(L0_ffn_w_down))
    wgSs, wuSs, wdSs = [], [], []
    for c in range(N_CORES):
        pts = [min(3 * c + s, II // 128 - 1) for s in range(3)]
        wgSs.append(np.concatenate([wg[t * 128:(t + 1) * 128] for t in pts], axis=0))
        wuSs.append(np.concatenate([wu[t * 128:(t + 1) * 128] for t in pts], axis=0))
        wdSs.append(np.ascontiguousarray(wd[c * 128:(c + 1) * 128]))
    clips = np.array([[np.asarray(L0_ffn_clip_x).reshape(-1)[0],
                       np.asarray(L0_ffn_clip_down).reshape(-1)[0]]], dtype=np.float32)
    ident = np.eye(128, dtype=np.float32)
    identb = np.eye(128, dtype=ml_dtypes.bfloat16)
    onesbc = np.ones((1, 128), dtype=np.float32)
    onesm = np.full((128, 1), 1.0 / 1024.0, dtype=ml_dtypes.bfloat16)
    r = np.arange(512)[None, :]
    pp = np.arange(128)[:, None]
    masks = np.concatenate(
        [np.where(r >= 128 * m + pp, 0.0, NEG).astype(np.float32) for m in range(4)],
        axis=1)

    if "nc" not in _NC_CACHE:
        _NC_CACHE["nc"] = build_nc()
    nc = _NC_CACHE["nc"]

    in_maps = []
    for b in range(N_CORES):
        in_maps.append({
            "xin": np.ascontiguousarray(x[b].T),
            "wqT": wq, "wkT": wk, "wvT": wv,
            "wgS": wgSs[b], "wuS": wuSs[b], "wdS": wdSs[b],
            "clips": clips, "identin": ident, "identbin": identb, "onesbcin": onesbc,
            "onesmin": onesm, "masksin": masks,
        })
    res = run_bass_kernel_spmd(nc, in_maps, core_ids=list(range(N_CORES)))
    out = np.stack([np.ascontiguousarray(res.results[b]["outT"].T)
                    for b in range(N_CORES)], axis=0)
    return out


# version 50
# speedup vs baseline: 1.0921x; 1.0251x over previous
import contextlib
import numpy as np
import concourse.bass as bass
import concourse.mybir as mybir
from concourse.bass_utils import run_bass_kernel_spmd

F32 = mybir.dt.float32
BF16 = mybir.dt.bfloat16
I32 = mybir.dt.int32
AF = mybir.ActivationFunctionType
ALU = mybir.AluOpType
AX = mybir.AxisListType.X

T = 2048
D = 1024
II = 2816
NDP = D // 128
NIP = II // 128
NTC = T // 512
NKT = T // 128
QMAX = 127.0
NEG = -1.0e30
N_CORES = 8

AR_BYTES = 204800
XA, XB = 0, 8192
XC, XD = 16384, 20480
FB, GB = 24576, 32768
XN = 16384
QIX = 16384
SILU, HPRE, HMC = 0, 8192, 16384
XQ = 40960
SHW = 8960
SHB = 118784
WDS_STG = 141312
WD_QB1 = 152576
WG_STG = [40960, 45056, 49152]
WG_QB = [53248, 55296]
C_GQ = 52
W1, W2 = 73728, 118784
HQ_A, HQ_B = 0, 22528
EVT2, EVT2B, XRA2, XRB2 = 47104, 53248, 49152, 51200
CST2 = [55296 + 1024 * i for i in range(4)]
SQ2 = [59392 + 1024 * i for i in range(4)]
HSB3 = [118784, 141312]
XG_A, XG_B = 0, 16384
WD = 73728
XNT = 40960
V_ = 0
WQB = 73728
QT_ = 98304
KT_ = 131072
ATT = 32768
EVT, EVT_B, XRA, XRB = 98304, 100352, 102400, 104448
IDENT = 163840
IDENTB = 199904
MASKS = 164352
ONESBC = 172544
ONESM = 173056
ROWS = 173120
COLS = 197696
CCS = 198208
SCAL = 199232
WKS = 199360
WKS_B = 201408

R_MEAN, R_RSTD, R_G = 0, 2048, 4096
C_WGS, C_WUS, C_WDS = 0, 22, 44
C_GSC, C_USC, C_DSC = 52, 74, 96
C_RS, C_RH, C_SX127, C_SH = 104, 105, 106, 107
C_T0 = 108
S_SIGX, S_SIGD, S_GMAX, S_SX, S_RSX, S_SX127, S_SH, S_RSH, S_SH127, S_T0 = range(10)


class Sem:
    __slots__ = ("h", "n")

    def __init__(self, h):
        self.h = h
        self.n = 0


class Prog:
    ENGS = ("sync", "vector", "scalar", "tensor", "gpsimd")

    def __init__(self):
        self.items = {e: [] for e in self.ENGS}
        self._lastw = {}

    def emit(self, eng, fn):
        self.items[eng].append(fn)

    def dma(self, q, out_ap, in_ap, sem, slow=False):
        sem.n += 16
        self.emit(q, lambda e, o=out_ap, i=in_ap, h=sem.h, s=slow:
                  e.dma_start(out=o, in_=i, allow_slow_non_contiguous=s).then_inc(h, 16))

    def wait(self, eng, sem, n=None):
        v = sem.n if n is None else n
        if v <= 0:
            return
        key = (eng, id(sem))
        if self._lastw.get(key, -1) >= v:
            return
        self._lastw[key] = v
        self.emit(eng, lambda e, h=sem.h, v=v: e.wait_ge(h, v))

    def op(self, eng, fn, sem=None, inc=1, drain=True):
        if drain and eng in ("vector", "scalar", "gpsimd"):
            self.emit(eng, lambda e: e.drain())
        if sem is not None:
            sem.n += inc
            self.emit(eng, lambda e, fn=fn, h=sem.h, k=inc: fn(e).then_inc(h, k))
        else:
            self.emit(eng, fn)


def build_nc():
    nc = bass.Bass("TRN2", target_bir_lowering=False)

    xin = nc.dram_tensor("xin", [D, T], F32, kind="ExternalInput").ap()
    wqT = nc.dram_tensor("wqT", [D, D], F32, kind="ExternalInput").ap()
    wkT = nc.dram_tensor("wkT", [D, D], F32, kind="ExternalInput").ap()
    wvT = nc.dram_tensor("wvT", [D, D], F32, kind="ExternalInput").ap()
    wgS = nc.dram_tensor("wgS", [384, D], F32, kind="ExternalInput").ap()
    wuS = nc.dram_tensor("wuS", [384, D], F32, kind="ExternalInput").ap()
    wdS = nc.dram_tensor("wdS", [128, II], F32, kind="ExternalInput").ap()
    clips = nc.dram_tensor("clips", [1, 2], F32, kind="ExternalInput").ap()
    identin = nc.dram_tensor("identin", [128, 128], F32, kind="ExternalInput").ap()
    identbin = nc.dram_tensor("identbin", [128, 128], BF16, kind="ExternalInput").ap()
    onesbcin = nc.dram_tensor("onesbcin", [1, 128], F32, kind="ExternalInput").ap()
    onesmin = nc.dram_tensor("onesmin", [128, 1], BF16, kind="ExternalInput").ap()
    masksin = nc.dram_tensor("masksin", [128, T], F32, kind="ExternalInput").ap()

    outT = nc.dram_tensor("outT", [D, T], F32, kind="ExternalOutput").ap()

    xn_dram = nc.dram_tensor("xn_dram", [D, T], F32).ap()
    x1_dram = nc.dram_tensor("x1_dram", [D, T], F32).ap()
    x2_dram = nc.dram_tensor("x2_dram", [D, T], F32).ap()
    h_dram = nc.dram_tensor("h_dram", [128, NIP * T], F32).ap()
    sh_d = nc.dram_tensor("sh_d", [128, SHW], BF16).ap()
    full_d = nc.dram_tensor("full_d", [128 * N_CORES, SHW], BF16,
                            addr_space="Shared").ap()
    sc_d = nc.dram_tensor("sc_d", [128, 8], F32).ap()
    scf_d = nc.dram_tensor("scf_d", [128 * N_CORES, 8], F32,
                           addr_space="Shared").ap()
    ccio = [(nc.dram_tensor(f"cc{i}_in", [1, 128], F32).ap(),
             nc.dram_tensor(f"cc{i}_out", [1, 128], F32).ap()) for i in range(4)]

    es = contextlib.ExitStack()
    arena = es.enter_context(nc.sbuf_tensor("arena", [128, AR_BYTES], mybir.dt.uint8)).ap()
    AF32 = arena.bitcast(F32)
    ABF = arena.bitcast(BF16)
    AI32 = arena.bitcast(I32)

    def f32(off, p0, p1, e0, e1):
        return AF32[p0:p1, off // 4 + e0: off // 4 + e1]

    def bf(off, p0, p1, e0, e1):
        return ABF[p0:p1, off // 2 + e0: off // 2 + e1]

    def i32(off, p0, p1, e0, e1):
        return AI32[p0:p1, off // 4 + e0: off // 4 + e1]

    ident = f32(IDENT, 0, 128, 0, 128)
    identb = bf(IDENTB, 0, 128, 0, 128)
    onesbc = f32(ONESBC, 0, 1, 0, 128)
    onesm = bf(ONESM, 0, 128, 0, 1)

    def rows(slot, a, b):
        return f32(ROWS, 0, 1, slot + a, slot + b)

    def col(idx, k=1):
        return f32(COLS, 0, 128, idx, idx + k)

    def scal(idx, k=1):
        return f32(SCAL, 0, 1, idx, idx + k)

    pb = [es.enter_context(nc.psum_tensor(f"pb{i}", [128, 512], F32)) for i in range(8)]
    pbb = [t.ap().bitcast(BF16) for t in pb]

    dS = Sem(es.enter_context(nc.semaphore("dS")))
    dA = Sem(es.enter_context(nc.semaphore("dA")))
    dG = Sem(es.enter_context(nc.semaphore("dG")))
    cV = Sem(es.enter_context(nc.semaphore("cV")))
    cS = Sem(es.enter_context(nc.semaphore("cS")))
    cT = Sem(es.enter_context(nc.semaphore("cT")))
    cc = Sem(es.enter_context(nc.semaphore("cc")))
    dQ = [Sem(es.enter_context(nc.semaphore(f"dQ{i}"))) for i in range(3)]
    dL = [Sem(es.enter_context(nc.semaphore(f"dL{i}"))) for i in range(2)]
    dGs = [Sem(es.enter_context(nc.semaphore(f"dGs{i}"))) for i in range(2)]
    dT = [Sem(es.enter_context(nc.semaphore(f"dT{i}"))) for i in range(8)]
    dR = Sem(es.enter_context(nc.semaphore("dR")))
    dC = Sem(es.enter_context(nc.semaphore("dC")))
    dRg = Sem(es.enter_context(nc.semaphore("dRg")))
    dD = [Sem(es.enter_context(nc.semaphore(f"dD{i}"))) for i in range(2)]
    dW = Sem(es.enter_context(nc.semaphore("dW")))
    dP = [Sem(es.enter_context(nc.semaphore(f"dP{i}"))) for i in range(2)]
    dXs = [Sem(es.enter_context(nc.semaphore(f"dXs{i}"))) for i in range(2)]

    p = Prog()
    bankg = [None] * 8

    def bank_wait(i):
        if bankg[i] is not None:
            p.wait("tensor", bankg[i][0], bankg[i][1])

    def bcast_scalar_to_col(s_ap, cidx):
        bank_wait(7)
        p.wait("tensor", cV)
        p.op("tensor", lambda e, s=s_ap: e.matmul(pb[7][:, 0:1], onesbc, s,
                                                  start=True, stop=True), cT)
        p.wait("vector", cT)
        p.op("vector", lambda e, c=col(cidx): e.tensor_copy(c, pb[7][:, 0:1]), cV)
        bankg[7] = (cV, cV.n)

    def bcast_rows_to_tiles(jobs, rows_mark=None):
        nj = len(jobs)
        first = True
        for j, (slot, dst) in enumerate(jobs):
            for ch in range(NTC):
                bk = 2 + ch
                bank_wait(bk)
                p.wait("tensor", cV, rows_mark)
                p.op("tensor", lambda e, slot=slot, ch=ch, bk=bk:
                     e.matmul(pb[bk][:, 0:512], onesbc,
                              rows(slot, ch * 512, (ch + 1) * 512),
                              start=True, stop=True),
                     cT if ch == NTC - 1 else None)
            p.wait("vector", cT)
            for ch in range(NTC):
                p.op("vector", lambda e, ch=ch, dst=dst: e.tensor_copy(
                    f32(dst, 0, 128, ch * 512, (ch + 1) * 512),
                    pb[2 + ch][:, 0:512]), cV, drain=first)
                first = False
            for ch in range(NTC):
                bankg[2 + ch] = (cV, cV.n)

    def ln_stats_gen(src_dram, src_guards=None,
                     bufs=(XA, XB, XC, XD, 24576, 28672)):
        for i in range(2, 6):
            bank_wait(i)
        ldd = {}
        consumed = {}

        BXA, BXB, BXC0, BXD0, BXC1, BXD1 = bufs

        def ld_stats(dp):
            if dp >= NDP:
                return
            buf = BXA if dp % 2 == 0 else BXB
            q = "sync" if dp % 2 == 0 else "scalar"
            d = dL[dp % 2]
            if dp - 2 in consumed:
                p.wait(q, cV, consumed[dp - 2][0])
                p.wait(q, cS, consumed[dp - 2][1])
            else:
                p.wait(q, cV)
                p.wait(q, cS)
            if src_guards is not None:
                g = src_guards.get(dp)
                if g is not None:
                    for gg in g:
                        p.wait(q, gg[0], gg[1])
            p.dma(q, f32(buf, 0, 128, 0, T), src_dram[dp * 128:(dp + 1) * 128, :], d)
            ldd[dp] = (d, d.n)

        ld_stats(0)
        ld_stats(1)
        smm = {}
        for dp in range(NDP):
            buf = BXA if dp % 2 == 0 else BXB
            BXC = BXC0 if dp % 2 == 0 else BXC1
            BXD = BXD0 if dp % 2 == 0 else BXD1
            p.wait("vector", ldd[dp][0], ldd[dp][1])
            p.wait("vector", cT, smm.get(dp - 2, 0))
            p.op("vector", lambda e, b=buf, BXC=BXC: e.tensor_copy(
                bf(BXC, 0, 128, 0, T), f32(b, 0, 128, 0, T)), cV, drain=False)
            castm = cV.n
            p.wait("scalar", ldd[dp][0], ldd[dp][1])
            p.wait("scalar", cT, smm.get(dp - 2, 0))
            p.op("scalar", lambda e, b=buf, BXD=BXD: e.activation(
                bf(BXD, 0, 128, 0, T), f32(b, 0, 128, 0, T), AF.Square), cS, drain=False)
            sqm = cS.n
            consumed[dp] = (castm, sqm)
            ld_stats(dp + 2)
            yield
            p.wait("tensor", cV, castm)
            p.wait("tensor", cS, sqm)
            for tc in range(NTC):
                st, sp = (dp == 0), (dp == NDP - 1)
                p.op("tensor", lambda e, tc=tc, st=st, sp=sp, BXC=BXC:
                     e.matmul(pb[2 + tc][0:1, :], onesm,
                              bf(BXC, 0, 128, tc * 512, (tc + 1) * 512),
                              start=st, stop=sp), None)
                p.op("tensor", lambda e, tc=tc, st=st, sp=sp, BXD=BXD:
                     e.matmul(pb[2 + tc][32:33, :], onesm,
                              bf(BXD, 0, 128, tc * 512, (tc + 1) * 512),
                              start=st, stop=sp),
                     cT if tc == NTC - 1 else None)
            smm[dp] = cT.n
        ln_stats_finish(smm[NDP - 1])
        yield

    def ln_stats_finish(mm_mark):
        p.wait("vector", cT, mm_mark)
        for tc in range(NTC):
            p.op("vector", lambda e, tc=tc: e.tensor_copy(
                rows(R_MEAN, tc * 512, (tc + 1) * 512), pb[2 + tc][0:1, :]), cV, drain=(tc == 0))
            p.op("vector", lambda e, tc=tc: e.tensor_copy(
                rows(R_RSTD, tc * 512, (tc + 1) * 512), pb[2 + tc][32:33, :]), cV, drain=False)
        for i in range(2, 6):
            bankg[i] = (cV, cV.n)
        p.wait("scalar", cV)
        p.op("scalar", lambda e: e.activation(rows(R_G, 0, T), rows(R_MEAN, 0, T), AF.Square), cS)
        p.wait("vector", cS)
        p.op("vector", lambda e: e.scalar_tensor_tensor(
            rows(R_RSTD, 0, T), rows(R_G, 0, T), -1.0, rows(R_RSTD, 0, T),
            ALU.mult, ALU.add), cV)
        p.op("vector", lambda e: e.tensor_scalar(
            rows(R_RSTD, 0, T), rows(R_RSTD, 0, T), 1e-5, None, ALU.add), cV)
        p.wait("scalar", cV)
        p.op("scalar", lambda e: e.activation(rows(R_RSTD, 0, T), rows(R_RSTD, 0, T),
                                              AF.Ln), cS)
        p.op("scalar", lambda e: e.activation(rows(R_RSTD, 0, T), rows(R_RSTD, 0, T),
                                              AF.Exp, scale=-0.5), cS)
        p.wait("vector", cS)
        p.op("vector", lambda e: e.scalar_tensor_tensor(
            rows(R_G, 0, T), rows(R_MEAN, 0, T), -1.0, rows(R_RSTD, 0, T),
            ALU.mult, ALU.mult), cV)
        bcast_rows_to_tiles([(R_RSTD, FB), (R_G, GB)], rows_mark=cV.n)

    def allreduce_scalar_max(col_ap, cc_pair, out_scal_idx):
        cin, cout = cc_pair
        stg = f32(CCS, 0, 1, 0, 128)
        p.wait("gpsimd", cV)
        p.dma("gpsimd", cin, col_ap, dC)
        p.wait("gpsimd", dC)
        p.op("gpsimd", lambda e: e.collective_compute(
            "AllReduce", ALU.max, replica_groups=[list(range(N_CORES))],
            ins=[cin], outs=[cout]), cc, drain=False)
        p.wait("gpsimd", cc)
        p.dma("gpsimd", stg, cout, dC)
        p.wait("vector", dC)
        p.op("vector", lambda e: e.reduce_max(scal(out_scal_idx), stg, axis=AX), cV)

    def qw_gen(w_dram, n_pt, width, c_slot, qdst_off, spill_dram, stgs, qbs,
               spill_guard=None, buf_guard=0, qb_buf_guard=0):
        nb = width // 128
        w3d = ABF[:, qdst_off // 2: qdst_off // 2 + nb * n_pt * 128].rearrange(
            "p (b i) -> p b i", b=nb)
        if spill_guard is not None:
            p.wait("scalar", spill_guard[0], spill_guard[1])
        qb_guard = {}
        q1_done = {}
        q2_done = {}
        ld = {}

        def load(pt):
            if pt >= n_pt:
                return
            q = "sync" if pt % 2 == 0 else "scalar"
            d = dQ[pt % 3]
            p.wait(q, cV, q2_done.get(pt - 3, buf_guard))
            p.dma(q, f32(stgs[pt % 3], 0, 128, 0, width),
                  w_dram[pt * 128:(pt + 1) * 128, :], d)
            ld[pt] = (d, d.n)

        def stage_a(pt):
            stg = stgs[pt % 3]
            rc = C_GQ + (pt % 2)
            p.wait("vector", ld[pt][0], ld[pt][1])
            p.op("vector", lambda e, s=stg, pt=pt: e.reduce_max(
                col(c_slot + pt), f32(s, 0, 128, 0, width), axis=AX,
                apply_absolute_value=True), cV, drain=(pt == 0))
            p.wait("vector", cS, q1_done.get(pt - 2, 0))
            p.op("vector", lambda e, pt=pt, rc=rc: e.reciprocal(
                col(rc), col(c_slot + pt)), cV)
            p.op("vector", lambda e, rc=rc: e.tensor_scalar(
                col(rc), col(rc), QMAX, None, ALU.mult), cV)
            p.wait("scalar", cV)
            p.op("scalar", lambda e, s=stg, rc=rc: e.activation(
                i32(s, 0, 128, 0, width), f32(s, 0, 128, 0, width),
                AF.Identity, scale=col(rc)), cS, drain=False)
            q1_done[pt] = cS.n

        def stage_b(pt):
            stg = stgs[pt % 3]
            qb = qbs[pt % 2]
            p.wait("vector", cS, q1_done[pt])
            p.wait("vector", cT, qb_guard.get(pt - 2, qb_buf_guard))
            p.op("vector", lambda e, s=stg, qb=qb: e.tensor_scalar(
                bf(qb, 0, 128, 0, width), i32(s, 0, 128, 0, width),
                -128.0, 127.0, ALU.max, ALU.min), cV, drain=False)
            q2_done[pt] = cV.n
            for g in range((nb + 7) // 8):
                bk = 6 + (g % 2)
                kblk = min(8, nb - g * 8)
                bank_wait(bk)
                p.wait("tensor", cV, q2_done[pt])
                for k in range(kblk):
                    wb = g * 8 + k
                    p.op("tensor", lambda e, wb=wb, k=k, bk=bk, qb=qb: e.transpose(
                        pbb[bk][:, k * 128:(k + 1) * 128],
                        bf(qb, 0, 128, wb * 128, (wb + 1) * 128), identb),
                         cT if k == kblk - 1 else None)
                p.wait("scalar", cT)
                psrc = pbb[bk][:, 0:kblk * 128].rearrange("p (b i) -> p b i", b=kblk)
                p.op("scalar", lambda e, g=g, pt=pt, bk=bk, kblk=kblk, psrc=psrc:
                     e.activation(w3d[:, g * 8:g * 8 + kblk, pt * 128:(pt + 1) * 128],
                                  psrc, AF.Identity), cS, drain=False)
                bankg[bk] = (cS, cS.n)
            qb_guard[pt] = cT.n

        load(0)
        load(1)
        for pt in range(n_pt):
            stage_a(pt)
            if pt > 0:
                stage_b(pt - 1)
            load(pt + 2)
            yield
        stage_b(n_pt - 1)
        p.wait("sync", cS)
        p.dma("sync", spill_dram[:, 0:n_pt * width], bf(qdst_off, 0, 128, 0, n_pt * width), dS)

    def qw_gen_psum(w_dram, n_pt, width, c_slot, spill_dram, stgs, qbs,
                    buf_guard=0):
        nb = width // 128
        qb_guard = {}
        q1_done = {}
        q2_done = {}
        ld = {}

        def load(pt):
            if pt >= n_pt:
                return
            q = "sync" if pt % 2 == 0 else "scalar"
            d = dQ[pt % 3]
            p.wait(q, cV, q2_done.get(pt - 3, buf_guard))
            p.dma(q, f32(stgs[pt % 3], 0, 128, 0, width),
                  w_dram[pt * 128:(pt + 1) * 128, :], d)
            ld[pt] = (d, d.n)

        def stage_a(pt):
            stg = stgs[pt % 3]
            rc = C_GQ + (pt % 2)
            p.wait("vector", ld[pt][0], ld[pt][1])
            p.op("vector", lambda e, s=stg, pt=pt: e.reduce_max(
                col(c_slot + pt), f32(s, 0, 128, 0, width), axis=AX,
                apply_absolute_value=True), cV, drain=(pt == 0))
            p.wait("vector", cS, q1_done.get(pt - 2, 0))
            p.op("vector", lambda e, pt=pt, rc=rc: e.reciprocal(
                col(rc), col(c_slot + pt)), cV)
            p.op("vector", lambda e, rc=rc: e.tensor_scalar(
                col(rc), col(rc), QMAX, None, ALU.mult), cV)
            p.wait("scalar", cV)
            p.op("scalar", lambda e, s=stg, rc=rc: e.activation(
                i32(s, 0, 128, 0, width), f32(s, 0, 128, 0, width),
                AF.Identity, scale=col(rc)), cS, drain=False)
            q1_done[pt] = cS.n

        def stage_b(pt):
            stg = stgs[pt % 3]
            qb = qbs[pt % 2]
            p.wait("vector", cS, q1_done[pt])
            p.wait("vector", cT, qb_guard.get(pt - 2, 0))
            p.op("vector", lambda e, s=stg, qb=qb: e.tensor_scalar(
                bf(qb, 0, 128, 0, width), i32(s, 0, 128, 0, width),
                -128.0, 127.0, ALU.max, ALU.min), cV, drain=False)
            q2_done[pt] = cV.n
            for g in range((nb + 7) // 8):
                bk = 6 + (g % 2)
                kblk = min(8, nb - g * 8)
                bank_wait(bk)
                p.wait("tensor", cV, q2_done[pt])
                for k in range(kblk):
                    wb = g * 8 + k
                    p.op("tensor", lambda e, wb=wb, k=k, bk=bk, qb=qb: e.transpose(
                        pbb[bk][:, k * 128:(k + 1) * 128],
                        bf(qb, 0, 128, wb * 128, (wb + 1) * 128), identb),
                         cT if k == kblk - 1 else None)
                p.wait("scalar", cT)
                slot = 59392 if g % 2 == 0 else 61440
                p.wait("scalar", dP[g % 2], dP[g % 2].n)
                p.op("scalar", lambda e, bk=bk, kblk=kblk, slot=slot: e.activation(
                    bf(slot, 0, 128, 0, kblk * 128), pbb[bk][:, 0:kblk * 128],
                    AF.Identity), cS, drain=False)
                bankg[bk] = (cS, cS.n)
                p.wait("gpsimd", cS)
                c0 = (pt * nb + g * 8) * 128
                p.dma("gpsimd", spill_dram[:, c0:c0 + kblk * 128],
                      bf(slot, 0, 128, 0, kblk * 128), dP[g % 2])
            qb_guard[pt] = cT.n

        load(0)
        load(1)
        for pt in range(n_pt):
            stage_a(pt)
            if pt > 0:
                stage_b(pt - 1)
            load(pt + 2)
            yield
        stage_b(n_pt - 1)

    wq_fin = []

    def wq_sh_gen():
        units = ([("wg", s, D) for s in range(3)] +
                 [("wu", s, D) for s in range(3)] + [("wd", 0, II)])
        tmpc = [C_WGS, C_WGS + 1, C_WGS + 2, C_WUS, C_WUS + 1, C_WUS + 2, C_WDS]
        q1_done = {}
        q2_done = {}
        qb_guard = {}
        ld = {}

        def load(u):
            if u >= len(units):
                return
            kind, s, width = units[u]
            q = "sync" if u % 2 == 0 else "scalar"
            d = dQ[u % 3]
            p.wait(q, cV, q2_done.get(u - 3, 0))
            if kind == "wd":
                p.dma(q, f32(WDS_STG, 0, 128, 0, width), wdS, d)
            else:
                srcd = wgS if kind == "wg" else wuS
                p.dma(q, f32(WG_STG[u % 3], 0, 128, 0, width),
                      srcd[s * 128:(s + 1) * 128, :], d)
            ld[u] = (d, d.n)

        def stage_a(u):
            kind, s, width = units[u]
            stg = WDS_STG if kind == "wd" else WG_STG[u % 3]
            rc = C_GQ + (u % 2)
            p.wait("vector", ld[u][0], ld[u][1])
            p.op("vector", lambda e, stg=stg, width=width, u=u: e.reduce_max(
                col(tmpc[u]), f32(stg, 0, 128, 0, width), axis=AX,
                apply_absolute_value=True), cV, drain=(u == 0))
            p.op("vector", lambda e, u=u: e.tensor_copy(
                f32(59392, 0, 128, u, u + 1), col(tmpc[u])), cV)
            p.wait("vector", cS, q1_done.get(u - 2, 0))
            p.op("vector", lambda e, u=u, rc=rc: e.reciprocal(
                col(rc), col(tmpc[u])), cV)
            p.op("vector", lambda e, rc=rc: e.tensor_scalar(
                col(rc), col(rc), QMAX, None, ALU.mult), cV)
            p.wait("scalar", cV)
            p.op("scalar", lambda e, stg=stg, width=width, rc=rc: e.activation(
                i32(stg, 0, 128, 0, width), f32(stg, 0, 128, 0, width),
                AF.Identity, scale=col(rc)), cS, drain=False)
            q1_done[u] = cS.n

        def stage_b(u):
            kind, s, width = units[u]
            stg = WDS_STG if kind == "wd" else WG_STG[u % 3]
            qb = WD_QB1 if kind == "wd" else WG_QB[u % 2]
            nb = width // 128
            if kind == "wg":
                dst0 = s * 1024
            elif kind == "wu":
                dst0 = 3072 + s * 1024
            else:
                dst0 = 6144
            p.wait("vector", cS, q1_done[u])
            p.wait("vector", cT, qb_guard.get(u - 2, 0))
            p.op("vector", lambda e, stg=stg, qb=qb, width=width: e.tensor_scalar(
                bf(qb, 0, 128, 0, width), i32(stg, 0, 128, 0, width),
                -128.0, 127.0, ALU.max, ALU.min), cV, drain=False)
            q2_done[u] = cV.n
            for g in range((nb + 7) // 8):
                bk = 6 + (g % 2)
                kblk = min(8, nb - g * 8)
                bank_wait(bk)
                p.wait("tensor", cV, q2_done[u])
                for k in range(kblk):
                    wb = g * 8 + k
                    p.op("tensor", lambda e, wb=wb, k=k, bk=bk, qb=qb: e.transpose(
                        pbb[bk][:, k * 128:(k + 1) * 128],
                        bf(qb, 0, 128, wb * 128, (wb + 1) * 128), identb),
                         cT if k == kblk - 1 else None)
                p.wait("scalar", cT)
                p.op("scalar", lambda e, bk=bk, kblk=kblk, dst0=dst0, g=g: e.activation(
                    bf(SHB, 0, 128, dst0 + g * 1024, dst0 + g * 1024 + kblk * 128),
                    pbb[bk][:, 0:kblk * 128], AF.Identity), cS, drain=False)
                bankg[bk] = (cS, cS.n)
            qb_guard[u] = cT.n

        load(0)
        load(1)
        for u in range(len(units)):
            stage_a(u)
            if u > 0:
                stage_b(u - 1)
            load(u + 2)
            yield
        stage_b(len(units) - 1)
        p.wait("sync", cS)
        p.wait("sync", cV)
        p.dma("sync", sh_d, bf(SHB, 0, 128, 0, SHW), dS)
        p.op("vector", lambda e: e.tensor_scalar(
            f32(59392, 0, 128, 7, 8), f32(59392, 0, 128, 6, 7), 0.0,
            None, ALU.mult), cV)
        p.wait("sync", cV)
        p.dma("sync", sc_d, f32(59392, 0, 128, 0, 8), dS)
        p.wait("gpsimd", dS)
        p.op("gpsimd", lambda e: e.collective_compute(
            "AllGather", ALU.bypass, replica_groups=[list(range(N_CORES))],
            ins=[sh_d], outs=[full_d]), cc, drain=False)
        p.op("gpsimd", lambda e: e.collective_compute(
            "AllGather", ALU.bypass, replica_groups=[list(range(N_CORES))],
            ins=[sc_d], outs=[scf_d]), cc, drain=False)
        gat = cc.n

        def finish():
            p.wait("gpsimd", cc, gat)
            for c in range(N_CORES):
                n = 3 if c < 7 else 1
                p.dma("gpsimd", bf(W1, 0, 128, 3 * c * 1024, (3 * c + n) * 1024),
                      full_d[c * 128:(c + 1) * 128, 0:n * 1024], dRg)
                p.dma("gpsimd", bf(W2, 0, 128, 3 * c * 1024, (3 * c + n) * 1024),
                      full_d[c * 128:(c + 1) * 128, 3072:3072 + n * 1024], dRg)
                p.dma("gpsimd", f32(59392, 0, 128, c * 8, c * 8 + 8),
                      scf_d[c * 128:(c + 1) * 128, :], dRg)
            p.wait("vector", dRg)
            for c in range(N_CORES):
                n = 3 if c < 7 else 1
                p.op("vector", lambda e, c=c, n=n: e.tensor_copy(
                    col(C_WGS + 3 * c, n), f32(59392, 0, 128, c * 8, c * 8 + n)),
                    cV, drain=(c == 0))
                p.op("vector", lambda e, c=c, n=n: e.tensor_copy(
                    col(C_WUS + 3 * c, n),
                    f32(59392, 0, 128, c * 8 + 3, c * 8 + 3 + n)), cV)
                p.op("vector", lambda e, c=c: e.tensor_copy(
                    col(C_WDS + c), f32(59392, 0, 128, c * 8 + 6, c * 8 + 7)), cV)
        wq_fin.append(finish)

    def drive(*gens):
        live = list(gens)
        while live:
            for g in list(live):
                try:
                    next(g)
                except StopIteration:
                    live.remove(g)

    def drive_with(main, aux, aux_steps=2):
        while True:
            try:
                next(main)
            except StopIteration:
                return
            for _ in range(aux_steps):
                try:
                    next(aux)
                except StopIteration:
                    break

    def ffn(src_dram, dst_dram, ccA, ccB, wq_gens=None, stats_done=False,
            feed_stats=False):
        p.wait("vector", cT)
        p.wait("scalar", cT)
        p.wait("sync", dS)
        for s in dT + dD:
            p.wait("sync", s)
            p.wait("scalar", s)
            p.wait("vector", s)
        p.wait("scalar", dS)
        if not stats_done:
            if wq_gens is not None:
                drive_with(ln_stats_gen(src_dram), wq_gens, aux_steps=1)
            else:
                drive(ln_stats_gen(src_dram))
        ldx = {}
        spg = {}
        xn_mark = [0]

        def ld_xn(dp):
            if dp >= NDP:
                return
            buf = XA if dp % 2 == 0 else XB
            q = "sync" if dp % 2 == 0 else "scalar"
            d = dL[dp % 2]
            if dp - 2 in spg:
                g = spg[dp - 2]
                p.wait(q, g[0], g[1])
            else:
                p.wait(q, cV)
                p.wait(q, cS)
            p.dma(q, f32(buf, 0, 128, 0, T), src_dram[dp * 128:(dp + 1) * 128, :], d)
            ldx[dp] = (d, d.n)

        def xn_gen():
            ld_xn(0)
            ld_xn(1)
            for dp in range(NDP):
                buf = XA if dp % 2 == 0 else XB
                p.wait("vector", ldx[dp][0], ldx[dp][1])
                p.op("vector", lambda e, b=buf: e.tensor_tensor(
                    f32(b, 0, 128, 0, T), f32(b, 0, 128, 0, T),
                    f32(FB, 0, 128, 0, T), ALU.mult), cV, drain=False)
                p.op("vector", lambda e, b=buf: e.tensor_tensor(
                    f32(b, 0, 128, 0, T), f32(b, 0, 128, 0, T),
                    f32(GB, 0, 128, 0, T), ALU.add), cV)
                p.op("vector", lambda e, dp=dp, b=buf: e.reduce_max(
                    col(C_T0 + dp), f32(b, 0, 128, 0, T), axis=AX,
                    apply_absolute_value=True), cV)
                q2 = "scalar" if dp % 2 == 0 else "sync"
                p.wait(q2, cV, cV.n)
                p.dma(q2, xn_dram[dp * 128:(dp + 1) * 128, :], f32(buf, 0, 128, 0, T),
                      dXs[dp % 2])
                spg[dp] = (dXs[dp % 2], dXs[dp % 2].n)
                ld_xn(dp + 2)
                yield
            p.op("vector", lambda e: e.reduce_max(col(C_T0 + 8), col(C_T0, NDP), axis=AX), cV)
            xn_mark[0] = cV.n
            allreduce_scalar_max(col(C_T0 + 8), ccA, S_GMAX)
            yield

        if wq_gens is not None:
            drive(xn_gen(), wq_gens)
            while wq_fin:
                wq_fin.pop()()
        else:
            drive(xn_gen())
        p.op("vector", lambda e: e.tensor_tensor(scal(S_SX), scal(S_SIGX), scal(S_GMAX), ALU.mult), cV)
        p.op("vector", lambda e: e.tensor_scalar(scal(S_SX), scal(S_SX), 1.0 / QMAX, None, ALU.mult), cV)
        p.op("vector", lambda e: e.reciprocal(scal(S_RSX), scal(S_SX)), cV)
        p.op("vector", lambda e: e.tensor_scalar(scal(S_SX127), scal(S_SX), 1.0 / QMAX, None, ALU.mult), cV)
        bcast_scalar_to_col(scal(S_RSX), C_RS)
        bcast_scalar_to_col(scal(S_SX127), C_SX127)
        p.wait("vector", cS)
        p.op("vector", lambda e: e.tensor_scalar(
            col(C_GSC, NIP), col(C_WGS, NIP), col(C_SX127), None, ALU.mult), cV)
        p.op("vector", lambda e: e.tensor_scalar(
            col(C_USC, NIP), col(C_WUS, NIP), col(C_SX127), None, ALU.mult), cV)
        p.wait("sync", dXs[0])
        p.wait("sync", dXs[1])
        p.wait("scalar", dXs[0])
        p.wait("scalar", dXs[1])
        xn3 = xn_dram.rearrange("(a p) t -> p a t", p=128)
        ldq = {}
        qdone = {}

        def ld_xq(u):
            if u >= NDP // 2:
                return
            g = XG_A if u % 2 == 0 else XG_B
            q = "sync" if u % 2 == 0 else "scalar"
            d = dL[u % 2]
            if u - 2 in qdone:
                p.wait(q, cV, qdone[u - 2])
            else:
                p.wait(q, cV, xn_mark[0])
            p.dma(q, f32(g, 0, 128, 0, 2 * T), xn3[:, 2 * u:2 * u + 2, :], d)
            ldq[u] = (d, d.n)

        ld_xq(0)
        ld_xq(1)
        p.wait("vector", dP[0])
        p.wait("vector", dP[1])
        for u in range(NDP // 2):
            g = XG_A if u % 2 == 0 else XG_B
            p.wait("scalar", ldq[u][0], ldq[u][1])
            p.wait("scalar", cV)
            p.op("scalar", lambda e, g=g: e.activation(
                i32(g, 0, 128, 0, 2 * T), f32(g, 0, 128, 0, 2 * T),
                AF.Identity, scale=col(C_RS)), cS, drain=False)
            p.wait("vector", cS)
            p.op("vector", lambda e, g=g, u=u: e.tensor_scalar(
                bf(XQ, 0, 128, 2 * u * T, (2 * u + 2) * T), i32(g, 0, 128, 0, 2 * T),
                -128.0, 127.0, ALU.max, ALU.min), cV, drain=False)
            qdone[u] = cV.n
            ld_xq(u + 2)
        p.wait("tensor", cV)
        p.wait("tensor", dS)
        p.wait("tensor", dR)
        p.wait("tensor", dRg)
        for ib in range(NIP):
            for i in range(4):
                bank_wait(i)
            for dp in range(NDP):
                for tc in range(NTC):
                    st, sp = (dp == 0), (dp == NDP - 1)
                    p.op("tensor", lambda e, ib=ib, dp=dp, tc=tc, st=st, sp=sp:
                         e.matmul(pb[tc][:, 0:512],
                                  bf(W1, 0, 128, (ib * NDP + dp) * 128,
                                     (ib * NDP + dp + 1) * 128),
                                  bf(XQ, 0, 128, dp * T + tc * 512, dp * T + (tc + 1) * 512),
                                  start=st, stop=sp),
                         cT if (sp and tc == NTC - 1) else None)
            gate_done = cT.n
            for i in range(4, 8):
                bank_wait(i)
            for dp in range(NDP):
                for tc in range(NTC):
                    st, sp = (dp == 0), (dp == NDP - 1)
                    p.op("tensor", lambda e, ib=ib, dp=dp, tc=tc, st=st, sp=sp:
                         e.matmul(pb[4 + tc][:, 0:512],
                                  bf(W2, 0, 128, (ib * NDP + dp) * 128,
                                     (ib * NDP + dp + 1) * 128),
                                  bf(XQ, 0, 128, dp * T + tc * 512, dp * T + (tc + 1) * 512),
                                  start=st, stop=sp),
                         cT if (sp and tc == NTC - 1) else None)
            up_done = cT.n
            p.wait("scalar", cT, gate_done)
            p.wait("scalar", cV)
            for tc in range(NTC):
                p.op("scalar", lambda e, ib=ib, tc=tc: e.activation(
                    f32(SILU, 0, 128, tc * 512, (tc + 1) * 512), pb[tc][:, 0:512],
                    AF.Silu, scale=col(C_GSC + ib)), cS)
            for i in range(4):
                bankg[i] = (cS, cS.n)
            p.wait("vector", cT, up_done)
            p.wait("vector", cS)
            p.wait("vector", dG)
            for tc in range(NTC):
                p.op("vector", lambda e, ib=ib, tc=tc: e.scalar_tensor_tensor(
                    f32(HPRE, 0, 128, tc * 512, (tc + 1) * 512), pb[4 + tc][:, 0:512],
                    col(C_USC + ib), f32(SILU, 0, 128, tc * 512, (tc + 1) * 512),
                    ALU.mult, ALU.mult), cV)
                p.op("vector", lambda e, ib=ib, tc=tc: e.reduce_max(
                    f32(HMC, 0, 128, ib * NTC + tc, ib * NTC + tc + 1),
                    f32(HPRE, 0, 128, tc * 512, (tc + 1) * 512),
                    axis=AX, apply_absolute_value=True), cV)
                p.wait("gpsimd", cV)
                p.dma("gpsimd", h_dram[:, (tc * NIP + ib) * 512:(tc * NIP + ib + 1) * 512],
                      f32(HPRE, 0, 128, tc * 512, (tc + 1) * 512), dG)
            for i in range(4, 8):
                bankg[i] = (cV, cV.n)
        p.op("vector", lambda e: e.reduce_max(col(C_T0), f32(HMC, 0, 128, 0, NIP * NTC), axis=AX), cV)
        p.wait("scalar", cT)
        p.wait("scalar", dG)
        uld = {}
        p.dma("scalar", f32(HSB3[0], 0, 128, 0, 5632), h_dram[:, 0:5632], dQ[0])
        uld[0] = (dQ[0], dQ[0].n)
        p.dma("scalar", f32(HSB3[1], 0, 128, 0, 5632), h_dram[:, 5632:11264], dQ[1])
        uld[1] = (dQ[1], dQ[1].n)
        allreduce_scalar_max(col(C_T0), ccB, S_GMAX)
        p.op("vector", lambda e: e.tensor_tensor(scal(S_SH), scal(S_SIGD), scal(S_GMAX), ALU.mult), cV)
        p.op("vector", lambda e: e.tensor_scalar(scal(S_SH), scal(S_SH), 1.0 / QMAX, None, ALU.mult), cV)
        p.op("vector", lambda e: e.reciprocal(scal(S_RSH), scal(S_SH)), cV)
        p.op("vector", lambda e: e.tensor_scalar(scal(S_SH127), scal(S_SH), 0.5 / QMAX, None, ALU.mult), cV)
        bcast_scalar_to_col(scal(S_RSH), C_RH)
        bcast_scalar_to_col(scal(S_SH127), C_SH)
        p.op("vector", lambda e: e.tensor_scalar(
            col(C_DSC, NDP), col(C_WDS, NDP), col(C_SH), None, ALU.mult), cV)
        rh_mark = cV.n
        p.wait("sync", cT)
        p.wait("sync", dS)
        for c in range(N_CORES):
            p.dma("sync", bf(WD, 0, 128, c * NIP * 128, (c + 1) * NIP * 128),
                  full_d[c * 128:(c + 1) * 128, 6144:6144 + NIP * 128], dR)
        p.wait("tensor", dR)
        units = [(tc, hf) for tc in range(NTC) for hf in range(2)]
        uq_done = {}
        dn_store = {}
        sm_mm = {}
        sm_cast = {}
        sm_sq = {}

        def emit_sm_mms(g):
            if g < 0 or g not in sm_cast:
                return
            tc_g, hb_g = g // NDP, g % NDP
            if hb_g == 0:
                bank_wait(2 + tc_g)
            p.wait("tensor", cV, sm_cast[g])
            p.wait("tensor", cS, sm_sq[g])
            st2, sp2 = (hb_g == 0), (hb_g == NDP - 1)
            cstb, sqb = CST2[g % 4], SQ2[g % 4]
            p.op("tensor", lambda e, tc_g=tc_g, cstb=cstb, st2=st2, sp2=sp2:
                 e.matmul(pb[2 + tc_g][0:1, :], onesm,
                          bf(cstb, 0, 128, 0, 512), start=st2, stop=sp2), None)
            p.op("tensor", lambda e, tc_g=tc_g, sqb=sqb, st2=st2, sp2=sp2:
                 e.matmul(pb[2 + tc_g][32:33, :], onesm,
                          bf(sqb, 0, 128, 0, 512), start=st2, stop=sp2), cT)
            sm_mm[g] = cT.n

        def emit_unit_load(ui):
            if ui >= len(units):
                return
            tc, hf = units[ui]
            g = HSB3[ui % 2]
            p.wait("scalar", cV, uq_done[ui - 2])
            p.dma("scalar", f32(g, 0, 128, 0, 5632),
                  h_dram[:, (tc * NIP + hf * 11) * 512:(tc * NIP + hf * 11) * 512 + 5632],
                  dQ[ui % 2])
            uld[ui] = (dQ[ui % 2], dQ[ui % 2].n)

        def emit_unit_quant(ui):
            tc, hf = units[ui]
            g = HSB3[ui % 2]
            hq = HQ_A if tc % 2 == 0 else HQ_B
            p.wait("scalar", uld[ui][0], uld[ui][1])
            p.wait("scalar", cV, rh_mark)
            p.op("scalar", lambda e, g=g: e.activation(
                i32(g, 0, 128, 0, 5632), f32(g, 0, 128, 0, 5632),
                AF.Identity, scale=col(C_RH)), cS, drain=False)
            q1_mark = cS.n
            p.wait("vector", cS, q1_mark)
            p.op("vector", lambda e, g=g, hq=hq, hf=hf: e.tensor_scalar(
                bf(hq, 0, 128, hf * 5632, (hf + 1) * 5632), i32(g, 0, 128, 0, 5632),
                -128.0, 127.0, ALU.max, ALU.min), cV, drain=False)
            uq_done[ui] = cV.n

        p.wait("vector", dP[0])
        p.wait("vector", dP[1])
        p.wait("scalar", dP[0])
        p.wait("scalar", dP[1])
        emit_unit_quant(0)
        emit_unit_quant(1)
        for tc in range(NTC):
            if 2 * tc + 2 < len(units):
                emit_unit_load(2 * tc + 2)
                emit_unit_load(2 * tc + 3)
                emit_unit_quant(2 * tc + 2)
                emit_unit_quant(2 * tc + 3)
            hq = HQ_A if tc % 2 == 0 else HQ_B
            for hb in range(NDP):
                bk = (tc * NDP + hb) % 2
                bank_wait(bk)
                for ip in range(NIP):
                    st, sp = (ip == 0), (ip == NIP - 1)
                    if ip == 0:
                        p.wait("tensor", cV, uq_done[2 * tc])
                    elif ip == 11:
                        p.wait("tensor", cV, uq_done[2 * tc + 1])
                    p.op("tensor", lambda e, ip=ip, hb=hb, hq=hq, bk=bk, st=st, sp=sp:
                         e.matmul(pb[bk][:, 0:512],
                                  bf(WD, 0, 128, (hb * NIP + ip) * 128,
                                     (hb * NIP + ip + 1) * 128),
                                  bf(hq, 0, 128, ip * 512, (ip + 1) * 512),
                                  start=st, stop=sp),
                         cT if sp else None)
                g_i = tc * NDP + hb
                xr = XRA2 if g_i % 2 == 0 else XRB2
                ev2 = EVT2 if g_i % 2 == 0 else EVT2B
                dxr = dL[g_i % 2]
                p.wait("sync", cS)
                p.dma("sync", f32(xr, 0, 128, 0, 512),
                      src_dram[hb * 128:(hb + 1) * 128, tc * 512:(tc + 1) * 512], dxr)
                xr_mark = dxr.n
                p.wait("scalar", cT)
                p.wait("scalar", dxr, xr_mark)
                p.wait("scalar", cV)
                if g_i - 2 in dn_store:
                    s_prev = dn_store[g_i - 2][0]
                    p.wait("scalar", s_prev, s_prev.n)
                p.op("scalar", lambda e, bk=bk, hb=hb, ev2=ev2: e.activation(
                    f32(ev2, 0, 128, 0, 512), pb[bk][:, 0:512], AF.Identity,
                    scale=col(C_DSC + hb)), cS)
                bankg[bk] = (cS, cS.n)
                p.wait("vector", cS)
                p.op("vector", lambda e, xr=xr, ev2=ev2: e.tensor_tensor(
                    f32(ev2, 0, 128, 0, 512), f32(ev2, 0, 128, 0, 512),
                    f32(xr, 0, 128, 0, 512), ALU.add), cV)
                add_mark = cV.n
                p.wait("sync", cV)
                p.dma("sync", dst_dram[hb * 128:(hb + 1) * 128, tc * 512:(tc + 1) * 512],
                      f32(ev2, 0, 128, 0, 512), dD[g_i % 2])
                dn_store[g_i] = (dD[g_i % 2], dD[g_i % 2].n)
                if feed_stats:
                    emit_sm_mms(g_i - 4)
                    cstb = CST2[g_i % 4]
                    sqb = SQ2[g_i % 4]
                    if g_i - 4 in sm_mm:
                        p.wait("vector", cT, sm_mm[g_i - 4])
                    p.op("vector", lambda e, ev2=ev2, cstb=cstb: e.tensor_copy(
                        bf(cstb, 0, 128, 0, 512), f32(ev2, 0, 128, 0, 512)),
                        cV, drain=True)
                    sm_cast[g_i] = cV.n
                    p.wait("scalar", cV, add_mark)
                    if g_i - 4 in sm_mm:
                        p.wait("scalar", cT, sm_mm[g_i - 4])
                    p.op("scalar", lambda e, ev2=ev2, sqb=sqb: e.activation(
                        bf(sqb, 0, 128, 0, 512), f32(ev2, 0, 128, 0, 512),
                        AF.Square), cS, drain=False)
                    sm_sq[g_i] = cS.n

        if feed_stats:
            for g in range(NTC * NDP - 4, NTC * NDP):
                emit_sm_mms(g)
        return sm_mm.get(NTC * NDP - 1)

    def attention(src_dram, dst_dram, stats2_gen=None, stats2_guards=None,
                  stats_premixed=None):
        p.wait("vector", cT)
        p.wait("scalar", cT)
        p.wait("sync", dS)
        for s in dT + dD:
            p.wait("sync", s)
            p.wait("scalar", s)
            p.wait("vector", s)
        p.wait("scalar", dS)
        if stats_premixed is not None:
            ln_stats_finish(stats_premixed)
        else:
            drive(ln_stats_gen(src_dram))
        for dp in range(NDP):
            buf = XA if dp % 2 == 0 else XB
            dxn = dL[dp % 2]
            p.wait("sync", cV)
            p.wait("sync", cS)
            p.dma("sync", f32(buf, 0, 128, 0, T), src_dram[dp * 128:(dp + 1) * 128, :], dxn)
            p.wait("vector", dxn, dxn.n)
            p.op("vector", lambda e, b=buf: e.tensor_tensor(
                f32(b, 0, 128, 0, T), f32(b, 0, 128, 0, T),
                f32(FB, 0, 128, 0, T), ALU.mult), cV, drain=False)
            p.op("vector", lambda e, dp=dp, b=buf: e.tensor_tensor(
                bf(XNT, 0, 128, dp * T, (dp + 1) * T), f32(b, 0, 128, 0, T),
                f32(GB, 0, 128, 0, T), ALU.add), cV)

        def proj_o(w_dram, dst_off, sc):
            p.wait("gpsimd", cT)
            for dp in range(NDP):
                p.dma("gpsimd", bf(WQB, 0, 128, dp * D, (dp + 1) * D),
                      w_dram[dp * 128:(dp + 1) * 128, :], dW)
            p.wait("tensor", dW)
            p.wait("tensor", cV)
            for ob in range(NDP):
                for tc in range(NTC):
                    bk = (ob * NTC + tc) % 2
                    bank_wait(bk)
                    for dp in range(NDP):
                        st, sp = (dp == 0), (dp == NDP - 1)
                        p.op("tensor", lambda e, dp=dp, ob=ob, tc=tc, bk=bk, st=st, sp=sp:
                             e.matmul(pb[bk][:, 0:512],
                                      bf(WQB, 0, 128, dp * D + ob * 128, dp * D + (ob + 1) * 128),
                                      bf(XNT, 0, 128, dp * T + tc * 512, dp * T + (tc + 1) * 512),
                                      start=st, stop=sp),
                             cT if sp else None)
                    p.wait("scalar", cT)
                    p.op("scalar", lambda e, ob=ob, tc=tc, bk=bk, dst=dst_off, sc=sc:
                         e.activation(bf(dst, 0, 128, ob * T + tc * 512, ob * T + (tc + 1) * 512),
                                      pb[bk][:, 0:512], AF.Identity, scale=sc), cS)
                    bankg[bk] = (cS, cS.n)

        def proj_v(w_dram, dst_off):
            p.wait("gpsimd", cT)
            for dp in range(NDP):
                p.dma("gpsimd", bf(WQB, 0, 128, dp * D, (dp + 1) * D),
                      w_dram[dp * 128:(dp + 1) * 128, :], dW)
            p.wait("tensor", dW)
            p.wait("tensor", cV)
            for tb in range(NKT):
                for oc in range(2):
                    bk = (tb * 2 + oc) % 2
                    bank_wait(bk)
                    for dp in range(NDP):
                        st, sp = (dp == 0), (dp == NDP - 1)
                        p.op("tensor", lambda e, dp=dp, tb=tb, oc=oc, bk=bk, st=st, sp=sp:
                             e.matmul(pb[bk][:, 0:512],
                                      bf(XNT, 0, 128, dp * T + tb * 128, dp * T + (tb + 1) * 128),
                                      bf(WQB, 0, 128, dp * D + oc * 512, dp * D + (oc + 1) * 512),
                                      start=st, stop=sp),
                             cT if sp else None)
                    p.wait("scalar", cT)
                    p.op("scalar", lambda e, tb=tb, oc=oc, bk=bk, dst=dst_off:
                         e.activation(bf(dst, 0, 128, tb * D + oc * 512, tb * D + (oc + 1) * 512),
                                      pb[bk][:, 0:512], AF.Identity, scale=1.0), cS)
                    bankg[bk] = (cS, cS.n)

        proj_o(wqT, QT_, 1.0 / 32.0)
        proj_o(wkT, KT_, 1.0)
        proj_v(wvT, V_)

        mm_done = {}
        exp_done = {}
        wks_of = {}

        def emit_scores_mm(kt):
            jd = kt // 4
            base = 0 if kt % 2 == 0 else 4
            for b2 in range(base, base + 4):
                bank_wait(b2)
            p.wait("tensor", cS, exp_done.get(kt - 2, 0))
            p.wait("tensor", cV, norm_done.get(kt - 2, 0))
            for j in range(jd, NTC):
                bk = base + (j - jd)
                for dp in range(NDP):
                    st, sp = (dp == 0), (dp == NDP - 1)
                    p.op("tensor", lambda e, kt=kt, j=j, dp=dp, bk=bk, st=st, sp=sp:
                         e.matmul(pb[bk][:, 0:512],
                                  bf(KT_, 0, 128, dp * T + kt * 128, dp * T + (kt + 1) * 128),
                                  bf(QT_, 0, 128, dp * T + j * 512, dp * T + (j + 1) * 512),
                                  start=st, stop=sp),
                         cT if sp else None)
            mm_done[kt] = cT.n

        norm_done = {}

        def emit_softmax(kt):
            jd, m = kt // 4, kt % 4
            nch = NTC - jd
            base = 0 if kt % 2 == 0 else 4
            wks = WKS if kt % 2 == 0 else WKS_B
            c0 = C_T0 if kt % 2 == 0 else C_T0 + 10
            cb_, cs_ = c0 + 4, c0 + 5
            cr_ = c0 + 9
            wks_of[kt] = wks
            p.wait("vector", cT, mm_done[kt])
            p.wait("vector", cS, exp_done.get(kt - 2, 0))
            p.op("vector", lambda e, m=m, bk=base, wks=wks: e.tensor_tensor(
                f32(wks, 0, 128, 0, 512), pb[bk][:, 0:512],
                f32(MASKS, 0, 128, m * 512, (m + 1) * 512), ALU.add), cV)
            p.op("vector", lambda e, wks=wks, c0=c0: e.reduce_max(
                col(c0), f32(wks, 0, 128, 0, 512), axis=AX, negate=True), cV)
            for j in range(jd + 1, NTC):
                p.op("vector", lambda e, j=j, jd=jd, bk=base + (j - jd), c0=c0: e.reduce_max(
                    col(c0 + j - jd), pb[bk][:, 0:512], axis=AX, negate=True), cV,
                    drain=False)
            if nch > 1:
                p.op("vector", lambda e, n=nch, c0=c0, cb_=cb_: e.tensor_reduce(
                    col(cb_), col(c0, n), axis=AX, op=ALU.min), cV)
            else:
                p.op("vector", lambda e, c0=c0, cb_=cb_: e.tensor_copy(col(cb_), col(c0)), cV)
            p.wait("scalar", cV)
            for j in range(jd, NTC):
                p.op("scalar", lambda e, kt=kt, j=j, jd=jd, bk=base + (j - jd), wks=wks,
                     cb_=cb_, cs_=cs_:
                     e.activation(bf(ATT, 0, 128, kt * T + j * 512, kt * T + (j + 1) * 512),
                                  (f32(wks, 0, 128, 0, 512) if j == jd else pb[bk][:, 0:512]),
                                  AF.Exp, bias=col(cb_),
                                  accum_out=col(cs_ + j - jd)), cS, drain=False)
            exp_done[kt] = cS.n
            p.wait("vector", cS)
            if nch > 1:
                p.op("vector", lambda e, n=nch, cs_=cs_, cr_=cr_: e.tensor_reduce(
                    col(cr_), col(cs_, n), axis=AX, op=ALU.add), cV)
            else:
                p.op("vector", lambda e, cs_=cs_, cr_=cr_: e.tensor_copy(col(cr_), col(cs_)), cV)
            p.op("vector", lambda e, cr_=cr_: e.reciprocal(col(cr_), col(cr_)), cV)
            p.op("vector", lambda e, kt=kt, jd=jd, cr_=cr_: e.tensor_scalar(
                bf(ATT, 0, 128, kt * T + jd * 512, (kt + 1) * T),
                bf(ATT, 0, 128, kt * T + jd * 512, (kt + 1) * T),
                col(cr_), None, ALU.mult), cV)
            norm_done[kt] = cV.n

        emit_scores_mm(0)
        for kt in range(1, NKT):
            emit_scores_mm(kt)
            emit_softmax(kt - 1)
        emit_softmax(NKT - 1)
        for i in range(8):
            bankg[i] = (cV, norm_done[NKT - 1])
        groups = [(db, j) for db in range(NDP) for j in range(NTC)]
        xl_done = {}
        ev_done = {}
        st_done = {}

        def ld_av(gi):
            if gi >= len(groups):
                return
            db, j = groups[gi]
            xr = XRA if gi % 2 == 0 else XRB
            p.wait("gpsimd", cT, mm_done[NKT - 1])
            p.wait("gpsimd", cV, ev_done.get(gi - 2, 0))
            p.dma("gpsimd", f32(xr, 0, 128, 0, 512),
                  src_dram[db * 128:(db + 1) * 128, j * 512:(j + 1) * 512], dGs[gi % 2])
            xl_done[gi] = (dGs[gi % 2], dGs[gi % 2].n)

        ld_av(0)
        ld_av(1)
        s2_steps = 0
        for gi, (db, j) in enumerate(groups):
            bk = gi % 2
            bank_wait(bk)
            nkt = min(4 * j + 4, NKT)
            for kt in range(nkt):
                st, sp = (kt == 0), (kt == nkt - 1)
                p.op("tensor", lambda e, kt=kt, db=db, j=j, bk=bk, st=st, sp=sp:
                     e.matmul(pb[bk][:, 0:512],
                              bf(V_, 0, 128, kt * D + db * 128, kt * D + (db + 1) * 128),
                              bf(ATT, 0, 128, kt * T + j * 512, kt * T + (j + 1) * 512),
                              start=st, stop=sp),
                     cT if sp else None)
            xr = XRA if gi % 2 == 0 else XRB
            ev = EVT if gi % 2 == 0 else EVT_B
            p.wait("vector", cT)
            p.wait("vector", xl_done[gi][0], xl_done[gi][1])
            if gi - 2 in st_done:
                s_prev = st_done[gi - 2][0]
                p.wait("vector", s_prev, s_prev.n)
            p.op("vector", lambda e, bk=bk, xr=xr, ev=ev: e.tensor_tensor(
                f32(ev, 0, 128, 0, 512), pb[bk][:, 0:512],
                f32(xr, 0, 128, 0, 512), ALU.add), cV)
            bankg[bk] = (cV, cV.n)
            ev_done[gi] = cV.n
            p.wait("sync", cV)
            p.dma("sync", dst_dram[db * 128:(db + 1) * 128, j * 512:(j + 1) * 512],
                  f32(ev, 0, 128, 0, 512), dT[db])
            st_done[gi] = (dT[db], dT[db].n)
            ld_av(gi + 2)
            if j == NTC - 1:
                if stats2_guards is not None:
                    stats2_guards[db] = [(dT[db], dT[db].n)]
                if stats2_gen is not None and db >= 2:
                    try:
                        next(stats2_gen)
                        s2_steps += 1
                    except StopIteration:
                        pass

    p.dma("sync", ident, identin, dS)
    p.dma("sync", identb, identbin, dS)
    p.dma("sync", onesbc, onesbcin, dS)
    p.dma("sync", onesm, onesmin, dS)
    p.dma("sync", f32(MASKS, 0, 128, 0, T), masksin, dS)
    p.dma("sync", scal(S_T0, 2), clips, dS)
    p.wait("scalar", dS)
    p.op("scalar", lambda e: e.activation(scal(S_SIGX, 2), scal(S_T0, 2), AF.Sigmoid), cS)
    p.wait("vector", dS)

    smm1 = ffn(xin, x1_dram, ccio[0], ccio[1], wq_gens=wq_sh_gen(),
               feed_stats=True)
    s2_guards = {}
    s2 = ln_stats_gen(x2_dram, src_guards=s2_guards,
                      bufs=(106496, 114688, 122880, 126976, 131072, 135168))
    attention(x1_dram, x2_dram, stats2_gen=s2, stats2_guards=s2_guards,
              stats_premixed=smm1)
    drive(s2)

    p.wait("gpsimd", cT)
    p.wait("gpsimd", cV)
    p.wait("gpsimd", cS)
    for c in range(N_CORES):
        n = 3 if c < 7 else 1
        p.dma("gpsimd", bf(W1, 0, 128, 3 * c * 1024, (3 * c + n) * 1024),
              full_d[c * 128:(c + 1) * 128, 0:n * 1024], dRg)
        p.dma("gpsimd", bf(W2, 0, 128, 3 * c * 1024, (3 * c + n) * 1024),
              full_d[c * 128:(c + 1) * 128, 3072:3072 + n * 1024], dRg)

    ffn(x2_dram, outT, ccio[2], ccio[3], stats_done=True)

    p.wait("sync", dS)
    p.wait("sync", dA)
    p.wait("sync", dG)
    for s in dQ + dL + dGs + dXs + dT + dD + dP + [dR, dRg, dW, dC]:
        p.wait("sync", s)
    p.wait("sync", cV)
    p.wait("sync", cS)
    p.wait("sync", cT)
    p.wait("sync", cc)

    with nc.Block() as block:
        for eng in Prog.ENGS:
            fns = p.items[eng]
            getattr(block, eng)(lambda e, fns=fns: [fn(e) for fn in fns])

    return nc


_NC_CACHE = {}


def kernel(L0_attn_x, L0_attn_W_q, L0_attn_W_k, L0_attn_W_v,
           L0_ffn_clip_x, L0_ffn_clip_down, L0_ffn_w_gate, L0_ffn_w_up,
           L0_ffn_w_down):
    import ml_dtypes
    x = np.asarray(L0_attn_x)
    wq = np.ascontiguousarray(np.asarray(L0_attn_W_q).T)
    wk = np.ascontiguousarray(np.asarray(L0_attn_W_k).T)
    wv = np.ascontiguousarray(np.asarray(L0_attn_W_v).T)
    wg = np.ascontiguousarray(np.asarray(L0_ffn_w_gate))
    wu = np.ascontiguousarray(np.asarray(L0_ffn_w_up))
    wd = np.ascontiguousarray(np.asarray(L0_ffn_w_down))
    wgSs, wuSs, wdSs = [], [], []
    for c in range(N_CORES):
        pts = [min(3 * c + s, II // 128 - 1) for s in range(3)]
        wgSs.append(np.concatenate([wg[t * 128:(t + 1) * 128] for t in pts], axis=0))
        wuSs.append(np.concatenate([wu[t * 128:(t + 1) * 128] for t in pts], axis=0))
        wdSs.append(np.ascontiguousarray(wd[c * 128:(c + 1) * 128]))
    clips = np.array([[np.asarray(L0_ffn_clip_x).reshape(-1)[0],
                       np.asarray(L0_ffn_clip_down).reshape(-1)[0]]], dtype=np.float32)
    ident = np.eye(128, dtype=np.float32)
    identb = np.eye(128, dtype=ml_dtypes.bfloat16)
    onesbc = np.ones((1, 128), dtype=np.float32)
    onesm = np.full((128, 1), 1.0 / 1024.0, dtype=ml_dtypes.bfloat16)
    r = np.arange(512)[None, :]
    pp = np.arange(128)[:, None]
    masks = np.concatenate(
        [np.where(r >= 128 * m + pp, 0.0, NEG).astype(np.float32) for m in range(4)],
        axis=1)

    if "nc" not in _NC_CACHE:
        _NC_CACHE["nc"] = build_nc()
    nc = _NC_CACHE["nc"]

    in_maps = []
    for b in range(N_CORES):
        in_maps.append({
            "xin": np.ascontiguousarray(x[b].T),
            "wqT": wq, "wkT": wk, "wvT": wv,
            "wgS": wgSs[b], "wuS": wuSs[b], "wdS": wdSs[b],
            "clips": clips, "identin": ident, "identbin": identb, "onesbcin": onesbc,
            "onesmin": onesm, "masksin": masks,
        })
    res = run_bass_kernel_spmd(nc, in_maps, core_ids=list(range(N_CORES)))
    out = np.stack([np.ascontiguousarray(res.results[b]["outT"].T)
                    for b in range(N_CORES)], axis=0)
    return out


# version 51
# speedup vs baseline: 1.0995x; 1.0068x over previous
import contextlib
import numpy as np
import concourse.bass as bass
import concourse.mybir as mybir
from concourse.bass_utils import run_bass_kernel_spmd

F32 = mybir.dt.float32
BF16 = mybir.dt.bfloat16
I32 = mybir.dt.int32
AF = mybir.ActivationFunctionType
ALU = mybir.AluOpType
AX = mybir.AxisListType.X

T = 2048
D = 1024
II = 2816
NDP = D // 128
NIP = II // 128
NTC = T // 512
NKT = T // 128
QMAX = 127.0
NEG = -1.0e30
N_CORES = 8

AR_BYTES = 204800
XA, XB = 0, 8192
XC, XD = 16384, 20480
FB, GB = 24576, 32768
XN = 16384
QIX = 16384
SILU, HPRE, HMC = 0, 8192, 16384
XQ = 40960
SHW = 8960
SHB = 118784
WDS_STG = 141312
WD_QB1 = 152576
WG_STG = [40960, 45056, 49152]
WG_QB = [53248, 55296]
C_GQ = 52
W1, W2 = 73728, 118784
HQ_A, HQ_B = 0, 22528
EVT2, EVT2B, XRA2, XRB2 = 47104, 53248, 49152, 51200
CST2 = [55296 + 1024 * i for i in range(4)]
SQ2 = [59392 + 1024 * i for i in range(4)]
HSB3 = [118784, 141312]
XG_A, XG_B = 0, 16384
WD = 73728
XNT = 40960
V_ = 0
WQB = 73728
QT_ = 98304
KT_ = 131072
ATT = 32768
EVT, EVT_B, XRA, XRB = 98304, 100352, 102400, 104448
IDENT = 163840
IDENTB = 199904
MASKS = 164352
ONESBC = 172544
ONESM = 173056
ROWS = 173120
COLS = 197696
CCS = 198208
SCAL = 199232
WKS = 199360
WKS_B = 201408

R_MEAN, R_RSTD, R_G = 0, 2048, 4096
C_WGS, C_WUS, C_WDS = 0, 22, 44
C_GSC, C_USC, C_DSC = 52, 74, 96
C_RS, C_RH, C_SX127, C_SH = 104, 105, 106, 107
C_T0 = 108
S_SIGX, S_SIGD, S_GMAX, S_SX, S_RSX, S_SX127, S_SH, S_RSH, S_SH127, S_T0 = range(10)


class Sem:
    __slots__ = ("h", "n")

    def __init__(self, h):
        self.h = h
        self.n = 0


class Prog:
    ENGS = ("sync", "vector", "scalar", "tensor", "gpsimd")

    def __init__(self):
        self.items = {e: [] for e in self.ENGS}
        self._lastw = {}

    def emit(self, eng, fn):
        self.items[eng].append(fn)

    def dma(self, q, out_ap, in_ap, sem, slow=False):
        sem.n += 16
        self.emit(q, lambda e, o=out_ap, i=in_ap, h=sem.h, s=slow:
                  e.dma_start(out=o, in_=i, allow_slow_non_contiguous=s).then_inc(h, 16))

    def wait(self, eng, sem, n=None):
        v = sem.n if n is None else n
        if v <= 0:
            return
        key = (eng, id(sem))
        if self._lastw.get(key, -1) >= v:
            return
        self._lastw[key] = v
        self.emit(eng, lambda e, h=sem.h, v=v: e.wait_ge(h, v))

    def op(self, eng, fn, sem=None, inc=1, drain=True):
        if drain and eng in ("vector", "scalar", "gpsimd"):
            self.emit(eng, lambda e: e.drain())
        if sem is not None:
            sem.n += inc
            self.emit(eng, lambda e, fn=fn, h=sem.h, k=inc: fn(e).then_inc(h, k))
        else:
            self.emit(eng, fn)


def build_nc():
    nc = bass.Bass("TRN2", target_bir_lowering=False)

    xin = nc.dram_tensor("xin", [D, T], F32, kind="ExternalInput").ap()
    wqT = nc.dram_tensor("wqT", [D, D], F32, kind="ExternalInput").ap()
    wkT = nc.dram_tensor("wkT", [D, D], F32, kind="ExternalInput").ap()
    wvT = nc.dram_tensor("wvT", [D, D], F32, kind="ExternalInput").ap()
    wgS = nc.dram_tensor("wgS", [384, D], F32, kind="ExternalInput").ap()
    wuS = nc.dram_tensor("wuS", [384, D], F32, kind="ExternalInput").ap()
    wdS = nc.dram_tensor("wdS", [128, II], F32, kind="ExternalInput").ap()
    clips = nc.dram_tensor("clips", [1, 2], F32, kind="ExternalInput").ap()
    identin = nc.dram_tensor("identin", [128, 128], F32, kind="ExternalInput").ap()
    identbin = nc.dram_tensor("identbin", [128, 128], BF16, kind="ExternalInput").ap()
    onesbcin = nc.dram_tensor("onesbcin", [1, 128], F32, kind="ExternalInput").ap()
    onesmin = nc.dram_tensor("onesmin", [128, 1], BF16, kind="ExternalInput").ap()
    masksin = nc.dram_tensor("masksin", [128, T], F32, kind="ExternalInput").ap()

    outT = nc.dram_tensor("outT", [D, T], F32, kind="ExternalOutput").ap()

    xn_dram = nc.dram_tensor("xn_dram", [D, T], F32).ap()
    x1_dram = nc.dram_tensor("x1_dram", [D, T], F32).ap()
    x2_dram = nc.dram_tensor("x2_dram", [D, T], F32).ap()
    h_dram = nc.dram_tensor("h_dram", [128, NIP * T], F32).ap()
    sh_d = nc.dram_tensor("sh_d", [128, SHW], BF16).ap()
    full_d = nc.dram_tensor("full_d", [128 * N_CORES, SHW], BF16,
                            addr_space="Shared").ap()
    sc_d = nc.dram_tensor("sc_d", [128, 8], F32).ap()
    scf_d = nc.dram_tensor("scf_d", [128 * N_CORES, 8], F32,
                           addr_space="Shared").ap()
    ccio = [(nc.dram_tensor(f"cc{i}_in", [1, 128], F32).ap(),
             nc.dram_tensor(f"cc{i}_out", [1, 128], F32).ap()) for i in range(4)]

    es = contextlib.ExitStack()
    arena = es.enter_context(nc.sbuf_tensor("arena", [128, AR_BYTES], mybir.dt.uint8)).ap()
    AF32 = arena.bitcast(F32)
    ABF = arena.bitcast(BF16)
    AI32 = arena.bitcast(I32)

    def f32(off, p0, p1, e0, e1):
        return AF32[p0:p1, off // 4 + e0: off // 4 + e1]

    def bf(off, p0, p1, e0, e1):
        return ABF[p0:p1, off // 2 + e0: off // 2 + e1]

    def i32(off, p0, p1, e0, e1):
        return AI32[p0:p1, off // 4 + e0: off // 4 + e1]

    ident = f32(IDENT, 0, 128, 0, 128)
    identb = bf(IDENTB, 0, 128, 0, 128)
    onesbc = f32(ONESBC, 0, 1, 0, 128)
    onesm = bf(ONESM, 0, 128, 0, 1)

    def rows(slot, a, b):
        return f32(ROWS, 0, 1, slot + a, slot + b)

    def col(idx, k=1):
        return f32(COLS, 0, 128, idx, idx + k)

    def scal(idx, k=1):
        return f32(SCAL, 0, 1, idx, idx + k)

    pb = [es.enter_context(nc.psum_tensor(f"pb{i}", [128, 512], F32)) for i in range(8)]
    pbb = [t.ap().bitcast(BF16) for t in pb]

    dS = Sem(es.enter_context(nc.semaphore("dS")))
    dA = Sem(es.enter_context(nc.semaphore("dA")))
    dG = Sem(es.enter_context(nc.semaphore("dG")))
    cV = Sem(es.enter_context(nc.semaphore("cV")))
    cS = Sem(es.enter_context(nc.semaphore("cS")))
    cT = Sem(es.enter_context(nc.semaphore("cT")))
    cc = Sem(es.enter_context(nc.semaphore("cc")))
    dQ = [Sem(es.enter_context(nc.semaphore(f"dQ{i}"))) for i in range(3)]
    dL = [Sem(es.enter_context(nc.semaphore(f"dL{i}"))) for i in range(2)]
    dGs = [Sem(es.enter_context(nc.semaphore(f"dGs{i}"))) for i in range(2)]
    dT = [Sem(es.enter_context(nc.semaphore(f"dT{i}"))) for i in range(8)]
    dR = Sem(es.enter_context(nc.semaphore("dR")))
    dC = Sem(es.enter_context(nc.semaphore("dC")))
    dRg = Sem(es.enter_context(nc.semaphore("dRg")))
    dD = [Sem(es.enter_context(nc.semaphore(f"dD{i}"))) for i in range(2)]
    dW = Sem(es.enter_context(nc.semaphore("dW")))
    dP = [Sem(es.enter_context(nc.semaphore(f"dP{i}"))) for i in range(2)]
    dXs = [Sem(es.enter_context(nc.semaphore(f"dXs{i}"))) for i in range(2)]
    dE = [Sem(es.enter_context(nc.semaphore(f"dE{i}"))) for i in range(2)]

    p = Prog()
    bankg = [None] * 8

    def bank_wait(i):
        if bankg[i] is not None:
            p.wait("tensor", bankg[i][0], bankg[i][1])

    def bcast_scalar_to_col(s_ap, cidx):
        bank_wait(7)
        p.wait("tensor", cV)
        p.op("tensor", lambda e, s=s_ap: e.matmul(pb[7][:, 0:1], onesbc, s,
                                                  start=True, stop=True), cT)
        p.wait("vector", cT)
        p.op("vector", lambda e, c=col(cidx): e.tensor_copy(c, pb[7][:, 0:1]), cV)
        bankg[7] = (cV, cV.n)

    def bcast_rows_to_tiles(jobs, rows_mark=None):
        nj = len(jobs)
        first = True
        for j, (slot, dst) in enumerate(jobs):
            for ch in range(NTC):
                bk = 2 + ch
                bank_wait(bk)
                p.wait("tensor", cV, rows_mark)
                p.op("tensor", lambda e, slot=slot, ch=ch, bk=bk:
                     e.matmul(pb[bk][:, 0:512], onesbc,
                              rows(slot, ch * 512, (ch + 1) * 512),
                              start=True, stop=True),
                     cT if ch == NTC - 1 else None)
            p.wait("vector", cT)
            for ch in range(NTC):
                p.op("vector", lambda e, ch=ch, dst=dst: e.tensor_copy(
                    f32(dst, 0, 128, ch * 512, (ch + 1) * 512),
                    pb[2 + ch][:, 0:512]), cV, drain=first)
                first = False
            for ch in range(NTC):
                bankg[2 + ch] = (cV, cV.n)

    def ln_stats_gen(src_dram, src_guards=None,
                     bufs=(XA, XB, XC, XD, 24576, 28672)):
        for i in range(2, 6):
            bank_wait(i)
        ldd = {}
        consumed = {}

        BXA, BXB, BXC0, BXD0, BXC1, BXD1 = bufs

        def ld_stats(dp):
            if dp >= NDP:
                return
            buf = BXA if dp % 2 == 0 else BXB
            q = "sync" if dp % 2 == 0 else "scalar"
            d = dL[dp % 2]
            if dp - 2 in consumed:
                p.wait(q, cV, consumed[dp - 2][0])
                p.wait(q, cS, consumed[dp - 2][1])
            else:
                p.wait(q, cV)
                p.wait(q, cS)
            if src_guards is not None:
                g = src_guards.get(dp)
                if g is not None:
                    for gg in g:
                        p.wait(q, gg[0], gg[1])
            p.dma(q, f32(buf, 0, 128, 0, T), src_dram[dp * 128:(dp + 1) * 128, :], d)
            ldd[dp] = (d, d.n)

        ld_stats(0)
        ld_stats(1)
        smm = {}
        for dp in range(NDP):
            buf = BXA if dp % 2 == 0 else BXB
            BXC = BXC0 if dp % 2 == 0 else BXC1
            BXD = BXD0 if dp % 2 == 0 else BXD1
            p.wait("vector", ldd[dp][0], ldd[dp][1])
            p.wait("vector", cT, smm.get(dp - 2, 0))
            p.op("vector", lambda e, b=buf, BXC=BXC: e.tensor_copy(
                bf(BXC, 0, 128, 0, T), f32(b, 0, 128, 0, T)), cV, drain=False)
            castm = cV.n
            p.wait("scalar", ldd[dp][0], ldd[dp][1])
            p.wait("scalar", cT, smm.get(dp - 2, 0))
            p.op("scalar", lambda e, b=buf, BXD=BXD: e.activation(
                bf(BXD, 0, 128, 0, T), f32(b, 0, 128, 0, T), AF.Square), cS, drain=False)
            sqm = cS.n
            consumed[dp] = (castm, sqm)
            ld_stats(dp + 2)
            yield
            p.wait("tensor", cV, castm)
            p.wait("tensor", cS, sqm)
            for tc in range(NTC):
                st, sp = (dp == 0), (dp == NDP - 1)
                p.op("tensor", lambda e, tc=tc, st=st, sp=sp, BXC=BXC:
                     e.matmul(pb[2 + tc][0:1, :], onesm,
                              bf(BXC, 0, 128, tc * 512, (tc + 1) * 512),
                              start=st, stop=sp), None)
                p.op("tensor", lambda e, tc=tc, st=st, sp=sp, BXD=BXD:
                     e.matmul(pb[2 + tc][32:33, :], onesm,
                              bf(BXD, 0, 128, tc * 512, (tc + 1) * 512),
                              start=st, stop=sp),
                     cT if tc == NTC - 1 else None)
            smm[dp] = cT.n
        ln_stats_finish(smm[NDP - 1])
        yield

    def ln_stats_finish(mm_mark):
        p.wait("vector", cT, mm_mark)
        for tc in range(NTC):
            p.op("vector", lambda e, tc=tc: e.tensor_copy(
                rows(R_MEAN, tc * 512, (tc + 1) * 512), pb[2 + tc][0:1, :]), cV, drain=(tc == 0))
            p.op("vector", lambda e, tc=tc: e.tensor_copy(
                rows(R_RSTD, tc * 512, (tc + 1) * 512), pb[2 + tc][32:33, :]), cV, drain=False)
        for i in range(2, 6):
            bankg[i] = (cV, cV.n)
        p.wait("scalar", cV)
        p.op("scalar", lambda e: e.activation(rows(R_G, 0, T), rows(R_MEAN, 0, T), AF.Square), cS)
        p.wait("vector", cS)
        p.op("vector", lambda e: e.scalar_tensor_tensor(
            rows(R_RSTD, 0, T), rows(R_G, 0, T), -1.0, rows(R_RSTD, 0, T),
            ALU.mult, ALU.add), cV)
        p.op("vector", lambda e: e.tensor_scalar(
            rows(R_RSTD, 0, T), rows(R_RSTD, 0, T), 1e-5, None, ALU.add), cV)
        p.wait("scalar", cV)
        p.op("scalar", lambda e: e.activation(rows(R_RSTD, 0, T), rows(R_RSTD, 0, T),
                                              AF.Ln), cS)
        p.op("scalar", lambda e: e.activation(rows(R_RSTD, 0, T), rows(R_RSTD, 0, T),
                                              AF.Exp, scale=-0.5), cS)
        p.wait("vector", cS)
        p.op("vector", lambda e: e.scalar_tensor_tensor(
            rows(R_G, 0, T), rows(R_MEAN, 0, T), -1.0, rows(R_RSTD, 0, T),
            ALU.mult, ALU.mult), cV)
        bcast_rows_to_tiles([(R_RSTD, FB), (R_G, GB)], rows_mark=cV.n)

    def allreduce_scalar_max(col_ap, cc_pair, out_scal_idx):
        cin, cout = cc_pair
        stg = f32(CCS, 0, 1, 0, 128)
        p.wait("gpsimd", cV)
        p.dma("gpsimd", cin, col_ap, dC)
        p.wait("gpsimd", dC)
        p.op("gpsimd", lambda e: e.collective_compute(
            "AllReduce", ALU.max, replica_groups=[list(range(N_CORES))],
            ins=[cin], outs=[cout]), cc, drain=False)
        p.wait("gpsimd", cc)
        p.dma("gpsimd", stg, cout, dC)
        p.wait("vector", dC)
        p.op("vector", lambda e: e.reduce_max(scal(out_scal_idx), stg, axis=AX), cV)

    def qw_gen(w_dram, n_pt, width, c_slot, qdst_off, spill_dram, stgs, qbs,
               spill_guard=None, buf_guard=0, qb_buf_guard=0):
        nb = width // 128
        w3d = ABF[:, qdst_off // 2: qdst_off // 2 + nb * n_pt * 128].rearrange(
            "p (b i) -> p b i", b=nb)
        if spill_guard is not None:
            p.wait("scalar", spill_guard[0], spill_guard[1])
        qb_guard = {}
        q1_done = {}
        q2_done = {}
        ld = {}

        def load(pt):
            if pt >= n_pt:
                return
            q = "sync" if pt % 2 == 0 else "scalar"
            d = dQ[pt % 3]
            p.wait(q, cV, q2_done.get(pt - 3, buf_guard))
            p.dma(q, f32(stgs[pt % 3], 0, 128, 0, width),
                  w_dram[pt * 128:(pt + 1) * 128, :], d)
            ld[pt] = (d, d.n)

        def stage_a(pt):
            stg = stgs[pt % 3]
            rc = C_GQ + (pt % 2)
            p.wait("vector", ld[pt][0], ld[pt][1])
            p.op("vector", lambda e, s=stg, pt=pt: e.reduce_max(
                col(c_slot + pt), f32(s, 0, 128, 0, width), axis=AX,
                apply_absolute_value=True), cV, drain=(pt == 0))
            p.wait("vector", cS, q1_done.get(pt - 2, 0))
            p.op("vector", lambda e, pt=pt, rc=rc: e.reciprocal(
                col(rc), col(c_slot + pt)), cV)
            p.op("vector", lambda e, rc=rc: e.tensor_scalar(
                col(rc), col(rc), QMAX, None, ALU.mult), cV)
            p.wait("scalar", cV)
            p.op("scalar", lambda e, s=stg, rc=rc: e.activation(
                i32(s, 0, 128, 0, width), f32(s, 0, 128, 0, width),
                AF.Identity, scale=col(rc)), cS, drain=False)
            q1_done[pt] = cS.n

        def stage_b(pt):
            stg = stgs[pt % 3]
            qb = qbs[pt % 2]
            p.wait("vector", cS, q1_done[pt])
            p.wait("vector", cT, qb_guard.get(pt - 2, qb_buf_guard))
            p.op("vector", lambda e, s=stg, qb=qb: e.tensor_scalar(
                bf(qb, 0, 128, 0, width), i32(s, 0, 128, 0, width),
                -128.0, 127.0, ALU.max, ALU.min), cV, drain=False)
            q2_done[pt] = cV.n
            for g in range((nb + 7) // 8):
                bk = 6 + (g % 2)
                kblk = min(8, nb - g * 8)
                bank_wait(bk)
                p.wait("tensor", cV, q2_done[pt])
                for k in range(kblk):
                    wb = g * 8 + k
                    p.op("tensor", lambda e, wb=wb, k=k, bk=bk, qb=qb: e.transpose(
                        pbb[bk][:, k * 128:(k + 1) * 128],
                        bf(qb, 0, 128, wb * 128, (wb + 1) * 128), identb),
                         cT if k == kblk - 1 else None)
                p.wait("scalar", cT)
                psrc = pbb[bk][:, 0:kblk * 128].rearrange("p (b i) -> p b i", b=kblk)
                p.op("scalar", lambda e, g=g, pt=pt, bk=bk, kblk=kblk, psrc=psrc:
                     e.activation(w3d[:, g * 8:g * 8 + kblk, pt * 128:(pt + 1) * 128],
                                  psrc, AF.Identity), cS, drain=False)
                bankg[bk] = (cS, cS.n)
            qb_guard[pt] = cT.n

        load(0)
        load(1)
        for pt in range(n_pt):
            stage_a(pt)
            if pt > 0:
                stage_b(pt - 1)
            load(pt + 2)
            yield
        stage_b(n_pt - 1)
        p.wait("sync", cS)
        p.dma("sync", spill_dram[:, 0:n_pt * width], bf(qdst_off, 0, 128, 0, n_pt * width), dS)

    def qw_gen_psum(w_dram, n_pt, width, c_slot, spill_dram, stgs, qbs,
                    buf_guard=0):
        nb = width // 128
        qb_guard = {}
        q1_done = {}
        q2_done = {}
        ld = {}

        def load(pt):
            if pt >= n_pt:
                return
            q = "sync" if pt % 2 == 0 else "scalar"
            d = dQ[pt % 3]
            p.wait(q, cV, q2_done.get(pt - 3, buf_guard))
            p.dma(q, f32(stgs[pt % 3], 0, 128, 0, width),
                  w_dram[pt * 128:(pt + 1) * 128, :], d)
            ld[pt] = (d, d.n)

        def stage_a(pt):
            stg = stgs[pt % 3]
            rc = C_GQ + (pt % 2)
            p.wait("vector", ld[pt][0], ld[pt][1])
            p.op("vector", lambda e, s=stg, pt=pt: e.reduce_max(
                col(c_slot + pt), f32(s, 0, 128, 0, width), axis=AX,
                apply_absolute_value=True), cV, drain=(pt == 0))
            p.wait("vector", cS, q1_done.get(pt - 2, 0))
            p.op("vector", lambda e, pt=pt, rc=rc: e.reciprocal(
                col(rc), col(c_slot + pt)), cV)
            p.op("vector", lambda e, rc=rc: e.tensor_scalar(
                col(rc), col(rc), QMAX, None, ALU.mult), cV)
            p.wait("scalar", cV)
            p.op("scalar", lambda e, s=stg, rc=rc: e.activation(
                i32(s, 0, 128, 0, width), f32(s, 0, 128, 0, width),
                AF.Identity, scale=col(rc)), cS, drain=False)
            q1_done[pt] = cS.n

        def stage_b(pt):
            stg = stgs[pt % 3]
            qb = qbs[pt % 2]
            p.wait("vector", cS, q1_done[pt])
            p.wait("vector", cT, qb_guard.get(pt - 2, 0))
            p.op("vector", lambda e, s=stg, qb=qb: e.tensor_scalar(
                bf(qb, 0, 128, 0, width), i32(s, 0, 128, 0, width),
                -128.0, 127.0, ALU.max, ALU.min), cV, drain=False)
            q2_done[pt] = cV.n
            for g in range((nb + 7) // 8):
                bk = 6 + (g % 2)
                kblk = min(8, nb - g * 8)
                bank_wait(bk)
                p.wait("tensor", cV, q2_done[pt])
                for k in range(kblk):
                    wb = g * 8 + k
                    p.op("tensor", lambda e, wb=wb, k=k, bk=bk, qb=qb: e.transpose(
                        pbb[bk][:, k * 128:(k + 1) * 128],
                        bf(qb, 0, 128, wb * 128, (wb + 1) * 128), identb),
                         cT if k == kblk - 1 else None)
                p.wait("scalar", cT)
                slot = 59392 if g % 2 == 0 else 61440
                p.wait("scalar", dP[g % 2], dP[g % 2].n)
                p.op("scalar", lambda e, bk=bk, kblk=kblk, slot=slot: e.activation(
                    bf(slot, 0, 128, 0, kblk * 128), pbb[bk][:, 0:kblk * 128],
                    AF.Identity), cS, drain=False)
                bankg[bk] = (cS, cS.n)
                p.wait("gpsimd", cS)
                c0 = (pt * nb + g * 8) * 128
                p.dma("gpsimd", spill_dram[:, c0:c0 + kblk * 128],
                      bf(slot, 0, 128, 0, kblk * 128), dP[g % 2])
            qb_guard[pt] = cT.n

        load(0)
        load(1)
        for pt in range(n_pt):
            stage_a(pt)
            if pt > 0:
                stage_b(pt - 1)
            load(pt + 2)
            yield
        stage_b(n_pt - 1)

    wq_fin = []

    def wq_sh_gen():
        units = ([("wg", s, D) for s in range(3)] +
                 [("wu", s, D) for s in range(3)] + [("wd", 0, II)])
        tmpc = [C_WGS, C_WGS + 1, C_WGS + 2, C_WUS, C_WUS + 1, C_WUS + 2, C_WDS]
        q1_done = {}
        q2_done = {}
        qb_guard = {}
        ld = {}

        def load(u):
            if u >= len(units):
                return
            kind, s, width = units[u]
            q = "sync" if u % 2 == 0 else "scalar"
            d = dQ[u % 3]
            p.wait(q, cV, q2_done.get(u - 3, 0))
            if kind == "wd":
                p.dma(q, f32(WDS_STG, 0, 128, 0, width), wdS, d)
            else:
                srcd = wgS if kind == "wg" else wuS
                p.dma(q, f32(WG_STG[u % 3], 0, 128, 0, width),
                      srcd[s * 128:(s + 1) * 128, :], d)
            ld[u] = (d, d.n)

        def stage_a(u):
            kind, s, width = units[u]
            stg = WDS_STG if kind == "wd" else WG_STG[u % 3]
            rc = C_GQ + (u % 2)
            p.wait("vector", ld[u][0], ld[u][1])
            p.op("vector", lambda e, stg=stg, width=width, u=u: e.reduce_max(
                col(tmpc[u]), f32(stg, 0, 128, 0, width), axis=AX,
                apply_absolute_value=True), cV, drain=(u == 0))
            p.op("vector", lambda e, u=u: e.tensor_copy(
                f32(59392, 0, 128, u, u + 1), col(tmpc[u])), cV)
            p.wait("vector", cS, q1_done.get(u - 2, 0))
            p.op("vector", lambda e, u=u, rc=rc: e.reciprocal(
                col(rc), col(tmpc[u])), cV)
            p.op("vector", lambda e, rc=rc: e.tensor_scalar(
                col(rc), col(rc), QMAX, None, ALU.mult), cV)
            p.wait("scalar", cV)
            p.op("scalar", lambda e, stg=stg, width=width, rc=rc: e.activation(
                i32(stg, 0, 128, 0, width), f32(stg, 0, 128, 0, width),
                AF.Identity, scale=col(rc)), cS, drain=False)
            q1_done[u] = cS.n

        def stage_b(u):
            kind, s, width = units[u]
            stg = WDS_STG if kind == "wd" else WG_STG[u % 3]
            qb = WD_QB1 if kind == "wd" else WG_QB[u % 2]
            nb = width // 128
            if kind == "wg":
                dst0 = s * 1024
            elif kind == "wu":
                dst0 = 3072 + s * 1024
            else:
                dst0 = 6144
            p.wait("vector", cS, q1_done[u])
            p.wait("vector", cT, qb_guard.get(u - 2, 0))
            p.op("vector", lambda e, stg=stg, qb=qb, width=width: e.tensor_scalar(
                bf(qb, 0, 128, 0, width), i32(stg, 0, 128, 0, width),
                -128.0, 127.0, ALU.max, ALU.min), cV, drain=False)
            q2_done[u] = cV.n
            for g in range((nb + 7) // 8):
                bk = 6 + (g % 2)
                kblk = min(8, nb - g * 8)
                bank_wait(bk)
                p.wait("tensor", cV, q2_done[u])
                for k in range(kblk):
                    wb = g * 8 + k
                    p.op("tensor", lambda e, wb=wb, k=k, bk=bk, qb=qb: e.transpose(
                        pbb[bk][:, k * 128:(k + 1) * 128],
                        bf(qb, 0, 128, wb * 128, (wb + 1) * 128), identb),
                         cT if k == kblk - 1 else None)
                p.wait("scalar", cT)
                p.op("scalar", lambda e, bk=bk, kblk=kblk, dst0=dst0, g=g: e.activation(
                    bf(SHB, 0, 128, dst0 + g * 1024, dst0 + g * 1024 + kblk * 128),
                    pbb[bk][:, 0:kblk * 128], AF.Identity), cS, drain=False)
                bankg[bk] = (cS, cS.n)
            qb_guard[u] = cT.n

        load(0)
        load(1)
        for u in range(len(units)):
            stage_a(u)
            if u > 0:
                stage_b(u - 1)
            load(u + 2)
            yield
        stage_b(len(units) - 1)
        p.wait("sync", cS)
        p.wait("sync", cV)
        p.dma("sync", sh_d, bf(SHB, 0, 128, 0, SHW), dS)
        p.op("vector", lambda e: e.tensor_scalar(
            f32(59392, 0, 128, 7, 8), f32(59392, 0, 128, 6, 7), 0.0,
            None, ALU.mult), cV)
        p.wait("sync", cV)
        p.dma("sync", sc_d, f32(59392, 0, 128, 0, 8), dS)
        p.wait("gpsimd", dS)
        p.op("gpsimd", lambda e: e.collective_compute(
            "AllGather", ALU.bypass, replica_groups=[list(range(N_CORES))],
            ins=[sh_d], outs=[full_d]), cc, drain=False)
        p.op("gpsimd", lambda e: e.collective_compute(
            "AllGather", ALU.bypass, replica_groups=[list(range(N_CORES))],
            ins=[sc_d], outs=[scf_d]), cc, drain=False)
        gat = cc.n

        def finish():
            p.wait("gpsimd", cc, gat)
            for c in range(N_CORES):
                n = 3 if c < 7 else 1
                p.dma("gpsimd", bf(W1, 0, 128, 3 * c * 1024, (3 * c + n) * 1024),
                      full_d[c * 128:(c + 1) * 128, 0:n * 1024], dRg)
                p.dma("gpsimd", bf(W2, 0, 128, 3 * c * 1024, (3 * c + n) * 1024),
                      full_d[c * 128:(c + 1) * 128, 3072:3072 + n * 1024], dRg)
                p.dma("gpsimd", f32(59392, 0, 128, c * 8, c * 8 + 8),
                      scf_d[c * 128:(c + 1) * 128, :], dRg)
            p.wait("vector", dRg)
            for c in range(N_CORES):
                n = 3 if c < 7 else 1
                p.op("vector", lambda e, c=c, n=n: e.tensor_copy(
                    col(C_WGS + 3 * c, n), f32(59392, 0, 128, c * 8, c * 8 + n)),
                    cV, drain=(c == 0))
                p.op("vector", lambda e, c=c, n=n: e.tensor_copy(
                    col(C_WUS + 3 * c, n),
                    f32(59392, 0, 128, c * 8 + 3, c * 8 + 3 + n)), cV)
                p.op("vector", lambda e, c=c: e.tensor_copy(
                    col(C_WDS + c), f32(59392, 0, 128, c * 8 + 6, c * 8 + 7)), cV)
        wq_fin.append(finish)

    def drive(*gens):
        live = list(gens)
        while live:
            for g in list(live):
                try:
                    next(g)
                except StopIteration:
                    live.remove(g)

    def drive_with(main, aux, aux_steps=2):
        while True:
            try:
                next(main)
            except StopIteration:
                return
            for _ in range(aux_steps):
                try:
                    next(aux)
                except StopIteration:
                    break

    def ffn(src_dram, dst_dram, ccA, ccB, wq_gens=None, stats_done=False,
            feed_stats=False, early=None, w2_tail=None):
        p.wait("vector", cT)
        p.wait("scalar", cT)
        p.wait("sync", dS)
        for s in dT + dD:
            p.wait("sync", s)
            p.wait("scalar", s)
            p.wait("vector", s)
        p.wait("scalar", dS)
        if not stats_done:
            if wq_gens is not None:
                drive_with(ln_stats_gen(src_dram), wq_gens, aux_steps=1)
            else:
                drive(ln_stats_gen(src_dram))
        ldx = {}
        spg = {}
        xn_mark = [0]

        def buf_of(dp):
            if early is not None and dp < 2:
                return 143360 if dp == 0 else 151552
            return XA if dp % 2 == 0 else XB

        def ld_xn(dp):
            if dp >= NDP:
                return
            buf = buf_of(dp)
            q = "sync" if dp % 2 == 0 else "scalar"
            d = dE[dp % 2] if (early is not None and dp < 2) else dL[dp % 2]
            if early is not None and dp < 2:
                p.wait(q, cT, early[0])
                for gg in early[1].get(dp, []):
                    p.wait(q, gg[0], gg[1])
            elif dp - 2 in spg:
                g = spg[dp - 2]
                p.wait(q, g[0], g[1])
            else:
                p.wait(q, cV)
                p.wait(q, cS)
            p.dma(q, f32(buf, 0, 128, 0, T), src_dram[dp * 128:(dp + 1) * 128, :], d)
            ldx[dp] = (d, d.n)

        def xn_gen():
            ld_xn(0)
            ld_xn(1)
            for dp in range(NDP):
                buf = buf_of(dp)
                p.wait("vector", ldx[dp][0], ldx[dp][1])
                p.op("vector", lambda e, b=buf: e.tensor_tensor(
                    f32(b, 0, 128, 0, T), f32(b, 0, 128, 0, T),
                    f32(FB, 0, 128, 0, T), ALU.mult), cV, drain=False)
                p.op("vector", lambda e, b=buf: e.tensor_tensor(
                    f32(b, 0, 128, 0, T), f32(b, 0, 128, 0, T),
                    f32(GB, 0, 128, 0, T), ALU.add), cV)
                p.op("vector", lambda e, dp=dp, b=buf: e.reduce_max(
                    col(C_T0 + dp), f32(b, 0, 128, 0, T), axis=AX,
                    apply_absolute_value=True), cV)
                q2 = "scalar" if dp % 2 == 0 else "sync"
                p.wait(q2, cV, cV.n)
                p.dma(q2, xn_dram[dp * 128:(dp + 1) * 128, :], f32(buf, 0, 128, 0, T),
                      dXs[dp % 2])
                spg[dp] = (dXs[dp % 2], dXs[dp % 2].n)
                if dp == 1 and w2_tail is not None:
                    w2_tail(cV.n, spg[0], spg[1])
                ld_xn(dp + 2)
                yield
            p.op("vector", lambda e: e.reduce_max(col(C_T0 + 8), col(C_T0, NDP), axis=AX), cV)
            xn_mark[0] = cV.n
            allreduce_scalar_max(col(C_T0 + 8), ccA, S_GMAX)
            yield

        if wq_gens is not None:
            drive(xn_gen(), wq_gens)
            while wq_fin:
                wq_fin.pop()()
        else:
            drive(xn_gen())
        p.op("vector", lambda e: e.tensor_tensor(scal(S_SX), scal(S_SIGX), scal(S_GMAX), ALU.mult), cV)
        p.op("vector", lambda e: e.tensor_scalar(scal(S_SX), scal(S_SX), 1.0 / QMAX, None, ALU.mult), cV)
        p.op("vector", lambda e: e.reciprocal(scal(S_RSX), scal(S_SX)), cV)
        p.op("vector", lambda e: e.tensor_scalar(scal(S_SX127), scal(S_SX), 1.0 / QMAX, None, ALU.mult), cV)
        bcast_scalar_to_col(scal(S_RSX), C_RS)
        bcast_scalar_to_col(scal(S_SX127), C_SX127)
        p.wait("vector", cS)
        p.op("vector", lambda e: e.tensor_scalar(
            col(C_GSC, NIP), col(C_WGS, NIP), col(C_SX127), None, ALU.mult), cV)
        p.op("vector", lambda e: e.tensor_scalar(
            col(C_USC, NIP), col(C_WUS, NIP), col(C_SX127), None, ALU.mult), cV)
        p.wait("sync", dXs[0])
        p.wait("sync", dXs[1])
        p.wait("scalar", dXs[0])
        p.wait("scalar", dXs[1])
        xn3 = xn_dram.rearrange("(a p) t -> p a t", p=128)
        ldq = {}
        qdone = {}

        def ld_xq(u):
            if u >= NDP // 2:
                return
            g = XG_A if u % 2 == 0 else XG_B
            q = "sync" if u % 2 == 0 else "scalar"
            d = dL[u % 2]
            if u - 2 in qdone:
                p.wait(q, cV, qdone[u - 2])
            else:
                p.wait(q, cV, xn_mark[0])
            p.dma(q, f32(g, 0, 128, 0, 2 * T), xn3[:, 2 * u:2 * u + 2, :], d)
            ldq[u] = (d, d.n)

        ld_xq(0)
        ld_xq(1)
        p.wait("vector", dP[0])
        p.wait("vector", dP[1])
        for u in range(NDP // 2):
            g = XG_A if u % 2 == 0 else XG_B
            p.wait("scalar", ldq[u][0], ldq[u][1])
            p.wait("scalar", cV)
            p.op("scalar", lambda e, g=g: e.activation(
                i32(g, 0, 128, 0, 2 * T), f32(g, 0, 128, 0, 2 * T),
                AF.Identity, scale=col(C_RS)), cS, drain=False)
            p.wait("vector", cS)
            p.op("vector", lambda e, g=g, u=u: e.tensor_scalar(
                bf(XQ, 0, 128, 2 * u * T, (2 * u + 2) * T), i32(g, 0, 128, 0, 2 * T),
                -128.0, 127.0, ALU.max, ALU.min), cV, drain=False)
            qdone[u] = cV.n
            ld_xq(u + 2)
        p.wait("tensor", cV)
        p.wait("tensor", dS)
        p.wait("tensor", dR)
        p.wait("tensor", dRg)
        for ib in range(NIP):
            for i in range(4):
                bank_wait(i)
            for dp in range(NDP):
                for tc in range(NTC):
                    st, sp = (dp == 0), (dp == NDP - 1)
                    p.op("tensor", lambda e, ib=ib, dp=dp, tc=tc, st=st, sp=sp:
                         e.matmul(pb[tc][:, 0:512],
                                  bf(W1, 0, 128, (ib * NDP + dp) * 128,
                                     (ib * NDP + dp + 1) * 128),
                                  bf(XQ, 0, 128, dp * T + tc * 512, dp * T + (tc + 1) * 512),
                                  start=st, stop=sp),
                         cT if (sp and tc == NTC - 1) else None)
            gate_done = cT.n
            for i in range(4, 8):
                bank_wait(i)
            for dp in range(NDP):
                for tc in range(NTC):
                    st, sp = (dp == 0), (dp == NDP - 1)
                    p.op("tensor", lambda e, ib=ib, dp=dp, tc=tc, st=st, sp=sp:
                         e.matmul(pb[4 + tc][:, 0:512],
                                  bf(W2, 0, 128, (ib * NDP + dp) * 128,
                                     (ib * NDP + dp + 1) * 128),
                                  bf(XQ, 0, 128, dp * T + tc * 512, dp * T + (tc + 1) * 512),
                                  start=st, stop=sp),
                         cT if (sp and tc == NTC - 1) else None)
            up_done = cT.n
            p.wait("scalar", cT, gate_done)
            p.wait("scalar", cV)
            for tc in range(NTC):
                p.op("scalar", lambda e, ib=ib, tc=tc: e.activation(
                    f32(SILU, 0, 128, tc * 512, (tc + 1) * 512), pb[tc][:, 0:512],
                    AF.Silu, scale=col(C_GSC + ib)), cS)
            for i in range(4):
                bankg[i] = (cS, cS.n)
            p.wait("vector", cT, up_done)
            p.wait("vector", cS)
            p.wait("vector", dG)
            for tc in range(NTC):
                p.op("vector", lambda e, ib=ib, tc=tc: e.scalar_tensor_tensor(
                    f32(HPRE, 0, 128, tc * 512, (tc + 1) * 512), pb[4 + tc][:, 0:512],
                    col(C_USC + ib), f32(SILU, 0, 128, tc * 512, (tc + 1) * 512),
                    ALU.mult, ALU.mult), cV)
                p.op("vector", lambda e, ib=ib, tc=tc: e.reduce_max(
                    f32(HMC, 0, 128, ib * NTC + tc, ib * NTC + tc + 1),
                    f32(HPRE, 0, 128, tc * 512, (tc + 1) * 512),
                    axis=AX, apply_absolute_value=True), cV)
                p.wait("gpsimd", cV)
                p.dma("gpsimd", h_dram[:, (tc * NIP + ib) * 512:(tc * NIP + ib + 1) * 512],
                      f32(HPRE, 0, 128, tc * 512, (tc + 1) * 512), dG)
            for i in range(4, 8):
                bankg[i] = (cV, cV.n)
        p.op("vector", lambda e: e.reduce_max(col(C_T0), f32(HMC, 0, 128, 0, NIP * NTC), axis=AX), cV)
        p.wait("scalar", cT)
        p.wait("scalar", dG)
        uld = {}
        p.dma("scalar", f32(HSB3[0], 0, 128, 0, 5632), h_dram[:, 0:5632], dQ[0])
        uld[0] = (dQ[0], dQ[0].n)
        p.dma("scalar", f32(HSB3[1], 0, 128, 0, 5632), h_dram[:, 5632:11264], dQ[1])
        uld[1] = (dQ[1], dQ[1].n)
        allreduce_scalar_max(col(C_T0), ccB, S_GMAX)
        p.op("vector", lambda e: e.tensor_tensor(scal(S_SH), scal(S_SIGD), scal(S_GMAX), ALU.mult), cV)
        p.op("vector", lambda e: e.tensor_scalar(scal(S_SH), scal(S_SH), 1.0 / QMAX, None, ALU.mult), cV)
        p.op("vector", lambda e: e.reciprocal(scal(S_RSH), scal(S_SH)), cV)
        p.op("vector", lambda e: e.tensor_scalar(scal(S_SH127), scal(S_SH), 0.5 / QMAX, None, ALU.mult), cV)
        bcast_scalar_to_col(scal(S_RSH), C_RH)
        bcast_scalar_to_col(scal(S_SH127), C_SH)
        p.op("vector", lambda e: e.tensor_scalar(
            col(C_DSC, NDP), col(C_WDS, NDP), col(C_SH), None, ALU.mult), cV)
        rh_mark = cV.n
        p.wait("sync", cT)
        p.wait("sync", dS)
        for c in range(N_CORES):
            p.dma("sync", bf(WD, 0, 128, c * NIP * 128, (c + 1) * NIP * 128),
                  full_d[c * 128:(c + 1) * 128, 6144:6144 + NIP * 128], dR)
        p.wait("tensor", dR)
        units = [(tc, hf) for tc in range(NTC) for hf in range(2)]
        uq_done = {}
        dn_store = {}
        sm_mm = {}
        sm_cast = {}
        sm_sq = {}

        def emit_sm_mms(g):
            if g < 0 or g not in sm_cast:
                return
            tc_g, hb_g = g // NDP, g % NDP
            if hb_g == 0:
                bank_wait(2 + tc_g)
            p.wait("tensor", cV, sm_cast[g])
            p.wait("tensor", cS, sm_sq[g])
            st2, sp2 = (hb_g == 0), (hb_g == NDP - 1)
            cstb, sqb = CST2[g % 4], SQ2[g % 4]
            p.op("tensor", lambda e, tc_g=tc_g, cstb=cstb, st2=st2, sp2=sp2:
                 e.matmul(pb[2 + tc_g][0:1, :], onesm,
                          bf(cstb, 0, 128, 0, 512), start=st2, stop=sp2), None)
            p.op("tensor", lambda e, tc_g=tc_g, sqb=sqb, st2=st2, sp2=sp2:
                 e.matmul(pb[2 + tc_g][32:33, :], onesm,
                          bf(sqb, 0, 128, 0, 512), start=st2, stop=sp2), cT)
            sm_mm[g] = cT.n

        def emit_unit_load(ui):
            if ui >= len(units):
                return
            tc, hf = units[ui]
            g = HSB3[ui % 2]
            p.wait("scalar", cV, uq_done[ui - 2])
            p.dma("scalar", f32(g, 0, 128, 0, 5632),
                  h_dram[:, (tc * NIP + hf * 11) * 512:(tc * NIP + hf * 11) * 512 + 5632],
                  dQ[ui % 2])
            uld[ui] = (dQ[ui % 2], dQ[ui % 2].n)

        def emit_unit_quant(ui):
            tc, hf = units[ui]
            g = HSB3[ui % 2]
            hq = HQ_A if tc % 2 == 0 else HQ_B
            p.wait("scalar", uld[ui][0], uld[ui][1])
            p.wait("scalar", cV, rh_mark)
            p.op("scalar", lambda e, g=g: e.activation(
                i32(g, 0, 128, 0, 5632), f32(g, 0, 128, 0, 5632),
                AF.Identity, scale=col(C_RH)), cS, drain=False)
            q1_mark = cS.n
            p.wait("vector", cS, q1_mark)
            p.op("vector", lambda e, g=g, hq=hq, hf=hf: e.tensor_scalar(
                bf(hq, 0, 128, hf * 5632, (hf + 1) * 5632), i32(g, 0, 128, 0, 5632),
                -128.0, 127.0, ALU.max, ALU.min), cV, drain=False)
            uq_done[ui] = cV.n

        p.wait("vector", dP[0])
        p.wait("vector", dP[1])
        p.wait("scalar", dP[0])
        p.wait("scalar", dP[1])
        emit_unit_quant(0)
        emit_unit_quant(1)
        for tc in range(NTC):
            if 2 * tc + 2 < len(units):
                emit_unit_load(2 * tc + 2)
                emit_unit_load(2 * tc + 3)
                emit_unit_quant(2 * tc + 2)
                emit_unit_quant(2 * tc + 3)
            hq = HQ_A if tc % 2 == 0 else HQ_B
            for hb in range(NDP):
                bk = (tc * NDP + hb) % 2
                bank_wait(bk)
                for ip in range(NIP):
                    st, sp = (ip == 0), (ip == NIP - 1)
                    if ip == 0:
                        p.wait("tensor", cV, uq_done[2 * tc])
                    elif ip == 11:
                        p.wait("tensor", cV, uq_done[2 * tc + 1])
                    p.op("tensor", lambda e, ip=ip, hb=hb, hq=hq, bk=bk, st=st, sp=sp:
                         e.matmul(pb[bk][:, 0:512],
                                  bf(WD, 0, 128, (hb * NIP + ip) * 128,
                                     (hb * NIP + ip + 1) * 128),
                                  bf(hq, 0, 128, ip * 512, (ip + 1) * 512),
                                  start=st, stop=sp),
                         cT if sp else None)
                g_i = tc * NDP + hb
                xr = XRA2 if g_i % 2 == 0 else XRB2
                ev2 = EVT2 if g_i % 2 == 0 else EVT2B
                dxr = dL[g_i % 2]
                p.wait("sync", cS)
                p.dma("sync", f32(xr, 0, 128, 0, 512),
                      src_dram[hb * 128:(hb + 1) * 128, tc * 512:(tc + 1) * 512], dxr)
                xr_mark = dxr.n
                p.wait("scalar", cT)
                p.wait("scalar", dxr, xr_mark)
                p.wait("scalar", cV)
                if g_i - 2 in dn_store:
                    s_prev = dn_store[g_i - 2][0]
                    p.wait("scalar", s_prev, s_prev.n)
                p.op("scalar", lambda e, bk=bk, hb=hb, ev2=ev2: e.activation(
                    f32(ev2, 0, 128, 0, 512), pb[bk][:, 0:512], AF.Identity,
                    scale=col(C_DSC + hb)), cS)
                bankg[bk] = (cS, cS.n)
                p.wait("vector", cS)
                p.op("vector", lambda e, xr=xr, ev2=ev2: e.tensor_tensor(
                    f32(ev2, 0, 128, 0, 512), f32(ev2, 0, 128, 0, 512),
                    f32(xr, 0, 128, 0, 512), ALU.add), cV)
                add_mark = cV.n
                p.wait("sync", cV)
                p.dma("sync", dst_dram[hb * 128:(hb + 1) * 128, tc * 512:(tc + 1) * 512],
                      f32(ev2, 0, 128, 0, 512), dD[g_i % 2])
                dn_store[g_i] = (dD[g_i % 2], dD[g_i % 2].n)
                if feed_stats:
                    emit_sm_mms(g_i - 4)
                    cstb = CST2[g_i % 4]
                    sqb = SQ2[g_i % 4]
                    if g_i - 4 in sm_mm:
                        p.wait("vector", cT, sm_mm[g_i - 4])
                    p.op("vector", lambda e, ev2=ev2, cstb=cstb: e.tensor_copy(
                        bf(cstb, 0, 128, 0, 512), f32(ev2, 0, 128, 0, 512)),
                        cV, drain=True)
                    sm_cast[g_i] = cV.n
                    p.wait("scalar", cV, add_mark)
                    if g_i - 4 in sm_mm:
                        p.wait("scalar", cT, sm_mm[g_i - 4])
                    p.op("scalar", lambda e, ev2=ev2, sqb=sqb: e.activation(
                        bf(sqb, 0, 128, 0, 512), f32(ev2, 0, 128, 0, 512),
                        AF.Square), cS, drain=False)
                    sm_sq[g_i] = cS.n

        if feed_stats:
            for g in range(NTC * NDP - 4, NTC * NDP):
                emit_sm_mms(g)
        return sm_mm.get(NTC * NDP - 1)

    scores_ct = [0]

    def attention(src_dram, dst_dram, stats2_gen=None, stats2_guards=None,
                  stats_premixed=None):
        p.wait("vector", cT)
        p.wait("scalar", cT)
        p.wait("sync", dS)
        for s in dT + dD:
            p.wait("sync", s)
            p.wait("scalar", s)
            p.wait("vector", s)
        p.wait("scalar", dS)
        if stats_premixed is not None:
            ln_stats_finish(stats_premixed)
        else:
            drive(ln_stats_gen(src_dram))
        for dp in range(NDP):
            buf = XA if dp % 2 == 0 else XB
            dxn = dL[dp % 2]
            p.wait("sync", cV)
            p.wait("sync", cS)
            p.dma("sync", f32(buf, 0, 128, 0, T), src_dram[dp * 128:(dp + 1) * 128, :], dxn)
            p.wait("vector", dxn, dxn.n)
            p.op("vector", lambda e, b=buf: e.tensor_tensor(
                f32(b, 0, 128, 0, T), f32(b, 0, 128, 0, T),
                f32(FB, 0, 128, 0, T), ALU.mult), cV, drain=False)
            p.op("vector", lambda e, dp=dp, b=buf: e.tensor_tensor(
                bf(XNT, 0, 128, dp * T, (dp + 1) * T), f32(b, 0, 128, 0, T),
                f32(GB, 0, 128, 0, T), ALU.add), cV)

        def proj_o(w_dram, dst_off, sc):
            p.wait("gpsimd", cT)
            for dp in range(NDP):
                p.dma("gpsimd", bf(WQB, 0, 128, dp * D, (dp + 1) * D),
                      w_dram[dp * 128:(dp + 1) * 128, :], dW)
            p.wait("tensor", dW)
            p.wait("tensor", cV)
            for ob in range(NDP):
                for tc in range(NTC):
                    bk = (ob * NTC + tc) % 2
                    bank_wait(bk)
                    for dp in range(NDP):
                        st, sp = (dp == 0), (dp == NDP - 1)
                        p.op("tensor", lambda e, dp=dp, ob=ob, tc=tc, bk=bk, st=st, sp=sp:
                             e.matmul(pb[bk][:, 0:512],
                                      bf(WQB, 0, 128, dp * D + ob * 128, dp * D + (ob + 1) * 128),
                                      bf(XNT, 0, 128, dp * T + tc * 512, dp * T + (tc + 1) * 512),
                                      start=st, stop=sp),
                             cT if sp else None)
                    p.wait("scalar", cT)
                    p.op("scalar", lambda e, ob=ob, tc=tc, bk=bk, dst=dst_off, sc=sc:
                         e.activation(bf(dst, 0, 128, ob * T + tc * 512, ob * T + (tc + 1) * 512),
                                      pb[bk][:, 0:512], AF.Identity, scale=sc), cS)
                    bankg[bk] = (cS, cS.n)

        def proj_v(w_dram, dst_off):
            p.wait("gpsimd", cT)
            for dp in range(NDP):
                p.dma("gpsimd", bf(WQB, 0, 128, dp * D, (dp + 1) * D),
                      w_dram[dp * 128:(dp + 1) * 128, :], dW)
            p.wait("tensor", dW)
            p.wait("tensor", cV)
            for tb in range(NKT):
                for oc in range(2):
                    bk = (tb * 2 + oc) % 2
                    bank_wait(bk)
                    for dp in range(NDP):
                        st, sp = (dp == 0), (dp == NDP - 1)
                        p.op("tensor", lambda e, dp=dp, tb=tb, oc=oc, bk=bk, st=st, sp=sp:
                             e.matmul(pb[bk][:, 0:512],
                                      bf(XNT, 0, 128, dp * T + tb * 128, dp * T + (tb + 1) * 128),
                                      bf(WQB, 0, 128, dp * D + oc * 512, dp * D + (oc + 1) * 512),
                                      start=st, stop=sp),
                             cT if sp else None)
                    p.wait("scalar", cT)
                    p.op("scalar", lambda e, tb=tb, oc=oc, bk=bk, dst=dst_off:
                         e.activation(bf(dst, 0, 128, tb * D + oc * 512, tb * D + (oc + 1) * 512),
                                      pb[bk][:, 0:512], AF.Identity, scale=1.0), cS)
                    bankg[bk] = (cS, cS.n)

        proj_o(wqT, QT_, 1.0 / 32.0)
        proj_o(wkT, KT_, 1.0)
        proj_v(wvT, V_)

        mm_done = {}
        exp_done = {}
        wks_of = {}

        def emit_scores_mm(kt):
            jd = kt // 4
            base = 0 if kt % 2 == 0 else 4
            for b2 in range(base, base + 4):
                bank_wait(b2)
            p.wait("tensor", cS, exp_done.get(kt - 2, 0))
            p.wait("tensor", cV, norm_done.get(kt - 2, 0))
            for j in range(jd, NTC):
                bk = base + (j - jd)
                for dp in range(NDP):
                    st, sp = (dp == 0), (dp == NDP - 1)
                    p.op("tensor", lambda e, kt=kt, j=j, dp=dp, bk=bk, st=st, sp=sp:
                         e.matmul(pb[bk][:, 0:512],
                                  bf(KT_, 0, 128, dp * T + kt * 128, dp * T + (kt + 1) * 128),
                                  bf(QT_, 0, 128, dp * T + j * 512, dp * T + (j + 1) * 512),
                                  start=st, stop=sp),
                         cT if sp else None)
            mm_done[kt] = cT.n

        norm_done = {}

        def emit_softmax(kt):
            jd, m = kt // 4, kt % 4
            nch = NTC - jd
            base = 0 if kt % 2 == 0 else 4
            wks = WKS if kt % 2 == 0 else WKS_B
            c0 = C_T0 if kt % 2 == 0 else C_T0 + 10
            cb_, cs_ = c0 + 4, c0 + 5
            cr_ = c0 + 9
            wks_of[kt] = wks
            p.wait("vector", cT, mm_done[kt])
            p.wait("vector", cS, exp_done.get(kt - 2, 0))
            p.op("vector", lambda e, m=m, bk=base, wks=wks: e.tensor_tensor(
                f32(wks, 0, 128, 0, 512), pb[bk][:, 0:512],
                f32(MASKS, 0, 128, m * 512, (m + 1) * 512), ALU.add), cV)
            p.op("vector", lambda e, wks=wks, c0=c0: e.reduce_max(
                col(c0), f32(wks, 0, 128, 0, 512), axis=AX, negate=True), cV)
            for j in range(jd + 1, NTC):
                p.op("vector", lambda e, j=j, jd=jd, bk=base + (j - jd), c0=c0: e.reduce_max(
                    col(c0 + j - jd), pb[bk][:, 0:512], axis=AX, negate=True), cV,
                    drain=False)
            if nch > 1:
                p.op("vector", lambda e, n=nch, c0=c0, cb_=cb_: e.tensor_reduce(
                    col(cb_), col(c0, n), axis=AX, op=ALU.min), cV)
            else:
                p.op("vector", lambda e, c0=c0, cb_=cb_: e.tensor_copy(col(cb_), col(c0)), cV)
            p.wait("scalar", cV)
            for j in range(jd, NTC):
                p.op("scalar", lambda e, kt=kt, j=j, jd=jd, bk=base + (j - jd), wks=wks,
                     cb_=cb_, cs_=cs_:
                     e.activation(bf(ATT, 0, 128, kt * T + j * 512, kt * T + (j + 1) * 512),
                                  (f32(wks, 0, 128, 0, 512) if j == jd else pb[bk][:, 0:512]),
                                  AF.Exp, bias=col(cb_),
                                  accum_out=col(cs_ + j - jd)), cS, drain=False)
            exp_done[kt] = cS.n
            p.wait("vector", cS)
            if nch > 1:
                p.op("vector", lambda e, n=nch, cs_=cs_, cr_=cr_: e.tensor_reduce(
                    col(cr_), col(cs_, n), axis=AX, op=ALU.add), cV)
            else:
                p.op("vector", lambda e, cs_=cs_, cr_=cr_: e.tensor_copy(col(cr_), col(cs_)), cV)
            p.op("vector", lambda e, cr_=cr_: e.reciprocal(col(cr_), col(cr_)), cV)
            p.op("vector", lambda e, kt=kt, jd=jd, cr_=cr_: e.tensor_scalar(
                bf(ATT, 0, 128, kt * T + jd * 512, (kt + 1) * T),
                bf(ATT, 0, 128, kt * T + jd * 512, (kt + 1) * T),
                col(cr_), None, ALU.mult), cV)
            norm_done[kt] = cV.n

        emit_scores_mm(0)
        for kt in range(1, NKT):
            emit_scores_mm(kt)
            emit_softmax(kt - 1)
        emit_softmax(NKT - 1)
        scores_ct[0] = mm_done[NKT - 1]
        for i in range(8):
            bankg[i] = (cV, norm_done[NKT - 1])
        groups = [(db, j) for db in range(NDP) for j in range(NTC)]
        xl_done = {}
        ev_done = {}
        st_done = {}

        def ld_av(gi):
            if gi >= len(groups):
                return
            db, j = groups[gi]
            xr = XRA if gi % 2 == 0 else XRB
            p.wait("gpsimd", cT, mm_done[NKT - 1])
            p.wait("gpsimd", cV, ev_done.get(gi - 2, 0))
            p.dma("gpsimd", f32(xr, 0, 128, 0, 512),
                  src_dram[db * 128:(db + 1) * 128, j * 512:(j + 1) * 512], dGs[gi % 2])
            xl_done[gi] = (dGs[gi % 2], dGs[gi % 2].n)

        ld_av(0)
        ld_av(1)
        s2_steps = 0
        for gi, (db, j) in enumerate(groups):
            bk = gi % 2
            bank_wait(bk)
            nkt = min(4 * j + 4, NKT)
            for kt in range(nkt):
                st, sp = (kt == 0), (kt == nkt - 1)
                p.op("tensor", lambda e, kt=kt, db=db, j=j, bk=bk, st=st, sp=sp:
                     e.matmul(pb[bk][:, 0:512],
                              bf(V_, 0, 128, kt * D + db * 128, kt * D + (db + 1) * 128),
                              bf(ATT, 0, 128, kt * T + j * 512, kt * T + (j + 1) * 512),
                              start=st, stop=sp),
                     cT if sp else None)
            xr = XRA if gi % 2 == 0 else XRB
            ev = EVT if gi % 2 == 0 else EVT_B
            p.wait("vector", cT)
            p.wait("vector", xl_done[gi][0], xl_done[gi][1])
            if gi - 2 in st_done:
                s_prev = st_done[gi - 2][0]
                p.wait("vector", s_prev, s_prev.n)
            p.op("vector", lambda e, bk=bk, xr=xr, ev=ev: e.tensor_tensor(
                f32(ev, 0, 128, 0, 512), pb[bk][:, 0:512],
                f32(xr, 0, 128, 0, 512), ALU.add), cV)
            bankg[bk] = (cV, cV.n)
            ev_done[gi] = cV.n
            p.wait("sync", cV)
            p.dma("sync", dst_dram[db * 128:(db + 1) * 128, j * 512:(j + 1) * 512],
                  f32(ev, 0, 128, 0, 512), dT[db])
            st_done[gi] = (dT[db], dT[db].n)
            ld_av(gi + 2)
            if j == NTC - 1:
                if stats2_guards is not None:
                    stats2_guards[db] = [(dT[db], dT[db].n)]
                if stats2_gen is not None and db >= 2:
                    try:
                        next(stats2_gen)
                        s2_steps += 1
                    except StopIteration:
                        pass

    p.dma("sync", ident, identin, dS)
    p.dma("sync", identb, identbin, dS)
    p.dma("sync", onesbc, onesbcin, dS)
    p.dma("sync", onesm, onesmin, dS)
    p.dma("sync", f32(MASKS, 0, 128, 0, T), masksin, dS)
    p.dma("sync", scal(S_T0, 2), clips, dS)
    p.wait("scalar", dS)
    p.op("scalar", lambda e: e.activation(scal(S_SIGX, 2), scal(S_T0, 2), AF.Sigmoid), cS)
    p.wait("vector", dS)

    smm1 = ffn(xin, x1_dram, ccio[0], ccio[1], wq_gens=wq_sh_gen(),
               feed_stats=True)
    s2_guards = {}
    s2 = ln_stats_gen(x2_dram, src_guards=s2_guards,
                      bufs=(106496, 114688, 122880, 126976, 131072, 135168))
    attention(x1_dram, x2_dram, stats2_gen=s2, stats2_guards=s2_guards,
              stats_premixed=smm1)
    drive(s2)

    p.wait("gpsimd", cT)
    p.wait("gpsimd", cV)
    p.wait("gpsimd", cS)
    for c in range(N_CORES):
        n = 3 if c < 7 else 1
        p.dma("gpsimd", bf(W1, 0, 128, 3 * c * 1024, (3 * c + n) * 1024),
              full_d[c * 128:(c + 1) * 128, 0:n * 1024], dRg)
        if c < 4:
            p.dma("gpsimd", bf(W2, 0, 128, 3 * c * 1024, (3 * c + n) * 1024),
                  full_d[c * 128:(c + 1) * 128, 3072:3072 + n * 1024], dRg)

    def w2_tail_reload(mark, sp0, sp1):
        p.wait("gpsimd", cV, mark)
        p.wait("gpsimd", sp0[0], sp0[1])
        p.wait("gpsimd", sp1[0], sp1[1])
        for c in range(4, N_CORES):
            n = 3 if c < 7 else 1
            p.dma("gpsimd", bf(W2, 0, 128, 3 * c * 1024, (3 * c + n) * 1024),
                  full_d[c * 128:(c + 1) * 128, 3072:3072 + n * 1024], dRg)

    ffn(x2_dram, outT, ccio[2], ccio[3], stats_done=True,
        early=(scores_ct[0], s2_guards), w2_tail=w2_tail_reload)

    p.wait("sync", dS)
    p.wait("sync", dA)
    p.wait("sync", dG)
    for s in dQ + dL + dGs + dXs + dE + dT + dD + dP + [dR, dRg, dW, dC]:
        p.wait("sync", s)
    p.wait("sync", cV)
    p.wait("sync", cS)
    p.wait("sync", cT)
    p.wait("sync", cc)

    with nc.Block() as block:
        for eng in Prog.ENGS:
            fns = p.items[eng]
            getattr(block, eng)(lambda e, fns=fns: [fn(e) for fn in fns])

    return nc


_NC_CACHE = {}


def kernel(L0_attn_x, L0_attn_W_q, L0_attn_W_k, L0_attn_W_v,
           L0_ffn_clip_x, L0_ffn_clip_down, L0_ffn_w_gate, L0_ffn_w_up,
           L0_ffn_w_down):
    import ml_dtypes
    x = np.asarray(L0_attn_x)
    wq = np.ascontiguousarray(np.asarray(L0_attn_W_q).T)
    wk = np.ascontiguousarray(np.asarray(L0_attn_W_k).T)
    wv = np.ascontiguousarray(np.asarray(L0_attn_W_v).T)
    wg = np.ascontiguousarray(np.asarray(L0_ffn_w_gate))
    wu = np.ascontiguousarray(np.asarray(L0_ffn_w_up))
    wd = np.ascontiguousarray(np.asarray(L0_ffn_w_down))
    wgSs, wuSs, wdSs = [], [], []
    for c in range(N_CORES):
        pts = [min(3 * c + s, II // 128 - 1) for s in range(3)]
        wgSs.append(np.concatenate([wg[t * 128:(t + 1) * 128] for t in pts], axis=0))
        wuSs.append(np.concatenate([wu[t * 128:(t + 1) * 128] for t in pts], axis=0))
        wdSs.append(np.ascontiguousarray(wd[c * 128:(c + 1) * 128]))
    clips = np.array([[np.asarray(L0_ffn_clip_x).reshape(-1)[0],
                       np.asarray(L0_ffn_clip_down).reshape(-1)[0]]], dtype=np.float32)
    ident = np.eye(128, dtype=np.float32)
    identb = np.eye(128, dtype=ml_dtypes.bfloat16)
    onesbc = np.ones((1, 128), dtype=np.float32)
    onesm = np.full((128, 1), 1.0 / 1024.0, dtype=ml_dtypes.bfloat16)
    r = np.arange(512)[None, :]
    pp = np.arange(128)[:, None]
    masks = np.concatenate(
        [np.where(r >= 128 * m + pp, 0.0, NEG).astype(np.float32) for m in range(4)],
        axis=1)

    if "nc" not in _NC_CACHE:
        _NC_CACHE["nc"] = build_nc()
    nc = _NC_CACHE["nc"]

    in_maps = []
    for b in range(N_CORES):
        in_maps.append({
            "xin": np.ascontiguousarray(x[b].T),
            "wqT": wq, "wkT": wk, "wvT": wv,
            "wgS": wgSs[b], "wuS": wuSs[b], "wdS": wdSs[b],
            "clips": clips, "identin": ident, "identbin": identb, "onesbcin": onesbc,
            "onesmin": onesm, "masksin": masks,
        })
    res = run_bass_kernel_spmd(nc, in_maps, core_ids=list(range(N_CORES)))
    out = np.stack([np.ascontiguousarray(res.results[b]["outT"].T)
                    for b in range(N_CORES)], axis=0)
    return out
